# version 01
# baseline (speedup 1.0000x reference)
import numpy as np
import concourse.bass as bass
import concourse.mybir as mybir
from concourse.alu_op_type import AluOpType as op
from concourse.bass_utils import run_bass_kernel_spmd

F32 = mybir.dt.float32
BF16 = mybir.dt.bfloat16
AF = mybir.ActivationFunctionType
AXX = mybir.AxisListType.X

T, D = 2048, 1024
NT, NC = T // 128, D // 128
NEXP, FH = 16, 512
HH, NW, HD, FI = 16, 16, 64, 2816
TC, NTC = 256, T // 256


class B:

    def __init__(self):
        self.nc = bass.Bass("TRN2", target_bir_lowering=False)
        self.sems, self.cnt, self.wm = {}, {}, {}

    def sem(self, name):
        if name not in self.sems:
            self.sems[name] = self.nc.alloc_semaphore(name=name)
            self.cnt[name] = 0
        return self.sems[name]

    def inc(self, instr, name, n=1):
        instr.then_inc(self.sem(name), n)
        self.cnt[name] += n
        return self.cnt[name]

    def wait(self, eng, name, val):
        if val is None or val <= 0:
            return
        wm = self.wm.setdefault(str(eng.engine), {})
        if wm.get(name, -1) >= val:
            return
        eng.wait_ge(self.sem(name), val)
        wm[name] = val

    def step(self, **fns):
        with self.nc.Block() as blk:
            for nm, fn in fns.items():
                getattr(blk, nm)(fn)


def build(upto="full", dbg=(), l1a_tiles=NT, l1a_parts=7):
    b = B()
    nc = b.nc
    SB = nc.alloc_sbuf_tensor
    PS = nc.alloc_psum_tensor

    x_in = nc.dram_tensor("x", [T, D], F32, kind="ExternalInput").ap()
    w_qkv = nc.dram_tensor("L0_attn_w_qkv", [3 * D, D], F32, kind="ExternalInput").ap()
    w_o0 = nc.dram_tensor("L0_attn_w_o", [D, D], F32, kind="ExternalInput").ap()
    Wr = nc.dram_tensor("L0_ffn_Wr", [D, NEXP], F32, kind="ExternalInput").ap()
    Wg = nc.dram_tensor("L0_ffn_Wg", [NEXP, D, FH], F32, kind="ExternalInput").ap()
    Wu = nc.dram_tensor("L0_ffn_Wu", [NEXP, D, FH], F32, kind="ExternalInput").ap()
    Wd = nc.dram_tensor("L0_ffn_Wd", [NEXP, FH, D], F32, kind="ExternalInput").ap()
    Sg = nc.dram_tensor("L0_ffn_Sg", [D, 2 * FH], F32, kind="ExternalInput").ap()
    Su = nc.dram_tensor("L0_ffn_Su", [D, 2 * FH], F32, kind="ExternalInput").ap()
    Sd = nc.dram_tensor("L0_ffn_Sd", [2 * FH, D], F32, kind="ExternalInput").ap()
    Wq1 = nc.dram_tensor("L1_attn_Wq", [D, HH * NW], F32, kind="ExternalInput").ap()
    bq1 = nc.dram_tensor("L1_attn_bq", [HH * NW], F32, kind="ExternalInput").ap()
    Wk1 = nc.dram_tensor("L1_attn_Wk", [D, HH * NW], F32, kind="ExternalInput").ap()
    bk1 = nc.dram_tensor("L1_attn_bk", [HH * NW], F32, kind="ExternalInput").ap()
    Wv1 = nc.dram_tensor("L1_attn_Wv", [D, D], F32, kind="ExternalInput").ap()
    bv1 = nc.dram_tensor("L1_attn_bv", [D], F32, kind="ExternalInput").ap()
    Wo1 = nc.dram_tensor("L1_attn_Wo", [D, D], F32, kind="ExternalInput").ap()
    bo1 = nc.dram_tensor("L1_attn_bo", [D], F32, kind="ExternalInput").ap()
    qph = nc.dram_tensor("L1_attn_q_phase", [1, HH, 1, NW], F32, kind="ExternalInput").ap()
    kph = nc.dram_tensor("L1_attn_k_phase", [1, HH, 1, NW], F32, kind="ExternalInput").ap()
    temp = nc.dram_tensor("L1_attn_temperature", [1], F32, kind="ExternalInput").ap()
    wg1 = nc.dram_tensor("L1_ffn_w_gate", [FI, D], F32, kind="ExternalInput").ap()
    wu1 = nc.dram_tensor("L1_ffn_w_up", [FI, D], F32, kind="ExternalInput").ap()
    wd1 = nc.dram_tensor("L1_ffn_w_down", [D, FI], F32, kind="ExternalInput").ap()
    out = nc.dram_tensor("out", [T, D], F32, kind="ExternalOutput").ap()
    xspill = nc.dram_tensor("xspill", [T, D], F32, kind="Internal").ap()
    xqspill = nc.dram_tensor("xqspill", [128, NC, T], BF16, kind="Internal").ap()
    dbg_out = {}
    for name, shape, dt in dbg:
        dbg_out[name] = nc.dram_tensor("dbg_" + name, shape, dt, kind="ExternalOutput").ap()

    _off = [16512]

    def SBp(name, shape, dtype, at=None):
        o = _off[0] if at is None else at
        h = nc.alloc_sbuf_tensor_at(name, shape, dtype, offset=o)
        if at is None:
            nbytes = int(np.prod(shape[1:])) * (2 if dtype == BF16 else 4)
            _off[0] = o + ((nbytes + 31) // 32) * 32
        return h.ap(), o

    def SBn(name, shape, dtype, at=None):
        return SBp(name, shape, dtype, at)[0]

    ident_f = SBn("ident_f", [128, 128], F32)
    ident_b = SBn("ident_b", [128, 128], BF16)
    ones_col = SBn("ones_col", [128, 1], F32)
    ones_colb = SBn("ones_colb", [128, 1], BF16)
    ones_row = SBn("ones_row", [1, 128], F32)
    neg1 = SBn("neg1", [128, 1], F32)
    rtemp = SBn("rtemp", [128, 1], F32)
    mv_all = SBn("mv_all", [128, NT, 2], F32)
    rstd_all = SBn("rstd_all", [128, NT], F32)
    bn6 = SBn("bn6", [128, 2, 6], F32)
    cw_all, cw_off = SBp("cw_all", [128, NT, NEXP], F32)
    cwb = SBn("cwb", [128, TC], F32)
    cwrow = SBn("cwrow", [1, TC], F32)
    wr_hi, wrh_off = SBp("wr_hi", [128, NC, NEXP], BF16)
    wr_lo = SBn("wr_lo", [128, NC, NEXP], BF16)
    wr_f = SBn("wr_f", [128, NC, NEXP], F32)
    val8 = SBn("val8", [128, 8], F32)
    a08 = SBn("a08", [128, 8], F32)
    row_scr, row_off = SBp("row_scr", [1, D], F32)
    h2_scr = SBn("h2_scr", [128, 4, TC], BF16, at=row_off)
    sg_scr, sg_off = SBp("sg_scr", [128, 4, TC], F32)
    h_scr, h_off = SBp("h_scr", [128, 4, TC], BF16)
    attn0b, a0_off = SBp("attn0b", [128, D], F32)
    bcast_a, bc_off = SBp("bcast_a", [128, D], F32)
    xn_scr, xns_off = SBp("xn_scr", [128, 2, D], F32)
    xnT_hi, xh_off = SBp("xnT_hi", [128, NC, T], BF16)
    xnT_lo, xl_off = SBp("xnT_lo", [128, NC, T], BF16)
    racc = xnT_lo
    ws0, ws0_off = SBp("wslot0", [128, 16 * FH], BF16)
    ws1, ws1_off = SBp("wslot1", [128, 16 * FH], BF16)
    wslot = [ws0, ws1]
    wd_buf, wd_off = SBp("wd_buf", [128, 4, D], BF16)
    wd_buf2, wd2_off = SBp("wd_buf2", [128, 4, D], BF16)
    x_res, x_off = SBp("x_res", [128, NT, D], F32)
    assert _off[0] <= 229344, f"sbuf map overflow: {_off[0]}"
    comb_scr = bcast_a.rearrange("p (c x) -> p c x", c=8)
    stream_a = [xn_scr[:, i, :] for i in range(2)]
    ttr_scr = sg_scr.rearrange("p a c -> p (a c)")
    qTpad = racc[:, 0:4, :]
    kTpad = racc[:, 4:8, :]
    Vbuf = SBn("Vbuf", [128, NT, D], BF16, at=x_off)
    attn1_tok = SBn("attn1_tok", [128, NT, D], BF16, at=x_off + 32768)
    wgqT = SBn("wgqT", [128, NC, FI], BF16, at=xh_off)
    wuqT = SBn("wuqT", [128, NC, FI], BF16, at=xh_off + 45056)
    wdqT = SBn("wdqT", [128, 22, D], BF16, at=xh_off + 90112)
    P_pool = [SBn(f"Ppool{i}", [128, 512], BF16, at=xns_off + 1024 * i) for i in range(12)]
    attnT = SBn("attnT", [128, NC, 512], BF16, at=wd2_off)
    hq_strip = SBn("hq_strip", [128, 22, 128], F32, at=x_off + 20480)
    hqT_scr = SBn("hqT_scr", [128, 22, 128], BF16, at=xns_off)
    hf_scr = SBn("hf_scr", [128, 512], F32, at=sg_off)
    sgf_scr = SBn("sgf_scr", [128, 512], F32, at=sg_off + 2048)
    out_sb = [SBn(f"out_sb{i}", [128, D], F32, at=row_off if i == 0 else a0_off)
              for i in range(2)]
    xq_tileA = [SBn(f"xq_tileA{i}", [128, NC, 128], BF16, at=xl_off + 2048 * i)
                for i in range(2)]
    xq_tile = [SBn(f"xq_tileD{i}", [128, NC, 128], BF16, at=bc_off + 2048 * i)
               for i in range(2)]
    wstr = [SBn("wstr0", [128, 1024], F32, at=a0_off),
            SBn("wstr1", [128, 1024], F32, at=bc_off)]
    l1scr = SBn("l1scr", [128, 768], F32, at=sg_off)

    banks = [PS(f"bank{i}", [128, 512], F32).ap() for i in range(8)]

    tok = {}

    def ph_consts():
        def g(g):
            g.memset(ident_f, 0.0)
            g.memset(ident_b, 0.0)
            g.memset(ones_col, 1.0)
            g.memset(ones_colb, 1.0)
            g.memset(ones_row, 1.0)
            g.memset(neg1, -1.0)
            g.drain()
            g.affine_select(out=ident_f, in_=ident_f, compare_op=op.not_equal,
                            fill=1.0, base=0, pattern=[[-1, 128]], channel_multiplier=1)
            i2 = g.affine_select(out=ident_b, in_=ident_b, compare_op=op.not_equal,
                                 fill=1.0, base=0, pattern=[[-1, 128]], channel_multiplier=1)
            b.inc(i2, "g")
            tok["consts"] = b.cnt["g"]
        b.step(gpsimd=g)

    def ph_ln_stats(tag, gate=None):
        def v(v):
            if gate is None:
                b.wait(v, "sx", tok["x_loaded"])
            else:
                b.wait(v, gate[0], gate[1])
            for t in range(NT):
                xt = x_res[:, t, :]
                v.bn_stats(bn6[:, 0:1, :], xt[:, 0:512].rearrange("p (a c) -> p a c", a=1))
                v.bn_stats(bn6[:, 1:2, :], xt[:, 512:1024].rearrange("p (a c) -> p a c", a=1))
                v.drain()
                v.bn_aggr(mv_all[:, t, :], bn6.rearrange("p a b -> p (a b)"))
                v.drain()
            ii = v.tensor_scalar(rstd_all, mv_all[:, :, 1], 1e-5, None, op.add)
            tok[f"vareps{tag}"] = b.inc(ii, "v")

        def s(s):
            b.wait(s, "v", tok[f"vareps{tag}"])
            ii = s.activation(rstd_all, rstd_all, AF.Sqrt)
            tok[f"sqrt{tag}"] = b.inc(ii, "s")
        b.step(vector=v, scalar=s)

        def v2(v):
            b.wait(v, "s", tok[f"sqrt{tag}"])
            ii = v.reciprocal(rstd_all, rstd_all)
            tok[f"rstd{tag}"] = b.inc(ii, "v")
        b.step(vector=v2)

    def ph_l0a():
        def y(y):
            for i in range(4):
                dma = y.dma_start(out=x_res[:, 4 * i:4 * i + 4, :],
                                  in_=x_in.rearrange("(n p) d -> p n d", p=128)[:, 4 * i:4 * i + 4, :])
                b.inc(dma, "sx", 16)
            tok["x_loaded"] = b.cnt["sx"]
            dma = y.dma_start(out=wr_f, in_=Wr.rearrange("(c p) e -> p c e", p=128))
            b.inc(dma, "sw0", 16)
            dma = y.dma_start(out=rtemp[0:1, :], in_=temp[None, :])
            b.inc(dma, "sw0", 16)
            tok["wr_loaded"] = b.cnt["sw0"]
        b.step(sync=y)

        ph_ln_stats("0")

        for t in range(NT):
            scr = xn_scr[:, t % 2, :]
            p0, p1 = banks[t % 2], banks[2 + t % 2]

            def v(v, t=t, scr=scr):
                b.wait(v, "v", tok["rstd0"])
                if t >= 2:
                    b.wait(v, "t", tok[f"tr{t - 2}"])
                v.drain()
                ii = v.tensor_scalar(scr, x_res[:, t, :], mv_all[:, t, 0:1],
                                     rstd_all[:, t:t + 1], op.subtract, op.mult)
                tok[f"ln{t}"] = b.inc(ii, "v")

            def tt(tt, t=t, scr=scr, p0=p0, p1=p1):
                b.wait(tt, "g", tok["consts"])
                b.wait(tt, "v", tok[f"ln{t}"])
                if t >= 2:
                    b.wait(tt, "v", tok[f"ev{t - 2}"])
                for c in range(8):
                    half = p0 if c < 4 else p1
                    tt.transpose(half[:, (c % 4) * 128:(c % 4) * 128 + 128],
                                 scr[:, c * 128:c * 128 + 128], ident_f)
                tt.matmul(banks[4][0:1, 0:512], ones_col, scr[:, 0:512],
                          start=(t == 0), stop=(t == NT - 1))
                ii = tt.matmul(banks[5][0:1, 0:512], ones_col, scr[:, 512:1024],
                               start=(t == 0), stop=(t == NT - 1))
                tok[f"tr{t}"] = b.inc(ii, "t")

            def v2(v, t=t, p0=p0, p1=p1):
                b.wait(v, "t", tok[f"tr{t}"])
                ii = None
                for h, half in ((0, p0), (1, p1)):
                    hi = xnT_hi[:, 4 * h:4 * h + 4, t * 128:(t + 1) * 128]
                    lo = xnT_lo[:, 4 * h:4 * h + 4, t * 128:(t + 1) * 128]
                    h3 = half.rearrange("p (c x) -> p c x", c=4)
                    v.tensor_copy(hi, h3)
                    v.drain()
                    ii = v.scalar_tensor_tensor(lo, hi, -1.0, h3, op.mult, op.add)
                tok[f"ev{t}"] = b.inc(ii, "v")

            b.step(vector=v)
            b.step(tensor=tt, vector=v2)

    def ph_l0b():
        def v(v):
            b.wait(v, "sw0", tok["wr_loaded"])
            v.tensor_copy(wr_hi.rearrange("p c e -> p (c e)"), wr_f.rearrange("p c e -> p (c e)"))
            v.drain()
            v.scalar_tensor_tensor(wr_lo.rearrange("p c e -> p (c e)"),
                                   wr_hi.rearrange("p c e -> p (c e)"), -1.0,
                                   wr_f.rearrange("p c e -> p (c e)"), op.mult, op.add)
            v.drain()
            ii = v.reciprocal(rtemp[0:1, :], rtemp[0:1, :])
            tok["wrhl"] = b.inc(ii, "v")
        b.step(vector=v)

        for t in range(NT):
            pr = banks[6 + t % 2][:, 0:NEXP]

            def tt(tt, t=t, pr=pr):
                b.wait(tt, "v", max(tok["wrhl"], tok[f"ev{t}"]))
                if t >= 2:
                    b.wait(tt, "v", tok[f"cw{t - 2}"])
                ii = None
                for c in range(NC):
                    lhs_hi = xnT_hi[:, c, t * 128:(t + 1) * 128]
                    lhs_lo = xnT_lo[:, c, t * 128:(t + 1) * 128]
                    tt.matmul(pr, lhs_hi, wr_hi[:, c, :], start=(c == 0), stop=False)
                    tt.matmul(pr, lhs_hi, wr_lo[:, c, :], start=False, stop=False)
                    ii = tt.matmul(pr, lhs_lo, wr_hi[:, c, :], start=False,
                                   stop=(c == NC - 1))
                tok[f"rt{t}"] = b.inc(ii, "t")

            def v1(v, t=t, pr=pr):
                b.wait(v, "t", tok[f"rt{t}"])
                v.drain()
                ii = v.tensor_reduce(ttr_scr[:, 0:1], pr, AXX, op.max, negate=True)
                tok[f"nmax{t}"] = b.inc(ii, "v")

            def s1(s, t=t, pr=pr):
                b.wait(s, "v", tok[f"nmax{t}"])
                ii = s.activation(ttr_scr[:, 4:4 + NEXP], pr, AF.Exp,
                                  bias=ttr_scr[:, 0:1], scale=1.0,
                                  accum_out=ttr_scr[:, 1:2])
                tok[f"exp{t}"] = b.inc(ii, "s")

            def v2(v, t=t):
                b.wait(v, "s", tok[f"exp{t}"])
                pe = ttr_scr[:, 4:4 + NEXP]
                v.tensor_reduce(ttr_scr[:, 2:3], pe, AXX, op.max)
                v.drain()
                v.tensor_scalar(ttr_scr[:, 32:32 + NEXP], pe, ttr_scr[:, 2:3], None, op.is_equal)
                v.drain()
                v.scalar_tensor_tensor(ttr_scr[:, 64:64 + NEXP], ttr_scr[:, 32:32 + NEXP],
                                       -1e30, pe, op.mult, op.add)
                v.drain()
                v.tensor_reduce(ttr_scr[:, 3:4], ttr_scr[:, 64:64 + NEXP], AXX, op.max)
                v.drain()
                v.tensor_scalar(ttr_scr[:, 32:32 + NEXP], pe, ttr_scr[:, 3:4], None, op.is_ge)
                v.reciprocal(ttr_scr[:, 1:2], ttr_scr[:, 1:2])
                v.drain()
                ii = v.scalar_tensor_tensor(cw_all[:, t, :], pe, ttr_scr[:, 1:2],
                                            ttr_scr[:, 32:32 + NEXP], op.mult, op.mult)
                tok[f"cw{t}"] = b.inc(ii, "v")

            b.step(tensor=tt, vector=v1)
            b.step(scalar=s1)
            b.step(vector=v2)

        def v3(v):
            b.wait(v, "t", tok[f"tr{NT - 1}"])
            v.tensor_copy(row_scr[0:1, 0:512], banks[4][0:1, 0:512])
            ii = v.tensor_copy(row_scr[0:1, 512:1024], banks[5][0:1, 0:512])
            tok["vsrow"] = b.inc(ii, "v")

        def t3(tt):
            b.wait(tt, "v", tok["vsrow"])
            tt.matmul(banks[4], ones_row, row_scr[0:1, 0:512], start=True, stop=True)
            ii = tt.matmul(banks[5], ones_row, row_scr[0:1, 512:1024], start=True, stop=True)
            tok["vsrep"] = b.inc(ii, "t")

        def v4(v):
            b.wait(v, "t", tok["vsrep"])
            v.tensor_copy(bcast_a[:, 0:512], banks[4])
            ii = v.tensor_copy(bcast_a[:, 512:1024], banks[5])
            tok["vsb"] = b.inc(ii, "v")
        b.step(vector=v3)
        b.step(tensor=t3, vector=v4)

        def matvec(src_ap, res8, toks):
            for c in range(NC):
                def y(y, c=c):
                    b.wait(y, "v", toks[c - 2])
                    dma = y.dma_start(out=stream_a[c % 2],
                                      in_=src_ap[128 * c:128 * (c + 1), :])
                    b.inc(dma, f"swv{c % 2}", 16)
                    tok["_mv_dma"] = b.cnt[f"swv{c % 2}"]

                def v(v, c=c):
                    b.wait(v, f"swv{c % 2}", tok["_mv_dma"])
                    b.wait(v, "v", tok["_mv_src"])
                    v.drain()
                    v.tensor_tensor(ttr_scr, stream_a[c % 2], bcast_a, op.mult)
                    v.drain()
                    ii = v.tensor_reduce(res8[:, c:c + 1], ttr_scr, AXX, op.add)
                    toks[c] = b.inc(ii, "v")
                b.step(sync=y, vector=v)

        tok["_mv_src"] = tok["vsb"]
        wv_toks = {-2: 0, -1: 0}
        matvec(w_qkv[2 * D:3 * D, :], val8, wv_toks)

        def y5(y):
            b.wait(y, "v", wv_toks[NC - 1])
            for c in range(NC):
                ii = y.dma_start(out=row_scr[0:1, c * 128:(c + 1) * 128], in_=val8[:, c:c + 1])
                b.inc(ii, "sg2", 16)
            tok["valrow"] = b.cnt["sg2"]

        def t5(tt):
            b.wait(tt, "sg2", tok["valrow"])
            b.wait(tt, "v", wv_toks[NC - 1])
            tt.matmul(banks[4], ones_row, row_scr[0:1, 0:512], start=True, stop=True)
            ii = tt.matmul(banks[5], ones_row, row_scr[0:1, 512:1024], start=True, stop=True)
            tok["valrep"] = b.inc(ii, "t")

        def v5(v):
            b.wait(v, "t", tok["valrep"])
            v.drain()
            v.tensor_copy(bcast_a[:, 0:512], banks[4])
            ii = v.tensor_copy(bcast_a[:, 512:1024], banks[5])
            tok["valb"] = b.inc(ii, "v")
        b.step(sync=y5)
        b.step(tensor=t5, vector=v5)

        tok["_mv_src"] = tok["valb"]
        wo_toks = {-2: wv_toks[NC - 2], -1: wv_toks[NC - 1]}
        matvec(w_o0, a08, wo_toks)

        def y6(y):
            b.wait(y, "v", wo_toks[NC - 1])
            b.wait(y, "t", tok["valrep"])
            for c in range(NC):
                ii = y.dma_start(out=row_scr[0:1, c * 128:(c + 1) * 128], in_=a08[:, c:c + 1])
                b.inc(ii, "sg2", 16)
            tok["a0row"] = b.cnt["sg2"]

        def t6(tt):
            b.wait(tt, "sg2", tok["a0row"])
            b.wait(tt, "v", wo_toks[NC - 1])
            tt.matmul(banks[4], ones_row, row_scr[0:1, 0:512], start=True, stop=True)
            ii = tt.matmul(banks[5], ones_row, row_scr[0:1, 512:1024], start=True, stop=True)
            tok["a0rep"] = b.inc(ii, "t")

        def v6(v):
            b.wait(v, "t", tok["a0rep"])
            v.tensor_copy(attn0b[:, 0:512], banks[4])
            ii = v.tensor_copy(attn0b[:, 512:1024], banks[5])
            tok["a0b"] = b.inc(ii, "v")
        b.step(sync=y6)
        b.step(tensor=t6, vector=v6)

    def wslot_views(i):
        w = wslot[i]
        wg = w[:, 0:8 * FH].rearrange("p (c f) -> p c f", c=8)
        wu = w[:, 8 * FH:16 * FH].rearrange("p (c f) -> p c f", c=8)
        return wg, wu, wd_buf

    def gu_srcs(s):
        if s < NEXP:
            return Wg[s], Wu[s]
        hh = s - NEXP
        return Sg[:, hh * FH:(hh + 1) * FH], Su[:, hh * FH:(hh + 1) * FH]

    def wd_src(s):
        return Wd[s] if s < NEXP else Sd[(s - NEXP) * FH:(s - NEXP + 1) * FH, :]

    def emit_slot_load(g, s):
        slot = s % 2
        g_src, u_src = gu_srcs(s)
        wgv, wuv, _ = wslot_views(slot)
        i1 = g.dma_start(out=wgv, in_=g_src.rearrange("(c p) f -> p c f", p=128))
        b.inc(i1, f"wsl{slot}", 16)
        i2 = g.dma_start(out=wuv, in_=u_src.rearrange("(c p) f -> p c f", p=128))
        b.inc(i2, f"wsl{slot}", 16)
        tok[f"wload{s}"] = b.cnt[f"wsl{slot}"]

    def emit_wd_load(g, s):
        ii = g.dma_start(out=wd_buf, in_=wd_src(s).rearrange("(m p) f -> p m f", p=128))
        b.inc(ii, "wdl", 16)
        tok[f"wdload{s}"] = b.cnt["wdl"]

    def emit_wd_load2(g, s):
        ii = g.dma_start(out=wd_buf2, in_=wd_src(s).rearrange("(m p) f -> p m f", p=128))
        b.inc(ii, "wdl", 16)
        tok[f"wdload{s}"] = b.cnt["wdl"]

    def ph_l0c():
        def g0(g):
            emit_slot_load(g, 0)
            emit_slot_load(g, 1)
            emit_wd_load(g, 0)
        b.step(gpsimd=g0)

        def emit_sltc(s, tc, wgv, wuv, wdv, shared):
            tsl = slice(tc * TC, (tc + 1) * TC)
            t0, t1 = 2 * tc, 2 * tc + 1
            if True:
                if not shared:
                    def y(y, s=s, tc=tc, t0=t0, t1=t1):
                        b.wait(y, "v", max(tok[f"cw{t0}"], tok[f"cw{t1}"]))
                        if (s, tc) != (0, 0):
                            b.wait(y, "t", tok["cwrep"])
                        i1 = y.dma_start(out=cwrow[0:1, 0:128], in_=cw_all[:, t0, s:s + 1])
                        b.inc(i1, "sg2", 16)
                        i2 = y.dma_start(out=cwrow[0:1, 128:256], in_=cw_all[:, t1, s:s + 1])
                        b.inc(i2, "sg2", 16)
                        tok["cwdma"] = b.cnt["sg2"]

                    def tcw(tt, s=s, tc=tc):
                        b.wait(tt, "sg2", tok["cwdma"])
                        ii = tt.matmul(banks[2][:, 0:TC], ones_row, cwrow[0:1, :],
                                       start=True, stop=True)
                        tok["cwrep"] = b.inc(ii, "t")

                    def vcw(v, s=s, tc=tc):
                        b.wait(v, "t", tok["cwrep"])
                        v.drain()
                        ii = v.tensor_copy(cwb, banks[2][:, 0:TC])
                        tok["cwb"] = b.inc(ii, "v")
                    b.step(sync=y, tensor=tcw)
                    b.step(vector=vcw)

                def tgu(tt, s=s, tc=tc, tsl=tsl, wgv=wgv, wuv=wuv, shared=shared):
                    b.wait(tt, f"wsl{s % 2}", tok[f"wload{s}"])
                    if not shared:
                        b.wait(tt, "v", tok["cwb"])
                    elif s == NEXP and "xadd" in tok:
                        b.wait(tt, "v", tok["xadd"])
                    for m in range(4):
                        pg = banks[m // 2][:, (m % 2) * TC:(m % 2) * TC + TC]
                        pu = banks[2 + m // 2][:, (m % 2) * TC:(m % 2) * TC + TC]
                        for c in range(NC):
                            rhs = xnT_hi[:, c, tsl]
                            tt.matmul(pg, wgv[:, c, m * 128:(m + 1) * 128], rhs,
                                      start=(c == 0), stop=(c == NC - 1))
                            ii = tt.matmul(pu, wuv[:, c, m * 128:(m + 1) * 128], rhs,
                                           start=(c == 0), stop=(c == NC - 1))
                    tok["gu"] = b.inc(ii, "t")

                def ssg(ss, s=s):
                    b.wait(ss, "t", tok["gu"])
                    ii = None
                    for m2 in range(2):
                        ii = ss.activation(sg_scr[:, 2 * m2:2 * m2 + 2, :]
                                           .rearrange("p a c -> p (a c)"),
                                           banks[m2], AF.Silu)

                    tok["sg"] = b.inc(ii, "s")

                def vh(v, s=s, tc=tc, shared=shared):
                    b.wait(v, "t", tok["gu"])
                    b.wait(v, "s", tok["sg"])
                    v.drain()
                    ii = None
                    if shared:
                        for m2 in range(2):
                            ii = v.tensor_tensor(h_scr[:, 2 * m2:2 * m2 + 2, :],
                                                 sg_scr[:, 2 * m2:2 * m2 + 2, :],
                                                 banks[2 + m2].rearrange("p (a x) -> p a x", a=2),
                                                 op.mult)
                    else:
                        cb = cwb.rearrange("p x -> p () x").broadcast_to([128, 2, TC])
                        for m2 in range(2):
                            v.tensor_tensor(h_scr[:, 2 * m2:2 * m2 + 2, :],
                                            banks[2 + m2].rearrange("p (a x) -> p a x", a=2),
                                            cb, op.mult)
                        v.drain()
                        ii = v.tensor_tensor(h_scr.rearrange("p a c -> p (a c)"),
                                             sg_scr.rearrange("p a c -> p (a c)"),
                                             h_scr.rearrange("p a c -> p (a c)"), op.mult)
                    tok["h"] = b.inc(ii, "v")

                def tdn(tt, s=s, tc=tc, shared=shared, wdv=wdv):
                    b.wait(tt, "v", tok["h"])
                    b.wait(tt, "wdl", tok[f"wdload{s}"])
                    ii = None
                    for o in range(8):
                        pd = banks[4 + o // 2][:, (o % 2) * TC:(o % 2) * TC + TC]
                        for m in range(4):
                            ii = tt.matmul(pd, wdv[:, m, o * 128:(o + 1) * 128],
                                           h_scr[:, m, :], start=(m == 0), stop=(m == 3))
                    tok["dn"] = b.inc(ii, "t")
                    if tc == NTC - 1:
                        tok[f"mmdone{s}"] = b.cnt["t"]

                b.step(tensor=tgu, scalar=ssg)
                b.step(vector=vh)
                b.step(tensor=tdn)

                if not shared:
                    def vr(v, s=s, tc=tc, tsl=tsl):
                        b.wait(v, "t", tok["dn"])
                        v.drain()
                        ii = None
                        for b4 in range(4):
                            dst = racc[:, 2 * b4:2 * b4 + 2, tsl]
                            src = banks[4 + b4].rearrange("p (c x) -> p c x", c=2)
                            if s == 0:
                                ii = v.tensor_copy(dst, src)
                            else:
                                ii = v.tensor_tensor(dst, dst, src, op.add)
                            v.drain()
                        tok[f"racc{tc}"] = b.inc(ii, "v")
                    b.step(vector=vr)
                elif s == NEXP + 1:
                    for tb in range(2):
                        tglob = 2 * tc + tb

                        def vc(v, tc=tc, tb=tb, tglob=tglob):
                            b.wait(v, "t", tok["dn"])
                            if "combtr" in tok:
                                b.wait(v, "t", tok["combtr"])
                            v.drain()
                            ii = None
                            for b4 in range(4):
                                src = racc[:, 2 * b4:2 * b4 + 2,
                                           tglob * 128:(tglob + 1) * 128]
                                pk = banks[4 + b4].rearrange("p (c y x) -> p c y x", c=2, y=2)[:, :, tb, :]
                                ii = v.tensor_tensor(comb_scr[:, 2 * b4:2 * b4 + 2, :],
                                                     src, pk, op.add)
                            tok["comb"] = b.inc(ii, "v")
                        b.step(vector=vc)

                        def ttr(tt, tc=tc, tb=tb):
                            b.wait(tt, "v", tok["comb"])
                            if tc > 0 or tb > 0:
                                b.wait(tt, "v", tok["xadd"])
                            ii = None
                            for c in range(NC):
                                dstb = banks[2 + c // 4]
                                ii = tt.transpose(dstb[:, (c % 4) * 128:(c % 4) * 128 + 128],
                                                  comb_scr[:, c, :], ident_f)
                            tok["combtr"] = b.inc(ii, "t")
                        b.step(tensor=ttr)

                        def vx(v, tglob=tglob):
                            b.wait(v, "t", tok["combtr"])
                            v.drain()
                            xt = x_res[:, tglob, :]
                            v.tensor_tensor(xt[:, 0:512], xt[:, 0:512], banks[2], op.add)
                            ii = v.tensor_tensor(xt[:, 512:1024], xt[:, 512:1024],
                                                 banks[3], op.add)
                            tok["xadd"] = b.inc(ii, "v")
                        b.step(vector=vx)

        for s in range(NEXP):
            wgv, wuv, wdv = wslot_views(s % 2)

            def gpre(g, s=s):
                b.wait(g, "t", tok[f"mmdone{s - 1}"])
                if s + 1 < NEXP:
                    emit_slot_load(g, s + 1)
                emit_wd_load(g, s)
            if s >= 1:
                b.step(gpsimd=gpre)
            for tc in range(NTC):
                emit_sltc(s, tc, wgv, wuv, wdv, False)

        def gsh(g):
            b.wait(g, "t", tok[f"mmdone{NEXP - 2}"])
            emit_slot_load(g, NEXP)
            b.wait(g, "t", tok[f"mmdone{NEXP - 1}"])
            emit_slot_load(g, NEXP + 1)
            emit_wd_load(g, NEXP)
            emit_wd_load2(g, NEXP + 1)
        b.step(gpsimd=gsh)
        for tc in range(NTC):
            tsl = slice(tc * TC, (tc + 1) * TC)
            halves = ((NEXP, wslot_views(0)[0], wslot_views(0)[1], h_scr),
                      (NEXP + 1, wslot_views(1)[0], wslot_views(1)[1], h2_scr))
            for s_sh, wgv, wuv, hbuf in halves:
                def tgu(tt, s=s_sh, tsl=tsl):
                    b.wait(tt, f"wsl{s % 2}", tok[f"wload{s}"])
                    if s == NEXP and "xadd" in tok:
                        b.wait(tt, "v", tok["xadd"])
                    if s == NEXP + 1:
                        b.wait(tt, "v", tok[f"h{NEXP}"])
                    ii = None
                    for m in range(4):
                        pg = banks[m // 2][:, (m % 2) * TC:(m % 2) * TC + TC]
                        pu = banks[2 + m // 2][:, (m % 2) * TC:(m % 2) * TC + TC]
                        for c in range(NC):
                            rhs = xnT_hi[:, c, tsl]
                            tt.matmul(pg, wgv[:, c, m * 128:(m + 1) * 128], rhs,
                                      start=(c == 0), stop=(c == NC - 1))
                            ii = tt.matmul(pu, wuv[:, c, m * 128:(m + 1) * 128], rhs,
                                           start=(c == 0), stop=(c == NC - 1))
                    tok["gu"] = b.inc(ii, "t")

                def ssg(ss):
                    b.wait(ss, "t", tok["gu"])
                    ii = None
                    for m2 in range(2):
                        ii = ss.activation(sg_scr[:, 2 * m2:2 * m2 + 2, :]
                                           .rearrange("p a c -> p (a c)"),
                                           banks[m2], AF.Silu)
                    tok["sg"] = b.inc(ii, "s")

                def vh(v, hbuf=hbuf):
                    b.wait(v, "t", tok["gu"])
                    b.wait(v, "s", tok["sg"])
                    v.drain()
                    ii = None
                    for m2 in range(2):
                        ii = v.tensor_tensor(hbuf[:, 2 * m2:2 * m2 + 2, :],
                                             sg_scr[:, 2 * m2:2 * m2 + 2, :],
                                             banks[2 + m2].rearrange("p (a x) -> p a x", a=2),
                                             op.mult)
                    tok[f"h{s_sh}"] = b.inc(ii, "v")
                b.step(tensor=tgu, scalar=ssg)
                b.step(vector=vh)

            def tdn(tt, tc=tc):
                b.wait(tt, "v", tok[f"h{NEXP}"])
                b.wait(tt, "v", tok[f"h{NEXP + 1}"])
                b.wait(tt, "wdl", tok[f"wdload{NEXP + 1}"])
                ii = None
                for o in range(8):
                    pd = banks[4 + o // 2][:, (o % 2) * TC:(o % 2) * TC + TC]
                    for hb, wdv_ in ((h_scr, wd_buf), (h2_scr, wd_buf2)):
                        for m in range(4):
                            ii = tt.matmul(pd, wdv_[:, m, o * 128:(o + 1) * 128],
                                           hb[:, m, :],
                                           start=(hb is h_scr and m == 0),
                                           stop=(hb is h2_scr and m == 3))
                tok["dn"] = b.inc(ii, "t")

            b.step(tensor=tdn)

            for tb in range(2):
                tglob = 2 * tc + tb

                def vc(v, tc=tc, tb=tb, tglob=tglob):
                    b.wait(v, "t", tok["dn"])
                    if "combtr" in tok:
                        b.wait(v, "t", tok["combtr"])
                    v.drain()
                    ii = None
                    for b4 in range(4):
                        src = racc[:, 2 * b4:2 * b4 + 2,
                                   tglob * 128:(tglob + 1) * 128]
                        pk = banks[4 + b4].rearrange("p (c y x) -> p c y x", c=2, y=2)[:, :, tb, :]
                        ii = v.tensor_tensor(comb_scr[:, 2 * b4:2 * b4 + 2, :],
                                             src, pk, op.add)
                    tok["comb"] = b.inc(ii, "v")
                b.step(vector=vc)

                def ttr(tt, tc=tc, tb=tb):
                    b.wait(tt, "v", tok["comb"])
                    if tc > 0 or tb > 0:
                        b.wait(tt, "v", tok["xadd"])
                    ii = None
                    for c in range(NC):
                        dstb = banks[2 + c // 4]
                        ii = tt.transpose(dstb[:, (c % 4) * 128:(c % 4) * 128 + 128],
                                          comb_scr[:, c, :], ident_f)
                    tok["combtr"] = b.inc(ii, "t")
                b.step(tensor=ttr)

                def vx(v, tglob=tglob):
                    b.wait(v, "t", tok["combtr"])
                    v.drain()
                    xt = x_res[:, tglob, :]
                    v.tensor_tensor(xt[:, 0:512], xt[:, 0:512], banks[2], op.add)
                    ii = v.tensor_tensor(xt[:, 512:1024], xt[:, 512:1024],
                                         banks[3], op.add)
                    tok["xadd"] = b.inc(ii, "v")
                b.step(vector=vx)
        tok["mmdone17"] = b.cnt["t"]

        def vfin(v):
            b.wait(v, "v", tok["xadd"])
            b.wait(v, "v", tok["a0b"])
            v.drain()
            ii = None
            for t in range(NT):
                ii = v.tensor_tensor(x_res[:, t, :], x_res[:, t, :], attn0b, op.add)
            v.drain()
            tok["x1"] = b.inc(ii, "v")
        b.step(vector=vfin)

    ph_consts()
    ph_l0a()
    if upto.startswith("l1skip"):
        tok["x1"] = tok[f"ev{NT - 1}"]
    else:
        ph_l0b()
        ph_l0c()

    if upto == "l0":
        def yout(y):
            b.wait(y, "v", tok["x1"])
            for i in range(4):
                dma = y.dma_start(out=out.rearrange("(n p) d -> p n d", p=128)[:, 4 * i:4 * i + 4, :],
                                  in_=x_res[:, 4 * i:4 * i + 4, :])
                b.inc(dma, "so", 16)
            srcs = {"xnT_hi": xnT_hi, "cw": cw_all, "attn0b": attn0b, "racc": racc,
                    "comb": comb_scr}
            for name, ap_dbg in dbg_out.items():
                nm = name.replace("dbg_", "")
                dma = y.dma_start(out=ap_dbg, in_=srcs[nm[4:] if False else nm])
                b.inc(dma, "so", 16)
            y.wait_ge(b.sem("so"), b.cnt["so"])
        b.step(sync=yout)
        return nc, dbg_out

    def emit_quant(eng, ename, x_ap, out_ap, qa, qb, bm, rs6, nblk):
        F = nblk * 16
        x3 = x_ap.rearrange("p (n i) -> p n i", i=16)
        eng.tensor_reduce(bm, x3, AXX, op.max, apply_absolute_value=True)
        eng.drain()
        eng.tensor_scalar(rs6, bm, 6e-12, None, op.max)
        eng.drain()
        eng.reciprocal(rs6, rs6)
        eng.drain()
        eng.tensor_scalar(rs6, rs6, 6.0, None, op.mult)
        eng.drain()
        rs6b = rs6.rearrange("p n -> p n ()").broadcast_to([128, nblk, 16])
        qa3 = qa.rearrange("p (n i) -> p n i", i=16)
        eng.tensor_scalar(qa, x_ap, -1.0, None, op.mult)
        eng.drain()
        eng.tensor_tensor(qa, qa, x_ap, op.max)
        eng.drain()
        eng.tensor_tensor(qa3, qa3, rs6b, op.mult)
        eng.drain()
        eng.tensor_scalar(qb, qa, 5.0, 2.0, op.is_ge, op.mult)
        eng.drain()
        eng.scalar_tensor_tensor(qb, qa, 2.5, qb, op.is_ge, op.add)
        eng.drain()
        eng.scalar_tensor_tensor(qb, qa, 3.5, qb, op.is_ge, op.add)
        eng.drain()
        eng.tensor_scalar(qb, qb, 2.0, None, op.mult)
        eng.drain()
        for t_ in (0.25, 0.75, 1.25, 1.75):
            eng.scalar_tensor_tensor(qb, qa, t_, qb, op.is_ge, op.add)
            eng.drain()
        eng.reciprocal(rs6, rs6)
        eng.drain()
        eng.tensor_scalar(qa, x_ap, 0.0, 1.0, op.is_ge, op.subtract)
        eng.drain()
        eng.tensor_scalar(qa, qa, 2.0, 1.0, op.mult, op.add)
        eng.drain()
        scb = rs6.rearrange("p n -> p n ()").broadcast_to([128, nblk, 16])
        eng.tensor_scalar(qa, qa, 0.5, None, op.mult)
        eng.drain()
        eng.tensor_tensor(qa3, qa3, scb, op.mult)
        eng.drain()
        ii = eng.tensor_tensor(out_ap, qb, qa, op.mult)
        eng.drain()
        return b.inc(ii, ename)

    xn2T = xnT_hi
    qa_v = bcast_a
    qb_v = attn0b
    bm_v = l1scr[:, 0:64]
    rs6_v = l1scr[:, 64:128]

    def ph_l1a():
        ph_ln_stats("1", gate=("v", tok["x1"]))
        for t in range(l1a_tiles):
            scr = xn_scr[:, t % 2, :]
            p0, p1 = banks[t % 2], banks[2 + t % 2]

            def v(v, t=t, scr=scr):
                b.wait(v, "v", tok["rstd1"])
                if t >= 2:
                    b.wait(v, "t", tok[f"l1tr{t - 2}"])
                v.drain()
                ii = v.tensor_scalar(scr, x_res[:, t, :], mv_all[:, t, 0:1],
                                     rstd_all[:, t:t + 1], op.subtract, op.mult)
                tok[f"l1ln{t}"] = b.inc(ii, "v")

            def y(y, t=t):
                b.wait(y, "v", tok[f"l1ln{t}"])
                dma = y.dma_start(out=xspill.rearrange("(n p) d -> p n d", p=128)[:, t, :],
                                  in_=x_res[:, t, :])
                b.inc(dma, "sxp", 16)

            def tt(tt, t=t, scr=scr, p0=p0, p1=p1):
                b.wait(tt, "v", tok[f"l1ln{t}"])
                if t >= 2:
                    b.wait(tt, "v", tok[f"l1ev{t - 2}"])
                ii = None
                for c in range(8):
                    half = p0 if c < 4 else p1
                    ii = tt.transpose(half[:, (c % 4) * 128:(c % 4) * 128 + 128],
                                      scr[:, c * 128:c * 128 + 128], ident_f)
                tok[f"l1tr{t}"] = b.inc(ii, "t")

            def v2(v, t=t, p0=p0, p1=p1):
                b.wait(v, "t", tok[f"l1tr{t}"])
                ii = None
                for h, half in ((0, p0), (1, p1)):
                    hi = xn2T[:, 4 * h:4 * h + 4, t * 128:(t + 1) * 128]
                    ii = v.tensor_copy(hi, half.rearrange("p (c x) -> p c x", c=4))
                tok[f"l1ev{t}"] = b.inc(ii, "v")

            b.step(vector=v)
            if l1a_parts >= 2:
                b.step(sync=y, tensor=tt, vector=v2)
            else:
                b.step(tensor=tt, vector=v2)

            if l1a_parts < 3:
                continue
            def vq(v, t=t, scr=scr):
                b.wait(v, "v", tok[f"l1ln{t}"])
                if t >= 1:
                    b.wait(v, "t", tok[f"xqtr{t - 1}"])
                v.drain()
                tok[f"q{t}"] = emit_quant(v, "v", scr, qb_v,
                                          qa_v, qb_v, bm_v, rs6_v, 64)

            def tq(tt, t=t):
                b.wait(tt, "v", tok[f"q{t}"])
                if t >= 2:
                    b.wait(tt, "v", tok[f"xqev{t - 2}"])
                pb = banks[4 + t % 2]
                ii = None
                for c in range(8):
                    ii = tt.transpose(pb[:, (c % 4) * 128:(c % 4) * 128 + 128]
                                      if c < 4 else banks[6 + t % 2][:, (c % 4) * 128:(c % 4) * 128 + 128],
                                      qb_v[:, c * 128:c * 128 + 128], ident_f)
                tok[f"xqtr{t}"] = b.inc(ii, "t")

            def vq2(v, t=t):
                b.wait(v, "t", tok[f"xqtr{t}"])
                b.wait(v, "sw0", b.cnt.get("sw0", 0))
                b.wait(v, "sg2", b.cnt.get("sg2", 0))
                if t >= 2:
                    b.wait(v, f"sq{t % 2}", tok[f"sqd{t - 2}"])
                v.drain()
                v.tensor_copy(xq_tileA[t % 2][:, 0:4, :],
                              banks[4 + t % 2].rearrange("p (c x) -> p c x", c=4))
                ii = v.tensor_copy(xq_tileA[t % 2][:, 4:8, :],
                                   banks[6 + t % 2].rearrange("p (c x) -> p c x", c=4))
                tok[f"xqev{t}"] = b.inc(ii, "v")

            def yq(y, t=t):
                b.wait(y, "v", tok[f"xqev{t}"])
                dma = y.dma_start(out=xqspill[:, :, t * 128:(t + 1) * 128],
                                  in_=xq_tileA[t % 2])
                tok[f"sqd{t}"] = b.inc(dma, f"sq{t % 2}", 16)
            b.step(vector=vq)
            if l1a_parts >= 4:
                b.step(tensor=tq, vector=vq2)
            if l1a_parts >= 5:
                b.step(sync=yq)
        tok["spilled"] = b.cnt.get("sxp", 0)
        tok["xq_spilled"] = (b.cnt.get("sq0", 0), b.cnt.get("sq1", 0))
        tok["l1a_v"] = b.cnt["v"]

    wqb = SBn("wqb", [128, NC, 256], BF16, at=ws0_off)
    wkb = SBn("wkb", [128, NC, 256], BF16, at=ws0_off + 4096)
    wvb = SBn("wvb", [128, NC, D], BF16, at=ws0_off + 8192)
    wob = SBn("wob", [128, NC, D], BF16, at=ws1_off + 8192)
    bo_f = mv_all.rearrange("p a c -> p (a c)")[:, 0:8]
    sq_s = SBn("sq_s", [128, 256], F32, at=wrh_off)
    qpad = attn0b[:, 0:512]
    kpad = attn0b[:, 512:1024]
    bqb = l1scr[:, 0:256]
    bkb = l1scr[:, 256:512]
    qpb_s = l1scr[:, 512:768]
    bvb = bcast_a

    def ph_l1b():
        def g(g):
            b.wait(g, "t", tok["mmdone17"] if "mmdone17" in tok else 0)
            for dst, src in ((wqb, Wq1), (wkb, Wk1)):
                dma = g.dma_start(out=dst, in_=src.rearrange("(c p) w -> p c w", p=128))
                b.inc(dma, "wl1", 16)
            for dst, src in ((wvb, Wv1), (wob, Wo1)):
                dma = g.dma_start(out=dst, in_=src.rearrange("(c p) w -> p c w", p=128))
                b.inc(dma, "wl1", 16)
            tok["wl1"] = b.cnt["wl1"]

        def y(y):
            b.wait(y, "v", tok["x1"])
            if "mmdone17" in tok:
                b.wait(y, "t", tok["mmdone17"])
            dma = y.dma_start(out=row_scr[0:1, 0:256], in_=bq1[None, :])
            b.inc(dma, "sb1", 16)
            dma = y.dma_start(out=row_scr[0:1, 256:512], in_=qph.rearrange("a h b w -> a (h b w)"))
            b.inc(dma, "sb1", 16)
            dma = y.dma_start(out=row_scr[0:1, 512:768], in_=bk1[None, :])
            b.inc(dma, "sb1", 16)
            dma = y.dma_start(out=row_scr[0:1, 768:1024], in_=kph.rearrange("a h b w -> a (h b w)"))
            b.inc(dma, "sb1", 16)
            dma = y.dma_start(out=bo_f, in_=bo1.rearrange("(c p) -> p c", p=128),
                              allow_slow_non_contiguous=True)
            b.inc(dma, "sb1", 16)
            tok["sb1"] = b.cnt["sb1"]

        def v(v):
            b.wait(v, "sb1", tok["sb1"])
            b.wait(v, "v", tok["l1a_v"])
            v.drain()
            v.tensor_tensor(row_scr[0:1, 0:256], row_scr[0:1, 0:256],
                            row_scr[0:1, 256:512], op.add)
            v.tensor_tensor(row_scr[0:1, 512:768], row_scr[0:1, 512:768],
                            row_scr[0:1, 768:1024], op.add)
            v.drain()
            ii = v.memset(qpad, 0.0)
            ii = v.memset(kpad, 0.0)
            tok["brow"] = b.inc(ii, "v")

        def t2(tt):
            b.wait(tt, "v", tok["brow"])
            tt.matmul(banks[0][:, 0:256], ones_row, row_scr[0:1, 0:256], start=True, stop=True)
            tt.matmul(banks[0][:, 256:512], ones_row, row_scr[0:1, 512:768], start=True, stop=True)
            ii = tt.matmul(banks[1][0:128, 0:1], ones_row, rtemp[0:1, 0:1], start=True, stop=True)
            tok["brep"] = b.inc(ii, "t")

        def v2(v):
            b.wait(v, "t", tok["brep"])
            v.drain()
            v.tensor_copy(bqb, banks[0][:, 0:256])
            v.tensor_copy(bkb, banks[0][:, 256:512])
            ii = v.tensor_copy(rtemp, banks[1][:, 0:1])
            tok["bqkb"] = b.inc(ii, "v")

        b.step(gpsimd=g, sync=y)
        b.step(vector=v)
        b.step(tensor=t2, vector=v2)

        def y3(y):
            b.wait(y, "v", tok["bqkb"])
            b.wait(y, "t", tok["brep"])
            dma = y.dma_start(out=row_scr[0:1, :], in_=bv1[None, :])
            b.inc(dma, "sb1", 16)
            tok["sbv"] = b.cnt["sb1"]

        def t3(tt):
            b.wait(tt, "sb1", tok["sbv"])
            tt.matmul(banks[0], ones_row, row_scr[0:1, 0:512], start=True, stop=True)
            ii = tt.matmul(banks[1], ones_row, row_scr[0:1, 512:1024], start=True, stop=True)
            tok["bvrep"] = b.inc(ii, "t")

        def v3(v):
            b.wait(v, "t", tok["bvrep"])
            b.wait(v, "v", tok["l1a_v"])
            b.wait(v, "sq0", tok["xq_spilled"][0])
            b.wait(v, "sq1", tok["xq_spilled"][1])
            v.drain()
            v.tensor_copy(bvb[:, 0:512], banks[0])
            ii = v.tensor_copy(bvb[:, 512:1024], banks[1])
            tok["bvb"] = b.inc(ii, "v")
        b.step(sync=y3)
        b.step(tensor=t3, vector=v3)

        for t in range(NT):
            def tp(tt, t=t):
                b.wait(tt, "wl1", tok["wl1"])
                b.wait(tt, "v", max(tok["bvb"], tok[f"l1ev{t}"]))
                if t >= 1:
                    b.wait(tt, "v", tok[f"vev{t - 1}"])
                    b.wait(tt, "v", tok[f"qkn{t - 1}"])
                ii = None
                for c in range(NC):
                    lhs = xn2T[:, c, t * 128:(t + 1) * 128]
                    tt.matmul(banks[0][:, 0:256], lhs, wqb[:, c, :],
                              start=(c == 0), stop=(c == NC - 1))
                    tt.matmul(banks[1], lhs, wvb[:, c, 0:512],
                              start=(c == 0), stop=(c == NC - 1))
                    tt.matmul(banks[2], lhs, wvb[:, c, 512:1024],
                              start=(c == 0), stop=(c == NC - 1))
                for c in range(NC):
                    lhs = xn2T[:, c, t * 128:(t + 1) * 128]
                    ii = tt.matmul(banks[0][:, 256:512], lhs, wkb[:, c, :],
                                   start=(c == 0), stop=(c == NC - 1))
                tok[f"proj{t}"] = b.inc(ii, "t")

            def vv(v, t=t):
                b.wait(v, "t", tok[f"proj{t}"])
                b.wait(v, "sxp", tok["spilled"])
                v.drain()
                v.scalar_tensor_tensor(Vbuf[:, t, 0:512], banks[1], 0.0,
                                       bvb[:, 0:512], op.bypass, op.add)
                ii = v.scalar_tensor_tensor(Vbuf[:, t, 512:1024], banks[2], 0.0,
                                            bvb[:, 512:1024], op.bypass, op.add)
                tok[f"vev{t}"] = b.inc(ii, "v")

            b.step(tensor=tp)
            b.step(vector=vv)

            for qk, bias_b, pslc in (("q", bqb, slice(0, 256)), ("k", bkb, slice(256, 512))):
                def vn(v, t=t, qk=qk, bias_b=bias_b, pslc=pslc):
                    b.wait(v, "t", tok[f"proj{t}"])
                    b.wait(v, "sq0", tok["xq_spilled"][0])
                    b.wait(v, "sq1", tok["xq_spilled"][1])
                    if t >= 1:
                        b.wait(v, "t", tok[f"qktr{t - 1}"])
                    v.drain()
                    v.tensor_tensor(qpb_s, banks[0][:, pslc], bias_b, op.add)
                    v.drain()
                    ii = v.tensor_tensor(sq_s, qpb_s, qpb_s, op.mult)
                    tok["_sq"] = b.inc(ii, "v")

                b.step(vector=vn)

                def vr2(v, t=t, qk=qk):
                    sm3 = sq_s.rearrange("p (h w) -> p h w", w=16)
                    v.drain()
                    v.tensor_reduce(a08[:, 0:8], sm3[:, 0:8, :], AXX, op.add)
                    v.tensor_reduce(val8[:, 0:8], sm3[:, 8:16, :], AXX, op.add)
                    v.drain()
                    v.tensor_scalar(a08, a08, 1e-24, None, op.max)
                    ii = v.tensor_scalar(val8, val8, 1e-24, None, op.max)
                    tok["_ssq"] = b.inc(ii, "v")

                def sq2(s_, t=t):
                    b.wait(s_, "v", tok["_ssq"])
                    s_.activation(a08, a08, AF.Sqrt)
                    ii = s_.activation(val8, val8, AF.Sqrt)
                    tok["_sqr"] = b.inc(ii, "s")

                def vr3(v, t=t, qk=qk):
                    b.wait(v, "s", tok["_sqr"])
                    v.drain()
                    v.reciprocal(a08, a08)
                    v.reciprocal(val8, val8)
                    v.drain()
                    dst = (qpad if qk == "q" else kpad)
                    d3 = dst.rearrange("p (h z) -> p h z", z=32)[:, :, 0:16]
                    q3 = qpb_s.rearrange("p (h w) -> p h w", w=16)
                    rn8a = a08.rearrange("p c -> p c ()").broadcast_to([128, 8, 16])
                    rn8b = val8.rearrange("p c -> p c ()").broadcast_to([128, 8, 16])
                    v.tensor_tensor(d3[:, 0:8, :], q3[:, 0:8, :], rn8a, op.mult)
                    ii = v.tensor_tensor(d3[:, 8:16, :], q3[:, 8:16, :], rn8b, op.mult)
                    tok[f"nrm{qk}{t}"] = b.inc(ii, "v")
                b.step(vector=vr2)
                b.step(scalar=sq2)
                b.step(vector=vr3)
            tok[f"qkn{t}"] = tok[f"nrmk{t}"]

            def tq2(tt, t=t):
                b.wait(tt, "v", tok[f"qkn{t}"])
                if t >= 1:
                    b.wait(tt, "v", tok[f"qkev{t - 1}"])
                for g4 in range(4):
                    tt.transpose(banks[3][:, g4 * 128:(g4 + 1) * 128],
                                 qpad[:, g4 * 128:(g4 + 1) * 128], ident_f)
                ii = None
                for g4 in range(4):
                    ii = tt.transpose(banks[4][:, g4 * 128:(g4 + 1) * 128],
                                      kpad[:, g4 * 128:(g4 + 1) * 128], ident_f)
                tok[f"qktr{t}"] = b.inc(ii, "t")

            def vq3(v, t=t):
                b.wait(v, "t", tok[f"qktr{t}"])
                v.drain()
                v.tensor_copy(qTpad[:, :, t * 128:(t + 1) * 128],
                              banks[3].rearrange("p (g x) -> p g x", g=4))
                ii = v.tensor_copy(kTpad[:, :, t * 128:(t + 1) * 128],
                                   banks[4].rearrange("p (g x) -> p g x", g=4))
                tok[f"qkev{t}"] = b.inc(ii, "v")
            b.step(tensor=tq2)
            b.step(vector=vq3)
        tok["qkdone"] = tok[f"qkev{NT - 1}"]
        tok["vdone"] = tok[f"vev{NT - 1}"]

        def vz(v):
            ii = None
            for var, idx in ((1, 6), (1, 7), (2, 8), (2, 9), (3, 10), (3, 11)):
                ii = v.memset(P_pool[idx][:, 0:128 * var], 0.0)
            tok["pz"] = b.inc(ii, "v")
        b.step(vector=vz)

        plain_i = [0]
        var_i = {1: 0, 2: 0, 3: 0}
        p_tok = {}

        for c in range(4):
            for h in range(HH):
                a = 32 * (h % 4)
                g4 = h // 4
                pvb = banks[4 + h % 2]
                dnb = banks[6 + h % 2][0:1, 0:512]
                njt = 4 * c + 4
                for j in range(njt):
                    diag = j >= 4 * c
                    ov = j - 4 * c if diag else 0
                    if diag and ov > 0:
                        pi = 6 + (ov - 1) * 2 + var_i[ov]
                        var_i[ov] ^= 1
                    else:
                        pi = plain_i[0]
                        plain_i[0] = (plain_i[0] + 1) % 6
                    P = P_pool[pi]
                    bi = j % 3
                    sb_ = banks[bi]

                    def ts_(tt, c=c, h=h, j=j, a=a, g4=g4, sb_=sb_, pi=pi, bi=bi):
                        b.wait(tt, "v", max(tok["qkdone"], tok["vdone"], tok["pz"]))
                        if c > 0 and h == 0 and j == 0:
                            b.wait(tt, "v", tok[f"oev{c - 1}_7"])
                        b.wait(tt, "s", p_tok.get(("exp_prev", bi), 0))
                        ii = tt.matmul(sb_, kTpad[a:a + 16, g4, j * 128:(j + 1) * 128],
                                       qTpad[a:a + 16, g4, c * 512:(c + 1) * 512],
                                       start=True, stop=True, tile_position=(a, 0))
                        p_tok[("qk", c, h, j)] = b.inc(ii, "t")

                    def se_(s_, c=c, h=h, j=j, sb_=sb_, P=P, ov=ov, diag=diag, pi=pi, bi=bi):
                        b.wait(s_, "t", p_tok[("qk", c, h, j)])
                        b.wait(s_, "t", p_tok.get(("pv_prevuse", pi), 0))
                        lo = 128 * ov if diag else 0
                        ii = s_.activation(P[:, lo:512], sb_[:, lo:512], AF.Exp,
                                           bias=neg1, scale=rtemp)
                        p_tok[("exp", c, h, j)] = b.inc(ii, "s")
                        p_tok[("exp_prev", bi)] = b.cnt["s"]

                    def gm_(g_, c=c, h=h, j=j, P=P, ov=ov):
                        b.wait(g_, "s", p_tok[("exp", c, h, j)])
                        ii = g_.affine_select(out=P[:, 128 * ov:128 * ov + 128],
                                              in_=P[:, 128 * ov:128 * ov + 128],
                                              compare_op=op.is_ge, fill=0.0, base=0,
                                              pattern=[[1, 128]], channel_multiplier=-1)
                        p_tok[("msk", c, h, j)] = b.inc(ii, "g")

                    def tpv_(tt, c=c, h=h, j=j, P=P, pvb=pvb, dnb=dnb, diag=diag, pi=pi):
                        if diag:
                            b.wait(tt, "g", p_tok[("msk", c, h, j)])
                        else:
                            b.wait(tt, "s", p_tok[("exp", c, h, j)])
                        if j == 0:
                            hp = h - 2 if h >= 2 else (HH - 2 + h if c > 0 else None)
                            cp = c if h >= 2 else c - 1
                            if hp is not None:
                                b.wait(tt, "v", tok[f"at{cp}_{hp}"])
                        tt.matmul(pvb[0:64, :], Vbuf[:, j, h * 64:(h + 1) * 64], P,
                                  start=(j == 0), stop=(j == 4 * c + 3))
                        ii = tt.matmul(dnb, ones_colb, P,
                                       start=(j == 0), stop=(j == 4 * c + 3))
                        p_tok[("pv", c, h, j)] = b.inc(ii, "t")
                        p_tok[("pv_prevuse", pi)] = b.cnt["t"]

                    b.step(tensor=ts_)
                    b.step(scalar=se_)
                    if diag:
                        b.step(gpsimd=gm_)
                    b.step(tensor=tpv_)

                def vd_(v, c=c, h=h, pvb=pvb, dnb=dnb):
                    b.wait(v, "t", p_tok[("pv", c, h, 4 * c + 3)])
                    if (c, h) != (0, 0):
                        b.wait(v, "t", tok.get("rbrep0", 0))
                    v.drain()
                    ii = v.reciprocal(row_scr[0:1, 0:512], dnb)
                    tok["_rd"] = b.inc(ii, "v")

                def td_(tt, c=c, h=h):
                    b.wait(tt, "v", tok["_rd"])
                    ii = tt.matmul(banks[3][0:64, :], ones_row[0:1, 0:64],
                                   row_scr[0:1, 0:512], start=True, stop=True)
                    tok["rbrep0"] = b.inc(ii, "t")

                def va_(v, c=c, h=h, pvb=pvb):
                    b.wait(v, "t", tok["rbrep0"])
                    v.drain()
                    v.tensor_copy(l1scr[0:64, 0:512], banks[3][0:64, :])
                    v.drain()
                    ii = v.scalar_tensor_tensor(
                        attnT[64 * (h % 2):64 * (h % 2) + 64, h // 2, :],
                        pvb[0:64, :], 0.0, l1scr[0:64, 0:512], op.bypass, op.mult)
                    tok[f"at{c}_{h}"] = b.inc(ii, "v")
                b.step(vector=vd_)
                b.step(tensor=td_)
                b.step(vector=va_)

            for o in range(8):
                ob = banks[o % 2]

                def to_(tt, c=c, o=o, ob=ob):
                    b.wait(tt, "v", tok[f"at{c}_{HH - 1}"])
                    if o >= 2:
                        b.wait(tt, "v", tok[f"oev{c}_{o - 2}"])
                    ii = None
                    for cin in range(NC):
                        ii = tt.matmul(ob, wob[:, cin, o * 128:(o + 1) * 128], attnT[:, cin, :],
                                       start=(cin == 0), stop=(cin == NC - 1))
                    tok["_opj"] = b.inc(ii, "t")

                def vo_(v, c=c, o=o, ob=ob):
                    b.wait(v, "t", tok["_opj"])
                    v.drain()
                    ii = v.tensor_scalar(bvb[:, 0:512], ob, bo_f[:, o:o + 1], None, op.add)
                    tok["_oev"] = b.inc(ii, "v")

                def to2_(tt, c=c, o=o):
                    b.wait(tt, "v", tok["_oev"])
                    if o >= 1:
                        b.wait(tt, "v", tok[f"oev{c}_{o - 1}"])
                    ii = None
                    for tb in range(4):
                        ii = tt.transpose(banks[2][:, tb * 128:(tb + 1) * 128],
                                          bvb[:, 0:512][:, tb * 128:(tb + 1) * 128], ident_f)
                    tok["_otr"] = b.inc(ii, "t")

                def vo2_(v, c=c, o=o):
                    b.wait(v, "t", tok["_otr"])
                    v.drain()
                    ii = v.tensor_copy(attn1_tok[:, 4 * c:4 * c + 4, o * 128:(o + 1) * 128],
                                       banks[2].rearrange("p (z x) -> p z x", z=4))
                    tok[f"oev{c}_{o}"] = b.inc(ii, "v")
                b.step(tensor=to_)
                b.step(vector=vo_)
                b.step(tensor=to2_)
                b.step(vector=vo2_)
        tok["attn_done"] = tok[f"oev3_7"]
        tok["wq_gate_t"] = b.cnt["t"]

    qaG = SBn("qaG", [128, 1024], F32, at=row_off)
    qbG = SBn("qbG", [128, 1024], F32, at=xns_off + 4096)
    qa_c = SBn("qa_c", [128, 1024], F32, at=sg_off)
    qb_c = SBn("qb_c", [128, 1024], F32, at=xns_off)
    bm_c = SBn("bm_c", [128, 64], F32, at=cw_off)
    rs6_c = SBn("rs6_c", [128, 64], F32, at=cw_off + 256)
    bm_g = SBn("bm_g", [128, 64], F32, at=cw_off + 512)
    rs6_g = SBn("rs6_g", [128, 64], F32, at=cw_off + 768)
    bm_d = sq_s[:, 0:32]
    rs6_d = sq_s[:, 32:64]
    qaf = SBn("qaf", [128, 512], F32, at=sg_off + 4096)
    qbf = SBn("qbf", [128, 512], F32, at=cw_off)

    def ph_l1c():
        pieces = []
        for snum in range(22):
            pieces.append(("g", snum, 1024))
        for snum in range(22):
            pieces.append(("u", snum, 1024))
        for snum in range(8):
            for qq in range(3):
                pieces.append(("d", (snum, qq), 1024 if qq < 2 else 768))

        pv_tok = {}
        for idx, (kind, meta, W) in enumerate(pieces):
            par = idx % 2
            eng_v = True
            nblk = W // 128

            def yl(y, idx=idx, kind=kind, meta=meta, par=par, W=W):
                b.wait(y, "v", pv_tok.get(("cons", par), 0))
                b.wait(y, "g", pv_tok.get(("consG", par), 0))
                if idx < 2:
                    b.wait(y, "v", tok["attn_done"])
                    b.wait(y, "t", tok["wq_gate_t"])
                    b.wait(y, "sq0", tok["xq_spilled"][0])
                    b.wait(y, "sq1", tok["xq_spilled"][1])
                if kind == "d":
                    snum, qq = meta
                    dma = y.dma_start(out=wstr[par][:, 0:W],
                                      in_=wd1[128 * snum:128 * (snum + 1),
                                              qq * 1024:qq * 1024 + W])
                elif kind == "g":
                    dma = y.dma_start(out=wstr[par][:, 0:1024],
                                      in_=wg1[128 * meta:128 * (meta + 1), :])
                else:
                    dma = y.dma_start(out=wstr[par][:, 0:1024],
                                      in_=wu1[128 * meta:128 * (meta + 1), :])
                b.inc(dma, f"wst{par}", 16)
                pv_tok[("ld", idx)] = b.cnt[f"wst{par}"]

            def q_(eng, idx=idx, par=par, W=W, eng_v=eng_v):
                ename = "v" if eng_v else "g"
                b.wait(eng, f"wst{par}", pv_tok[("ld", idx)])
                b.wait(eng, "t", pv_tok.get("wqfree_v" if eng_v else "wqfree_g", 0))
                if idx < 4:
                    b.wait(eng, "v", tok["attn_done"])
                qa_, qb_, bm_, rs_ = (qa_c, qb_c, bm_c, rs6_c) if eng_v else \
                    (qaG, qbG, bm_g, rs6_g)
                iitok = emit_quant(eng, ename, wstr[par][:, 0:W], qb_[:, 0:W],
                                   qa_[:, 0:W], qb_[:, 0:W],
                                   bm_[:, 0:W // 16], rs_[:, 0:W // 16], W // 16)
                pv_tok[("q", idx)] = (ename, iitok)
                pv_tok[("cons" if eng_v else "consG", par)] = iitok

            b.step(sync=yl)
            if eng_v:
                b.step(vector=q_)
            else:
                b.step(gpsimd=q_)

            wq_ = qb_c if eng_v else qbG
            nbat = (nblk + 3) // 4
            for bb in range(nbat):
                blks = list(range(4 * bb, min(4 * bb + 4, nblk)))
                pb = banks[(idx * nbat + bb) % 4]

                def tt_(tt, idx=idx, bb=bb, blks=blks, pb=pb, wq_=wq_, eng_v=eng_v):
                    en, it = pv_tok[("q", idx)]
                    b.wait(tt, en, it)
                    b.wait(tt, "v", pv_tok.get(("bank_free", pb.tensor.name), 0))
                    ii = None
                    for zi, cb in enumerate(blks):
                        ii = tt.transpose(pb[:, zi * 128:zi * 128 + 128],
                                          wq_[:, cb * 128:(cb + 1) * 128], ident_f)
                    pv_tok[("tr", idx, bb)] = b.inc(ii, "t")

                def vv_(v, idx=idx, bb=bb, blks=blks, pb=pb, kind=kind, meta=meta,
                        nbat=nbat):
                    b.wait(v, "t", pv_tok[("tr", idx, bb)])
                    v.drain()
                    ii = None
                    for zi, cb in enumerate(blks):
                        srcb = pb[:, zi * 128:zi * 128 + 128]
                        if kind == "g":
                            dst = wgqT[:, cb, meta * 128:(meta + 1) * 128]
                        elif kind == "u":
                            dst = wuqT[:, cb, meta * 128:(meta + 1) * 128]
                        else:
                            snum, qq = meta
                            dst = wdqT[:, qq * 8 + cb, snum * 128:(snum + 1) * 128]
                        ii = v.tensor_copy(dst, srcb)
                    pv_tok[("bank_free", pb.tensor.name)] = b.inc(ii, "v")
                b.step(tensor=tt_)
                b.step(vector=vv_)
            pv_tok["wqfree_v" if eng_v else "wqfree_g"] = pv_tok[("tr", idx, nbat - 1)]

        tok["wq_done"] = b.cnt["v"]
        tok["wq_done_t"] = b.cnt["t"]

    def ph_l1d():
        fcw = [(0, 512), (512, 512), (1024, 512), (1536, 512), (2048, 512), (2560, 256)]
        for t in range(NT):
            def yl(y, t=t):
                if t >= 2:
                    b.wait(y, "t", tok[f"gud{t - 2}"])
                    b.wait(y, f"so{t % 2}", tok.get(f"st{t - 2}", 0))
                b.wait(y, "sq0", tok["xq_spilled"][0])
                b.wait(y, "sq1", tok["xq_spilled"][1])
                b.wait(y, "sxp", tok["spilled"])
                if t < 2:
                    b.wait(y, "v", tok["wq_done"])
                    b.wait(y, "t", tok["wq_done_t"])
                dma = y.dma_start(out=xq_tile[t % 2], in_=xqspill[:, :, t * 128:(t + 1) * 128])
                b.inc(dma, "sr1", 16)
                dma = y.dma_start(out=out_sb[t % 2],
                                  in_=xspill.rearrange("(n p) d -> p n d", p=128)[:, t, :])
                b.inc(dma, "sr1", 16)
                tok[f"rl{t}"] = b.cnt["sr1"]

            b.step(sync=yl)

            for fi, (f0, fw) in enumerate(fcw):
                def tg(tt, t=t, fi=fi, f0=f0, fw=fw):
                    b.wait(tt, "sr1", tok[f"rl{t}"])
                    if (t, fi) != (0, 0):
                        b.wait(tt, "v", tok.get("_hq", 0))
                    ii = None
                    for cc in range(NC):
                        tt.matmul(banks[0][:, 0:fw], xq_tile[t % 2][:, cc, :],
                                  wgqT[:, cc, f0:f0 + fw], start=(cc == 0), stop=(cc == NC - 1))
                        ii = tt.matmul(banks[1][:, 0:fw], xq_tile[t % 2][:, cc, :],
                                       wuqT[:, cc, f0:f0 + fw], start=(cc == 0), stop=(cc == NC - 1))
                    tok["_gu1"] = b.inc(ii, "t")

                def sg_(s_, t=t, fi=fi, fw=fw):
                    b.wait(s_, "t", tok["_gu1"])
                    ii = s_.activation(sgf_scr[:, 0:fw], banks[0][:, 0:fw], AF.Silu)
                    tok["_sg1"] = b.inc(ii, "s")

                def vh_(v, t=t, fi=fi, f0=f0, fw=fw):
                    b.wait(v, "t", tok["_gu1"])
                    b.wait(v, "s", tok["_sg1"])
                    v.drain()
                    v.tensor_tensor(hf_scr[:, 0:fw], sgf_scr[:, 0:fw], banks[1][:, 0:fw],
                                    op.mult)
                    v.drain()
                    hq_flat = hq_strip.rearrange("p a c -> p (a c)")
                    tok["_hq"] = emit_quant(v, "v", hf_scr[:, 0:fw], hq_flat[:, f0:f0 + fw],
                                            qaf[:, 0:fw], qbf[:, 0:fw],
                                            bm_d[:, 0:fw // 16], rs6_d[:, 0:fw // 16],
                                            fw // 16)
                b.step(tensor=tg, scalar=sg_)
                b.step(vector=vh_)

            for bb in range(6):
                blks = list(range(4 * bb, min(4 * bb + 4, 22)))
                pb = banks[2 + bb % 2]

                def th(tt, t=t, bb=bb, blks=blks, pb=pb):
                    b.wait(tt, "v", tok["_hq"])
                    b.wait(tt, "v", tok.get(("hqb", pb.tensor.name), 0))
                    hq_flat = hq_strip.rearrange("p a c -> p (a c)")
                    ii = None
                    for zi, cb in enumerate(blks):
                        ii = tt.transpose(pb[:, zi * 128:zi * 128 + 128],
                                          hq_flat[:, cb * 128:(cb + 1) * 128], ident_f)
                    tok["_htr"] = b.inc(ii, "t")

                def vhq(v, t=t, bb=bb, blks=blks, pb=pb):
                    b.wait(v, "t", tok["_htr"])
                    v.drain()
                    ii = None
                    for zi, cb in enumerate(blks):
                        ii = v.tensor_copy(hqT_scr[:, cb, :],
                                           pb[:, zi * 128:zi * 128 + 128])
                    tok[("hqb", pb.tensor.name)] = b.inc(ii, "v")
                    if bb == 5:
                        tok[f"hqev{t}"] = b.cnt["v"]
                b.step(tensor=th)
                b.step(vector=vhq)

            def td(tt, t=t):
                b.wait(tt, "v", tok[f"hqev{t}"])
                if t >= 1:
                    b.wait(tt, "v", tok.get(f"fin{t - 1}", 0))
                ii = None
                for oh in range(2):
                    pd = banks[4 + oh]
                    for cb in range(22):
                        ii = tt.matmul(pd, hqT_scr[:, cb, :], wdqT[:, cb, oh * 512:(oh + 1) * 512],
                                       start=(cb == 0), stop=(cb == 21))
                tok[f"gud{t}"] = b.inc(ii, "t")

            def vf(v, t=t):
                b.wait(v, "t", tok[f"gud{t}"])
                v.drain()
                ii = None
                for oh in range(2):
                    sl = slice(oh * 512, (oh + 1) * 512)
                    v.tensor_tensor(out_sb[t % 2][:, sl], out_sb[t % 2][:, sl],
                                    banks[4 + oh], op.add)
                    v.drain()
                    ii = v.tensor_tensor(out_sb[t % 2][:, sl], out_sb[t % 2][:, sl],
                                         attn1_tok[:, t, sl], op.add)
                    v.drain()
                tok[f"fin{t}"] = b.inc(ii, "v")

            def yo(y, t=t):
                b.wait(y, "v", tok[f"fin{t}"])
                dma = y.dma_start(out=out.rearrange("(n p) d -> p n d", p=128)[:, t, :],
                                  in_=out_sb[t % 2])
                tok[f"st{t}"] = b.inc(dma, f"so{t % 2}", 16)
            b.step(tensor=td)
            b.step(vector=vf)
            b.step(sync=yo)

        def yend(y):
            y.wait_ge(b.sem("so0"), b.cnt["so0"])
            y.wait_ge(b.sem("so1"), b.cnt["so1"])
        b.step(sync=yend)

    def dump_x_out(gate_sem, gate_val):
        def yout(y):
            b.wait(y, gate_sem, gate_val)
            for i in range(4):
                dma = y.dma_start(out=out.rearrange("(n p) d -> p n d", p=128)[:, 4 * i:4 * i + 4, :],
                                  in_=x_res[:, 4 * i:4 * i + 4, :])
                b.inc(dma, "so", 16)
            y.wait_ge(b.sem("so"), b.cnt["so"])
        b.step(sync=yout)

    ph_l1a()
    if upto in ("l1a", "l1skip_a"):
        dump_x_out("v", b.cnt["v"])
        return nc, dbg_out
    ph_l1b()
    if upto in ("l1b", "l1skip_b"):
        def yo2(g):
            b.wait(g, "v", tok["attn_done"])
            for i in range(4):
                dma = g.dma_start(out=out.rearrange("(n p) d -> p n d", p=128)[:, 4 * i:4 * i + 4, :],
                                  in_=attn1_tok[:, 4 * i:4 * i + 4, :])
                b.inc(dma, "so", 16)
            g.wait_ge(b.sem("so"), b.cnt["so"])
        b.step(gpsimd=yo2)
        return nc, dbg_out
    ph_l1c()
    if upto in ("l1c", "l1skip_c"):
        dump_x_out("v", tok["wq_done"])
        return nc, dbg_out
    ph_l1d()
    return nc, dbg_out


def kernel(**inputs):
    nc, _ = build(upto="full")
    xs = np.ascontiguousarray(inputs["L0_attn_x"], dtype=np.float32)
    weights = {kk: np.ascontiguousarray(v) for kk, v in inputs.items() if kk != "L0_attn_x"}
    in_maps = []
    for c in range(8):
        m = {"x": xs[c]}
        m.update(weights)
        in_maps.append(m)
    res = run_bass_kernel_spmd(nc, in_maps, core_ids=list(range(8)))
    return np.stack([r["out"] for r in res.results], axis=0)


# version 02
# speedup vs baseline: 1.0380x; 1.0380x over previous
import numpy as np
import concourse.bass as bass
import concourse.mybir as mybir
from concourse.alu_op_type import AluOpType as op
from concourse.bass_utils import run_bass_kernel_spmd

F32 = mybir.dt.float32
BF16 = mybir.dt.bfloat16
AF = mybir.ActivationFunctionType
AXX = mybir.AxisListType.X

T, D = 2048, 1024
NT, NC = T // 128, D // 128
NEXP, FH = 16, 512
HH, NW, HD, FI = 16, 16, 64, 2816
TC, NTC = 256, T // 256


class B:

    def __init__(self):
        self.nc = bass.Bass("TRN2", target_bir_lowering=False)
        self.sems, self.cnt, self.wm = {}, {}, {}

    def sem(self, name):
        if name not in self.sems:
            self.sems[name] = self.nc.alloc_semaphore(name=name)
            self.cnt[name] = 0
        return self.sems[name]

    def inc(self, instr, name, n=1):
        instr.then_inc(self.sem(name), n)
        self.cnt[name] += n
        return self.cnt[name]

    def wait(self, eng, name, val):
        if val is None or val <= 0:
            return
        wm = self.wm.setdefault(str(eng.engine), {})
        if wm.get(name, -1) >= val:
            return
        eng.wait_ge(self.sem(name), val)
        wm[name] = val

    def step(self, **fns):
        with self.nc.Block() as blk:
            for nm, fn in fns.items():
                getattr(blk, nm)(fn)


def build(upto="full", dbg=(), l1a_tiles=NT, l1a_parts=7):
    b = B()
    nc = b.nc
    SB = nc.alloc_sbuf_tensor
    PS = nc.alloc_psum_tensor

    x_in = nc.dram_tensor("x", [T, D], F32, kind="ExternalInput").ap()
    w_qkv = nc.dram_tensor("L0_attn_w_qkv", [3 * D, D], F32, kind="ExternalInput").ap()
    w_o0 = nc.dram_tensor("L0_attn_w_o", [D, D], F32, kind="ExternalInput").ap()
    Wr = nc.dram_tensor("L0_ffn_Wr", [D, NEXP], F32, kind="ExternalInput").ap()
    Wg = nc.dram_tensor("L0_ffn_Wg", [NEXP, D, FH], F32, kind="ExternalInput").ap()
    Wu = nc.dram_tensor("L0_ffn_Wu", [NEXP, D, FH], F32, kind="ExternalInput").ap()
    Wd = nc.dram_tensor("L0_ffn_Wd", [NEXP, FH, D], F32, kind="ExternalInput").ap()
    Sg = nc.dram_tensor("L0_ffn_Sg", [D, 2 * FH], F32, kind="ExternalInput").ap()
    Su = nc.dram_tensor("L0_ffn_Su", [D, 2 * FH], F32, kind="ExternalInput").ap()
    Sd = nc.dram_tensor("L0_ffn_Sd", [2 * FH, D], F32, kind="ExternalInput").ap()
    Wq1 = nc.dram_tensor("L1_attn_Wq", [D, HH * NW], F32, kind="ExternalInput").ap()
    bq1 = nc.dram_tensor("L1_attn_bq", [HH * NW], F32, kind="ExternalInput").ap()
    Wk1 = nc.dram_tensor("L1_attn_Wk", [D, HH * NW], F32, kind="ExternalInput").ap()
    bk1 = nc.dram_tensor("L1_attn_bk", [HH * NW], F32, kind="ExternalInput").ap()
    Wv1 = nc.dram_tensor("L1_attn_Wv", [D, D], F32, kind="ExternalInput").ap()
    bv1 = nc.dram_tensor("L1_attn_bv", [D], F32, kind="ExternalInput").ap()
    Wo1 = nc.dram_tensor("L1_attn_Wo", [D, D], F32, kind="ExternalInput").ap()
    bo1 = nc.dram_tensor("L1_attn_bo", [D], F32, kind="ExternalInput").ap()
    qph = nc.dram_tensor("L1_attn_q_phase", [1, HH, 1, NW], F32, kind="ExternalInput").ap()
    kph = nc.dram_tensor("L1_attn_k_phase", [1, HH, 1, NW], F32, kind="ExternalInput").ap()
    temp = nc.dram_tensor("L1_attn_temperature", [1], F32, kind="ExternalInput").ap()
    wg1 = nc.dram_tensor("L1_ffn_w_gate", [FI, D], F32, kind="ExternalInput").ap()
    wu1 = nc.dram_tensor("L1_ffn_w_up", [FI, D], F32, kind="ExternalInput").ap()
    wd1 = nc.dram_tensor("L1_ffn_w_down", [D, FI], F32, kind="ExternalInput").ap()
    out = nc.dram_tensor("out", [T, D], F32, kind="ExternalOutput").ap()
    xspill = nc.dram_tensor("xspill", [T, D], F32, kind="Internal").ap()
    xqspill = nc.dram_tensor("xqspill", [128, NC, T], BF16, kind="Internal").ap()
    dbg_out = {}
    for name, shape, dt in dbg:
        dbg_out[name] = nc.dram_tensor("dbg_" + name, shape, dt, kind="ExternalOutput").ap()

    _off = [16512]

    def SBp(name, shape, dtype, at=None):
        o = _off[0] if at is None else at
        h = nc.alloc_sbuf_tensor_at(name, shape, dtype, offset=o)
        if at is None:
            nbytes = int(np.prod(shape[1:])) * (2 if dtype == BF16 else 4)
            _off[0] = o + ((nbytes + 31) // 32) * 32
        return h.ap(), o

    def SBn(name, shape, dtype, at=None):
        return SBp(name, shape, dtype, at)[0]

    ident_f = SBn("ident_f", [128, 128], F32)
    ident_b = SBn("ident_b", [128, 128], BF16)
    ones_col = SBn("ones_col", [128, 1], F32)
    ones_colb = SBn("ones_colb", [128, 1], BF16)
    ones_row = SBn("ones_row", [1, 128], F32)
    neg1 = SBn("neg1", [128, 1], F32)
    rtemp = SBn("rtemp", [128, 1], F32)
    mv_all = SBn("mv_all", [128, NT, 2], F32)
    rstd_all = SBn("rstd_all", [128, NT], F32)
    bn6 = SBn("bn6", [128, 2, 6], F32)
    cw_all, cw_off = SBp("cw_all", [128, NT, NEXP], F32)
    cwb = SBn("cwb", [128, TC], F32)
    cwrow = SBn("cwrow", [1, TC], F32)
    wr_hi, wrh_off = SBp("wr_hi", [128, NC, NEXP], BF16)
    wr_lo = SBn("wr_lo", [128, NC, NEXP], BF16)
    wr_f = SBn("wr_f", [128, NC, NEXP], F32)
    val8 = SBn("val8", [128, 8], F32)
    a08 = SBn("a08", [128, 8], F32)
    row_scr, row_off = SBp("row_scr", [1, D], F32)
    h2_scr = SBn("h2_scr", [128, 4, TC], BF16, at=row_off)
    sg_scr, sg_off = SBp("sg_scr", [128, 4, TC], F32)
    h_scr, h_off = SBp("h_scr", [128, 4, TC], BF16)
    attn0b, a0_off = SBp("attn0b", [128, D], F32)
    bcast_a, bc_off = SBp("bcast_a", [128, D], F32)
    xn_scr, xns_off = SBp("xn_scr", [128, 2, D], F32)
    xnT_hi, xh_off = SBp("xnT_hi", [128, NC, T], BF16)
    xnT_lo, xl_off = SBp("xnT_lo", [128, NC, T], BF16)
    racc = xnT_lo
    ws0, ws0_off = SBp("wslot0", [128, 16 * FH], BF16)
    ws1, ws1_off = SBp("wslot1", [128, 16 * FH], BF16)
    wslot = [ws0, ws1]
    wd_buf, wd_off = SBp("wd_buf", [128, 4, D], BF16)
    wd_buf2, wd2_off = SBp("wd_buf2", [128, 4, D], BF16)
    x_res, x_off = SBp("x_res", [128, NT, D], F32)
    assert _off[0] <= 229344, f"sbuf map overflow: {_off[0]}"
    comb_scr = bcast_a.rearrange("p (c x) -> p c x", c=8)
    stream_a = [xn_scr[:, i, :] for i in range(2)]
    ttr_scr = sg_scr.rearrange("p a c -> p (a c)")
    cwrow_e = SBn("cwrow_e", [1, T], F32, at=xns_off)
    cwb_e = SBn("cwb_e", [128, T], BF16, at=bc_off)
    qTpad = racc[:, 0:4, :]
    kTpad = racc[:, 4:8, :]
    Vbuf = SBn("Vbuf", [128, NT, D], BF16, at=x_off)
    attn1_tok = SBn("attn1_tok", [128, NT, D], BF16, at=x_off + 32768)
    wgqT = SBn("wgqT", [128, NC, FI], BF16, at=xh_off)
    wuqT = SBn("wuqT", [128, NC, FI], BF16, at=xh_off + 45056)
    wdqT = SBn("wdqT", [128, 22, D], BF16, at=xh_off + 90112)
    P_pool = [SBn(f"Ppool{i}", [128, 512], BF16, at=xns_off + 1024 * i) for i in range(12)]
    attnT = SBn("attnT", [128, NC, 512], BF16, at=wd2_off)
    hq_strip = SBn("hq_strip", [128, 22, 128], F32, at=x_off + 20480)
    hqT_scr = SBn("hqT_scr", [128, 22, 128], BF16, at=xns_off)
    hf_scr = SBn("hf_scr", [128, 512], F32, at=sg_off)
    sgf_scr = SBn("sgf_scr", [128, 512], F32, at=sg_off + 2048)
    out_sb = [SBn(f"out_sb{i}", [128, D], F32, at=row_off if i == 0 else a0_off)
              for i in range(2)]
    xq_tileA = [SBn(f"xq_tileA{i}", [128, NC, 128], BF16, at=xl_off + 2048 * i)
                for i in range(2)]
    xq_tile = [SBn(f"xq_tileD{i}", [128, NC, 128], BF16, at=bc_off + 2048 * i)
               for i in range(2)]
    wstr = [SBn("wstr0", [128, 1024], F32, at=a0_off),
            SBn("wstr1", [128, 1024], F32, at=bc_off)]
    l1scr = SBn("l1scr", [128, 768], F32, at=sg_off)

    banks = [PS(f"bank{i}", [128, 512], F32).ap() for i in range(8)]

    tok = {}

    def ph_consts():
        def g(g):
            g.memset(ident_f, 0.0)
            g.memset(ident_b, 0.0)
            g.memset(ones_col, 1.0)
            g.memset(ones_colb, 1.0)
            g.memset(ones_row, 1.0)
            g.memset(neg1, -1.0)
            g.drain()
            g.affine_select(out=ident_f, in_=ident_f, compare_op=op.not_equal,
                            fill=1.0, base=0, pattern=[[-1, 128]], channel_multiplier=1)
            i2 = g.affine_select(out=ident_b, in_=ident_b, compare_op=op.not_equal,
                                 fill=1.0, base=0, pattern=[[-1, 128]], channel_multiplier=1)
            b.inc(i2, "g")
            tok["consts"] = b.cnt["g"]
        b.step(gpsimd=g)

    def ph_ln_stats(tag, gate=None):
        def v(v):
            if gate is None:
                b.wait(v, "sx", tok["x_loaded"])
            else:
                b.wait(v, gate[0], gate[1])
            for t in range(NT):
                xt = x_res[:, t, :]
                v.bn_stats(bn6[:, 0:1, :], xt[:, 0:512].rearrange("p (a c) -> p a c", a=1))
                v.bn_stats(bn6[:, 1:2, :], xt[:, 512:1024].rearrange("p (a c) -> p a c", a=1))
                v.drain()
                v.bn_aggr(mv_all[:, t, :], bn6.rearrange("p a b -> p (a b)"))
                v.drain()
            ii = v.tensor_scalar(rstd_all, mv_all[:, :, 1], 1e-5, None, op.add)
            tok[f"vareps{tag}"] = b.inc(ii, "v")

        def s(s):
            b.wait(s, "v", tok[f"vareps{tag}"])
            ii = s.activation(rstd_all, rstd_all, AF.Sqrt)
            tok[f"sqrt{tag}"] = b.inc(ii, "s")
        b.step(vector=v, scalar=s)

        def v2(v):
            b.wait(v, "s", tok[f"sqrt{tag}"])
            ii = v.reciprocal(rstd_all, rstd_all)
            tok[f"rstd{tag}"] = b.inc(ii, "v")
        b.step(vector=v2)

    def ph_l0a():
        def y(y):
            for i in range(4):
                dma = y.dma_start(out=x_res[:, 4 * i:4 * i + 4, :],
                                  in_=x_in.rearrange("(n p) d -> p n d", p=128)[:, 4 * i:4 * i + 4, :])
                b.inc(dma, "sx", 16)
            tok["x_loaded"] = b.cnt["sx"]
            dma = y.dma_start(out=wr_f, in_=Wr.rearrange("(c p) e -> p c e", p=128))
            b.inc(dma, "sw0", 16)
            dma = y.dma_start(out=rtemp[0:1, :], in_=temp[None, :])
            b.inc(dma, "sw0", 16)
            tok["wr_loaded"] = b.cnt["sw0"]
        b.step(sync=y)

        ph_ln_stats("0")

        for t in range(NT):
            scr = xn_scr[:, t % 2, :]
            p0, p1 = banks[t % 2], banks[2 + t % 2]

            def v(v, t=t, scr=scr):
                b.wait(v, "v", tok["rstd0"])
                if t >= 2:
                    b.wait(v, "t", tok[f"tr{t - 2}"])
                v.drain()
                ii = v.tensor_scalar(scr, x_res[:, t, :], mv_all[:, t, 0:1],
                                     rstd_all[:, t:t + 1], op.subtract, op.mult)
                tok[f"ln{t}"] = b.inc(ii, "v")

            def tt(tt, t=t, scr=scr, p0=p0, p1=p1):
                b.wait(tt, "g", tok["consts"])
                b.wait(tt, "v", tok[f"ln{t}"])
                if t >= 2:
                    b.wait(tt, "v", tok[f"ev{t - 2}"])
                for c in range(8):
                    half = p0 if c < 4 else p1
                    tt.transpose(half[:, (c % 4) * 128:(c % 4) * 128 + 128],
                                 scr[:, c * 128:c * 128 + 128], ident_f)
                tt.matmul(banks[4][0:1, 0:512], ones_col, scr[:, 0:512],
                          start=(t == 0), stop=(t == NT - 1))
                ii = tt.matmul(banks[5][0:1, 0:512], ones_col, scr[:, 512:1024],
                               start=(t == 0), stop=(t == NT - 1))
                tok[f"tr{t}"] = b.inc(ii, "t")

            def v2(v, t=t, p0=p0, p1=p1):
                b.wait(v, "t", tok[f"tr{t}"])
                ii = None
                for h, half in ((0, p0), (1, p1)):
                    hi = xnT_hi[:, 4 * h:4 * h + 4, t * 128:(t + 1) * 128]
                    lo = xnT_lo[:, 4 * h:4 * h + 4, t * 128:(t + 1) * 128]
                    h3 = half.rearrange("p (c x) -> p c x", c=4)
                    v.tensor_copy(hi, h3)
                    v.drain()
                    ii = v.scalar_tensor_tensor(lo, hi, -1.0, h3, op.mult, op.add)
                tok[f"ev{t}"] = b.inc(ii, "v")

            b.step(vector=v)
            b.step(tensor=tt, vector=v2)

    def ph_l0b():
        def v(v):
            b.wait(v, "sw0", tok["wr_loaded"])
            v.tensor_copy(wr_hi.rearrange("p c e -> p (c e)"), wr_f.rearrange("p c e -> p (c e)"))
            v.drain()
            v.scalar_tensor_tensor(wr_lo.rearrange("p c e -> p (c e)"),
                                   wr_hi.rearrange("p c e -> p (c e)"), -1.0,
                                   wr_f.rearrange("p c e -> p (c e)"), op.mult, op.add)
            v.drain()
            ii = v.reciprocal(rtemp[0:1, :], rtemp[0:1, :])
            tok["wrhl"] = b.inc(ii, "v")
        b.step(vector=v)

        for t in range(NT):
            pr = banks[6 + t % 2][:, 0:NEXP]

            def tt(tt, t=t, pr=pr):
                b.wait(tt, "v", max(tok["wrhl"], tok[f"ev{t}"]))
                if t >= 2:
                    b.wait(tt, "v", tok[f"cw{t - 2}"])
                ii = None
                for c in range(NC):
                    lhs_hi = xnT_hi[:, c, t * 128:(t + 1) * 128]
                    lhs_lo = xnT_lo[:, c, t * 128:(t + 1) * 128]
                    tt.matmul(pr, lhs_hi, wr_hi[:, c, :], start=(c == 0), stop=False)
                    tt.matmul(pr, lhs_hi, wr_lo[:, c, :], start=False, stop=False)
                    ii = tt.matmul(pr, lhs_lo, wr_hi[:, c, :], start=False,
                                   stop=(c == NC - 1))
                tok[f"rt{t}"] = b.inc(ii, "t")

            def v1(v, t=t, pr=pr):
                b.wait(v, "t", tok[f"rt{t}"])
                v.drain()
                ii = v.tensor_reduce(ttr_scr[:, 0:1], pr, AXX, op.max, negate=True)
                tok[f"nmax{t}"] = b.inc(ii, "v")

            def s1(s, t=t, pr=pr):
                b.wait(s, "v", tok[f"nmax{t}"])
                ii = s.activation(ttr_scr[:, 4:4 + NEXP], pr, AF.Exp,
                                  bias=ttr_scr[:, 0:1], scale=1.0,
                                  accum_out=ttr_scr[:, 1:2])
                tok[f"exp{t}"] = b.inc(ii, "s")

            def v2(v, t=t):
                b.wait(v, "s", tok[f"exp{t}"])
                pe = ttr_scr[:, 4:4 + NEXP]
                v.tensor_reduce(ttr_scr[:, 2:3], pe, AXX, op.max)
                v.drain()
                v.tensor_scalar(ttr_scr[:, 32:32 + NEXP], pe, ttr_scr[:, 2:3], None, op.is_equal)
                v.drain()
                v.scalar_tensor_tensor(ttr_scr[:, 64:64 + NEXP], ttr_scr[:, 32:32 + NEXP],
                                       -1e30, pe, op.mult, op.add)
                v.drain()
                v.tensor_reduce(ttr_scr[:, 3:4], ttr_scr[:, 64:64 + NEXP], AXX, op.max)
                v.drain()
                v.tensor_scalar(ttr_scr[:, 32:32 + NEXP], pe, ttr_scr[:, 3:4], None, op.is_ge)
                v.reciprocal(ttr_scr[:, 1:2], ttr_scr[:, 1:2])
                v.drain()
                ii = v.scalar_tensor_tensor(cw_all[:, t, :], pe, ttr_scr[:, 1:2],
                                            ttr_scr[:, 32:32 + NEXP], op.mult, op.mult)
                tok[f"cw{t}"] = b.inc(ii, "v")

            b.step(tensor=tt, vector=v1)
            b.step(scalar=s1)
            b.step(vector=v2)

        def v3(v):
            b.wait(v, "t", tok[f"tr{NT - 1}"])
            v.tensor_copy(row_scr[0:1, 0:512], banks[4][0:1, 0:512])
            ii = v.tensor_copy(row_scr[0:1, 512:1024], banks[5][0:1, 0:512])
            tok["vsrow"] = b.inc(ii, "v")

        def t3(tt):
            b.wait(tt, "v", tok["vsrow"])
            tt.matmul(banks[4], ones_row, row_scr[0:1, 0:512], start=True, stop=True)
            ii = tt.matmul(banks[5], ones_row, row_scr[0:1, 512:1024], start=True, stop=True)
            tok["vsrep"] = b.inc(ii, "t")

        def v4(v):
            b.wait(v, "t", tok["vsrep"])
            v.tensor_copy(bcast_a[:, 0:512], banks[4])
            ii = v.tensor_copy(bcast_a[:, 512:1024], banks[5])
            tok["vsb"] = b.inc(ii, "v")
        b.step(vector=v3)
        b.step(tensor=t3, vector=v4)

        def matvec(src_ap, res8, toks):
            for c in range(NC):
                def y(y, c=c):
                    b.wait(y, "v", toks[c - 2])
                    dma = y.dma_start(out=stream_a[c % 2],
                                      in_=src_ap[128 * c:128 * (c + 1), :])
                    b.inc(dma, f"swv{c % 2}", 16)
                    tok["_mv_dma"] = b.cnt[f"swv{c % 2}"]

                def v(v, c=c):
                    b.wait(v, f"swv{c % 2}", tok["_mv_dma"])
                    b.wait(v, "v", tok["_mv_src"])
                    v.drain()
                    v.tensor_tensor(ttr_scr, stream_a[c % 2], bcast_a, op.mult)
                    v.drain()
                    ii = v.tensor_reduce(res8[:, c:c + 1], ttr_scr, AXX, op.add)
                    toks[c] = b.inc(ii, "v")
                b.step(sync=y, vector=v)

        tok["_mv_src"] = tok["vsb"]
        wv_toks = {-2: 0, -1: 0}
        matvec(w_qkv[2 * D:3 * D, :], val8, wv_toks)

        def y5(y):
            b.wait(y, "v", wv_toks[NC - 1])
            for c in range(NC):
                ii = y.dma_start(out=row_scr[0:1, c * 128:(c + 1) * 128], in_=val8[:, c:c + 1])
                b.inc(ii, "sg2", 16)
            tok["valrow"] = b.cnt["sg2"]

        def t5(tt):
            b.wait(tt, "sg2", tok["valrow"])
            b.wait(tt, "v", wv_toks[NC - 1])
            tt.matmul(banks[4], ones_row, row_scr[0:1, 0:512], start=True, stop=True)
            ii = tt.matmul(banks[5], ones_row, row_scr[0:1, 512:1024], start=True, stop=True)
            tok["valrep"] = b.inc(ii, "t")

        def v5(v):
            b.wait(v, "t", tok["valrep"])
            v.drain()
            v.tensor_copy(bcast_a[:, 0:512], banks[4])
            ii = v.tensor_copy(bcast_a[:, 512:1024], banks[5])
            tok["valb"] = b.inc(ii, "v")
        b.step(sync=y5)
        b.step(tensor=t5, vector=v5)

        tok["_mv_src"] = tok["valb"]
        wo_toks = {-2: wv_toks[NC - 2], -1: wv_toks[NC - 1]}
        matvec(w_o0, a08, wo_toks)

        def y6(y):
            b.wait(y, "v", wo_toks[NC - 1])
            b.wait(y, "t", tok["valrep"])
            for c in range(NC):
                ii = y.dma_start(out=row_scr[0:1, c * 128:(c + 1) * 128], in_=a08[:, c:c + 1])
                b.inc(ii, "sg2", 16)
            tok["a0row"] = b.cnt["sg2"]

        def t6(tt):
            b.wait(tt, "sg2", tok["a0row"])
            b.wait(tt, "v", wo_toks[NC - 1])
            tt.matmul(banks[4], ones_row, row_scr[0:1, 0:512], start=True, stop=True)
            ii = tt.matmul(banks[5], ones_row, row_scr[0:1, 512:1024], start=True, stop=True)
            tok["a0rep"] = b.inc(ii, "t")

        def v6(v):
            b.wait(v, "t", tok["a0rep"])
            v.tensor_copy(attn0b[:, 0:512], banks[4])
            ii = v.tensor_copy(attn0b[:, 512:1024], banks[5])
            tok["a0b"] = b.inc(ii, "v")
        b.step(sync=y6)
        b.step(tensor=t6, vector=v6)

    def wslot_views(i):
        w = wslot[i]
        wg = w[:, 0:8 * FH].rearrange("p (c f) -> p c f", c=8)
        wu = w[:, 8 * FH:16 * FH].rearrange("p (c f) -> p c f", c=8)
        return wg, wu, wd_buf

    def gu_srcs(s):
        if s < NEXP:
            return Wg[s], Wu[s]
        hh = s - NEXP
        return Sg[:, hh * FH:(hh + 1) * FH], Su[:, hh * FH:(hh + 1) * FH]

    def wd_src(s):
        return Wd[s] if s < NEXP else Sd[(s - NEXP) * FH:(s - NEXP + 1) * FH, :]

    def emit_slot_load(g, s):
        slot = s % 2
        g_src, u_src = gu_srcs(s)
        wgv, wuv, _ = wslot_views(slot)
        i1 = g.dma_start(out=wgv, in_=g_src.rearrange("(c p) f -> p c f", p=128))
        b.inc(i1, f"wsl{slot}", 16)
        i2 = g.dma_start(out=wuv, in_=u_src.rearrange("(c p) f -> p c f", p=128))
        b.inc(i2, f"wsl{slot}", 16)
        tok[f"wload{s}"] = b.cnt[f"wsl{slot}"]

    def emit_wd_load(g, s):
        ii = g.dma_start(out=wd_buf, in_=wd_src(s).rearrange("(m p) f -> p m f", p=128))
        b.inc(ii, "wdl", 16)
        tok[f"wdload{s}"] = b.cnt["wdl"]

    def emit_wd_load2(g, s):
        ii = g.dma_start(out=wd_buf2, in_=wd_src(s).rearrange("(m p) f -> p m f", p=128))
        b.inc(ii, "wdl", 16)
        tok[f"wdload{s}"] = b.cnt["wdl"]

    def ph_l0c():
        def g0(g):
            emit_slot_load(g, 0)
            emit_slot_load(g, 1)
            emit_wd_load(g, 0)
        b.step(gpsimd=g0)

        def emit_sltc(s, tc, wgv, wuv, wdv, shared):
            tsl = slice(tc * TC, (tc + 1) * TC)
            t0, t1 = 2 * tc, 2 * tc + 1
            if True:
                if not shared:
                    def y(y, s=s, tc=tc, t0=t0, t1=t1):
                        b.wait(y, "v", max(tok[f"cw{t0}"], tok[f"cw{t1}"]))
                        if (s, tc) != (0, 0):
                            b.wait(y, "t", tok["cwrep"])
                        i1 = y.dma_start(out=cwrow[0:1, 0:128], in_=cw_all[:, t0, s:s + 1])
                        b.inc(i1, "sg2", 16)
                        i2 = y.dma_start(out=cwrow[0:1, 128:256], in_=cw_all[:, t1, s:s + 1])
                        b.inc(i2, "sg2", 16)
                        tok["cwdma"] = b.cnt["sg2"]

                    def tcw(tt, s=s, tc=tc):
                        b.wait(tt, "sg2", tok["cwdma"])
                        ii = tt.matmul(banks[2][:, 0:TC], ones_row, cwrow[0:1, :],
                                       start=True, stop=True)
                        tok["cwrep"] = b.inc(ii, "t")

                    def vcw(v, s=s, tc=tc):
                        b.wait(v, "t", tok["cwrep"])
                        v.drain()
                        ii = v.tensor_copy(cwb, banks[2][:, 0:TC])
                        tok["cwb"] = b.inc(ii, "v")
                    b.step(sync=y, tensor=tcw)
                    b.step(vector=vcw)

                def tgu(tt, s=s, tc=tc, tsl=tsl, wgv=wgv, wuv=wuv, shared=shared):
                    b.wait(tt, f"wsl{s % 2}", tok[f"wload{s}"])
                    if not shared:
                        b.wait(tt, "v", tok["cwb"])
                    elif s == NEXP and "xadd" in tok:
                        b.wait(tt, "v", tok["xadd"])
                    for m in range(4):
                        pg = banks[m // 2][:, (m % 2) * TC:(m % 2) * TC + TC]
                        pu = banks[2 + m // 2][:, (m % 2) * TC:(m % 2) * TC + TC]
                        for c in range(NC):
                            rhs = xnT_hi[:, c, tsl]
                            tt.matmul(pg, wgv[:, c, m * 128:(m + 1) * 128], rhs,
                                      start=(c == 0), stop=(c == NC - 1))
                            ii = tt.matmul(pu, wuv[:, c, m * 128:(m + 1) * 128], rhs,
                                           start=(c == 0), stop=(c == NC - 1))
                    tok["gu"] = b.inc(ii, "t")

                def ssg(ss, s=s):
                    b.wait(ss, "t", tok["gu"])
                    ii = None
                    for m2 in range(2):
                        ii = ss.activation(sg_scr[:, 2 * m2:2 * m2 + 2, :]
                                           .rearrange("p a c -> p (a c)"),
                                           banks[m2], AF.Silu)

                    tok["sg"] = b.inc(ii, "s")

                def vh(v, s=s, tc=tc, shared=shared):
                    b.wait(v, "t", tok["gu"])
                    b.wait(v, "s", tok["sg"])
                    v.drain()
                    ii = None
                    if shared:
                        for m2 in range(2):
                            ii = v.tensor_tensor(h_scr[:, 2 * m2:2 * m2 + 2, :],
                                                 sg_scr[:, 2 * m2:2 * m2 + 2, :],
                                                 banks[2 + m2].rearrange("p (a x) -> p a x", a=2),
                                                 op.mult)
                    else:
                        cb = cwb.rearrange("p x -> p () x").broadcast_to([128, 2, TC])
                        for m2 in range(2):
                            v.tensor_tensor(h_scr[:, 2 * m2:2 * m2 + 2, :],
                                            banks[2 + m2].rearrange("p (a x) -> p a x", a=2),
                                            cb, op.mult)
                        v.drain()
                        ii = v.tensor_tensor(h_scr.rearrange("p a c -> p (a c)"),
                                             sg_scr.rearrange("p a c -> p (a c)"),
                                             h_scr.rearrange("p a c -> p (a c)"), op.mult)
                    tok["h"] = b.inc(ii, "v")

                def tdn(tt, s=s, tc=tc, shared=shared, wdv=wdv):
                    b.wait(tt, "v", tok["h"])
                    b.wait(tt, "wdl", tok[f"wdload{s}"])
                    ii = None
                    for o in range(8):
                        pd = banks[4 + o // 2][:, (o % 2) * TC:(o % 2) * TC + TC]
                        for m in range(4):
                            ii = tt.matmul(pd, wdv[:, m, o * 128:(o + 1) * 128],
                                           h_scr[:, m, :], start=(m == 0), stop=(m == 3))
                    tok["dn"] = b.inc(ii, "t")
                    if tc == NTC - 1:
                        tok[f"mmdone{s}"] = b.cnt["t"]

                b.step(tensor=tgu, scalar=ssg)
                b.step(vector=vh)
                b.step(tensor=tdn)

                if not shared:
                    def vr(v, s=s, tc=tc, tsl=tsl):
                        b.wait(v, "t", tok["dn"])
                        v.drain()
                        ii = None
                        for b4 in range(4):
                            dst = racc[:, 2 * b4:2 * b4 + 2, tsl]
                            src = banks[4 + b4].rearrange("p (c x) -> p c x", c=2)
                            if s == 0:
                                ii = v.tensor_copy(dst, src)
                            else:
                                ii = v.tensor_tensor(dst, dst, src, op.add)
                            v.drain()
                        tok[f"racc{tc}"] = b.inc(ii, "v")
                    b.step(vector=vr)
                elif s == NEXP + 1:
                    for tb in range(2):
                        tglob = 2 * tc + tb

                        def vc(v, tc=tc, tb=tb, tglob=tglob):
                            b.wait(v, "t", tok["dn"])
                            if "combtr" in tok:
                                b.wait(v, "t", tok["combtr"])
                            v.drain()
                            ii = None
                            for b4 in range(4):
                                src = racc[:, 2 * b4:2 * b4 + 2,
                                           tglob * 128:(tglob + 1) * 128]
                                pk = banks[4 + b4].rearrange("p (c y x) -> p c y x", c=2, y=2)[:, :, tb, :]
                                ii = v.tensor_tensor(comb_scr[:, 2 * b4:2 * b4 + 2, :],
                                                     src, pk, op.add)
                            tok["comb"] = b.inc(ii, "v")
                        b.step(vector=vc)

                        def ttr(tt, tc=tc, tb=tb):
                            b.wait(tt, "v", tok["comb"])
                            if tc > 0 or tb > 0:
                                b.wait(tt, "v", tok["xadd"])
                            ii = None
                            for c in range(NC):
                                dstb = banks[2 + c // 4]
                                ii = tt.transpose(dstb[:, (c % 4) * 128:(c % 4) * 128 + 128],
                                                  comb_scr[:, c, :], ident_f)
                            tok["combtr"] = b.inc(ii, "t")
                        b.step(tensor=ttr)

                        def vx(v, tglob=tglob):
                            b.wait(v, "t", tok["combtr"])
                            v.drain()
                            xt = x_res[:, tglob, :]
                            v.tensor_tensor(xt[:, 0:512], xt[:, 0:512], banks[2], op.add)
                            ii = v.tensor_tensor(xt[:, 512:1024], xt[:, 512:1024],
                                                 banks[3], op.add)
                            tok["xadd"] = b.inc(ii, "v")
                        b.step(vector=vx)

        pend_dn = [None]

        def emit_dn(s, tc):
            def tdn(tt, s=s, tc=tc):
                b.wait(tt, "v", tok[f"h{s}_{tc}"])
                b.wait(tt, "wdl", tok[f"wdload{s}"])
                if (s, tc) != (0, 0):
                    b.wait(tt, "v", tok["racc_last"])
                wdv = wd_buf
                hb = h_scr if tc % 2 == 0 else h2_scr
                ii = None
                for o in range(8):
                    pd = banks[4 + o // 2][:, (o % 2) * TC:(o % 2) * TC + TC]
                    for m in range(4):
                        ii = tt.matmul(pd, wdv[:, m, o * 128:(o + 1) * 128],
                                       hb[:, m, :], start=(m == 0), stop=(m == 3))
                tok[f"dn{s}_{tc}"] = b.inc(ii, "t")
                if tc == NTC - 1:
                    tok[f"mmdone{s}"] = b.cnt["t"]

            def vr(v, s=s, tc=tc):
                b.wait(v, "t", tok[f"dn{s}_{tc}"])
                v.drain()
                ii = None
                for b4 in range(4):
                    dst = racc[:, 2 * b4:2 * b4 + 2, tc * TC:(tc + 1) * TC]
                    src = banks[4 + b4].rearrange("p (c x) -> p c x", c=2)
                    if s == 0:
                        ii = v.tensor_copy(dst, src)
                    else:
                        ii = v.tensor_tensor(dst, dst, src, op.add)
                    v.drain()
                tok["racc_last"] = b.inc(ii, "v")
            b.step(tensor=tdn)
            b.step(vector=vr)

        for s in range(NEXP):
            wgv, wuv, wdv = wslot_views(s % 2)

            if s == 0:
                def ycw0(y):
                    b.wait(y, "v", tok[f"cw{NT - 1}"])
                    for t in range(NT):
                        dma = y.dma_start(out=cwrow_e[0:1, t * 128:(t + 1) * 128],
                                          in_=cw_all[:, t, 0:1])
                        b.inc(dma, "sg2", 16)
                    tok["cwg"] = b.cnt["sg2"]
                b.step(sync=ycw0)

            def tcw(tt, s=s):
                b.wait(tt, "sg2", tok["cwg"])
                if s >= 1:
                    b.wait(tt, "v", tok[f"h{s - 1}_{NTC - 1}"])
                ii = None
                for q4 in range(4):
                    ii = tt.matmul(banks[q4], ones_row, cwrow_e[0:1, 512 * q4:512 * (q4 + 1)],
                                   start=True, stop=True)
                tok["cwrep"] = b.inc(ii, "t")

            def vcw(v, s=s):
                b.wait(v, "t", tok["cwrep"])
                v.drain()
                ii = None
                for q4 in range(4):
                    ii = v.tensor_copy(cwb_e[:, 512 * q4:512 * (q4 + 1)], banks[q4])
                tok["cwb"] = b.inc(ii, "v")
            b.step(tensor=tcw)
            b.step(vector=vcw)

            def gpre(g, s=s):
                b.wait(g, "t", tok[f"mmdone{s - 1}"])
                emit_slot_load(g, s + 1)
                emit_wd_load(g, s)

            def ycw(y, s=s):
                b.wait(y, "t", tok["cwrep"])
                for t in range(NT):
                    dma = y.dma_start(out=cwrow_e[0:1, t * 128:(t + 1) * 128],
                                      in_=cw_all[:, t, s + 1:s + 2])
                    b.inc(dma, "sg2", 16)
                tok["cwg"] = b.cnt["sg2"]

            for tc in range(NTC):
                tsl = slice(tc * TC, (tc + 1) * TC)

                def tgu(tt, s=s, tc=tc, tsl=tsl, wgv=wgv, wuv=wuv):
                    b.wait(tt, f"wsl{s % 2}", tok[f"wload{s}"])
                    b.wait(tt, "v", tok["cwb"])
                    if (s, tc) != (0, 0):
                        prev = tok[f"h{s}_{tc - 1}"] if tc > 0 else tok[f"h{s - 1}_{NTC - 1}"]
                        b.wait(tt, "v", prev)
                    ii = None
                    for m in range(4):
                        pg = banks[m // 2][:, (m % 2) * TC:(m % 2) * TC + TC]
                        pu = banks[2 + m // 2][:, (m % 2) * TC:(m % 2) * TC + TC]
                        for c in range(NC):
                            rhs = xnT_hi[:, c, tsl]
                            tt.matmul(pg, wgv[:, c, m * 128:(m + 1) * 128], rhs,
                                      start=(c == 0), stop=(c == NC - 1))
                            ii = tt.matmul(pu, wuv[:, c, m * 128:(m + 1) * 128], rhs,
                                           start=(c == 0), stop=(c == NC - 1))
                    tok[f"gu{s}_{tc}"] = b.inc(ii, "t")

                def ssg(ss, s=s, tc=tc):
                    b.wait(ss, "t", tok[f"gu{s}_{tc}"])
                    ii = None
                    for m2 in range(2):
                        ii = ss.activation(sg_scr[:, 2 * m2:2 * m2 + 2, :]
                                           .rearrange("p a c -> p (a c)"),
                                           banks[m2], AF.Silu)
                    tok[f"sg{s}_{tc}"] = b.inc(ii, "s")

                def vh(v, s=s, tc=tc, tsl=tsl):
                    b.wait(v, "t", tok[f"gu{s}_{tc}"])
                    b.wait(v, "s", tok[f"sg{s}_{tc}"])
                    v.drain()
                    hb = h_scr if tc % 2 == 0 else h2_scr
                    cb = cwb_e[:, tsl].rearrange("p x -> p () x").broadcast_to([128, 2, TC])
                    for m2 in range(2):
                        v.tensor_tensor(hb[:, 2 * m2:2 * m2 + 2, :],
                                        banks[2 + m2].rearrange("p (a x) -> p a x", a=2),
                                        cb, op.mult)
                    v.drain()
                    ii = v.tensor_tensor(hb.rearrange("p a c -> p (a c)"),
                                         sg_scr.rearrange("p a c -> p (a c)"),
                                         hb.rearrange("p a c -> p (a c)"), op.mult)
                    tok[f"h{s}_{tc}"] = b.inc(ii, "v")

                b.step(tensor=tgu, scalar=ssg)
                b.step(vector=vh)
                if pend_dn[0] is not None:
                    emit_dn(*pend_dn[0])
                pend_dn[0] = (s, tc)
                if tc == 0 and s >= 1:
                    b.step(gpsimd=gpre)
                if tc == 0 and s + 1 < NEXP:
                    b.step(sync=ycw)
        emit_dn(*pend_dn[0])
        pend_dn[0] = None

        def gsh(g):
            b.wait(g, "t", tok[f"mmdone{NEXP - 1}"])
            emit_slot_load(g, NEXP + 1)
            emit_wd_load(g, NEXP)
            emit_wd_load2(g, NEXP + 1)
        b.step(gpsimd=gsh)
        for tc in range(NTC):
            tsl = slice(tc * TC, (tc + 1) * TC)
            halves = ((NEXP, wslot_views(0)[0], wslot_views(0)[1], h_scr),
                      (NEXP + 1, wslot_views(1)[0], wslot_views(1)[1], h2_scr))
            for s_sh, wgv, wuv, hbuf in halves:
                def tgu(tt, s=s_sh, tsl=tsl):
                    b.wait(tt, f"wsl{s % 2}", tok[f"wload{s}"])
                    b.wait(tt, "v", tok[f"h{NEXP - 1}_{NTC - 1}"])
                    if s == NEXP and "xadd" in tok:
                        b.wait(tt, "v", tok["xadd"])
                    if s == NEXP + 1:
                        b.wait(tt, "v", tok[f"h{NEXP}"])
                    ii = None
                    for m in range(4):
                        pg = banks[m // 2][:, (m % 2) * TC:(m % 2) * TC + TC]
                        pu = banks[2 + m // 2][:, (m % 2) * TC:(m % 2) * TC + TC]
                        for c in range(NC):
                            rhs = xnT_hi[:, c, tsl]
                            tt.matmul(pg, wgv[:, c, m * 128:(m + 1) * 128], rhs,
                                      start=(c == 0), stop=(c == NC - 1))
                            ii = tt.matmul(pu, wuv[:, c, m * 128:(m + 1) * 128], rhs,
                                           start=(c == 0), stop=(c == NC - 1))
                    tok["gu"] = b.inc(ii, "t")

                def ssg(ss):
                    b.wait(ss, "t", tok["gu"])
                    ii = None
                    for m2 in range(2):
                        ii = ss.activation(sg_scr[:, 2 * m2:2 * m2 + 2, :]
                                           .rearrange("p a c -> p (a c)"),
                                           banks[m2], AF.Silu)
                    tok["sg"] = b.inc(ii, "s")

                def vh(v, hbuf=hbuf):
                    b.wait(v, "t", tok["gu"])
                    b.wait(v, "s", tok["sg"])
                    v.drain()
                    ii = None
                    for m2 in range(2):
                        ii = v.tensor_tensor(hbuf[:, 2 * m2:2 * m2 + 2, :],
                                             sg_scr[:, 2 * m2:2 * m2 + 2, :],
                                             banks[2 + m2].rearrange("p (a x) -> p a x", a=2),
                                             op.mult)
                    tok[f"h{s_sh}"] = b.inc(ii, "v")
                b.step(tensor=tgu, scalar=ssg)
                b.step(vector=vh)

            def tdn(tt, tc=tc):
                b.wait(tt, "v", tok[f"h{NEXP}"])
                b.wait(tt, "v", tok[f"h{NEXP + 1}"])
                b.wait(tt, "v", tok["racc_last"])
                b.wait(tt, "wdl", tok[f"wdload{NEXP + 1}"])
                ii = None
                for o in range(8):
                    pd = banks[4 + o // 2][:, (o % 2) * TC:(o % 2) * TC + TC]
                    for hb, wdv_ in ((h_scr, wd_buf), (h2_scr, wd_buf2)):
                        for m in range(4):
                            ii = tt.matmul(pd, wdv_[:, m, o * 128:(o + 1) * 128],
                                           hb[:, m, :],
                                           start=(hb is h_scr and m == 0),
                                           stop=(hb is h2_scr and m == 3))
                tok["dn"] = b.inc(ii, "t")

            b.step(tensor=tdn)

            for tb in range(2):
                tglob = 2 * tc + tb

                def vc(v, tc=tc, tb=tb, tglob=tglob):
                    b.wait(v, "t", tok["dn"])
                    if "combtr" in tok:
                        b.wait(v, "t", tok["combtr"])
                    v.drain()
                    ii = None
                    for b4 in range(4):
                        src = racc[:, 2 * b4:2 * b4 + 2,
                                   tglob * 128:(tglob + 1) * 128]
                        pk = banks[4 + b4].rearrange("p (c y x) -> p c y x", c=2, y=2)[:, :, tb, :]
                        ii = v.tensor_tensor(comb_scr[:, 2 * b4:2 * b4 + 2, :],
                                             src, pk, op.add)
                    tok["comb"] = b.inc(ii, "v")
                b.step(vector=vc)

                def ttr(tt, tc=tc, tb=tb):
                    b.wait(tt, "v", tok["comb"])
                    if tc > 0 or tb > 0:
                        b.wait(tt, "v", tok["xadd"])
                    ii = None
                    for c in range(NC):
                        dstb = banks[2 + c // 4]
                        ii = tt.transpose(dstb[:, (c % 4) * 128:(c % 4) * 128 + 128],
                                          comb_scr[:, c, :], ident_f)
                    tok["combtr"] = b.inc(ii, "t")
                b.step(tensor=ttr)

                def vx(v, tglob=tglob):
                    b.wait(v, "t", tok["combtr"])
                    v.drain()
                    xt = x_res[:, tglob, :]
                    v.tensor_tensor(xt[:, 0:512], xt[:, 0:512], banks[2], op.add)
                    ii = v.tensor_tensor(xt[:, 512:1024], xt[:, 512:1024],
                                         banks[3], op.add)
                    tok["xadd"] = b.inc(ii, "v")
                b.step(vector=vx)
        tok["mmdone17"] = b.cnt["t"]

        def vfin(v):
            b.wait(v, "v", tok["xadd"])
            b.wait(v, "v", tok["a0b"])
            v.drain()
            ii = None
            for t in range(NT):
                ii = v.tensor_tensor(x_res[:, t, :], x_res[:, t, :], attn0b, op.add)
            v.drain()
            tok["x1"] = b.inc(ii, "v")
        b.step(vector=vfin)

    ph_consts()
    ph_l0a()
    if upto.startswith("l1skip"):
        tok["x1"] = tok[f"ev{NT - 1}"]
    else:
        ph_l0b()
        ph_l0c()

    if upto == "l0":
        def yout(y):
            b.wait(y, "v", tok["x1"])
            for i in range(4):
                dma = y.dma_start(out=out.rearrange("(n p) d -> p n d", p=128)[:, 4 * i:4 * i + 4, :],
                                  in_=x_res[:, 4 * i:4 * i + 4, :])
                b.inc(dma, "so", 16)
            srcs = {"xnT_hi": xnT_hi, "cw": cw_all, "attn0b": attn0b, "racc": racc,
                    "comb": comb_scr}
            for name, ap_dbg in dbg_out.items():
                nm = name.replace("dbg_", "")
                dma = y.dma_start(out=ap_dbg, in_=srcs[nm[4:] if False else nm])
                b.inc(dma, "so", 16)
            y.wait_ge(b.sem("so"), b.cnt["so"])
        b.step(sync=yout)
        return nc, dbg_out

    def emit_quant(eng, ename, x_ap, out_ap, qa, qb, bm, rs6, nblk):
        F = nblk * 16
        x3 = x_ap.rearrange("p (n i) -> p n i", i=16)
        eng.tensor_reduce(bm, x3, AXX, op.max, apply_absolute_value=True)
        eng.drain()
        eng.tensor_scalar(rs6, bm, 6e-12, None, op.max)
        eng.drain()
        eng.reciprocal(rs6, rs6)
        eng.drain()
        eng.tensor_scalar(rs6, rs6, 6.0, None, op.mult)
        eng.drain()
        rs6b = rs6.rearrange("p n -> p n ()").broadcast_to([128, nblk, 16])
        qa3 = qa.rearrange("p (n i) -> p n i", i=16)
        eng.tensor_scalar(qa, x_ap, -1.0, None, op.mult)
        eng.drain()
        eng.tensor_tensor(qa, qa, x_ap, op.max)
        eng.drain()
        eng.tensor_tensor(qa3, qa3, rs6b, op.mult)
        eng.drain()
        eng.tensor_scalar(qb, qa, 5.0, 2.0, op.is_ge, op.mult)
        eng.drain()
        eng.scalar_tensor_tensor(qb, qa, 2.5, qb, op.is_ge, op.add)
        eng.drain()
        eng.scalar_tensor_tensor(qb, qa, 3.5, qb, op.is_ge, op.add)
        eng.drain()
        eng.tensor_scalar(qb, qb, 2.0, None, op.mult)
        eng.drain()
        for t_ in (0.25, 0.75, 1.25, 1.75):
            eng.scalar_tensor_tensor(qb, qa, t_, qb, op.is_ge, op.add)
            eng.drain()
        eng.reciprocal(rs6, rs6)
        eng.drain()
        eng.tensor_scalar(qa, x_ap, 0.0, 1.0, op.is_ge, op.subtract)
        eng.drain()
        eng.tensor_scalar(qa, qa, 2.0, 1.0, op.mult, op.add)
        eng.drain()
        scb = rs6.rearrange("p n -> p n ()").broadcast_to([128, nblk, 16])
        eng.tensor_scalar(qa, qa, 0.5, None, op.mult)
        eng.drain()
        eng.tensor_tensor(qa3, qa3, scb, op.mult)
        eng.drain()
        ii = eng.tensor_tensor(out_ap, qb, qa, op.mult)
        eng.drain()
        return b.inc(ii, ename)

    xn2T = xnT_hi
    qa_v = bcast_a
    qb_v = attn0b
    bm_v = l1scr[:, 0:64]
    rs6_v = l1scr[:, 64:128]

    def ph_l1a():
        ph_ln_stats("1", gate=("v", tok["x1"]))
        for t in range(l1a_tiles):
            scr = xn_scr[:, t % 2, :]
            p0, p1 = banks[t % 2], banks[2 + t % 2]

            def v(v, t=t, scr=scr):
                b.wait(v, "v", tok["rstd1"])
                if t >= 2:
                    b.wait(v, "t", tok[f"l1tr{t - 2}"])
                v.drain()
                ii = v.tensor_scalar(scr, x_res[:, t, :], mv_all[:, t, 0:1],
                                     rstd_all[:, t:t + 1], op.subtract, op.mult)
                tok[f"l1ln{t}"] = b.inc(ii, "v")

            def y(y, t=t):
                b.wait(y, "v", tok[f"l1ln{t}"])
                dma = y.dma_start(out=xspill.rearrange("(n p) d -> p n d", p=128)[:, t, :],
                                  in_=x_res[:, t, :])
                b.inc(dma, "sxp", 16)

            def tt(tt, t=t, scr=scr, p0=p0, p1=p1):
                b.wait(tt, "v", tok[f"l1ln{t}"])
                if t >= 2:
                    b.wait(tt, "v", tok[f"l1ev{t - 2}"])
                ii = None
                for c in range(8):
                    half = p0 if c < 4 else p1
                    ii = tt.transpose(half[:, (c % 4) * 128:(c % 4) * 128 + 128],
                                      scr[:, c * 128:c * 128 + 128], ident_f)
                tok[f"l1tr{t}"] = b.inc(ii, "t")

            def v2(v, t=t, p0=p0, p1=p1):
                b.wait(v, "t", tok[f"l1tr{t}"])
                ii = None
                for h, half in ((0, p0), (1, p1)):
                    hi = xn2T[:, 4 * h:4 * h + 4, t * 128:(t + 1) * 128]
                    ii = v.tensor_copy(hi, half.rearrange("p (c x) -> p c x", c=4))
                tok[f"l1ev{t}"] = b.inc(ii, "v")

            b.step(vector=v)
            if l1a_parts >= 2:
                b.step(sync=y, tensor=tt, vector=v2)
            else:
                b.step(tensor=tt, vector=v2)

            if l1a_parts < 3:
                continue
            def vq(v, t=t, scr=scr):
                b.wait(v, "v", tok[f"l1ln{t}"])
                if t >= 1:
                    b.wait(v, "t", tok[f"xqtr{t - 1}"])
                v.drain()
                tok[f"q{t}"] = emit_quant(v, "v", scr, qb_v,
                                          qa_v, qb_v, bm_v, rs6_v, 64)

            def tq(tt, t=t):
                b.wait(tt, "v", tok[f"q{t}"])
                if t >= 2:
                    b.wait(tt, "v", tok[f"xqev{t - 2}"])
                pb = banks[4 + t % 2]
                ii = None
                for c in range(8):
                    ii = tt.transpose(pb[:, (c % 4) * 128:(c % 4) * 128 + 128]
                                      if c < 4 else banks[6 + t % 2][:, (c % 4) * 128:(c % 4) * 128 + 128],
                                      qb_v[:, c * 128:c * 128 + 128], ident_f)
                tok[f"xqtr{t}"] = b.inc(ii, "t")

            def vq2(v, t=t):
                b.wait(v, "t", tok[f"xqtr{t}"])
                b.wait(v, "sw0", b.cnt.get("sw0", 0))
                b.wait(v, "sg2", b.cnt.get("sg2", 0))
                if t >= 2:
                    b.wait(v, f"sq{t % 2}", tok[f"sqd{t - 2}"])
                v.drain()
                v.tensor_copy(xq_tileA[t % 2][:, 0:4, :],
                              banks[4 + t % 2].rearrange("p (c x) -> p c x", c=4))
                ii = v.tensor_copy(xq_tileA[t % 2][:, 4:8, :],
                                   banks[6 + t % 2].rearrange("p (c x) -> p c x", c=4))
                tok[f"xqev{t}"] = b.inc(ii, "v")

            def yq(y, t=t):
                b.wait(y, "v", tok[f"xqev{t}"])
                dma = y.dma_start(out=xqspill[:, :, t * 128:(t + 1) * 128],
                                  in_=xq_tileA[t % 2])
                tok[f"sqd{t}"] = b.inc(dma, f"sq{t % 2}", 16)
            b.step(vector=vq)
            if l1a_parts >= 4:
                b.step(tensor=tq, vector=vq2)
            if l1a_parts >= 5:
                b.step(sync=yq)
        tok["spilled"] = b.cnt.get("sxp", 0)
        tok["xq_spilled"] = (b.cnt.get("sq0", 0), b.cnt.get("sq1", 0))
        tok["l1a_v"] = b.cnt["v"]

    wqb = SBn("wqb", [128, NC, 256], BF16, at=ws0_off)
    wkb = SBn("wkb", [128, NC, 256], BF16, at=ws0_off + 4096)
    wvb = SBn("wvb", [128, NC, D], BF16, at=ws0_off + 8192)
    wob = SBn("wob", [128, NC, D], BF16, at=ws1_off + 8192)
    bo_f = mv_all.rearrange("p a c -> p (a c)")[:, 0:8]
    sq_s = SBn("sq_s", [128, 256], F32, at=wrh_off)
    qpad = attn0b[:, 0:512]
    kpad = attn0b[:, 512:1024]
    bqb = l1scr[:, 0:256]
    bkb = l1scr[:, 256:512]
    qpb_s = l1scr[:, 512:768]
    bvb = bcast_a

    def ph_l1b():
        def g(g):
            b.wait(g, "t", tok["mmdone17"] if "mmdone17" in tok else 0)
            for dst, src in ((wqb, Wq1), (wkb, Wk1)):
                dma = g.dma_start(out=dst, in_=src.rearrange("(c p) w -> p c w", p=128))
                b.inc(dma, "wl1", 16)
            for dst, src in ((wvb, Wv1), (wob, Wo1)):
                dma = g.dma_start(out=dst, in_=src.rearrange("(c p) w -> p c w", p=128))
                b.inc(dma, "wl1", 16)
            tok["wl1"] = b.cnt["wl1"]

        def y(y):
            b.wait(y, "v", tok["x1"])
            if "mmdone17" in tok:
                b.wait(y, "t", tok["mmdone17"])
            dma = y.dma_start(out=row_scr[0:1, 0:256], in_=bq1[None, :])
            b.inc(dma, "sb1", 16)
            dma = y.dma_start(out=row_scr[0:1, 256:512], in_=qph.rearrange("a h b w -> a (h b w)"))
            b.inc(dma, "sb1", 16)
            dma = y.dma_start(out=row_scr[0:1, 512:768], in_=bk1[None, :])
            b.inc(dma, "sb1", 16)
            dma = y.dma_start(out=row_scr[0:1, 768:1024], in_=kph.rearrange("a h b w -> a (h b w)"))
            b.inc(dma, "sb1", 16)
            dma = y.dma_start(out=bo_f, in_=bo1.rearrange("(c p) -> p c", p=128),
                              allow_slow_non_contiguous=True)
            b.inc(dma, "sb1", 16)
            tok["sb1"] = b.cnt["sb1"]

        def v(v):
            b.wait(v, "sb1", tok["sb1"])
            b.wait(v, "v", tok["l1a_v"])
            v.drain()
            v.tensor_tensor(row_scr[0:1, 0:256], row_scr[0:1, 0:256],
                            row_scr[0:1, 256:512], op.add)
            v.tensor_tensor(row_scr[0:1, 512:768], row_scr[0:1, 512:768],
                            row_scr[0:1, 768:1024], op.add)
            v.drain()
            ii = v.memset(qpad, 0.0)
            ii = v.memset(kpad, 0.0)
            tok["brow"] = b.inc(ii, "v")

        def t2(tt):
            b.wait(tt, "v", tok["brow"])
            tt.matmul(banks[0][:, 0:256], ones_row, row_scr[0:1, 0:256], start=True, stop=True)
            tt.matmul(banks[0][:, 256:512], ones_row, row_scr[0:1, 512:768], start=True, stop=True)
            ii = tt.matmul(banks[1][0:128, 0:1], ones_row, rtemp[0:1, 0:1], start=True, stop=True)
            tok["brep"] = b.inc(ii, "t")

        def v2(v):
            b.wait(v, "t", tok["brep"])
            v.drain()
            v.tensor_copy(bqb, banks[0][:, 0:256])
            v.tensor_copy(bkb, banks[0][:, 256:512])
            ii = v.tensor_copy(rtemp, banks[1][:, 0:1])
            tok["bqkb"] = b.inc(ii, "v")

        b.step(gpsimd=g, sync=y)
        b.step(vector=v)
        b.step(tensor=t2, vector=v2)

        def y3(y):
            b.wait(y, "v", tok["bqkb"])
            b.wait(y, "t", tok["brep"])
            dma = y.dma_start(out=row_scr[0:1, :], in_=bv1[None, :])
            b.inc(dma, "sb1", 16)
            tok["sbv"] = b.cnt["sb1"]

        def t3(tt):
            b.wait(tt, "sb1", tok["sbv"])
            tt.matmul(banks[0], ones_row, row_scr[0:1, 0:512], start=True, stop=True)
            ii = tt.matmul(banks[1], ones_row, row_scr[0:1, 512:1024], start=True, stop=True)
            tok["bvrep"] = b.inc(ii, "t")

        def v3(v):
            b.wait(v, "t", tok["bvrep"])
            b.wait(v, "v", tok["l1a_v"])
            b.wait(v, "sq0", tok["xq_spilled"][0])
            b.wait(v, "sq1", tok["xq_spilled"][1])
            v.drain()
            v.tensor_copy(bvb[:, 0:512], banks[0])
            ii = v.tensor_copy(bvb[:, 512:1024], banks[1])
            tok["bvb"] = b.inc(ii, "v")
        b.step(sync=y3)
        b.step(tensor=t3, vector=v3)

        for t in range(NT):
            def tp(tt, t=t):
                b.wait(tt, "wl1", tok["wl1"])
                b.wait(tt, "v", max(tok["bvb"], tok[f"l1ev{t}"]))
                if t >= 1:
                    b.wait(tt, "v", tok[f"vev{t - 1}"])
                    b.wait(tt, "v", tok[f"qkn{t - 1}"])
                ii = None
                for c in range(NC):
                    lhs = xn2T[:, c, t * 128:(t + 1) * 128]
                    tt.matmul(banks[0][:, 0:256], lhs, wqb[:, c, :],
                              start=(c == 0), stop=(c == NC - 1))
                    tt.matmul(banks[1], lhs, wvb[:, c, 0:512],
                              start=(c == 0), stop=(c == NC - 1))
                    tt.matmul(banks[2], lhs, wvb[:, c, 512:1024],
                              start=(c == 0), stop=(c == NC - 1))
                for c in range(NC):
                    lhs = xn2T[:, c, t * 128:(t + 1) * 128]
                    ii = tt.matmul(banks[0][:, 256:512], lhs, wkb[:, c, :],
                                   start=(c == 0), stop=(c == NC - 1))
                tok[f"proj{t}"] = b.inc(ii, "t")

            def vv(v, t=t):
                b.wait(v, "t", tok[f"proj{t}"])
                b.wait(v, "sxp", tok["spilled"])
                v.drain()
                v.scalar_tensor_tensor(Vbuf[:, t, 0:512], banks[1], 0.0,
                                       bvb[:, 0:512], op.bypass, op.add)
                ii = v.scalar_tensor_tensor(Vbuf[:, t, 512:1024], banks[2], 0.0,
                                            bvb[:, 512:1024], op.bypass, op.add)
                tok[f"vev{t}"] = b.inc(ii, "v")

            b.step(tensor=tp)
            b.step(vector=vv)

            for qk, bias_b, pslc in (("q", bqb, slice(0, 256)), ("k", bkb, slice(256, 512))):
                def vn(v, t=t, qk=qk, bias_b=bias_b, pslc=pslc):
                    b.wait(v, "t", tok[f"proj{t}"])
                    b.wait(v, "sq0", tok["xq_spilled"][0])
                    b.wait(v, "sq1", tok["xq_spilled"][1])
                    if t >= 1:
                        b.wait(v, "t", tok[f"qktr{t - 1}"])
                    v.drain()
                    v.tensor_tensor(qpb_s, banks[0][:, pslc], bias_b, op.add)
                    v.drain()
                    ii = v.tensor_tensor(sq_s, qpb_s, qpb_s, op.mult)
                    tok["_sq"] = b.inc(ii, "v")

                b.step(vector=vn)

                def vr2(v, t=t, qk=qk):
                    sm3 = sq_s.rearrange("p (h w) -> p h w", w=16)
                    v.drain()
                    v.tensor_reduce(a08[:, 0:8], sm3[:, 0:8, :], AXX, op.add)
                    v.tensor_reduce(val8[:, 0:8], sm3[:, 8:16, :], AXX, op.add)
                    v.drain()
                    v.tensor_scalar(a08, a08, 1e-24, None, op.max)
                    ii = v.tensor_scalar(val8, val8, 1e-24, None, op.max)
                    tok["_ssq"] = b.inc(ii, "v")

                def sq2(s_, t=t):
                    b.wait(s_, "v", tok["_ssq"])
                    s_.activation(a08, a08, AF.Sqrt)
                    ii = s_.activation(val8, val8, AF.Sqrt)
                    tok["_sqr"] = b.inc(ii, "s")

                def vr3(v, t=t, qk=qk):
                    b.wait(v, "s", tok["_sqr"])
                    v.drain()
                    v.reciprocal(a08, a08)
                    v.reciprocal(val8, val8)
                    v.drain()
                    dst = (qpad if qk == "q" else kpad)
                    d3 = dst.rearrange("p (h z) -> p h z", z=32)[:, :, 0:16]
                    q3 = qpb_s.rearrange("p (h w) -> p h w", w=16)
                    rn8a = a08.rearrange("p c -> p c ()").broadcast_to([128, 8, 16])
                    rn8b = val8.rearrange("p c -> p c ()").broadcast_to([128, 8, 16])
                    v.tensor_tensor(d3[:, 0:8, :], q3[:, 0:8, :], rn8a, op.mult)
                    ii = v.tensor_tensor(d3[:, 8:16, :], q3[:, 8:16, :], rn8b, op.mult)
                    tok[f"nrm{qk}{t}"] = b.inc(ii, "v")
                b.step(vector=vr2)
                b.step(scalar=sq2)
                b.step(vector=vr3)
            tok[f"qkn{t}"] = tok[f"nrmk{t}"]

            def tq2(tt, t=t):
                b.wait(tt, "v", tok[f"qkn{t}"])
                if t >= 1:
                    b.wait(tt, "v", tok[f"qkev{t - 1}"])
                for g4 in range(4):
                    tt.transpose(banks[3][:, g4 * 128:(g4 + 1) * 128],
                                 qpad[:, g4 * 128:(g4 + 1) * 128], ident_f)
                ii = None
                for g4 in range(4):
                    ii = tt.transpose(banks[4][:, g4 * 128:(g4 + 1) * 128],
                                      kpad[:, g4 * 128:(g4 + 1) * 128], ident_f)
                tok[f"qktr{t}"] = b.inc(ii, "t")

            def vq3(v, t=t):
                b.wait(v, "t", tok[f"qktr{t}"])
                v.drain()
                v.tensor_copy(qTpad[:, :, t * 128:(t + 1) * 128],
                              banks[3].rearrange("p (g x) -> p g x", g=4))
                ii = v.tensor_copy(kTpad[:, :, t * 128:(t + 1) * 128],
                                   banks[4].rearrange("p (g x) -> p g x", g=4))
                tok[f"qkev{t}"] = b.inc(ii, "v")
            b.step(tensor=tq2)
            b.step(vector=vq3)
        tok["qkdone"] = tok[f"qkev{NT - 1}"]
        tok["vdone"] = tok[f"vev{NT - 1}"]

        def vz(v):
            ii = None
            for var, idx in ((1, 6), (1, 7), (2, 8), (2, 9), (3, 10), (3, 11)):
                ii = v.memset(P_pool[idx][:, 0:128 * var], 0.0)
            tok["pz"] = b.inc(ii, "v")
        b.step(vector=vz)

        plain_i = [0]
        var_i = {1: 0, 2: 0, 3: 0}
        p_tok = {}

        for c in range(4):
            for h in range(HH):
                a = 32 * (h % 4)
                g4 = h // 4
                pvb = banks[4 + h % 2]
                dnb = banks[6 + h % 2][0:1, 0:512]
                njt = 4 * c + 4
                emitters = []
                for j in range(njt):
                    diag = j >= 4 * c
                    ov = j - 4 * c if diag else 0
                    if diag and ov > 0:
                        pi = 6 + (ov - 1) * 2 + var_i[ov]
                        var_i[ov] ^= 1
                    else:
                        pi = plain_i[0]
                        plain_i[0] = (plain_i[0] + 1) % 6
                    P = P_pool[pi]
                    bi = j % 3
                    sb_ = banks[bi]

                    def ts_(tt, c=c, h=h, j=j, a=a, g4=g4, sb_=sb_, pi=pi, bi=bi):
                        b.wait(tt, "v", max(tok["qkdone"], tok["vdone"], tok["pz"]))
                        if c > 0 and h == 0 and j == 0:
                            b.wait(tt, "v", tok[f"oev{c - 1}_7"])
                        b.wait(tt, "s", p_tok.get(("exp_prev", bi), 0))
                        ii = tt.matmul(sb_, kTpad[a:a + 16, g4, j * 128:(j + 1) * 128],
                                       qTpad[a:a + 16, g4, c * 512:(c + 1) * 512],
                                       start=True, stop=True, tile_position=(a, 0))
                        p_tok[("qk", c, h, j)] = b.inc(ii, "t")

                    def se_(s_, c=c, h=h, j=j, sb_=sb_, P=P, ov=ov, diag=diag, pi=pi, bi=bi):
                        b.wait(s_, "t", p_tok[("qk", c, h, j)])
                        b.wait(s_, "t", p_tok.get(("pv_prevuse", pi), 0))
                        lo = 128 * ov if diag else 0
                        ii = s_.activation(P[:, lo:512], sb_[:, lo:512], AF.Exp,
                                           bias=neg1, scale=rtemp)
                        p_tok[("exp", c, h, j)] = b.inc(ii, "s")
                        p_tok[("exp_prev", bi)] = b.cnt["s"]

                    def gm_(g_, c=c, h=h, j=j, P=P, ov=ov):
                        b.wait(g_, "s", p_tok[("exp", c, h, j)])
                        ii = g_.affine_select(out=P[:, 128 * ov:128 * ov + 128],
                                              in_=P[:, 128 * ov:128 * ov + 128],
                                              compare_op=op.is_ge, fill=0.0, base=0,
                                              pattern=[[1, 128]], channel_multiplier=-1)
                        p_tok[("msk", c, h, j)] = b.inc(ii, "g")

                    def tpv_(tt, c=c, h=h, j=j, P=P, pvb=pvb, dnb=dnb, diag=diag, pi=pi):
                        if diag:
                            b.wait(tt, "g", p_tok[("msk", c, h, j)])
                        else:
                            b.wait(tt, "s", p_tok[("exp", c, h, j)])
                        if j == 0:
                            hp = h - 2 if h >= 2 else (HH - 2 + h if c > 0 else None)
                            cp = c if h >= 2 else c - 1
                            if hp is not None:
                                b.wait(tt, "v", tok[f"at{cp}_{hp}"])
                        tt.matmul(pvb[0:64, :], Vbuf[:, j, h * 64:(h + 1) * 64], P,
                                  start=(j == 0), stop=(j == 4 * c + 3))
                        ii = tt.matmul(dnb, ones_colb, P,
                                       start=(j == 0), stop=(j == 4 * c + 3))
                        p_tok[("pv", c, h, j)] = b.inc(ii, "t")
                        p_tok[("pv_prevuse", pi)] = b.cnt["t"]
                    emitters.append((ts_, se_, gm_ if diag else None, tpv_))

                b.step(tensor=emitters[0][0])
                if njt > 1:
                    b.step(tensor=emitters[1][0])
                for j in range(njt):
                    ts_n = emitters[j + 2][0] if j + 2 < njt else None
                    se_j, gm_j, tpv_j = emitters[j][1], emitters[j][2], emitters[j][3]
                    b.step(scalar=se_j)
                    if gm_j is not None:
                        b.step(gpsimd=gm_j)

                    def tcomb(tt, tpv_j=tpv_j, ts_n=ts_n):
                        tpv_j(tt)
                        if ts_n is not None:
                            ts_n(tt)
                    b.step(tensor=tcomb)

                def vd_(v, c=c, h=h, pvb=pvb, dnb=dnb):
                    b.wait(v, "t", p_tok[("pv", c, h, 4 * c + 3)])
                    if (c, h) != (0, 0):
                        b.wait(v, "t", tok.get("rbrep0", 0))
                    v.drain()
                    ii = v.reciprocal(row_scr[0:1, 0:512], dnb)
                    tok["_rd"] = b.inc(ii, "v")

                def td_(tt, c=c, h=h):
                    b.wait(tt, "v", tok["_rd"])
                    ii = tt.matmul(banks[3][0:64, :], ones_row[0:1, 0:64],
                                   row_scr[0:1, 0:512], start=True, stop=True)
                    tok["rbrep0"] = b.inc(ii, "t")

                def va_(v, c=c, h=h, pvb=pvb):
                    b.wait(v, "t", tok["rbrep0"])
                    v.drain()
                    v.tensor_copy(l1scr[0:64, 0:512], banks[3][0:64, :])
                    v.drain()
                    ii = v.scalar_tensor_tensor(
                        attnT[64 * (h % 2):64 * (h % 2) + 64, h // 2, :],
                        pvb[0:64, :], 0.0, l1scr[0:64, 0:512], op.bypass, op.mult)
                    tok[f"at{c}_{h}"] = b.inc(ii, "v")
                b.step(vector=vd_)
                b.step(tensor=td_)
                b.step(vector=va_)

            for o in range(8):
                ob = banks[o % 2]

                def to_(tt, c=c, o=o, ob=ob):
                    b.wait(tt, "v", tok[f"at{c}_{HH - 1}"])
                    if o >= 2:
                        b.wait(tt, "v", tok[f"oev{c}_{o - 2}"])
                    ii = None
                    for cin in range(NC):
                        ii = tt.matmul(ob, wob[:, cin, o * 128:(o + 1) * 128], attnT[:, cin, :],
                                       start=(cin == 0), stop=(cin == NC - 1))
                    tok["_opj"] = b.inc(ii, "t")

                def vo_(v, c=c, o=o, ob=ob):
                    b.wait(v, "t", tok["_opj"])
                    v.drain()
                    ii = v.tensor_scalar(bvb[:, 0:512], ob, bo_f[:, o:o + 1], None, op.add)
                    tok["_oev"] = b.inc(ii, "v")

                def to2_(tt, c=c, o=o):
                    b.wait(tt, "v", tok["_oev"])
                    if o >= 1:
                        b.wait(tt, "v", tok[f"oev{c}_{o - 1}"])
                    ii = None
                    for tb in range(4):
                        ii = tt.transpose(banks[2][:, tb * 128:(tb + 1) * 128],
                                          bvb[:, 0:512][:, tb * 128:(tb + 1) * 128], ident_f)
                    tok["_otr"] = b.inc(ii, "t")

                def vo2_(v, c=c, o=o):
                    b.wait(v, "t", tok["_otr"])
                    v.drain()
                    ii = v.tensor_copy(attn1_tok[:, 4 * c:4 * c + 4, o * 128:(o + 1) * 128],
                                       banks[2].rearrange("p (z x) -> p z x", z=4))
                    tok[f"oev{c}_{o}"] = b.inc(ii, "v")
                b.step(tensor=to_)
                b.step(vector=vo_)
                b.step(tensor=to2_)
                b.step(vector=vo2_)
        tok["attn_done"] = tok[f"oev3_7"]
        tok["wq_gate_t"] = b.cnt["t"]

    qaG = SBn("qaG", [128, 1024], F32, at=row_off)
    qbG = SBn("qbG", [128, 1024], F32, at=xns_off + 4096)
    qa_c = SBn("qa_c", [128, 1024], F32, at=sg_off)
    qb_c = SBn("qb_c", [128, 1024], F32, at=xns_off)
    bm_c = SBn("bm_c", [128, 64], F32, at=cw_off)
    rs6_c = SBn("rs6_c", [128, 64], F32, at=cw_off + 256)
    bm_g = SBn("bm_g", [128, 64], F32, at=cw_off + 512)
    rs6_g = SBn("rs6_g", [128, 64], F32, at=cw_off + 768)
    bm_d = sq_s[:, 0:32]
    rs6_d = sq_s[:, 32:64]
    qaf = SBn("qaf", [128, 512], F32, at=sg_off + 4096)
    qbf = SBn("qbf", [128, 512], F32, at=cw_off)

    def ph_l1c():
        pieces = []
        for snum in range(22):
            pieces.append(("g", snum, 1024))
        for snum in range(22):
            pieces.append(("u", snum, 1024))
        for snum in range(8):
            for qq in range(3):
                pieces.append(("d", (snum, qq), 1024 if qq < 2 else 768))

        pv_tok = {}
        for idx, (kind, meta, W) in enumerate(pieces):
            par = idx % 2
            eng_v = True
            nblk = W // 128

            def yl(y, idx=idx, kind=kind, meta=meta, par=par, W=W):
                b.wait(y, "v", pv_tok.get(("cons", par), 0))
                b.wait(y, "g", pv_tok.get(("consG", par), 0))
                if idx < 2:
                    b.wait(y, "v", tok["attn_done"])
                    b.wait(y, "t", tok["wq_gate_t"])
                    b.wait(y, "sq0", tok["xq_spilled"][0])
                    b.wait(y, "sq1", tok["xq_spilled"][1])
                if kind == "d":
                    snum, qq = meta
                    dma = y.dma_start(out=wstr[par][:, 0:W],
                                      in_=wd1[128 * snum:128 * (snum + 1),
                                              qq * 1024:qq * 1024 + W])
                elif kind == "g":
                    dma = y.dma_start(out=wstr[par][:, 0:1024],
                                      in_=wg1[128 * meta:128 * (meta + 1), :])
                else:
                    dma = y.dma_start(out=wstr[par][:, 0:1024],
                                      in_=wu1[128 * meta:128 * (meta + 1), :])
                b.inc(dma, f"wst{par}", 16)
                pv_tok[("ld", idx)] = b.cnt[f"wst{par}"]

            def q_(eng, idx=idx, par=par, W=W, eng_v=eng_v):
                ename = "v" if eng_v else "g"
                b.wait(eng, f"wst{par}", pv_tok[("ld", idx)])
                b.wait(eng, "t", pv_tok.get("wqfree_v" if eng_v else "wqfree_g", 0))
                if idx < 4:
                    b.wait(eng, "v", tok["attn_done"])
                qa_, qb_, bm_, rs_ = (qa_c, qb_c, bm_c, rs6_c) if eng_v else \
                    (qaG, qbG, bm_g, rs6_g)
                iitok = emit_quant(eng, ename, wstr[par][:, 0:W], qb_[:, 0:W],
                                   qa_[:, 0:W], qb_[:, 0:W],
                                   bm_[:, 0:W // 16], rs_[:, 0:W // 16], W // 16)
                pv_tok[("q", idx)] = (ename, iitok)
                pv_tok[("cons" if eng_v else "consG", par)] = iitok

            b.step(sync=yl)
            if eng_v:
                b.step(vector=q_)
            else:
                b.step(gpsimd=q_)

            wq_ = qb_c if eng_v else qbG
            nbat = (nblk + 3) // 4
            for bb in range(nbat):
                blks = list(range(4 * bb, min(4 * bb + 4, nblk)))
                pb = banks[(idx * nbat + bb) % 4]

                def tt_(tt, idx=idx, bb=bb, blks=blks, pb=pb, wq_=wq_, eng_v=eng_v):
                    en, it = pv_tok[("q", idx)]
                    b.wait(tt, en, it)
                    b.wait(tt, "v", pv_tok.get(("bank_free", pb.tensor.name), 0))
                    ii = None
                    for zi, cb in enumerate(blks):
                        ii = tt.transpose(pb[:, zi * 128:zi * 128 + 128],
                                          wq_[:, cb * 128:(cb + 1) * 128], ident_f)
                    pv_tok[("tr", idx, bb)] = b.inc(ii, "t")

                def vv_(v, idx=idx, bb=bb, blks=blks, pb=pb, kind=kind, meta=meta,
                        nbat=nbat):
                    b.wait(v, "t", pv_tok[("tr", idx, bb)])
                    v.drain()
                    ii = None
                    for zi, cb in enumerate(blks):
                        srcb = pb[:, zi * 128:zi * 128 + 128]
                        if kind == "g":
                            dst = wgqT[:, cb, meta * 128:(meta + 1) * 128]
                        elif kind == "u":
                            dst = wuqT[:, cb, meta * 128:(meta + 1) * 128]
                        else:
                            snum, qq = meta
                            dst = wdqT[:, qq * 8 + cb, snum * 128:(snum + 1) * 128]
                        ii = v.tensor_copy(dst, srcb)
                    pv_tok[("bank_free", pb.tensor.name)] = b.inc(ii, "v")
                b.step(tensor=tt_)
                b.step(vector=vv_)
            pv_tok["wqfree_v" if eng_v else "wqfree_g"] = pv_tok[("tr", idx, nbat - 1)]

        tok["wq_done"] = b.cnt["v"]
        tok["wq_done_t"] = b.cnt["t"]

    def ph_l1d():
        fcw = [(0, 512), (512, 512), (1024, 512), (1536, 512), (2048, 512), (2560, 256)]
        for t in range(NT):
            def yl(y, t=t):
                if t >= 2:
                    b.wait(y, "t", tok[f"gud{t - 2}"])
                    b.wait(y, f"so{t % 2}", tok.get(f"st{t - 2}", 0))
                b.wait(y, "sq0", tok["xq_spilled"][0])
                b.wait(y, "sq1", tok["xq_spilled"][1])
                b.wait(y, "sxp", tok["spilled"])
                if t < 2:
                    b.wait(y, "v", tok["wq_done"])
                    b.wait(y, "t", tok["wq_done_t"])
                dma = y.dma_start(out=xq_tile[t % 2], in_=xqspill[:, :, t * 128:(t + 1) * 128])
                b.inc(dma, "sr1", 16)
                dma = y.dma_start(out=out_sb[t % 2],
                                  in_=xspill.rearrange("(n p) d -> p n d", p=128)[:, t, :])
                b.inc(dma, "sr1", 16)
                tok[f"rl{t}"] = b.cnt["sr1"]

            b.step(sync=yl)

            for fi, (f0, fw) in enumerate(fcw):
                def tg(tt, t=t, fi=fi, f0=f0, fw=fw):
                    b.wait(tt, "sr1", tok[f"rl{t}"])
                    if (t, fi) != (0, 0):
                        b.wait(tt, "v", tok.get("_hq", 0))
                    ii = None
                    for cc in range(NC):
                        tt.matmul(banks[0][:, 0:fw], xq_tile[t % 2][:, cc, :],
                                  wgqT[:, cc, f0:f0 + fw], start=(cc == 0), stop=(cc == NC - 1))
                        ii = tt.matmul(banks[1][:, 0:fw], xq_tile[t % 2][:, cc, :],
                                       wuqT[:, cc, f0:f0 + fw], start=(cc == 0), stop=(cc == NC - 1))
                    tok["_gu1"] = b.inc(ii, "t")

                def sg_(s_, t=t, fi=fi, fw=fw):
                    b.wait(s_, "t", tok["_gu1"])
                    ii = s_.activation(sgf_scr[:, 0:fw], banks[0][:, 0:fw], AF.Silu)
                    tok["_sg1"] = b.inc(ii, "s")

                def vh_(v, t=t, fi=fi, f0=f0, fw=fw):
                    b.wait(v, "t", tok["_gu1"])
                    b.wait(v, "s", tok["_sg1"])
                    v.drain()
                    v.tensor_tensor(hf_scr[:, 0:fw], sgf_scr[:, 0:fw], banks[1][:, 0:fw],
                                    op.mult)
                    v.drain()
                    hq_flat = hq_strip.rearrange("p a c -> p (a c)")
                    tok["_hq"] = emit_quant(v, "v", hf_scr[:, 0:fw], hq_flat[:, f0:f0 + fw],
                                            qaf[:, 0:fw], qbf[:, 0:fw],
                                            bm_d[:, 0:fw // 16], rs6_d[:, 0:fw // 16],
                                            fw // 16)
                b.step(tensor=tg, scalar=sg_)
                b.step(vector=vh_)

            for bb in range(6):
                blks = list(range(4 * bb, min(4 * bb + 4, 22)))
                pb = banks[2 + bb % 2]

                def th(tt, t=t, bb=bb, blks=blks, pb=pb):
                    b.wait(tt, "v", tok["_hq"])
                    b.wait(tt, "v", tok.get(("hqb", pb.tensor.name), 0))
                    hq_flat = hq_strip.rearrange("p a c -> p (a c)")
                    ii = None
                    for zi, cb in enumerate(blks):
                        ii = tt.transpose(pb[:, zi * 128:zi * 128 + 128],
                                          hq_flat[:, cb * 128:(cb + 1) * 128], ident_f)
                    tok["_htr"] = b.inc(ii, "t")

                def vhq(v, t=t, bb=bb, blks=blks, pb=pb):
                    b.wait(v, "t", tok["_htr"])
                    v.drain()
                    ii = None
                    for zi, cb in enumerate(blks):
                        ii = v.tensor_copy(hqT_scr[:, cb, :],
                                           pb[:, zi * 128:zi * 128 + 128])
                    tok[("hqb", pb.tensor.name)] = b.inc(ii, "v")
                    if bb == 5:
                        tok[f"hqev{t}"] = b.cnt["v"]
                b.step(tensor=th)
                b.step(vector=vhq)

            def td(tt, t=t):
                b.wait(tt, "v", tok[f"hqev{t}"])
                if t >= 1:
                    b.wait(tt, "v", tok.get(f"fin{t - 1}", 0))
                ii = None
                for oh in range(2):
                    pd = banks[4 + oh]
                    for cb in range(22):
                        ii = tt.matmul(pd, hqT_scr[:, cb, :], wdqT[:, cb, oh * 512:(oh + 1) * 512],
                                       start=(cb == 0), stop=(cb == 21))
                tok[f"gud{t}"] = b.inc(ii, "t")

            def vf(v, t=t):
                b.wait(v, "t", tok[f"gud{t}"])
                v.drain()
                ii = None
                for oh in range(2):
                    sl = slice(oh * 512, (oh + 1) * 512)
                    v.tensor_tensor(out_sb[t % 2][:, sl], out_sb[t % 2][:, sl],
                                    banks[4 + oh], op.add)
                    v.drain()
                    ii = v.tensor_tensor(out_sb[t % 2][:, sl], out_sb[t % 2][:, sl],
                                         attn1_tok[:, t, sl], op.add)
                    v.drain()
                tok[f"fin{t}"] = b.inc(ii, "v")

            def yo(y, t=t):
                b.wait(y, "v", tok[f"fin{t}"])
                dma = y.dma_start(out=out.rearrange("(n p) d -> p n d", p=128)[:, t, :],
                                  in_=out_sb[t % 2])
                tok[f"st{t}"] = b.inc(dma, f"so{t % 2}", 16)
            b.step(tensor=td)
            b.step(vector=vf)
            b.step(sync=yo)

        def yend(y):
            y.wait_ge(b.sem("so0"), b.cnt["so0"])
            y.wait_ge(b.sem("so1"), b.cnt["so1"])
        b.step(sync=yend)

    def dump_x_out(gate_sem, gate_val):
        def yout(y):
            b.wait(y, gate_sem, gate_val)
            for i in range(4):
                dma = y.dma_start(out=out.rearrange("(n p) d -> p n d", p=128)[:, 4 * i:4 * i + 4, :],
                                  in_=x_res[:, 4 * i:4 * i + 4, :])
                b.inc(dma, "so", 16)
            y.wait_ge(b.sem("so"), b.cnt["so"])
        b.step(sync=yout)

    ph_l1a()
    if upto in ("l1a", "l1skip_a"):
        dump_x_out("v", b.cnt["v"])
        return nc, dbg_out
    ph_l1b()
    if upto in ("l1b", "l1skip_b"):
        def yo2(g):
            b.wait(g, "v", tok["attn_done"])
            for i in range(4):
                dma = g.dma_start(out=out.rearrange("(n p) d -> p n d", p=128)[:, 4 * i:4 * i + 4, :],
                                  in_=attn1_tok[:, 4 * i:4 * i + 4, :])
                b.inc(dma, "so", 16)
            g.wait_ge(b.sem("so"), b.cnt["so"])
        b.step(gpsimd=yo2)
        return nc, dbg_out
    ph_l1c()
    if upto in ("l1c", "l1skip_c"):
        dump_x_out("v", tok["wq_done"])
        return nc, dbg_out
    ph_l1d()
    return nc, dbg_out


def kernel(**inputs):
    nc, _ = build(upto="full")
    xs = np.ascontiguousarray(inputs["L0_attn_x"], dtype=np.float32)
    weights = {kk: np.ascontiguousarray(v) for kk, v in inputs.items() if kk != "L0_attn_x"}
    in_maps = []
    for c in range(8):
        m = {"x": xs[c]}
        m.update(weights)
        in_maps.append(m)
    res = run_bass_kernel_spmd(nc, in_maps, core_ids=list(range(8)))
    return np.stack([r["out"] for r in res.results], axis=0)


# version 03
# speedup vs baseline: 1.0679x; 1.0288x over previous
import numpy as np
import concourse.bass as bass
import concourse.mybir as mybir
from concourse.alu_op_type import AluOpType as op
from concourse.bass_utils import run_bass_kernel_spmd

F32 = mybir.dt.float32
BF16 = mybir.dt.bfloat16
AF = mybir.ActivationFunctionType
AXX = mybir.AxisListType.X

T, D = 2048, 1024
NT, NC = T // 128, D // 128
NEXP, FH = 16, 512
HH, NW, HD, FI = 16, 16, 64, 2816
TC, NTC = 256, T // 256


class B:

    def __init__(self):
        self.nc = bass.Bass("TRN2", target_bir_lowering=False)
        self.sems, self.cnt, self.wm = {}, {}, {}

    def sem(self, name):
        if name not in self.sems:
            self.sems[name] = self.nc.alloc_semaphore(name=name)
            self.cnt[name] = 0
        return self.sems[name]

    def inc(self, instr, name, n=1):
        instr.then_inc(self.sem(name), n)
        self.cnt[name] += n
        return self.cnt[name]

    def wait(self, eng, name, val):
        if val is None or val <= 0:
            return
        wm = self.wm.setdefault(str(eng.engine), {})
        if wm.get(name, -1) >= val:
            return
        eng.wait_ge(self.sem(name), val)
        wm[name] = val

    def step(self, **fns):
        with self.nc.Block() as blk:
            for nm, fn in fns.items():
                getattr(blk, nm)(fn)


def build(upto="full", dbg=(), l1a_tiles=NT, l1a_parts=7):
    b = B()
    nc = b.nc
    SB = nc.alloc_sbuf_tensor
    PS = nc.alloc_psum_tensor

    x_in = nc.dram_tensor("x", [T, D], F32, kind="ExternalInput").ap()
    w_qkv = nc.dram_tensor("L0_attn_w_qkv", [3 * D, D], F32, kind="ExternalInput").ap()
    w_o0 = nc.dram_tensor("L0_attn_w_o", [D, D], F32, kind="ExternalInput").ap()
    Wr = nc.dram_tensor("L0_ffn_Wr", [D, NEXP], F32, kind="ExternalInput").ap()
    Wg = nc.dram_tensor("L0_ffn_Wg", [NEXP, D, FH], F32, kind="ExternalInput").ap()
    Wu = nc.dram_tensor("L0_ffn_Wu", [NEXP, D, FH], F32, kind="ExternalInput").ap()
    Wd = nc.dram_tensor("L0_ffn_Wd", [NEXP, FH, D], F32, kind="ExternalInput").ap()
    Sg = nc.dram_tensor("L0_ffn_Sg", [D, 2 * FH], F32, kind="ExternalInput").ap()
    Su = nc.dram_tensor("L0_ffn_Su", [D, 2 * FH], F32, kind="ExternalInput").ap()
    Sd = nc.dram_tensor("L0_ffn_Sd", [2 * FH, D], F32, kind="ExternalInput").ap()
    Wq1 = nc.dram_tensor("L1_attn_Wq", [D, HH * NW], F32, kind="ExternalInput").ap()
    bq1 = nc.dram_tensor("L1_attn_bq", [HH * NW], F32, kind="ExternalInput").ap()
    Wk1 = nc.dram_tensor("L1_attn_Wk", [D, HH * NW], F32, kind="ExternalInput").ap()
    bk1 = nc.dram_tensor("L1_attn_bk", [HH * NW], F32, kind="ExternalInput").ap()
    Wv1 = nc.dram_tensor("L1_attn_Wv", [D, D], F32, kind="ExternalInput").ap()
    bv1 = nc.dram_tensor("L1_attn_bv", [D], F32, kind="ExternalInput").ap()
    Wo1 = nc.dram_tensor("L1_attn_Wo", [D, D], F32, kind="ExternalInput").ap()
    bo1 = nc.dram_tensor("L1_attn_bo", [D], F32, kind="ExternalInput").ap()
    qph = nc.dram_tensor("L1_attn_q_phase", [1, HH, 1, NW], F32, kind="ExternalInput").ap()
    kph = nc.dram_tensor("L1_attn_k_phase", [1, HH, 1, NW], F32, kind="ExternalInput").ap()
    temp = nc.dram_tensor("L1_attn_temperature", [1], F32, kind="ExternalInput").ap()
    wg1 = nc.dram_tensor("L1_ffn_w_gate", [FI, D], F32, kind="ExternalInput").ap()
    wu1 = nc.dram_tensor("L1_ffn_w_up", [FI, D], F32, kind="ExternalInput").ap()
    wd1 = nc.dram_tensor("L1_ffn_w_down", [D, FI], F32, kind="ExternalInput").ap()
    out = nc.dram_tensor("out", [T, D], F32, kind="ExternalOutput").ap()
    xspill = nc.dram_tensor("xspill", [T, D], F32, kind="Internal").ap()
    xqspill = nc.dram_tensor("xqspill", [128, NC, T], BF16, kind="Internal").ap()
    dbg_out = {}
    for name, shape, dt in dbg:
        dbg_out[name] = nc.dram_tensor("dbg_" + name, shape, dt, kind="ExternalOutput").ap()

    _off = [16512]

    def SBp(name, shape, dtype, at=None):
        o = _off[0] if at is None else at
        h = nc.alloc_sbuf_tensor_at(name, shape, dtype, offset=o)
        if at is None:
            nbytes = int(np.prod(shape[1:])) * (2 if dtype == BF16 else 4)
            _off[0] = o + ((nbytes + 31) // 32) * 32
        return h.ap(), o

    def SBn(name, shape, dtype, at=None):
        return SBp(name, shape, dtype, at)[0]

    ident_f = SBn("ident_f", [128, 128], F32)
    ident_b = SBn("ident_b", [128, 128], BF16)
    ones_col = SBn("ones_col", [128, 1], F32)
    ones_colb = SBn("ones_colb", [128, 1], BF16)
    ones_row = SBn("ones_row", [1, 128], F32)
    neg1 = SBn("neg1", [128, 1], F32)
    rtemp = SBn("rtemp", [128, 1], F32)
    mv_all = SBn("mv_all", [128, NT, 2], F32)
    rstd_all = SBn("rstd_all", [128, NT], F32)
    bn6 = SBn("bn6", [128, 2, 6], F32)
    cw_all, cw_off = SBp("cw_all", [128, NT, NEXP], F32)
    cwb = SBn("cwb", [128, TC], F32)
    cwrow = SBn("cwrow", [1, TC], F32)
    wr_hi, wrh_off = SBp("wr_hi", [128, NC, NEXP], BF16)
    wr_lo = SBn("wr_lo", [128, NC, NEXP], BF16)
    wr_f = SBn("wr_f", [128, NC, NEXP], F32)
    val8 = SBn("val8", [128, 8], F32)
    a08 = SBn("a08", [128, 8], F32)
    row_scr, row_off = SBp("row_scr", [1, D], F32)
    h2_scr = SBn("h2_scr", [128, 4, TC], BF16, at=row_off)
    sg_scr, sg_off = SBp("sg_scr", [128, 4, TC], F32)
    h_scr, h_off = SBp("h_scr", [128, 4, TC], BF16)
    attn0b, a0_off = SBp("attn0b", [128, D], F32)
    bcast_a, bc_off = SBp("bcast_a", [128, D], F32)
    xn_scr, xns_off = SBp("xn_scr", [128, 2, D], F32)
    xnT_hi, xh_off = SBp("xnT_hi", [128, NC, T], BF16)
    xnT_lo, xl_off = SBp("xnT_lo", [128, NC, T], BF16)
    racc = xnT_lo
    ws0, ws0_off = SBp("wslot0", [128, 16 * FH], BF16)
    ws1, ws1_off = SBp("wslot1", [128, 16 * FH], BF16)
    wslot = [ws0, ws1]
    wd_buf, wd_off = SBp("wd_buf", [128, 4, D], BF16)
    wd_buf2, wd2_off = SBp("wd_buf2", [128, 4, D], BF16)
    x_res, x_off = SBp("x_res", [128, NT, D], F32)
    assert _off[0] <= 229344, f"sbuf map overflow: {_off[0]}"
    comb_scr = bcast_a.rearrange("p (c x) -> p c x", c=8)
    stream_a = [xn_scr[:, i, :] for i in range(2)]
    ttr_scr = sg_scr.rearrange("p a c -> p (a c)")
    cwrow_e = SBn("cwrow_e", [1, T], F32, at=wd2_off)
    cwb_e = SBn("cwb_e", [128, T], BF16, at=bc_off)
    sg_pair = SBn("sg_pair", [128, 4, 512], BF16, at=sg_off)
    h_pair = SBn("h_pair", [128, 4, 512], BF16, at=row_off)
    h_pairB = SBn("h_pairB", [128, 4, 512], BF16, at=xns_off + 4096)
    racc_tok = SBn("racc_tok", [128, NT, D], BF16, at=xl_off)
    h17_pair = SBn("h17_pair", [128, 4, 512], BF16, at=xns_off)
    qTpad = SBn("qTpad", [128, 4, T], BF16, at=xl_off)
    kTpad = SBn("kTpad", [128, 4, T], BF16, at=xl_off + 16384)
    Vbuf = SBn("Vbuf", [128, NT, D], BF16, at=x_off)
    attn1_tok = SBn("attn1_tok", [128, NT, D], BF16, at=x_off + 32768)
    wgqT = SBn("wgqT", [128, NC, FI], BF16, at=xh_off)
    wuqT = SBn("wuqT", [128, NC, FI], BF16, at=xh_off + 45056)
    wdqT = SBn("wdqT", [128, 22, D], BF16, at=xh_off + 90112)
    P_pool = [SBn(f"Ppool{i}", [128, 512], BF16, at=xns_off + 1024 * i) for i in range(12)]
    attnT = SBn("attnT", [128, NC, 512], BF16, at=wd2_off)
    hq_strip = SBn("hq_strip", [128, 22, 128], F32, at=x_off + 20480)
    hqT_scr = SBn("hqT_scr", [128, 22, 128], BF16, at=xns_off)
    hf_scr = SBn("hf_scr", [128, 512], F32, at=sg_off)
    sgf_scr = SBn("sgf_scr", [128, 512], F32, at=sg_off + 2048)
    out_sb = [SBn(f"out_sb{i}", [128, D], F32, at=row_off if i == 0 else a0_off)
              for i in range(2)]
    xq_tileA = [SBn(f"xq_tileA{i}", [128, NC, 128], BF16, at=xl_off + 2048 * i)
                for i in range(2)]
    xq_tile = [SBn(f"xq_tileD{i}", [128, NC, 128], BF16, at=bc_off + 2048 * i)
               for i in range(2)]
    wstr = [SBn("wstr0", [128, 1024], F32, at=a0_off),
            SBn("wstr1", [128, 1024], F32, at=bc_off)]
    l1scr = SBn("l1scr", [128, 768], F32, at=sg_off)

    banks = [PS(f"bank{i}", [128, 512], F32).ap() for i in range(8)]

    tok = {}

    def ph_consts():
        def g(g):
            g.memset(ident_f, 0.0)
            g.memset(ident_b, 0.0)
            g.memset(ones_col, 1.0)
            g.memset(ones_colb, 1.0)
            g.memset(ones_row, 1.0)
            g.memset(neg1, -1.0)
            g.drain()
            g.affine_select(out=ident_f, in_=ident_f, compare_op=op.not_equal,
                            fill=1.0, base=0, pattern=[[-1, 128]], channel_multiplier=1)
            i2 = g.affine_select(out=ident_b, in_=ident_b, compare_op=op.not_equal,
                                 fill=1.0, base=0, pattern=[[-1, 128]], channel_multiplier=1)
            b.inc(i2, "g")
            tok["consts"] = b.cnt["g"]
        b.step(gpsimd=g)

    def ph_ln_stats(tag, gate=None):
        def v(v):
            if gate is None:
                b.wait(v, "sx", tok["x_loaded"])
            else:
                b.wait(v, gate[0], gate[1])
            for t in range(NT):
                xt = x_res[:, t, :]
                v.bn_stats(bn6[:, 0:1, :], xt[:, 0:512].rearrange("p (a c) -> p a c", a=1))
                v.bn_stats(bn6[:, 1:2, :], xt[:, 512:1024].rearrange("p (a c) -> p a c", a=1))
                v.drain()
                v.bn_aggr(mv_all[:, t, :], bn6.rearrange("p a b -> p (a b)"))
                v.drain()
            ii = v.tensor_scalar(rstd_all, mv_all[:, :, 1], 1e-5, None, op.add)
            tok[f"vareps{tag}"] = b.inc(ii, "v")

        def s(s):
            b.wait(s, "v", tok[f"vareps{tag}"])
            ii = s.activation(rstd_all, rstd_all, AF.Sqrt)
            tok[f"sqrt{tag}"] = b.inc(ii, "s")
        b.step(vector=v, scalar=s)

        def v2(v):
            b.wait(v, "s", tok[f"sqrt{tag}"])
            ii = v.reciprocal(rstd_all, rstd_all)
            tok[f"rstd{tag}"] = b.inc(ii, "v")
        b.step(vector=v2)

    def ph_l0a():
        def y(y):
            for i in range(4):
                dma = y.dma_start(out=x_res[:, 4 * i:4 * i + 4, :],
                                  in_=x_in.rearrange("(n p) d -> p n d", p=128)[:, 4 * i:4 * i + 4, :])
                b.inc(dma, "sx", 16)
            tok["x_loaded"] = b.cnt["sx"]
            dma = y.dma_start(out=wr_f, in_=Wr.rearrange("(c p) e -> p c e", p=128))
            b.inc(dma, "sw0", 16)
            dma = y.dma_start(out=rtemp[0:1, :], in_=temp[None, :])
            b.inc(dma, "sw0", 16)
            tok["wr_loaded"] = b.cnt["sw0"]
        b.step(sync=y)

        ph_ln_stats("0")

        for t in range(NT):
            scr = xn_scr[:, t % 2, :]
            p0, p1 = banks[t % 2], banks[2 + t % 2]

            def v(v, t=t, scr=scr):
                b.wait(v, "v", tok["rstd0"])
                if t >= 2:
                    b.wait(v, "t", tok[f"tr{t - 2}"])
                v.drain()
                ii = v.tensor_scalar(scr, x_res[:, t, :], mv_all[:, t, 0:1],
                                     rstd_all[:, t:t + 1], op.subtract, op.mult)
                tok[f"ln{t}"] = b.inc(ii, "v")

            def tt(tt, t=t, scr=scr, p0=p0, p1=p1):
                b.wait(tt, "g", tok["consts"])
                b.wait(tt, "v", tok[f"ln{t}"])
                if t >= 2:
                    b.wait(tt, "v", tok[f"ev{t - 2}"])
                for c in range(8):
                    half = p0 if c < 4 else p1
                    tt.transpose(half[:, (c % 4) * 128:(c % 4) * 128 + 128],
                                 scr[:, c * 128:c * 128 + 128], ident_f)
                tt.matmul(banks[4][0:1, 0:512], ones_col, scr[:, 0:512],
                          start=(t == 0), stop=(t == NT - 1))
                ii = tt.matmul(banks[5][0:1, 0:512], ones_col, scr[:, 512:1024],
                               start=(t == 0), stop=(t == NT - 1))
                tok[f"tr{t}"] = b.inc(ii, "t")

            def v2(v, t=t, p0=p0, p1=p1):
                b.wait(v, "t", tok[f"tr{t}"])
                ii = None
                for h, half in ((0, p0), (1, p1)):
                    hi = xnT_hi[:, 4 * h:4 * h + 4, t * 128:(t + 1) * 128]
                    lo = xnT_lo[:, 4 * h:4 * h + 4, t * 128:(t + 1) * 128]
                    h3 = half.rearrange("p (c x) -> p c x", c=4)
                    v.tensor_copy(hi, h3)
                    v.drain()
                    ii = v.scalar_tensor_tensor(lo, hi, -1.0, h3, op.mult, op.add)
                tok[f"ev{t}"] = b.inc(ii, "v")

            b.step(vector=v)
            b.step(tensor=tt, vector=v2)

    def ph_l0b():
        def v(v):
            b.wait(v, "sw0", tok["wr_loaded"])
            v.tensor_copy(wr_hi.rearrange("p c e -> p (c e)"), wr_f.rearrange("p c e -> p (c e)"))
            v.drain()
            v.scalar_tensor_tensor(wr_lo.rearrange("p c e -> p (c e)"),
                                   wr_hi.rearrange("p c e -> p (c e)"), -1.0,
                                   wr_f.rearrange("p c e -> p (c e)"), op.mult, op.add)
            v.drain()
            ii = v.reciprocal(rtemp[0:1, :], rtemp[0:1, :])
            tok["wrhl"] = b.inc(ii, "v")
        b.step(vector=v)

        for t in range(NT):
            pr = banks[6 + t % 2][:, 0:NEXP]

            def tt(tt, t=t, pr=pr):
                b.wait(tt, "v", max(tok["wrhl"], tok[f"ev{t}"]))
                if t >= 2:
                    b.wait(tt, "v", tok[f"cw{t - 2}"])
                ii = None
                for c in range(NC):
                    lhs_hi = xnT_hi[:, c, t * 128:(t + 1) * 128]
                    lhs_lo = xnT_lo[:, c, t * 128:(t + 1) * 128]
                    tt.matmul(pr, lhs_hi, wr_hi[:, c, :], start=(c == 0), stop=False)
                    tt.matmul(pr, lhs_hi, wr_lo[:, c, :], start=False, stop=False)
                    ii = tt.matmul(pr, lhs_lo, wr_hi[:, c, :], start=False,
                                   stop=(c == NC - 1))
                tok[f"rt{t}"] = b.inc(ii, "t")

            def v1(v, t=t, pr=pr):
                b.wait(v, "t", tok[f"rt{t}"])
                v.drain()
                ii = v.tensor_reduce(ttr_scr[:, 0:1], pr, AXX, op.max, negate=True)
                tok[f"nmax{t}"] = b.inc(ii, "v")

            def s1(s, t=t, pr=pr):
                b.wait(s, "v", tok[f"nmax{t}"])
                ii = s.activation(ttr_scr[:, 4:4 + NEXP], pr, AF.Exp,
                                  bias=ttr_scr[:, 0:1], scale=1.0,
                                  accum_out=ttr_scr[:, 1:2])
                tok[f"exp{t}"] = b.inc(ii, "s")

            def v2(v, t=t):
                b.wait(v, "s", tok[f"exp{t}"])
                pe = ttr_scr[:, 4:4 + NEXP]
                v.tensor_reduce(ttr_scr[:, 2:3], pe, AXX, op.max)
                v.drain()
                v.tensor_scalar(ttr_scr[:, 32:32 + NEXP], pe, ttr_scr[:, 2:3], None, op.is_equal)
                v.drain()
                v.scalar_tensor_tensor(ttr_scr[:, 64:64 + NEXP], ttr_scr[:, 32:32 + NEXP],
                                       -1e30, pe, op.mult, op.add)
                v.drain()
                v.tensor_reduce(ttr_scr[:, 3:4], ttr_scr[:, 64:64 + NEXP], AXX, op.max)
                v.drain()
                v.tensor_scalar(ttr_scr[:, 32:32 + NEXP], pe, ttr_scr[:, 3:4], None, op.is_ge)
                v.reciprocal(ttr_scr[:, 1:2], ttr_scr[:, 1:2])
                v.drain()
                ii = v.scalar_tensor_tensor(cw_all[:, t, :], pe, ttr_scr[:, 1:2],
                                            ttr_scr[:, 32:32 + NEXP], op.mult, op.mult)
                tok[f"cw{t}"] = b.inc(ii, "v")

            b.step(tensor=tt, vector=v1)
            b.step(scalar=s1)
            b.step(vector=v2)

        def v3(v):
            b.wait(v, "t", tok[f"tr{NT - 1}"])
            v.tensor_copy(row_scr[0:1, 0:512], banks[4][0:1, 0:512])
            ii = v.tensor_copy(row_scr[0:1, 512:1024], banks[5][0:1, 0:512])
            tok["vsrow"] = b.inc(ii, "v")

        def t3(tt):
            b.wait(tt, "v", tok["vsrow"])
            tt.matmul(banks[4], ones_row, row_scr[0:1, 0:512], start=True, stop=True)
            ii = tt.matmul(banks[5], ones_row, row_scr[0:1, 512:1024], start=True, stop=True)
            tok["vsrep"] = b.inc(ii, "t")

        def v4(v):
            b.wait(v, "t", tok["vsrep"])
            v.tensor_copy(bcast_a[:, 0:512], banks[4])
            ii = v.tensor_copy(bcast_a[:, 512:1024], banks[5])
            tok["vsb"] = b.inc(ii, "v")
        b.step(vector=v3)
        b.step(tensor=t3, vector=v4)

        def matvec(src_ap, res8, toks):
            for c in range(NC):
                def y(y, c=c):
                    b.wait(y, "v", toks[c - 2])
                    dma = y.dma_start(out=stream_a[c % 2],
                                      in_=src_ap[128 * c:128 * (c + 1), :])
                    b.inc(dma, f"swv{c % 2}", 16)
                    tok["_mv_dma"] = b.cnt[f"swv{c % 2}"]

                def v(v, c=c):
                    b.wait(v, f"swv{c % 2}", tok["_mv_dma"])
                    b.wait(v, "v", tok["_mv_src"])
                    v.drain()
                    v.tensor_tensor(ttr_scr, stream_a[c % 2], bcast_a, op.mult)
                    v.drain()
                    ii = v.tensor_reduce(res8[:, c:c + 1], ttr_scr, AXX, op.add)
                    toks[c] = b.inc(ii, "v")
                b.step(sync=y, vector=v)

        tok["_mv_src"] = tok["vsb"]
        wv_toks = {-2: 0, -1: 0}
        matvec(w_qkv[2 * D:3 * D, :], val8, wv_toks)

        def y5(y):
            b.wait(y, "v", wv_toks[NC - 1])
            for c in range(NC):
                ii = y.dma_start(out=row_scr[0:1, c * 128:(c + 1) * 128], in_=val8[:, c:c + 1])
                b.inc(ii, "sg2", 16)
            tok["valrow"] = b.cnt["sg2"]

        def t5(tt):
            b.wait(tt, "sg2", tok["valrow"])
            b.wait(tt, "v", wv_toks[NC - 1])
            tt.matmul(banks[4], ones_row, row_scr[0:1, 0:512], start=True, stop=True)
            ii = tt.matmul(banks[5], ones_row, row_scr[0:1, 512:1024], start=True, stop=True)
            tok["valrep"] = b.inc(ii, "t")

        def v5(v):
            b.wait(v, "t", tok["valrep"])
            v.drain()
            v.tensor_copy(bcast_a[:, 0:512], banks[4])
            ii = v.tensor_copy(bcast_a[:, 512:1024], banks[5])
            tok["valb"] = b.inc(ii, "v")
        b.step(sync=y5)
        b.step(tensor=t5, vector=v5)

        tok["_mv_src"] = tok["valb"]
        wo_toks = {-2: wv_toks[NC - 2], -1: wv_toks[NC - 1]}
        matvec(w_o0, a08, wo_toks)

        def y6(y):
            b.wait(y, "v", wo_toks[NC - 1])
            b.wait(y, "t", tok["valrep"])
            for c in range(NC):
                ii = y.dma_start(out=row_scr[0:1, c * 128:(c + 1) * 128], in_=a08[:, c:c + 1])
                b.inc(ii, "sg2", 16)
            tok["a0row"] = b.cnt["sg2"]

        def t6(tt):
            b.wait(tt, "sg2", tok["a0row"])
            b.wait(tt, "v", wo_toks[NC - 1])
            tt.matmul(banks[4], ones_row, row_scr[0:1, 0:512], start=True, stop=True)
            ii = tt.matmul(banks[5], ones_row, row_scr[0:1, 512:1024], start=True, stop=True)
            tok["a0rep"] = b.inc(ii, "t")

        def v6(v):
            b.wait(v, "t", tok["a0rep"])
            v.tensor_copy(attn0b[:, 0:512], banks[4])
            ii = v.tensor_copy(attn0b[:, 512:1024], banks[5])
            tok["a0b"] = b.inc(ii, "v")
        b.step(sync=y6)
        b.step(tensor=t6, vector=v6)

    def wslot_views(i):
        w = wslot[i]
        wg = w[:, 0:8 * FH].rearrange("p (c f) -> p c f", c=8)
        wu = w[:, 8 * FH:16 * FH].rearrange("p (c f) -> p c f", c=8)
        return wg, wu, wd_buf

    def gu_srcs(s):
        if s < NEXP:
            return Wg[s], Wu[s]
        hh = s - NEXP
        return Sg[:, hh * FH:(hh + 1) * FH], Su[:, hh * FH:(hh + 1) * FH]

    def wd_src(s):
        return Wd[s] if s < NEXP else Sd[(s - NEXP) * FH:(s - NEXP + 1) * FH, :]

    def emit_slot_load(g, s):
        slot = s % 2
        g_src, u_src = gu_srcs(s)
        wgv, wuv, _ = wslot_views(slot)
        i1 = g.dma_start(out=wgv, in_=g_src.rearrange("(c p) f -> p c f", p=128))
        b.inc(i1, f"wsl{slot}", 16)
        i2 = g.dma_start(out=wuv, in_=u_src.rearrange("(c p) f -> p c f", p=128))
        b.inc(i2, f"wsl{slot}", 16)
        tok[f"wload{s}"] = b.cnt[f"wsl{slot}"]

    def emit_wd_load(g, s):
        ii = g.dma_start(out=wd_buf, in_=wd_src(s).rearrange("(m p) f -> p m f", p=128))
        b.inc(ii, "wdl", 16)
        tok[f"wdload{s}"] = b.cnt["wdl"]

    def emit_wd_load2(g, s):
        ii = g.dma_start(out=wd_buf2, in_=wd_src(s).rearrange("(m p) f -> p m f", p=128))
        b.inc(ii, "wdl", 16)
        tok[f"wdload{s}"] = b.cnt["wdl"]

    def ph_l0c():
        def g0(g):
            emit_slot_load(g, 0)
            emit_slot_load(g, 1)
            emit_wd_load(g, 0)
        b.step(gpsimd=g0)

        def emit_sltc(s, tc, wgv, wuv, wdv, shared):
            tsl = slice(tc * TC, (tc + 1) * TC)
            t0, t1 = 2 * tc, 2 * tc + 1
            if True:
                if not shared:
                    def y(y, s=s, tc=tc, t0=t0, t1=t1):
                        b.wait(y, "v", max(tok[f"cw{t0}"], tok[f"cw{t1}"]))
                        if (s, tc) != (0, 0):
                            b.wait(y, "t", tok["cwrep"])
                        i1 = y.dma_start(out=cwrow[0:1, 0:128], in_=cw_all[:, t0, s:s + 1])
                        b.inc(i1, "sg2", 16)
                        i2 = y.dma_start(out=cwrow[0:1, 128:256], in_=cw_all[:, t1, s:s + 1])
                        b.inc(i2, "sg2", 16)
                        tok["cwdma"] = b.cnt["sg2"]

                    def tcw(tt, s=s, tc=tc):
                        b.wait(tt, "sg2", tok["cwdma"])
                        ii = tt.matmul(banks[2][:, 0:TC], ones_row, cwrow[0:1, :],
                                       start=True, stop=True)
                        tok["cwrep"] = b.inc(ii, "t")

                    def vcw(v, s=s, tc=tc):
                        b.wait(v, "t", tok["cwrep"])
                        v.drain()
                        ii = v.tensor_copy(cwb, banks[2][:, 0:TC])
                        tok["cwb"] = b.inc(ii, "v")
                    b.step(sync=y, tensor=tcw)
                    b.step(vector=vcw)

                def tgu(tt, s=s, tc=tc, tsl=tsl, wgv=wgv, wuv=wuv, shared=shared):
                    b.wait(tt, f"wsl{s % 2}", tok[f"wload{s}"])
                    if not shared:
                        b.wait(tt, "v", tok["cwb"])
                    elif s == NEXP and "xadd" in tok:
                        b.wait(tt, "v", tok["xadd"])
                    for m in range(4):
                        pg = banks[m // 2][:, (m % 2) * TC:(m % 2) * TC + TC]
                        pu = banks[2 + m // 2][:, (m % 2) * TC:(m % 2) * TC + TC]
                        for c in range(NC):
                            rhs = xnT_hi[:, c, tsl]
                            tt.matmul(pg, wgv[:, c, m * 128:(m + 1) * 128], rhs,
                                      start=(c == 0), stop=(c == NC - 1))
                            ii = tt.matmul(pu, wuv[:, c, m * 128:(m + 1) * 128], rhs,
                                           start=(c == 0), stop=(c == NC - 1))
                    tok["gu"] = b.inc(ii, "t")

                def ssg(ss, s=s):
                    b.wait(ss, "t", tok["gu"])
                    ii = None
                    for m2 in range(2):
                        ii = ss.activation(sg_scr[:, 2 * m2:2 * m2 + 2, :]
                                           .rearrange("p a c -> p (a c)"),
                                           banks[m2], AF.Silu)

                    tok["sg"] = b.inc(ii, "s")

                def vh(v, s=s, tc=tc, shared=shared):
                    b.wait(v, "t", tok["gu"])
                    b.wait(v, "s", tok["sg"])
                    v.drain()
                    ii = None
                    if shared:
                        for m2 in range(2):
                            ii = v.tensor_tensor(h_scr[:, 2 * m2:2 * m2 + 2, :],
                                                 sg_scr[:, 2 * m2:2 * m2 + 2, :],
                                                 banks[2 + m2].rearrange("p (a x) -> p a x", a=2),
                                                 op.mult)
                    else:
                        cb = cwb.rearrange("p x -> p () x").broadcast_to([128, 2, TC])
                        for m2 in range(2):
                            v.tensor_tensor(h_scr[:, 2 * m2:2 * m2 + 2, :],
                                            banks[2 + m2].rearrange("p (a x) -> p a x", a=2),
                                            cb, op.mult)
                        v.drain()
                        ii = v.tensor_tensor(h_scr.rearrange("p a c -> p (a c)"),
                                             sg_scr.rearrange("p a c -> p (a c)"),
                                             h_scr.rearrange("p a c -> p (a c)"), op.mult)
                    tok["h"] = b.inc(ii, "v")

                def tdn(tt, s=s, tc=tc, shared=shared, wdv=wdv):
                    b.wait(tt, "v", tok["h"])
                    b.wait(tt, "wdl", tok[f"wdload{s}"])
                    ii = None
                    for o in range(8):
                        pd = banks[4 + o // 2][:, (o % 2) * TC:(o % 2) * TC + TC]
                        for m in range(4):
                            ii = tt.matmul(pd, wdv[:, m, o * 128:(o + 1) * 128],
                                           h_scr[:, m, :], start=(m == 0), stop=(m == 3))
                    tok["dn"] = b.inc(ii, "t")
                    if tc == NTC - 1:
                        tok[f"mmdone{s}"] = b.cnt["t"]

                b.step(tensor=tgu, scalar=ssg)
                b.step(vector=vh)
                b.step(tensor=tdn)

                if not shared:
                    def vr(v, s=s, tc=tc, tsl=tsl):
                        b.wait(v, "t", tok["dn"])
                        v.drain()
                        ii = None
                        for b4 in range(4):
                            dst = racc[:, 2 * b4:2 * b4 + 2, tsl]
                            src = banks[4 + b4].rearrange("p (c x) -> p c x", c=2)
                            if s == 0:
                                ii = v.tensor_copy(dst, src)
                            else:
                                ii = v.tensor_tensor(dst, dst, src, op.add)
                            v.drain()
                        tok[f"racc{tc}"] = b.inc(ii, "v")
                    b.step(vector=vr)
                elif s == NEXP + 1:
                    for tb in range(2):
                        tglob = 2 * tc + tb

                        def vc(v, tc=tc, tb=tb, tglob=tglob):
                            b.wait(v, "t", tok["dn"])
                            if "combtr" in tok:
                                b.wait(v, "t", tok["combtr"])
                            v.drain()
                            ii = None
                            for b4 in range(4):
                                src = racc[:, 2 * b4:2 * b4 + 2,
                                           tglob * 128:(tglob + 1) * 128]
                                pk = banks[4 + b4].rearrange("p (c y x) -> p c y x", c=2, y=2)[:, :, tb, :]
                                ii = v.tensor_tensor(comb_scr[:, 2 * b4:2 * b4 + 2, :],
                                                     src, pk, op.add)
                            tok["comb"] = b.inc(ii, "v")
                        b.step(vector=vc)

                        def ttr(tt, tc=tc, tb=tb):
                            b.wait(tt, "v", tok["comb"])
                            if tc > 0 or tb > 0:
                                b.wait(tt, "v", tok["xadd"])
                            ii = None
                            for c in range(NC):
                                dstb = banks[2 + c // 4]
                                ii = tt.transpose(dstb[:, (c % 4) * 128:(c % 4) * 128 + 128],
                                                  comb_scr[:, c, :], ident_f)
                            tok["combtr"] = b.inc(ii, "t")
                        b.step(tensor=ttr)

                        def vx(v, tglob=tglob):
                            b.wait(v, "t", tok["combtr"])
                            v.drain()
                            xt = x_res[:, tglob, :]
                            v.tensor_tensor(xt[:, 0:512], xt[:, 0:512], banks[2], op.add)
                            ii = v.tensor_tensor(xt[:, 512:1024], xt[:, 512:1024],
                                                 banks[3], op.add)
                            tok["xadd"] = b.inc(ii, "v")
                        b.step(vector=vx)

        pend = [None]

        hp = [h_pair, h_pairB]

        def emit_dnA(s, pr):
            def tdn(tt, s=s, pr=pr):
                b.wait(tt, "v", tok[f"h{s}_{pr}"])
                b.wait(tt, "wdl", tok[f"wdload{s}"])
                b.wait(tt, "v", tok.get("vrB", 0))
                ii = None
                for tl in range(2):
                    for m in range(4):
                        lhs = hp[pr % 2][:, m, tl * 128:(tl + 1) * 128]
                        tt.matmul(banks[4 + 2 * tl], lhs, wd_buf[:, m, 0:512],
                                  start=(m == 0), stop=(m == 3))
                        ii = tt.matmul(banks[5 + 2 * tl], lhs, wd_buf[:, m, 512:1024],
                                       start=(m == 0), stop=(m == 3))
                tok["dnA"] = b.inc(ii, "t")

            def vr(v, s=s, pr=pr):
                b.wait(v, "t", tok["dnA"])
                v.drain()
                ii = None
                for tl in range(2):
                    tglob = 4 * pr + tl
                    dst = racc_tok[:, tglob, :]
                    for hb in range(2):
                        pk = banks[4 + 2 * tl + hb]
                        d2 = dst[:, hb * 512:(hb + 1) * 512]
                        if s == 0:
                            ii = v.tensor_copy(d2, pk)
                        else:
                            ii = v.tensor_tensor(d2, d2, pk, op.add)
                        v.drain()
                tok["vrA"] = b.inc(ii, "v")
            b.step(tensor=tdn)
            b.step(vector=vr)

        def emit_dnB(s, pr):
            def tdn(tt, s=s, pr=pr):
                b.wait(tt, "v", tok["vrA"])
                ii = None
                for tl in range(2, 4):
                    for m in range(4):
                        lhs = hp[pr % 2][:, m, tl * 128:(tl + 1) * 128]
                        tt.matmul(banks[4 + 2 * (tl - 2)], lhs, wd_buf[:, m, 0:512],
                                  start=(m == 0), stop=(m == 3))
                        ii = tt.matmul(banks[5 + 2 * (tl - 2)], lhs, wd_buf[:, m, 512:1024],
                                       start=(m == 0), stop=(m == 3))
                tok["dnB"] = b.inc(ii, "t")
                if pr == 3:
                    tok[f"mmdone{s}"] = b.cnt["t"]

            def vr(v, s=s, pr=pr):
                b.wait(v, "t", tok["dnB"])
                v.drain()
                ii = None
                for tl in range(2, 4):
                    tglob = 4 * pr + tl
                    dst = racc_tok[:, tglob, :]
                    for hb in range(2):
                        pk = banks[4 + 2 * (tl - 2) + hb]
                        d2 = dst[:, hb * 512:(hb + 1) * 512]
                        if s == 0:
                            ii = v.tensor_copy(d2, pk)
                        else:
                            ii = v.tensor_tensor(d2, d2, pk, op.add)
                        v.drain()
                tok["vrB"] = b.inc(ii, "v")
            b.step(tensor=tdn)
            b.step(vector=vr)

        for s in range(NEXP):
            wgv, wuv, _ = wslot_views(s % 2)

            if s == 0:
                def ycw0(y):
                    b.wait(y, "v", tok[f"cw{NT - 1}"])
                    for t in range(NT):
                        dma = y.dma_start(out=cwrow_e[0:1, t * 128:(t + 1) * 128],
                                          in_=cw_all[:, t, 0:1])
                        b.inc(dma, "sg2", 16)
                    tok["cwg"] = b.cnt["sg2"]
                b.step(sync=ycw0)

            def tcw(tt, s=s):
                b.wait(tt, "sg2", tok["cwg"])
                if s >= 1:
                    b.wait(tt, "v", tok[f"h{s - 1}_3"])
                ii = None
                for q4 in range(4):
                    ii = tt.matmul(banks[q4], ones_row, cwrow_e[0:1, 512 * q4:512 * (q4 + 1)],
                                   start=True, stop=True)
                tok["cwrep"] = b.inc(ii, "t")

            def vcw(v, s=s):
                b.wait(v, "t", tok["cwrep"])
                v.drain()
                ii = None
                for q4 in range(4):
                    ii = v.tensor_copy(cwb_e[:, 512 * q4:512 * (q4 + 1)], banks[q4])
                tok["cwb"] = b.inc(ii, "v")
            b.step(tensor=tcw)
            b.step(vector=vcw)

            def gpre(g, s=s):
                b.wait(g, "t", tok[f"mmdone{s - 1}"])
                emit_slot_load(g, s + 1)
                emit_wd_load(g, s)

            def ycw(y, s=s):
                b.wait(y, "t", tok["cwrep"])
                for t in range(NT):
                    dma = y.dma_start(out=cwrow_e[0:1, t * 128:(t + 1) * 128],
                                      in_=cw_all[:, t, s + 1:s + 2])
                    b.inc(dma, "sg2", 16)
                tok["cwg"] = b.cnt["sg2"]

            for pr in range(4):
                psl = slice(pr * 512, (pr + 1) * 512)

                def tg(tt, s=s, pr=pr, psl=psl, wgv=wgv):
                    b.wait(tt, f"wsl{s % 2}", tok[f"wload{s}"])
                    b.wait(tt, "v", tok["cwb"])
                    if (s, pr) != (0, 0):
                        prev = tok[f"h{s}_{pr - 1}"] if pr > 0 else tok[f"h{s - 1}_3"]
                        b.wait(tt, "v", prev)
                    ii = None
                    for m in range(4):
                        for c in range(NC):
                            ii = tt.matmul(banks[m], wgv[:, c, m * 128:(m + 1) * 128],
                                           xnT_hi[:, c, psl],
                                           start=(c == 0), stop=(c == NC - 1))
                    tok[f"g{s}_{pr}"] = b.inc(ii, "t")

                def ssg(ss, s=s, pr=pr):
                    b.wait(ss, "t", tok[f"g{s}_{pr}"])
                    ii = None
                    for m in range(4):
                        ii = ss.activation(sg_pair[:, m, :], banks[m], AF.Silu)
                    tok[f"sg{s}_{pr}"] = b.inc(ii, "s")
                b.step(tensor=tg, scalar=ssg)

                if pend[0] is not None:
                    emit_dnA(*pend[0])

                def tu(tt, s=s, pr=pr, psl=psl, wuv=wuv):
                    b.wait(tt, "s", tok[f"sg{s}_{pr}"])
                    ii = None
                    for m in range(4):
                        for c in range(NC):
                            ii = tt.matmul(banks[m], wuv[:, c, m * 128:(m + 1) * 128],
                                           xnT_hi[:, c, psl],
                                           start=(c == 0), stop=(c == NC - 1))
                    tok[f"u{s}_{pr}"] = b.inc(ii, "t")

                def vh(v, s=s, pr=pr, psl=psl):
                    b.wait(v, "t", tok[f"u{s}_{pr}"])
                    v.drain()
                    hb = hp[pr % 2]
                    for m in range(4):
                        v.tensor_tensor(hb[:, m, :], banks[m], cwb_e[:, psl], op.mult)
                    v.drain()
                    ii = v.tensor_tensor(hb.rearrange("p a c -> p (a c)"),
                                         sg_pair.rearrange("p a c -> p (a c)"),
                                         hb.rearrange("p a c -> p (a c)"), op.mult)
                    tok[f"h{s}_{pr}"] = b.inc(ii, "v")
                b.step(tensor=tu)
                b.step(vector=vh)
                if pend[0] is not None:
                    emit_dnB(*pend[0])
                pend[0] = (s, pr)
                if pr == 0 and s >= 1:
                    b.step(gpsimd=gpre)
                if pr == 0 and s + 1 < NEXP:
                    b.step(sync=ycw)
        emit_dnA(*pend[0])
        emit_dnB(*pend[0])
        pend[0] = None

        def gsh(g):
            b.wait(g, "t", tok[f"mmdone{NEXP - 1}"])
            emit_slot_load(g, NEXP + 1)
            emit_wd_load(g, NEXP)
            emit_wd_load2(g, NEXP + 1)
        b.step(gpsimd=gsh)

        h16 = h_pair

        def h17_lhs(m, tl):
            return h17_pair[:, m, tl * 128:(tl + 1) * 128]

        for pr in range(4):
            psl = slice(pr * 512, (pr + 1) * 512)
            for s_sh in (NEXP, NEXP + 1):
                wgv, wuv, _ = wslot_views(s_sh % 2)

                def tg(tt, s=s_sh, pr=pr, psl=psl, wgv=wgv):
                    b.wait(tt, f"wsl{s % 2}", tok[f"wload{s}"])
                    if s == NEXP:
                        if pr == 0:
                            b.wait(tt, "v", tok[f"h{NEXP - 1}_3"])
                        else:
                            b.wait(tt, "v", tok[f"hsh{NEXP + 1}_{pr - 1}"])
                    else:
                        b.wait(tt, "v", tok[f"hsh{NEXP}_{pr}"])
                    ii = None
                    for m in range(4):
                        for c in range(NC):
                            ii = tt.matmul(banks[m], wgv[:, c, m * 128:(m + 1) * 128],
                                           xnT_hi[:, c, psl],
                                           start=(c == 0), stop=(c == NC - 1))
                    tok["gsh"] = b.inc(ii, "t")

                def ssg(ss, s=s_sh, pr=pr):
                    b.wait(ss, "t", tok["gsh"])
                    ii = None
                    for m in range(4):
                        ii = ss.activation(sg_pair[:, m, :], banks[m], AF.Silu)
                    tok["sgsh"] = b.inc(ii, "s")

                def tu(tt, s=s_sh, pr=pr, psl=psl, wuv=wuv):
                    b.wait(tt, "s", tok["sgsh"])
                    ii = None
                    for m in range(4):
                        for c in range(NC):
                            ii = tt.matmul(banks[m], wuv[:, c, m * 128:(m + 1) * 128],
                                           xnT_hi[:, c, psl],
                                           start=(c == 0), stop=(c == NC - 1))
                    tok["ush"] = b.inc(ii, "t")

                def vh(v, s=s_sh, pr=pr):
                    b.wait(v, "t", tok["ush"])
                    v.drain()
                    ii = None
                    hb = h16 if s == NEXP else h17_pair
                    for m in range(4):
                        ii = v.tensor_tensor(hb[:, m, :],
                                             sg_pair[:, m, :], banks[m], op.mult)
                    v.drain()
                    tok[f"hsh{s}_{pr}"] = b.inc(ii, "v")
                b.step(tensor=tg, scalar=ssg)
                b.step(tensor=tu)
                b.step(vector=vh)

            for half4 in range(2):
                def tdn(tt, pr=pr, half4=half4):
                    b.wait(tt, "v", tok[f"hsh{NEXP}_{pr}"])
                    b.wait(tt, "v", tok[f"hsh{NEXP + 1}_{pr}"])
                    b.wait(tt, "wdl", tok[f"wdload{NEXP + 1}"])
                    if (pr, half4) != (0, 0):
                        b.wait(tt, "v", tok["xcmb"])
                    else:
                        b.wait(tt, "v", tok["vrB"])
                    ii = None
                    for tl in range(2 * half4, 2 * half4 + 2):
                        bb0 = 4 + 2 * (tl - 2 * half4)
                        for m in range(4):
                            lhs16 = h16[:, m, tl * 128:(tl + 1) * 128]
                            tt.matmul(banks[bb0], lhs16, wd_buf[:, m, 0:512],
                                      start=(m == 0), stop=False)
                            tt.matmul(banks[bb0 + 1], lhs16, wd_buf[:, m, 512:1024],
                                      start=(m == 0), stop=False)
                            lhs17 = h17_lhs(m, tl)
                            tt.matmul(banks[bb0], lhs17, wd_buf2[:, m, 0:512],
                                      start=False, stop=(m == 3))
                            ii = tt.matmul(banks[bb0 + 1], lhs17, wd_buf2[:, m, 512:1024],
                                           start=False, stop=(m == 3))
                    tok["dnsh"] = b.inc(ii, "t")

                def vcmb(v, pr=pr, half4=half4):
                    b.wait(v, "t", tok["dnsh"])
                    b.wait(v, "v", tok["a0b"])
                    v.drain()
                    ii = None
                    for tl in range(2 * half4, 2 * half4 + 2):
                        tglob = 4 * pr + tl
                        bb0 = 4 + 2 * (tl - 2 * half4)
                        xt = x_res[:, tglob, :]
                        for hb in range(2):
                            sl = slice(hb * 512, (hb + 1) * 512)
                            v.tensor_tensor(xt[:, sl], xt[:, sl], banks[bb0 + hb], op.add)
                            v.drain()
                            v.tensor_tensor(xt[:, sl], xt[:, sl],
                                            racc_tok[:, tglob, sl], op.add)
                            v.drain()
                            ii = v.tensor_tensor(xt[:, sl], xt[:, sl], attn0b[:, sl], op.add)
                            v.drain()
                    tok["xcmb"] = b.inc(ii, "v")
                b.step(tensor=tdn)
                b.step(vector=vcmb)
        tok["mmdone17"] = b.cnt["t"]
        tok["x1"] = tok["xcmb"]

    ph_consts()
    ph_l0a()
    if upto.startswith("l1skip"):
        tok["x1"] = tok[f"ev{NT - 1}"]
    else:
        ph_l0b()
        ph_l0c()

    if upto == "l0":
        def yout(y):
            b.wait(y, "v", tok["x1"])
            for i in range(4):
                dma = y.dma_start(out=out.rearrange("(n p) d -> p n d", p=128)[:, 4 * i:4 * i + 4, :],
                                  in_=x_res[:, 4 * i:4 * i + 4, :])
                b.inc(dma, "so", 16)
            srcs = {"xnT_hi": xnT_hi, "cw": cw_all, "attn0b": attn0b, "racc": racc,
                    "comb": comb_scr}
            for name, ap_dbg in dbg_out.items():
                nm = name.replace("dbg_", "")
                dma = y.dma_start(out=ap_dbg, in_=srcs[nm[4:] if False else nm])
                b.inc(dma, "so", 16)
            y.wait_ge(b.sem("so"), b.cnt["so"])
        b.step(sync=yout)
        return nc, dbg_out

    def emit_quant(eng, ename, x_ap, out_ap, qa, qb, bm, rs6, nblk):
        F = nblk * 16
        x3 = x_ap.rearrange("p (n i) -> p n i", i=16)
        eng.tensor_reduce(bm, x3, AXX, op.max, apply_absolute_value=True)
        eng.drain()
        eng.tensor_scalar(rs6, bm, 6e-12, None, op.max)
        eng.drain()
        eng.reciprocal(rs6, rs6)
        eng.drain()
        eng.tensor_scalar(rs6, rs6, 6.0, None, op.mult)
        eng.drain()
        rs6b = rs6.rearrange("p n -> p n ()").broadcast_to([128, nblk, 16])
        qa3 = qa.rearrange("p (n i) -> p n i", i=16)
        eng.tensor_scalar(qa, x_ap, -1.0, None, op.mult)
        eng.drain()
        eng.tensor_tensor(qa, qa, x_ap, op.max)
        eng.drain()
        eng.tensor_tensor(qa3, qa3, rs6b, op.mult)
        eng.drain()
        eng.tensor_scalar(qb, qa, 5.0, 2.0, op.is_ge, op.mult)
        eng.drain()
        eng.scalar_tensor_tensor(qb, qa, 2.5, qb, op.is_ge, op.add)
        eng.drain()
        eng.scalar_tensor_tensor(qb, qa, 3.5, qb, op.is_ge, op.add)
        eng.drain()
        eng.tensor_scalar(qb, qb, 2.0, None, op.mult)
        eng.drain()
        for t_ in (0.25, 0.75, 1.25, 1.75):
            eng.scalar_tensor_tensor(qb, qa, t_, qb, op.is_ge, op.add)
            eng.drain()
        eng.reciprocal(rs6, rs6)
        eng.drain()
        eng.tensor_scalar(qa, x_ap, 0.0, 1.0, op.is_ge, op.subtract)
        eng.drain()
        eng.tensor_scalar(qa, qa, 2.0, 1.0, op.mult, op.add)
        eng.drain()
        scb = rs6.rearrange("p n -> p n ()").broadcast_to([128, nblk, 16])
        eng.tensor_scalar(qa, qa, 0.5, None, op.mult)
        eng.drain()
        eng.tensor_tensor(qa3, qa3, scb, op.mult)
        eng.drain()
        ii = eng.tensor_tensor(out_ap, qb, qa, op.mult)
        eng.drain()
        return b.inc(ii, ename)

    xn2T = xnT_hi
    qa_v = bcast_a
    qb_v = attn0b
    bm_v = l1scr[:, 0:64]
    rs6_v = l1scr[:, 64:128]

    def ph_l1a():
        ph_ln_stats("1", gate=("v", tok["x1"]))
        for t in range(l1a_tiles):
            scr = xn_scr[:, t % 2, :]
            p0, p1 = banks[t % 2], banks[2 + t % 2]

            def v(v, t=t, scr=scr):
                b.wait(v, "v", tok["rstd1"])
                if t >= 2:
                    b.wait(v, "t", tok[f"l1tr{t - 2}"])
                v.drain()
                ii = v.tensor_scalar(scr, x_res[:, t, :], mv_all[:, t, 0:1],
                                     rstd_all[:, t:t + 1], op.subtract, op.mult)
                tok[f"l1ln{t}"] = b.inc(ii, "v")

            def y(y, t=t):
                b.wait(y, "v", tok[f"l1ln{t}"])
                dma = y.dma_start(out=xspill.rearrange("(n p) d -> p n d", p=128)[:, t, :],
                                  in_=x_res[:, t, :])
                b.inc(dma, "sxp", 16)

            def tt(tt, t=t, scr=scr, p0=p0, p1=p1):
                b.wait(tt, "v", tok[f"l1ln{t}"])
                if t >= 2:
                    b.wait(tt, "v", tok[f"l1ev{t - 2}"])
                ii = None
                for c in range(8):
                    half = p0 if c < 4 else p1
                    ii = tt.transpose(half[:, (c % 4) * 128:(c % 4) * 128 + 128],
                                      scr[:, c * 128:c * 128 + 128], ident_f)
                tok[f"l1tr{t}"] = b.inc(ii, "t")

            def v2(v, t=t, p0=p0, p1=p1):
                b.wait(v, "t", tok[f"l1tr{t}"])
                ii = None
                for h, half in ((0, p0), (1, p1)):
                    hi = xn2T[:, 4 * h:4 * h + 4, t * 128:(t + 1) * 128]
                    ii = v.tensor_copy(hi, half.rearrange("p (c x) -> p c x", c=4))
                tok[f"l1ev{t}"] = b.inc(ii, "v")

            b.step(vector=v)
            if l1a_parts >= 2:
                b.step(sync=y, tensor=tt, vector=v2)
            else:
                b.step(tensor=tt, vector=v2)

            if l1a_parts < 3:
                continue
            def vq(v, t=t, scr=scr):
                b.wait(v, "v", tok[f"l1ln{t}"])
                if t >= 1:
                    b.wait(v, "t", tok[f"xqtr{t - 1}"])
                v.drain()
                tok[f"q{t}"] = emit_quant(v, "v", scr, qb_v,
                                          qa_v, qb_v, bm_v, rs6_v, 64)

            def tq(tt, t=t):
                b.wait(tt, "v", tok[f"q{t}"])
                if t >= 2:
                    b.wait(tt, "v", tok[f"xqev{t - 2}"])
                pb = banks[4 + t % 2]
                ii = None
                for c in range(8):
                    ii = tt.transpose(pb[:, (c % 4) * 128:(c % 4) * 128 + 128]
                                      if c < 4 else banks[6 + t % 2][:, (c % 4) * 128:(c % 4) * 128 + 128],
                                      qb_v[:, c * 128:c * 128 + 128], ident_f)
                tok[f"xqtr{t}"] = b.inc(ii, "t")

            def vq2(v, t=t):
                b.wait(v, "t", tok[f"xqtr{t}"])
                b.wait(v, "sw0", b.cnt.get("sw0", 0))
                b.wait(v, "sg2", b.cnt.get("sg2", 0))
                if t >= 2:
                    b.wait(v, f"sq{t % 2}", tok[f"sqd{t - 2}"])
                v.drain()
                v.tensor_copy(xq_tileA[t % 2][:, 0:4, :],
                              banks[4 + t % 2].rearrange("p (c x) -> p c x", c=4))
                ii = v.tensor_copy(xq_tileA[t % 2][:, 4:8, :],
                                   banks[6 + t % 2].rearrange("p (c x) -> p c x", c=4))
                tok[f"xqev{t}"] = b.inc(ii, "v")

            def yq(y, t=t):
                b.wait(y, "v", tok[f"xqev{t}"])
                dma = y.dma_start(out=xqspill[:, :, t * 128:(t + 1) * 128],
                                  in_=xq_tileA[t % 2])
                tok[f"sqd{t}"] = b.inc(dma, f"sq{t % 2}", 16)
            b.step(vector=vq)
            if l1a_parts >= 4:
                b.step(tensor=tq, vector=vq2)
            if l1a_parts >= 5:
                b.step(sync=yq)
        tok["spilled"] = b.cnt.get("sxp", 0)
        tok["xq_spilled"] = (b.cnt.get("sq0", 0), b.cnt.get("sq1", 0))
        tok["l1a_v"] = b.cnt["v"]

    wqb = SBn("wqb", [128, NC, 256], BF16, at=ws0_off)
    wkb = SBn("wkb", [128, NC, 256], BF16, at=ws0_off + 4096)
    wvb = SBn("wvb", [128, NC, D], BF16, at=ws0_off + 8192)
    wob = SBn("wob", [128, NC, D], BF16, at=ws1_off + 8192)
    bo_f = mv_all.rearrange("p a c -> p (a c)")[:, 0:8]
    sq_s = SBn("sq_s", [128, 256], F32, at=wrh_off)
    qpad = attn0b[:, 0:512]
    kpad = attn0b[:, 512:1024]
    bqb = l1scr[:, 0:256]
    bkb = l1scr[:, 256:512]
    qpb_s = l1scr[:, 512:768]
    bvb = bcast_a

    def ph_l1b():
        def g(g):
            b.wait(g, "t", tok["mmdone17"] if "mmdone17" in tok else 0)
            for dst, src in ((wqb, Wq1), (wkb, Wk1)):
                dma = g.dma_start(out=dst, in_=src.rearrange("(c p) w -> p c w", p=128))
                b.inc(dma, "wl1", 16)
            for dst, src in ((wvb, Wv1), (wob, Wo1)):
                dma = g.dma_start(out=dst, in_=src.rearrange("(c p) w -> p c w", p=128))
                b.inc(dma, "wl1", 16)
            tok["wl1"] = b.cnt["wl1"]

        def y(y):
            b.wait(y, "v", tok["x1"])
            if "mmdone17" in tok:
                b.wait(y, "t", tok["mmdone17"])
            dma = y.dma_start(out=row_scr[0:1, 0:256], in_=bq1[None, :])
            b.inc(dma, "sb1", 16)
            dma = y.dma_start(out=row_scr[0:1, 256:512], in_=qph.rearrange("a h b w -> a (h b w)"))
            b.inc(dma, "sb1", 16)
            dma = y.dma_start(out=row_scr[0:1, 512:768], in_=bk1[None, :])
            b.inc(dma, "sb1", 16)
            dma = y.dma_start(out=row_scr[0:1, 768:1024], in_=kph.rearrange("a h b w -> a (h b w)"))
            b.inc(dma, "sb1", 16)
            dma = y.dma_start(out=bo_f, in_=bo1.rearrange("(c p) -> p c", p=128),
                              allow_slow_non_contiguous=True)
            b.inc(dma, "sb1", 16)
            tok["sb1"] = b.cnt["sb1"]

        def v(v):
            b.wait(v, "sb1", tok["sb1"])
            b.wait(v, "v", tok["l1a_v"])
            v.drain()
            v.tensor_tensor(row_scr[0:1, 0:256], row_scr[0:1, 0:256],
                            row_scr[0:1, 256:512], op.add)
            v.tensor_tensor(row_scr[0:1, 512:768], row_scr[0:1, 512:768],
                            row_scr[0:1, 768:1024], op.add)
            v.drain()
            ii = v.memset(qpad, 0.0)
            ii = v.memset(kpad, 0.0)
            tok["brow"] = b.inc(ii, "v")

        def t2(tt):
            b.wait(tt, "v", tok["brow"])
            tt.matmul(banks[0][:, 0:256], ones_row, row_scr[0:1, 0:256], start=True, stop=True)
            tt.matmul(banks[0][:, 256:512], ones_row, row_scr[0:1, 512:768], start=True, stop=True)
            ii = tt.matmul(banks[1][0:128, 0:1], ones_row, rtemp[0:1, 0:1], start=True, stop=True)
            tok["brep"] = b.inc(ii, "t")

        def v2(v):
            b.wait(v, "t", tok["brep"])
            v.drain()
            v.tensor_copy(bqb, banks[0][:, 0:256])
            v.tensor_copy(bkb, banks[0][:, 256:512])
            ii = v.tensor_copy(rtemp, banks[1][:, 0:1])
            tok["bqkb"] = b.inc(ii, "v")

        b.step(gpsimd=g, sync=y)
        b.step(vector=v)
        b.step(tensor=t2, vector=v2)

        def y3(y):
            b.wait(y, "v", tok["bqkb"])
            b.wait(y, "t", tok["brep"])
            dma = y.dma_start(out=row_scr[0:1, :], in_=bv1[None, :])
            b.inc(dma, "sb1", 16)
            tok["sbv"] = b.cnt["sb1"]

        def t3(tt):
            b.wait(tt, "sb1", tok["sbv"])
            tt.matmul(banks[0], ones_row, row_scr[0:1, 0:512], start=True, stop=True)
            ii = tt.matmul(banks[1], ones_row, row_scr[0:1, 512:1024], start=True, stop=True)
            tok["bvrep"] = b.inc(ii, "t")

        def v3(v):
            b.wait(v, "t", tok["bvrep"])
            b.wait(v, "v", tok["l1a_v"])
            b.wait(v, "sq0", tok["xq_spilled"][0])
            b.wait(v, "sq1", tok["xq_spilled"][1])
            v.drain()
            v.tensor_copy(bvb[:, 0:512], banks[0])
            ii = v.tensor_copy(bvb[:, 512:1024], banks[1])
            tok["bvb"] = b.inc(ii, "v")
        b.step(sync=y3)
        b.step(tensor=t3, vector=v3)

        for t in range(NT):
            def tp(tt, t=t):
                b.wait(tt, "wl1", tok["wl1"])
                b.wait(tt, "v", max(tok["bvb"], tok[f"l1ev{t}"]))
                if t >= 1:
                    b.wait(tt, "v", tok[f"vev{t - 1}"])
                    b.wait(tt, "v", tok[f"qkn{t - 1}"])
                ii = None
                for c in range(NC):
                    lhs = xn2T[:, c, t * 128:(t + 1) * 128]
                    tt.matmul(banks[0][:, 0:256], lhs, wqb[:, c, :],
                              start=(c == 0), stop=(c == NC - 1))
                    tt.matmul(banks[1], lhs, wvb[:, c, 0:512],
                              start=(c == 0), stop=(c == NC - 1))
                    tt.matmul(banks[2], lhs, wvb[:, c, 512:1024],
                              start=(c == 0), stop=(c == NC - 1))
                for c in range(NC):
                    lhs = xn2T[:, c, t * 128:(t + 1) * 128]
                    ii = tt.matmul(banks[0][:, 256:512], lhs, wkb[:, c, :],
                                   start=(c == 0), stop=(c == NC - 1))
                tok[f"proj{t}"] = b.inc(ii, "t")

            def vv(v, t=t):
                b.wait(v, "t", tok[f"proj{t}"])
                b.wait(v, "sxp", tok["spilled"])
                v.drain()
                v.scalar_tensor_tensor(Vbuf[:, t, 0:512], banks[1], 0.0,
                                       bvb[:, 0:512], op.bypass, op.add)
                ii = v.scalar_tensor_tensor(Vbuf[:, t, 512:1024], banks[2], 0.0,
                                            bvb[:, 512:1024], op.bypass, op.add)
                tok[f"vev{t}"] = b.inc(ii, "v")

            b.step(tensor=tp)
            b.step(vector=vv)

            for qk, bias_b, pslc in (("q", bqb, slice(0, 256)), ("k", bkb, slice(256, 512))):
                def vn(v, t=t, qk=qk, bias_b=bias_b, pslc=pslc):
                    b.wait(v, "t", tok[f"proj{t}"])
                    b.wait(v, "sq0", tok["xq_spilled"][0])
                    b.wait(v, "sq1", tok["xq_spilled"][1])
                    if t >= 1:
                        b.wait(v, "t", tok[f"qktr{t - 1}"])
                    v.drain()
                    v.tensor_tensor(qpb_s, banks[0][:, pslc], bias_b, op.add)
                    v.drain()
                    ii = v.tensor_tensor(sq_s, qpb_s, qpb_s, op.mult)
                    tok["_sq"] = b.inc(ii, "v")

                b.step(vector=vn)

                def vr2(v, t=t, qk=qk):
                    sm3 = sq_s.rearrange("p (h w) -> p h w", w=16)
                    v.drain()
                    v.tensor_reduce(a08[:, 0:8], sm3[:, 0:8, :], AXX, op.add)
                    v.tensor_reduce(val8[:, 0:8], sm3[:, 8:16, :], AXX, op.add)
                    v.drain()
                    v.tensor_scalar(a08, a08, 1e-24, None, op.max)
                    ii = v.tensor_scalar(val8, val8, 1e-24, None, op.max)
                    tok["_ssq"] = b.inc(ii, "v")

                def sq2(s_, t=t):
                    b.wait(s_, "v", tok["_ssq"])
                    s_.activation(a08, a08, AF.Sqrt)
                    ii = s_.activation(val8, val8, AF.Sqrt)
                    tok["_sqr"] = b.inc(ii, "s")

                def vr3(v, t=t, qk=qk):
                    b.wait(v, "s", tok["_sqr"])
                    v.drain()
                    v.reciprocal(a08, a08)
                    v.reciprocal(val8, val8)
                    v.drain()
                    dst = (qpad if qk == "q" else kpad)
                    d3 = dst.rearrange("p (h z) -> p h z", z=32)[:, :, 0:16]
                    q3 = qpb_s.rearrange("p (h w) -> p h w", w=16)
                    rn8a = a08.rearrange("p c -> p c ()").broadcast_to([128, 8, 16])
                    rn8b = val8.rearrange("p c -> p c ()").broadcast_to([128, 8, 16])
                    v.tensor_tensor(d3[:, 0:8, :], q3[:, 0:8, :], rn8a, op.mult)
                    ii = v.tensor_tensor(d3[:, 8:16, :], q3[:, 8:16, :], rn8b, op.mult)
                    tok[f"nrm{qk}{t}"] = b.inc(ii, "v")
                b.step(vector=vr2)
                b.step(scalar=sq2)
                b.step(vector=vr3)
            tok[f"qkn{t}"] = tok[f"nrmk{t}"]

            def tq2(tt, t=t):
                b.wait(tt, "v", tok[f"qkn{t}"])
                if t >= 1:
                    b.wait(tt, "v", tok[f"qkev{t - 1}"])
                for g4 in range(4):
                    tt.transpose(banks[3][:, g4 * 128:(g4 + 1) * 128],
                                 qpad[:, g4 * 128:(g4 + 1) * 128], ident_f)
                ii = None
                for g4 in range(4):
                    ii = tt.transpose(banks[4][:, g4 * 128:(g4 + 1) * 128],
                                      kpad[:, g4 * 128:(g4 + 1) * 128], ident_f)
                tok[f"qktr{t}"] = b.inc(ii, "t")

            def vq3(v, t=t):
                b.wait(v, "t", tok[f"qktr{t}"])
                v.drain()
                v.tensor_copy(qTpad[:, :, t * 128:(t + 1) * 128],
                              banks[3].rearrange("p (g x) -> p g x", g=4))
                ii = v.tensor_copy(kTpad[:, :, t * 128:(t + 1) * 128],
                                   banks[4].rearrange("p (g x) -> p g x", g=4))
                tok[f"qkev{t}"] = b.inc(ii, "v")
            b.step(tensor=tq2)
            b.step(vector=vq3)
        tok["qkdone"] = tok[f"qkev{NT - 1}"]
        tok["vdone"] = tok[f"vev{NT - 1}"]

        def vz(v):
            ii = None
            for var, idx in ((1, 6), (1, 7), (2, 8), (2, 9), (3, 10), (3, 11)):
                ii = v.memset(P_pool[idx][:, 0:128 * var], 0.0)
            tok["pz"] = b.inc(ii, "v")
        b.step(vector=vz)

        plain_i = [0]
        var_i = {1: 0, 2: 0, 3: 0}
        p_tok = {}
        pend_tail = [None]

        for c in range(4):
            for h in range(HH):
                a = 32 * (h % 4)
                g4 = h // 4
                pvb = banks[4 + h % 2]
                dnb = banks[6 + h % 2][0:1, 0:512]
                njt = 4 * c + 4
                emitters = []
                for j in range(njt):
                    diag = j >= 4 * c
                    ov = j - 4 * c if diag else 0
                    if diag and ov > 0:
                        pi = 6 + (ov - 1) * 2 + var_i[ov]
                        var_i[ov] ^= 1
                    else:
                        pi = plain_i[0]
                        plain_i[0] = (plain_i[0] + 1) % 6
                    P = P_pool[pi]
                    bi = j % 3
                    sb_ = banks[bi]

                    def ts_(tt, c=c, h=h, j=j, a=a, g4=g4, sb_=sb_, pi=pi, bi=bi):
                        b.wait(tt, "v", max(tok["qkdone"], tok["vdone"], tok["pz"]))
                        if c > 0 and h == 0 and j == 0:
                            b.wait(tt, "v", tok[f"oev{c - 1}_7"])
                        b.wait(tt, "s", p_tok.get(("exp_prev", bi), 0))
                        ii = tt.matmul(sb_, kTpad[a:a + 16, g4, j * 128:(j + 1) * 128],
                                       qTpad[a:a + 16, g4, c * 512:(c + 1) * 512],
                                       start=True, stop=True, tile_position=(a, 0))
                        p_tok[("qk", c, h, j)] = b.inc(ii, "t")

                    def se_(s_, c=c, h=h, j=j, sb_=sb_, P=P, ov=ov, diag=diag, pi=pi, bi=bi):
                        b.wait(s_, "t", p_tok[("qk", c, h, j)])
                        b.wait(s_, "t", p_tok.get(("pv_prevuse", pi), 0))
                        lo = 128 * ov if diag else 0
                        ii = s_.activation(P[:, lo:512], sb_[:, lo:512], AF.Exp,
                                           bias=neg1, scale=rtemp)
                        p_tok[("exp", c, h, j)] = b.inc(ii, "s")
                        p_tok[("exp_prev", bi)] = b.cnt["s"]

                    def gm_(g_, c=c, h=h, j=j, P=P, ov=ov):
                        b.wait(g_, "s", p_tok[("exp", c, h, j)])
                        ii = g_.affine_select(out=P[:, 128 * ov:128 * ov + 128],
                                              in_=P[:, 128 * ov:128 * ov + 128],
                                              compare_op=op.is_ge, fill=0.0, base=0,
                                              pattern=[[1, 128]], channel_multiplier=-1)
                        p_tok[("msk", c, h, j)] = b.inc(ii, "g")

                    def tpv_(tt, c=c, h=h, j=j, P=P, pvb=pvb, dnb=dnb, diag=diag, pi=pi):
                        if diag:
                            b.wait(tt, "g", p_tok[("msk", c, h, j)])
                        else:
                            b.wait(tt, "s", p_tok[("exp", c, h, j)])
                        if j == 0:
                            hp = h - 2 if h >= 2 else (HH - 2 + h if c > 0 else None)
                            cp = c if h >= 2 else c - 1
                            if hp is not None:
                                b.wait(tt, "v", tok[f"at{cp}_{hp}"])
                        tt.matmul(pvb[0:64, :], Vbuf[:, j, h * 64:(h + 1) * 64], P,
                                  start=(j == 0), stop=(j == 4 * c + 3))
                        ii = tt.matmul(dnb, ones_colb, P,
                                       start=(j == 0), stop=(j == 4 * c + 3))
                        p_tok[("pv", c, h, j)] = b.inc(ii, "t")
                        p_tok[("pv_prevuse", pi)] = b.cnt["t"]
                    emitters.append((ts_, se_, gm_ if diag else None, tpv_))

                b.step(tensor=emitters[0][0])
                if njt > 1:
                    b.step(tensor=emitters[1][0])
                if pend_tail[0] is not None:
                    for st_ in pend_tail[0]:
                        st_()
                    pend_tail[0] = None
                for j in range(njt):
                    ts_n = emitters[j + 2][0] if j + 2 < njt else None
                    se_j, gm_j, tpv_j = emitters[j][1], emitters[j][2], emitters[j][3]
                    b.step(scalar=se_j)
                    if gm_j is not None:
                        b.step(gpsimd=gm_j)

                    def tcomb(tt, tpv_j=tpv_j, ts_n=ts_n):
                        tpv_j(tt)
                        if ts_n is not None:
                            ts_n(tt)
                    b.step(tensor=tcomb)

                def vd_(v, c=c, h=h, pvb=pvb, dnb=dnb):
                    b.wait(v, "t", p_tok[("pv", c, h, 4 * c + 3)])
                    if (c, h) != (0, 0):
                        b.wait(v, "t", tok.get("rbrep0", 0))
                    v.drain()
                    ii = v.reciprocal(row_scr[0:1, 0:512], dnb)
                    tok["_rd"] = b.inc(ii, "v")

                def td_(tt, c=c, h=h):
                    b.wait(tt, "v", tok["_rd"])
                    ii = tt.matmul(banks[3][0:64, :], ones_row[0:1, 0:64],
                                   row_scr[0:1, 0:512], start=True, stop=True)
                    tok["rbrep0"] = b.inc(ii, "t")

                def va_(v, c=c, h=h, pvb=pvb):
                    b.wait(v, "t", tok["rbrep0"])
                    v.drain()
                    v.tensor_copy(l1scr[0:64, 0:512], banks[3][0:64, :])
                    v.drain()
                    ii = v.scalar_tensor_tensor(
                        attnT[64 * (h % 2):64 * (h % 2) + 64, h // 2, :],
                        pvb[0:64, :], 0.0, l1scr[0:64, 0:512], op.bypass, op.mult)
                    tok[f"at{c}_{h}"] = b.inc(ii, "v")
                pend_tail[0] = (lambda vd_=vd_: b.step(vector=vd_),
                                lambda td_=td_: b.step(tensor=td_),
                                lambda va_=va_: b.step(vector=va_))

            if pend_tail[0] is not None:
                for st_ in pend_tail[0]:
                    st_()
                pend_tail[0] = None
            for o in range(8):
                ob = banks[o % 2]

                def to_(tt, c=c, o=o, ob=ob):
                    b.wait(tt, "v", tok[f"at{c}_{HH - 1}"])
                    if o >= 2:
                        b.wait(tt, "v", tok[f"oev{c}_{o - 2}"])
                    ii = None
                    for cin in range(NC):
                        ii = tt.matmul(ob, wob[:, cin, o * 128:(o + 1) * 128], attnT[:, cin, :],
                                       start=(cin == 0), stop=(cin == NC - 1))
                    tok["_opj"] = b.inc(ii, "t")

                def vo_(v, c=c, o=o, ob=ob):
                    b.wait(v, "t", tok["_opj"])
                    v.drain()
                    ii = v.tensor_scalar(bvb[:, 0:512], ob, bo_f[:, o:o + 1], None, op.add)
                    tok["_oev"] = b.inc(ii, "v")

                def to2_(tt, c=c, o=o):
                    b.wait(tt, "v", tok["_oev"])
                    if o >= 1:
                        b.wait(tt, "v", tok[f"oev{c}_{o - 1}"])
                    ii = None
                    for tb in range(4):
                        ii = tt.transpose(banks[2][:, tb * 128:(tb + 1) * 128],
                                          bvb[:, 0:512][:, tb * 128:(tb + 1) * 128], ident_f)
                    tok["_otr"] = b.inc(ii, "t")

                def vo2_(v, c=c, o=o):
                    b.wait(v, "t", tok["_otr"])
                    v.drain()
                    ii = v.tensor_copy(attn1_tok[:, 4 * c:4 * c + 4, o * 128:(o + 1) * 128],
                                       banks[2].rearrange("p (z x) -> p z x", z=4))
                    tok[f"oev{c}_{o}"] = b.inc(ii, "v")
                b.step(tensor=to_)
                b.step(vector=vo_)
                b.step(tensor=to2_)
                b.step(vector=vo2_)
        tok["attn_done"] = tok[f"oev3_7"]
        tok["wq_gate_t"] = b.cnt["t"]

    qaG = SBn("qaG", [128, 1024], F32, at=row_off)
    qbG = SBn("qbG", [128, 1024], F32, at=xns_off + 4096)
    qa_c = SBn("qa_c", [128, 1024], F32, at=sg_off)
    qb_c = SBn("qb_c", [128, 1024], F32, at=xns_off)
    bm_c = SBn("bm_c", [128, 64], F32, at=cw_off)
    rs6_c = SBn("rs6_c", [128, 64], F32, at=cw_off + 256)
    bm_g = SBn("bm_g", [128, 64], F32, at=cw_off + 512)
    rs6_g = SBn("rs6_g", [128, 64], F32, at=cw_off + 768)
    bm_d = sq_s[:, 0:32]
    rs6_d = sq_s[:, 32:64]
    qaf = SBn("qaf", [128, 512], F32, at=sg_off + 4096)
    qbf = SBn("qbf", [128, 512], F32, at=cw_off)

    def ph_l1c():
        pieces = []
        for snum in range(22):
            pieces.append(("g", snum, 1024))
        for snum in range(22):
            pieces.append(("u", snum, 1024))
        for snum in range(8):
            for qq in range(3):
                pieces.append(("d", (snum, qq), 1024 if qq < 2 else 768))

        pv_tok = {}
        for idx, (kind, meta, W) in enumerate(pieces):
            par = idx % 2
            eng_v = True
            nblk = W // 128

            def yl(y, idx=idx, kind=kind, meta=meta, par=par, W=W):
                b.wait(y, "v", pv_tok.get(("cons", par), 0))
                b.wait(y, "g", pv_tok.get(("consG", par), 0))
                if idx < 2:
                    b.wait(y, "v", tok["attn_done"])
                    b.wait(y, "t", tok["wq_gate_t"])
                    b.wait(y, "sq0", tok["xq_spilled"][0])
                    b.wait(y, "sq1", tok["xq_spilled"][1])
                if kind == "d":
                    snum, qq = meta
                    dma = y.dma_start(out=wstr[par][:, 0:W],
                                      in_=wd1[128 * snum:128 * (snum + 1),
                                              qq * 1024:qq * 1024 + W])
                elif kind == "g":
                    dma = y.dma_start(out=wstr[par][:, 0:1024],
                                      in_=wg1[128 * meta:128 * (meta + 1), :])
                else:
                    dma = y.dma_start(out=wstr[par][:, 0:1024],
                                      in_=wu1[128 * meta:128 * (meta + 1), :])
                b.inc(dma, f"wst{par}", 16)
                pv_tok[("ld", idx)] = b.cnt[f"wst{par}"]

            def q_(eng, idx=idx, par=par, W=W, eng_v=eng_v):
                ename = "v" if eng_v else "g"
                b.wait(eng, f"wst{par}", pv_tok[("ld", idx)])
                b.wait(eng, "t", pv_tok.get("wqfree_v" if eng_v else "wqfree_g", 0))
                if idx < 4:
                    b.wait(eng, "v", tok["attn_done"])
                qa_, qb_, bm_, rs_ = (qa_c, qb_c, bm_c, rs6_c) if eng_v else \
                    (qaG, qbG, bm_g, rs6_g)
                iitok = emit_quant(eng, ename, wstr[par][:, 0:W], qb_[:, 0:W],
                                   qa_[:, 0:W], qb_[:, 0:W],
                                   bm_[:, 0:W // 16], rs_[:, 0:W // 16], W // 16)
                pv_tok[("q", idx)] = (ename, iitok)
                pv_tok[("cons" if eng_v else "consG", par)] = iitok

            b.step(sync=yl)
            if eng_v:
                b.step(vector=q_)
            else:
                b.step(gpsimd=q_)

            wq_ = qb_c if eng_v else qbG
            nbat = (nblk + 3) // 4
            for bb in range(nbat):
                blks = list(range(4 * bb, min(4 * bb + 4, nblk)))
                pb = banks[(idx * nbat + bb) % 4]

                def tt_(tt, idx=idx, bb=bb, blks=blks, pb=pb, wq_=wq_, eng_v=eng_v):
                    en, it = pv_tok[("q", idx)]
                    b.wait(tt, en, it)
                    b.wait(tt, "v", pv_tok.get(("bank_free", pb.tensor.name), 0))
                    ii = None
                    for zi, cb in enumerate(blks):
                        ii = tt.transpose(pb[:, zi * 128:zi * 128 + 128],
                                          wq_[:, cb * 128:(cb + 1) * 128], ident_f)
                    pv_tok[("tr", idx, bb)] = b.inc(ii, "t")

                def vv_(v, idx=idx, bb=bb, blks=blks, pb=pb, kind=kind, meta=meta,
                        nbat=nbat):
                    b.wait(v, "t", pv_tok[("tr", idx, bb)])
                    v.drain()
                    ii = None
                    for zi, cb in enumerate(blks):
                        srcb = pb[:, zi * 128:zi * 128 + 128]
                        if kind == "g":
                            dst = wgqT[:, cb, meta * 128:(meta + 1) * 128]
                        elif kind == "u":
                            dst = wuqT[:, cb, meta * 128:(meta + 1) * 128]
                        else:
                            snum, qq = meta
                            dst = wdqT[:, qq * 8 + cb, snum * 128:(snum + 1) * 128]
                        ii = v.tensor_copy(dst, srcb)
                    pv_tok[("bank_free", pb.tensor.name)] = b.inc(ii, "v")
                b.step(tensor=tt_)
                b.step(vector=vv_)
            pv_tok["wqfree_v" if eng_v else "wqfree_g"] = pv_tok[("tr", idx, nbat - 1)]

        tok["wq_done"] = b.cnt["v"]
        tok["wq_done_t"] = b.cnt["t"]

    def ph_l1d():
        fcw = [(0, 512), (512, 512), (1024, 512), (1536, 512), (2048, 512), (2560, 256)]
        for t in range(NT):
            def yl(y, t=t):
                if t >= 2:
                    b.wait(y, "t", tok[f"gud{t - 2}"])
                    b.wait(y, f"so{t % 2}", tok.get(f"st{t - 2}", 0))
                b.wait(y, "sq0", tok["xq_spilled"][0])
                b.wait(y, "sq1", tok["xq_spilled"][1])
                b.wait(y, "sxp", tok["spilled"])
                if t < 2:
                    b.wait(y, "v", tok["wq_done"])
                    b.wait(y, "t", tok["wq_done_t"])
                dma = y.dma_start(out=xq_tile[t % 2], in_=xqspill[:, :, t * 128:(t + 1) * 128])
                b.inc(dma, "sr1", 16)
                dma = y.dma_start(out=out_sb[t % 2],
                                  in_=xspill.rearrange("(n p) d -> p n d", p=128)[:, t, :])
                b.inc(dma, "sr1", 16)
                tok[f"rl{t}"] = b.cnt["sr1"]

            b.step(sync=yl)

            for fi, (f0, fw) in enumerate(fcw):
                def tg(tt, t=t, fi=fi, f0=f0, fw=fw):
                    b.wait(tt, "sr1", tok[f"rl{t}"])
                    if (t, fi) != (0, 0):
                        b.wait(tt, "v", tok.get("_hq", 0))
                    ii = None
                    for cc in range(NC):
                        tt.matmul(banks[0][:, 0:fw], xq_tile[t % 2][:, cc, :],
                                  wgqT[:, cc, f0:f0 + fw], start=(cc == 0), stop=(cc == NC - 1))
                        ii = tt.matmul(banks[1][:, 0:fw], xq_tile[t % 2][:, cc, :],
                                       wuqT[:, cc, f0:f0 + fw], start=(cc == 0), stop=(cc == NC - 1))
                    tok["_gu1"] = b.inc(ii, "t")

                def sg_(s_, t=t, fi=fi, fw=fw):
                    b.wait(s_, "t", tok["_gu1"])
                    ii = s_.activation(sgf_scr[:, 0:fw], banks[0][:, 0:fw], AF.Silu)
                    tok["_sg1"] = b.inc(ii, "s")

                def vh_(v, t=t, fi=fi, f0=f0, fw=fw):
                    b.wait(v, "t", tok["_gu1"])
                    b.wait(v, "s", tok["_sg1"])
                    v.drain()
                    v.tensor_tensor(hf_scr[:, 0:fw], sgf_scr[:, 0:fw], banks[1][:, 0:fw],
                                    op.mult)
                    v.drain()
                    hq_flat = hq_strip.rearrange("p a c -> p (a c)")
                    tok["_hq"] = emit_quant(v, "v", hf_scr[:, 0:fw], hq_flat[:, f0:f0 + fw],
                                            qaf[:, 0:fw], qbf[:, 0:fw],
                                            bm_d[:, 0:fw // 16], rs6_d[:, 0:fw // 16],
                                            fw // 16)
                b.step(tensor=tg, scalar=sg_)
                b.step(vector=vh_)

            for bb in range(6):
                blks = list(range(4 * bb, min(4 * bb + 4, 22)))
                pb = banks[2 + bb % 2]

                def th(tt, t=t, bb=bb, blks=blks, pb=pb):
                    b.wait(tt, "v", tok["_hq"])
                    b.wait(tt, "v", tok.get(("hqb", pb.tensor.name), 0))
                    hq_flat = hq_strip.rearrange("p a c -> p (a c)")
                    ii = None
                    for zi, cb in enumerate(blks):
                        ii = tt.transpose(pb[:, zi * 128:zi * 128 + 128],
                                          hq_flat[:, cb * 128:(cb + 1) * 128], ident_f)
                    tok["_htr"] = b.inc(ii, "t")

                def vhq(v, t=t, bb=bb, blks=blks, pb=pb):
                    b.wait(v, "t", tok["_htr"])
                    v.drain()
                    ii = None
                    for zi, cb in enumerate(blks):
                        ii = v.tensor_copy(hqT_scr[:, cb, :],
                                           pb[:, zi * 128:zi * 128 + 128])
                    tok[("hqb", pb.tensor.name)] = b.inc(ii, "v")
                    if bb == 5:
                        tok[f"hqev{t}"] = b.cnt["v"]
                b.step(tensor=th)
                b.step(vector=vhq)

            def td(tt, t=t):
                b.wait(tt, "v", tok[f"hqev{t}"])
                if t >= 1:
                    b.wait(tt, "v", tok.get(f"fin{t - 1}", 0))
                ii = None
                for oh in range(2):
                    pd = banks[4 + oh]
                    for cb in range(22):
                        ii = tt.matmul(pd, hqT_scr[:, cb, :], wdqT[:, cb, oh * 512:(oh + 1) * 512],
                                       start=(cb == 0), stop=(cb == 21))
                tok[f"gud{t}"] = b.inc(ii, "t")

            def vf(v, t=t):
                b.wait(v, "t", tok[f"gud{t}"])
                v.drain()
                ii = None
                for oh in range(2):
                    sl = slice(oh * 512, (oh + 1) * 512)
                    v.tensor_tensor(out_sb[t % 2][:, sl], out_sb[t % 2][:, sl],
                                    banks[4 + oh], op.add)
                    v.drain()
                    ii = v.tensor_tensor(out_sb[t % 2][:, sl], out_sb[t % 2][:, sl],
                                         attn1_tok[:, t, sl], op.add)
                    v.drain()
                tok[f"fin{t}"] = b.inc(ii, "v")

            def yo(y, t=t):
                b.wait(y, "v", tok[f"fin{t}"])
                dma = y.dma_start(out=out.rearrange("(n p) d -> p n d", p=128)[:, t, :],
                                  in_=out_sb[t % 2])
                tok[f"st{t}"] = b.inc(dma, f"so{t % 2}", 16)
            b.step(tensor=td)
            b.step(vector=vf)
            b.step(sync=yo)

        def yend(y):
            y.wait_ge(b.sem("so0"), b.cnt["so0"])
            y.wait_ge(b.sem("so1"), b.cnt["so1"])
        b.step(sync=yend)

    def dump_x_out(gate_sem, gate_val):
        def yout(y):
            b.wait(y, gate_sem, gate_val)
            for i in range(4):
                dma = y.dma_start(out=out.rearrange("(n p) d -> p n d", p=128)[:, 4 * i:4 * i + 4, :],
                                  in_=x_res[:, 4 * i:4 * i + 4, :])
                b.inc(dma, "so", 16)
            y.wait_ge(b.sem("so"), b.cnt["so"])
        b.step(sync=yout)

    ph_l1a()
    if upto in ("l1a", "l1skip_a"):
        dump_x_out("v", b.cnt["v"])
        return nc, dbg_out
    ph_l1b()
    if upto in ("l1b", "l1skip_b"):
        def yo2(g):
            b.wait(g, "v", tok["attn_done"])
            for i in range(4):
                dma = g.dma_start(out=out.rearrange("(n p) d -> p n d", p=128)[:, 4 * i:4 * i + 4, :],
                                  in_=attn1_tok[:, 4 * i:4 * i + 4, :])
                b.inc(dma, "so", 16)
            g.wait_ge(b.sem("so"), b.cnt["so"])
        b.step(gpsimd=yo2)
        return nc, dbg_out
    ph_l1c()
    if upto in ("l1c", "l1skip_c"):
        dump_x_out("v", tok["wq_done"])
        return nc, dbg_out
    ph_l1d()
    return nc, dbg_out


def kernel(**inputs):
    nc, _ = build(upto="full")
    xs = np.ascontiguousarray(inputs["L0_attn_x"], dtype=np.float32)
    weights = {kk: np.ascontiguousarray(v) for kk, v in inputs.items() if kk != "L0_attn_x"}
    in_maps = []
    for c in range(8):
        m = {"x": xs[c]}
        m.update(weights)
        in_maps.append(m)
    res = run_bass_kernel_spmd(nc, in_maps, core_ids=list(range(8)))
    return np.stack([r["out"] for r in res.results], axis=0)


# version 07
# speedup vs baseline: 1.0717x; 1.0036x over previous
import numpy as np
import concourse.bass as bass
import concourse.mybir as mybir
from concourse.alu_op_type import AluOpType as op
from concourse.bass_utils import run_bass_kernel_spmd

F32 = mybir.dt.float32
BF16 = mybir.dt.bfloat16
AF = mybir.ActivationFunctionType
AXX = mybir.AxisListType.X

T, D = 2048, 1024
NT, NC = T // 128, D // 128
NEXP, FH = 16, 512
HH, NW, HD, FI = 16, 16, 64, 2816
TC, NTC = 256, T // 256


class B:

    def __init__(self):
        self.nc = bass.Bass("TRN2", target_bir_lowering=False)
        self.sems, self.cnt, self.wm = {}, {}, {}

    def sem(self, name):
        if name not in self.sems:
            self.sems[name] = self.nc.alloc_semaphore(name=name)
            self.cnt[name] = 0
        return self.sems[name]

    def inc(self, instr, name, n=1):
        instr.then_inc(self.sem(name), n)
        self.cnt[name] += n
        return self.cnt[name]

    def wait(self, eng, name, val):
        if val is None or val <= 0:
            return
        wm = self.wm.setdefault(str(eng.engine), {})
        if wm.get(name, -1) >= val:
            return
        eng.wait_ge(self.sem(name), val)
        wm[name] = val

    def step(self, **fns):
        with self.nc.Block() as blk:
            for nm, fn in fns.items():
                getattr(blk, nm)(fn)


def build(upto="full", dbg=(), l1a_tiles=NT, l1a_parts=7):
    b = B()
    nc = b.nc
    SB = nc.alloc_sbuf_tensor
    PS = nc.alloc_psum_tensor

    x_in = nc.dram_tensor("x", [T, D], F32, kind="ExternalInput").ap()
    w_qkv = nc.dram_tensor("L0_attn_w_qkv", [3 * D, D], F32, kind="ExternalInput").ap()
    w_o0 = nc.dram_tensor("L0_attn_w_o", [D, D], F32, kind="ExternalInput").ap()
    Wr = nc.dram_tensor("L0_ffn_Wr", [D, NEXP], F32, kind="ExternalInput").ap()
    Wg = nc.dram_tensor("L0_ffn_Wg", [NEXP, D, FH], F32, kind="ExternalInput").ap()
    Wu = nc.dram_tensor("L0_ffn_Wu", [NEXP, D, FH], F32, kind="ExternalInput").ap()
    Wd = nc.dram_tensor("L0_ffn_Wd", [NEXP, FH, D], F32, kind="ExternalInput").ap()
    Sg = nc.dram_tensor("L0_ffn_Sg", [D, 2 * FH], F32, kind="ExternalInput").ap()
    Su = nc.dram_tensor("L0_ffn_Su", [D, 2 * FH], F32, kind="ExternalInput").ap()
    Sd = nc.dram_tensor("L0_ffn_Sd", [2 * FH, D], F32, kind="ExternalInput").ap()
    Wq1 = nc.dram_tensor("L1_attn_Wq", [D, HH * NW], F32, kind="ExternalInput").ap()
    bq1 = nc.dram_tensor("L1_attn_bq", [HH * NW], F32, kind="ExternalInput").ap()
    Wk1 = nc.dram_tensor("L1_attn_Wk", [D, HH * NW], F32, kind="ExternalInput").ap()
    bk1 = nc.dram_tensor("L1_attn_bk", [HH * NW], F32, kind="ExternalInput").ap()
    Wv1 = nc.dram_tensor("L1_attn_Wv", [D, D], F32, kind="ExternalInput").ap()
    bv1 = nc.dram_tensor("L1_attn_bv", [D], F32, kind="ExternalInput").ap()
    Wo1 = nc.dram_tensor("L1_attn_Wo", [D, D], F32, kind="ExternalInput").ap()
    bo1 = nc.dram_tensor("L1_attn_bo", [D], F32, kind="ExternalInput").ap()
    qph = nc.dram_tensor("L1_attn_q_phase", [1, HH, 1, NW], F32, kind="ExternalInput").ap()
    kph = nc.dram_tensor("L1_attn_k_phase", [1, HH, 1, NW], F32, kind="ExternalInput").ap()
    temp = nc.dram_tensor("L1_attn_temperature", [1], F32, kind="ExternalInput").ap()
    wg1 = nc.dram_tensor("L1_ffn_w_gate", [FI, D], F32, kind="ExternalInput").ap()
    wu1 = nc.dram_tensor("L1_ffn_w_up", [FI, D], F32, kind="ExternalInput").ap()
    wd1 = nc.dram_tensor("L1_ffn_w_down", [D, FI], F32, kind="ExternalInput").ap()
    out = nc.dram_tensor("out", [T, D], F32, kind="ExternalOutput").ap()
    xspill = nc.dram_tensor("xspill", [T, D], F32, kind="Internal").ap()
    xqspill = nc.dram_tensor("xqspill", [128, NC, T], BF16, kind="Internal").ap()
    dbg_out = {}
    for name, shape, dt in dbg:
        dbg_out[name] = nc.dram_tensor("dbg_" + name, shape, dt, kind="ExternalOutput").ap()

    _off = [16512]

    def SBp(name, shape, dtype, at=None):
        o = _off[0] if at is None else at
        h = nc.alloc_sbuf_tensor_at(name, shape, dtype, offset=o)
        if at is None:
            nbytes = int(np.prod(shape[1:])) * (2 if dtype == BF16 else 4)
            _off[0] = o + ((nbytes + 31) // 32) * 32
        return h.ap(), o

    def SBn(name, shape, dtype, at=None):
        return SBp(name, shape, dtype, at)[0]

    ident_f = SBn("ident_f", [128, 128], F32)
    ident_b = SBn("ident_b", [128, 128], BF16)
    ones_col = SBn("ones_col", [128, 1], F32)
    ones_colb = SBn("ones_colb", [128, 1], BF16)
    ones_row = SBn("ones_row", [1, 128], F32)
    neg1 = SBn("neg1", [128, 1], F32)
    rtemp = SBn("rtemp", [128, 1], F32)
    mv_all = SBn("mv_all", [128, NT, 2], F32)
    rstd_all = SBn("rstd_all", [128, NT], F32)
    bn6 = SBn("bn6", [128, 2, 6], F32)
    cw_all, cw_off = SBp("cw_all", [128, NT, NEXP], F32)
    cwb = SBn("cwb", [128, TC], F32)
    cwrow = SBn("cwrow", [1, TC], F32)
    wr_hi, wrh_off = SBp("wr_hi", [128, NC, NEXP], BF16)
    wr_lo = SBn("wr_lo", [128, NC, NEXP], BF16)
    wr_f = SBn("wr_f", [128, NC, NEXP], F32)
    val8 = SBn("val8", [128, 8], F32)
    a08 = SBn("a08", [128, 8], F32)
    row_scr, row_off = SBp("row_scr", [1, D], F32)
    h2_scr = SBn("h2_scr", [128, 4, TC], BF16, at=row_off)
    sg_scr, sg_off = SBp("sg_scr", [128, 4, TC], F32)
    h_scr, h_off = SBp("h_scr", [128, 4, TC], BF16)
    attn0b, a0_off = SBp("attn0b", [128, D], F32)
    bcast_a, bc_off = SBp("bcast_a", [128, D], F32)
    xn_scr, xns_off = SBp("xn_scr", [128, 2, D], F32)
    xnT_hi, xh_off = SBp("xnT_hi", [128, NC, T], BF16)
    xnT_lo, xl_off = SBp("xnT_lo", [128, NC, T], BF16)
    racc = xnT_lo
    ws0, ws0_off = SBp("wslot0", [128, 16 * FH], BF16)
    ws1, ws1_off = SBp("wslot1", [128, 16 * FH], BF16)
    wslot = [ws0, ws1]
    wd_buf, wd_off = SBp("wd_buf", [128, 4, D], BF16)
    wd_buf2, wd2_off = SBp("wd_buf2", [128, 4, D], BF16)
    x_res, x_off = SBp("x_res", [128, NT, D], F32)
    assert _off[0] <= 229344, f"sbuf map overflow: {_off[0]}"
    comb_scr = bcast_a.rearrange("p (c x) -> p c x", c=8)
    stream_a = [xn_scr[:, i, :] for i in range(2)]
    ttr_scr = sg_scr.rearrange("p a c -> p (a c)")
    cwrow_e = SBn("cwrow_e", [1, T], F32, at=wd2_off)
    cwb_e = SBn("cwb_e", [128, T], BF16, at=bc_off)
    sg_pair = SBn("sg_pair", [128, 4, 512], BF16, at=sg_off)
    h_pair = SBn("h_pair", [128, 4, 512], BF16, at=row_off)
    h_pairB = SBn("h_pairB", [128, 4, 512], BF16, at=xns_off + 4096)
    racc_tok = SBn("racc_tok", [128, NT, D], BF16, at=xl_off)
    h17_pair = SBn("h17_pair", [128, 4, 512], BF16, at=xns_off)
    qaG_a = SBn("qaG_a", [128, 1024], F32, at=xl_off + 4096)
    qbG_a = SBn("qbG_a", [128, 1024], F32, at=xl_off + 8192)
    bm_ga = SBn("bm_ga", [128, 64], F32, at=xl_off + 12288)
    rs6_ga = SBn("rs6_ga", [128, 64], F32, at=xl_off + 12544)
    rs6F_a = SBn("rs6F_a", [128, 1024], F32, at=xl_off + 13056)
    scF_a = SBn("scF_a", [128, 1024], F32, at=xl_off + 17152)
    rs6F_c = SBn("rs6F_c", [128, 1024], F32, at=x_off + 20480)
    scF_c = SBn("scF_c", [128, 1024], F32, at=x_off + 24576)
    qTpad = SBn("qTpad", [128, 4, T], BF16, at=xl_off)
    kTpad = SBn("kTpad", [128, 4, T], BF16, at=xl_off + 16384)
    Vbuf = SBn("Vbuf", [128, NT, D], BF16, at=x_off)
    attn1_tok = SBn("attn1_tok", [128, NT, D], BF16, at=x_off + 32768)
    wgqT = SBn("wgqT", [128, NC, FI], BF16, at=xh_off)
    wuqT = SBn("wuqT", [128, NC, FI], BF16, at=xh_off + 45056)
    wdqT = SBn("wdqT", [128, 22, D], BF16, at=xh_off + 90112)
    P_pool = [SBn(f"Ppool{i}", [128, 512], BF16, at=xns_off + 1024 * i) for i in range(12)]
    attnT = SBn("attnT", [128, NC, 512], BF16, at=wd2_off)
    hq_strip = SBn("hq_strip", [128, 22, 128], F32, at=x_off + 20480)
    hqT_scr = SBn("hqT_scr", [128, 22, 128], BF16, at=xns_off)
    hf_scr = SBn("hf_scr", [128, 512], F32, at=sg_off)
    sgf_scr = SBn("sgf_scr", [128, 512], F32, at=sg_off + 2048)
    out_sb = [SBn(f"out_sb{i}", [128, D], F32, at=row_off if i == 0 else a0_off)
              for i in range(2)]
    xq_tileA = [SBn(f"xq_tileA{i}", [128, NC, 128], BF16, at=xl_off + 2048 * i)
                for i in range(2)]
    xq_tile = [SBn(f"xq_tileD{i}", [128, NC, 128], BF16, at=bc_off + 2048 * i)
               for i in range(2)]
    wstr = [SBn("wstr0", [128, 1024], F32, at=a0_off),
            SBn("wstr1", [128, 1024], F32, at=bc_off)]
    l1scr = SBn("l1scr", [128, 768], F32, at=sg_off)

    banks = [PS(f"bank{i}", [128, 512], F32).ap() for i in range(8)]

    tok = {}

    def ph_consts():
        def g(g):
            g.memset(ident_f, 0.0)
            g.memset(ident_b, 0.0)
            g.memset(ones_col, 1.0)
            g.memset(ones_colb, 1.0)
            g.memset(ones_row, 1.0)
            g.memset(neg1, -1.0)
            g.drain()
            g.affine_select(out=ident_f, in_=ident_f, compare_op=op.not_equal,
                            fill=1.0, base=0, pattern=[[-1, 128]], channel_multiplier=1)
            i2 = g.affine_select(out=ident_b, in_=ident_b, compare_op=op.not_equal,
                                 fill=1.0, base=0, pattern=[[-1, 128]], channel_multiplier=1)
            b.inc(i2, "g")
            tok["consts"] = b.cnt["g"]
        b.step(gpsimd=g)

    def ph_ln_stats(tag, gate=None):
        def v(v):
            if gate is None:
                b.wait(v, "sx", tok["x_loaded"])
            else:
                b.wait(v, gate[0], gate[1])
            for t in range(NT):
                xt = x_res[:, t, :]
                v.bn_stats(bn6[:, 0:1, :], xt[:, 0:512].rearrange("p (a c) -> p a c", a=1))
                v.bn_stats(bn6[:, 1:2, :], xt[:, 512:1024].rearrange("p (a c) -> p a c", a=1))
                v.drain()
                v.bn_aggr(mv_all[:, t, :], bn6.rearrange("p a b -> p (a b)"))
                v.drain()
            ii = v.tensor_scalar(rstd_all, mv_all[:, :, 1], 1e-5, None, op.add)
            tok[f"vareps{tag}"] = b.inc(ii, "v")

        def s(s):
            b.wait(s, "v", tok[f"vareps{tag}"])
            ii = s.activation(rstd_all, rstd_all, AF.Sqrt)
            tok[f"sqrt{tag}"] = b.inc(ii, "s")
        b.step(vector=v, scalar=s)

        def v2(v):
            b.wait(v, "s", tok[f"sqrt{tag}"])
            ii = v.reciprocal(rstd_all, rstd_all)
            tok[f"rstd{tag}"] = b.inc(ii, "v")
        b.step(vector=v2)

    def ph_l0a():
        def y(y):
            for i in range(4):
                dma = y.dma_start(out=x_res[:, 4 * i:4 * i + 4, :],
                                  in_=x_in.rearrange("(n p) d -> p n d", p=128)[:, 4 * i:4 * i + 4, :])
                b.inc(dma, "sx", 16)
            tok["x_loaded"] = b.cnt["sx"]
            dma = y.dma_start(out=wr_f, in_=Wr.rearrange("(c p) e -> p c e", p=128))
            b.inc(dma, "sw0", 16)
            dma = y.dma_start(out=rtemp[0:1, :], in_=temp[None, :])
            b.inc(dma, "sw0", 16)
            tok["wr_loaded"] = b.cnt["sw0"]
        b.step(sync=y)

        ph_ln_stats("0")

        for t in range(NT):
            scr = xn_scr[:, t % 2, :]
            p0, p1 = banks[t % 2], banks[2 + t % 2]

            def v(v, t=t, scr=scr):
                b.wait(v, "v", tok["rstd0"])
                if t >= 2:
                    b.wait(v, "t", tok[f"tr{t - 2}"])
                v.drain()
                ii = v.tensor_scalar(scr, x_res[:, t, :], mv_all[:, t, 0:1],
                                     rstd_all[:, t:t + 1], op.subtract, op.mult)
                tok[f"ln{t}"] = b.inc(ii, "v")

            def tt(tt, t=t, scr=scr, p0=p0, p1=p1):
                b.wait(tt, "g", tok["consts"])
                b.wait(tt, "v", tok[f"ln{t}"])
                if t >= 2:
                    b.wait(tt, "v", tok[f"ev{t - 2}"])
                for c in range(8):
                    half = p0 if c < 4 else p1
                    tt.transpose(half[:, (c % 4) * 128:(c % 4) * 128 + 128],
                                 scr[:, c * 128:c * 128 + 128], ident_f)
                tt.matmul(banks[4][0:1, 0:512], ones_col, scr[:, 0:512],
                          start=(t == 0), stop=(t == NT - 1))
                ii = tt.matmul(banks[5][0:1, 0:512], ones_col, scr[:, 512:1024],
                               start=(t == 0), stop=(t == NT - 1))
                tok[f"tr{t}"] = b.inc(ii, "t")

            def v2(v, t=t, p0=p0, p1=p1):
                b.wait(v, "t", tok[f"tr{t}"])
                ii = None
                for h, half in ((0, p0), (1, p1)):
                    hi = xnT_hi[:, 4 * h:4 * h + 4, t * 128:(t + 1) * 128]
                    lo = xnT_lo[:, 4 * h:4 * h + 4, t * 128:(t + 1) * 128]
                    h3 = half.rearrange("p (c x) -> p c x", c=4)
                    v.tensor_copy(hi, h3)
                    v.drain()
                    ii = v.scalar_tensor_tensor(lo, hi, -1.0, h3, op.mult, op.add)
                tok[f"ev{t}"] = b.inc(ii, "v")

            b.step(vector=v)
            b.step(tensor=tt, vector=v2)

    def ph_l0b():
        def v(v):
            b.wait(v, "sw0", tok["wr_loaded"])
            v.tensor_copy(wr_hi.rearrange("p c e -> p (c e)"), wr_f.rearrange("p c e -> p (c e)"))
            v.drain()
            v.scalar_tensor_tensor(wr_lo.rearrange("p c e -> p (c e)"),
                                   wr_hi.rearrange("p c e -> p (c e)"), -1.0,
                                   wr_f.rearrange("p c e -> p (c e)"), op.mult, op.add)
            v.drain()
            ii = v.reciprocal(rtemp[0:1, :], rtemp[0:1, :])
            tok["wrhl"] = b.inc(ii, "v")
        b.step(vector=v)

        for t in range(NT):
            pr = banks[6 + t % 2][:, 0:NEXP]

            def tt(tt, t=t, pr=pr):
                b.wait(tt, "v", max(tok["wrhl"], tok[f"ev{t}"]))
                if t >= 2:
                    b.wait(tt, "v", tok[f"cw{t - 2}"])
                ii = None
                for c in range(NC):
                    lhs_hi = xnT_hi[:, c, t * 128:(t + 1) * 128]
                    lhs_lo = xnT_lo[:, c, t * 128:(t + 1) * 128]
                    tt.matmul(pr, lhs_hi, wr_hi[:, c, :], start=(c == 0), stop=False)
                    tt.matmul(pr, lhs_hi, wr_lo[:, c, :], start=False, stop=False)
                    ii = tt.matmul(pr, lhs_lo, wr_hi[:, c, :], start=False,
                                   stop=(c == NC - 1))
                tok[f"rt{t}"] = b.inc(ii, "t")

            def v1(v, t=t, pr=pr):
                b.wait(v, "t", tok[f"rt{t}"])
                v.drain()
                ii = v.tensor_reduce(ttr_scr[:, 0:1], pr, AXX, op.max, negate=True)
                tok[f"nmax{t}"] = b.inc(ii, "v")

            def s1(s, t=t, pr=pr):
                b.wait(s, "v", tok[f"nmax{t}"])
                ii = s.activation(ttr_scr[:, 4:4 + NEXP], pr, AF.Exp,
                                  bias=ttr_scr[:, 0:1], scale=1.0,
                                  accum_out=ttr_scr[:, 1:2])
                tok[f"exp{t}"] = b.inc(ii, "s")

            def v2(v, t=t):
                b.wait(v, "s", tok[f"exp{t}"])
                pe = ttr_scr[:, 4:4 + NEXP]
                v.tensor_reduce(ttr_scr[:, 2:3], pe, AXX, op.max)
                v.drain()
                v.tensor_scalar(ttr_scr[:, 32:32 + NEXP], pe, ttr_scr[:, 2:3], None, op.is_equal)
                v.drain()
                v.scalar_tensor_tensor(ttr_scr[:, 64:64 + NEXP], ttr_scr[:, 32:32 + NEXP],
                                       -1e30, pe, op.mult, op.add)
                v.drain()
                v.tensor_reduce(ttr_scr[:, 3:4], ttr_scr[:, 64:64 + NEXP], AXX, op.max)
                v.drain()
                v.tensor_scalar(ttr_scr[:, 32:32 + NEXP], pe, ttr_scr[:, 3:4], None, op.is_ge)
                v.reciprocal(ttr_scr[:, 1:2], ttr_scr[:, 1:2])
                v.drain()
                ii = v.scalar_tensor_tensor(cw_all[:, t, :], pe, ttr_scr[:, 1:2],
                                            ttr_scr[:, 32:32 + NEXP], op.mult, op.mult)
                tok[f"cw{t}"] = b.inc(ii, "v")

            b.step(tensor=tt, vector=v1)
            b.step(scalar=s1)
            b.step(vector=v2)

        def v3(v):
            b.wait(v, "t", tok[f"tr{NT - 1}"])
            v.tensor_copy(row_scr[0:1, 0:512], banks[4][0:1, 0:512])
            ii = v.tensor_copy(row_scr[0:1, 512:1024], banks[5][0:1, 0:512])
            tok["vsrow"] = b.inc(ii, "v")

        def t3(tt):
            b.wait(tt, "v", tok["vsrow"])
            tt.matmul(banks[4], ones_row, row_scr[0:1, 0:512], start=True, stop=True)
            ii = tt.matmul(banks[5], ones_row, row_scr[0:1, 512:1024], start=True, stop=True)
            tok["vsrep"] = b.inc(ii, "t")

        def v4(v):
            b.wait(v, "t", tok["vsrep"])
            v.tensor_copy(bcast_a[:, 0:512], banks[4])
            ii = v.tensor_copy(bcast_a[:, 512:1024], banks[5])
            tok["vsb"] = b.inc(ii, "v")
        b.step(vector=v3)
        b.step(tensor=t3, vector=v4)

        def matvec(src_ap, res8, toks):
            for c in range(NC):
                def y(y, c=c):
                    b.wait(y, "v", toks[c - 2])
                    dma = y.dma_start(out=stream_a[c % 2],
                                      in_=src_ap[128 * c:128 * (c + 1), :])
                    b.inc(dma, f"swv{c % 2}", 16)
                    tok["_mv_dma"] = b.cnt[f"swv{c % 2}"]

                def v(v, c=c):
                    b.wait(v, f"swv{c % 2}", tok["_mv_dma"])
                    b.wait(v, "v", tok["_mv_src"])
                    v.drain()
                    v.tensor_tensor(ttr_scr, stream_a[c % 2], bcast_a, op.mult)
                    v.drain()
                    ii = v.tensor_reduce(res8[:, c:c + 1], ttr_scr, AXX, op.add)
                    toks[c] = b.inc(ii, "v")
                b.step(sync=y, vector=v)

        tok["_mv_src"] = tok["vsb"]
        wv_toks = {-2: 0, -1: 0}
        matvec(w_qkv[2 * D:3 * D, :], val8, wv_toks)

        def y5(y):
            b.wait(y, "v", wv_toks[NC - 1])
            for c in range(NC):
                ii = y.dma_start(out=row_scr[0:1, c * 128:(c + 1) * 128], in_=val8[:, c:c + 1])
                b.inc(ii, "sg2", 16)
            tok["valrow"] = b.cnt["sg2"]

        def t5(tt):
            b.wait(tt, "sg2", tok["valrow"])
            b.wait(tt, "v", wv_toks[NC - 1])
            tt.matmul(banks[4], ones_row, row_scr[0:1, 0:512], start=True, stop=True)
            ii = tt.matmul(banks[5], ones_row, row_scr[0:1, 512:1024], start=True, stop=True)
            tok["valrep"] = b.inc(ii, "t")

        def v5(v):
            b.wait(v, "t", tok["valrep"])
            v.drain()
            v.tensor_copy(bcast_a[:, 0:512], banks[4])
            ii = v.tensor_copy(bcast_a[:, 512:1024], banks[5])
            tok["valb"] = b.inc(ii, "v")
        b.step(sync=y5)
        b.step(tensor=t5, vector=v5)

        tok["_mv_src"] = tok["valb"]
        wo_toks = {-2: wv_toks[NC - 2], -1: wv_toks[NC - 1]}
        matvec(w_o0, a08, wo_toks)

        def y6(y):
            b.wait(y, "v", wo_toks[NC - 1])
            b.wait(y, "t", tok["valrep"])
            for c in range(NC):
                ii = y.dma_start(out=row_scr[0:1, c * 128:(c + 1) * 128], in_=a08[:, c:c + 1])
                b.inc(ii, "sg2", 16)
            tok["a0row"] = b.cnt["sg2"]

        def t6(tt):
            b.wait(tt, "sg2", tok["a0row"])
            b.wait(tt, "v", wo_toks[NC - 1])
            tt.matmul(banks[4], ones_row, row_scr[0:1, 0:512], start=True, stop=True)
            ii = tt.matmul(banks[5], ones_row, row_scr[0:1, 512:1024], start=True, stop=True)
            tok["a0rep"] = b.inc(ii, "t")

        def v6(v):
            b.wait(v, "t", tok["a0rep"])
            v.tensor_copy(attn0b[:, 0:512], banks[4])
            ii = v.tensor_copy(attn0b[:, 512:1024], banks[5])
            tok["a0b"] = b.inc(ii, "v")
        b.step(sync=y6)
        b.step(tensor=t6, vector=v6)

    def wslot_views(i):
        w = wslot[i]
        wg = w[:, 0:8 * FH].rearrange("p (c f) -> p c f", c=8)
        wu = w[:, 8 * FH:16 * FH].rearrange("p (c f) -> p c f", c=8)
        return wg, wu, wd_buf

    def gu_srcs(s):
        if s < NEXP:
            return Wg[s], Wu[s]
        hh = s - NEXP
        return Sg[:, hh * FH:(hh + 1) * FH], Su[:, hh * FH:(hh + 1) * FH]

    def wd_src(s):
        return Wd[s] if s < NEXP else Sd[(s - NEXP) * FH:(s - NEXP + 1) * FH, :]

    def emit_slot_load(g, s):
        slot = s % 2
        g_src, u_src = gu_srcs(s)
        wgv, wuv, _ = wslot_views(slot)
        i1 = g.dma_start(out=wgv, in_=g_src.rearrange("(c p) f -> p c f", p=128))
        b.inc(i1, f"wsl{slot}", 16)
        i2 = g.dma_start(out=wuv, in_=u_src.rearrange("(c p) f -> p c f", p=128))
        b.inc(i2, f"wsl{slot}", 16)
        tok[f"wload{s}"] = b.cnt[f"wsl{slot}"]

    def emit_wd_load(g, s):
        ii = g.dma_start(out=wd_buf, in_=wd_src(s).rearrange("(m p) f -> p m f", p=128))
        b.inc(ii, "wdl", 16)
        tok[f"wdload{s}"] = b.cnt["wdl"]

    def emit_wd_load2(g, s):
        ii = g.dma_start(out=wd_buf2, in_=wd_src(s).rearrange("(m p) f -> p m f", p=128))
        b.inc(ii, "wdl", 16)
        tok[f"wdload{s}"] = b.cnt["wdl"]

    def ph_l0c():
        def g0(g):
            emit_slot_load(g, 0)
            emit_slot_load(g, 1)
            emit_wd_load(g, 0)
        b.step(gpsimd=g0)

        def emit_sltc(s, tc, wgv, wuv, wdv, shared):
            tsl = slice(tc * TC, (tc + 1) * TC)
            t0, t1 = 2 * tc, 2 * tc + 1
            if True:
                if not shared:
                    def y(y, s=s, tc=tc, t0=t0, t1=t1):
                        b.wait(y, "v", max(tok[f"cw{t0}"], tok[f"cw{t1}"]))
                        if (s, tc) != (0, 0):
                            b.wait(y, "t", tok["cwrep"])
                        i1 = y.dma_start(out=cwrow[0:1, 0:128], in_=cw_all[:, t0, s:s + 1])
                        b.inc(i1, "sg2", 16)
                        i2 = y.dma_start(out=cwrow[0:1, 128:256], in_=cw_all[:, t1, s:s + 1])
                        b.inc(i2, "sg2", 16)
                        tok["cwdma"] = b.cnt["sg2"]

                    def tcw(tt, s=s, tc=tc):
                        b.wait(tt, "sg2", tok["cwdma"])
                        ii = tt.matmul(banks[2][:, 0:TC], ones_row, cwrow[0:1, :],
                                       start=True, stop=True)
                        tok["cwrep"] = b.inc(ii, "t")

                    def vcw(v, s=s, tc=tc):
                        b.wait(v, "t", tok["cwrep"])
                        v.drain()
                        ii = v.tensor_copy(cwb, banks[2][:, 0:TC])
                        tok["cwb"] = b.inc(ii, "v")
                    b.step(sync=y, tensor=tcw)
                    b.step(vector=vcw)

                def tgu(tt, s=s, tc=tc, tsl=tsl, wgv=wgv, wuv=wuv, shared=shared):
                    b.wait(tt, f"wsl{s % 2}", tok[f"wload{s}"])
                    if not shared:
                        b.wait(tt, "v", tok["cwb"])
                    elif s == NEXP and "xadd" in tok:
                        b.wait(tt, "v", tok["xadd"])
                    for m in range(4):
                        pg = banks[m // 2][:, (m % 2) * TC:(m % 2) * TC + TC]
                        pu = banks[2 + m // 2][:, (m % 2) * TC:(m % 2) * TC + TC]
                        for c in range(NC):
                            rhs = xnT_hi[:, c, tsl]
                            tt.matmul(pg, wgv[:, c, m * 128:(m + 1) * 128], rhs,
                                      start=(c == 0), stop=(c == NC - 1))
                            ii = tt.matmul(pu, wuv[:, c, m * 128:(m + 1) * 128], rhs,
                                           start=(c == 0), stop=(c == NC - 1))
                    tok["gu"] = b.inc(ii, "t")

                def ssg(ss, s=s):
                    b.wait(ss, "t", tok["gu"])
                    ii = None
                    for m2 in range(2):
                        ii = ss.activation(sg_scr[:, 2 * m2:2 * m2 + 2, :]
                                           .rearrange("p a c -> p (a c)"),
                                           banks[m2], AF.Silu)

                    tok["sg"] = b.inc(ii, "s")

                def vh(v, s=s, tc=tc, shared=shared):
                    b.wait(v, "t", tok["gu"])
                    b.wait(v, "s", tok["sg"])
                    v.drain()
                    ii = None
                    if shared:
                        for m2 in range(2):
                            ii = v.tensor_tensor(h_scr[:, 2 * m2:2 * m2 + 2, :],
                                                 sg_scr[:, 2 * m2:2 * m2 + 2, :],
                                                 banks[2 + m2].rearrange("p (a x) -> p a x", a=2),
                                                 op.mult)
                    else:
                        cb = cwb.rearrange("p x -> p () x").broadcast_to([128, 2, TC])
                        for m2 in range(2):
                            v.tensor_tensor(h_scr[:, 2 * m2:2 * m2 + 2, :],
                                            banks[2 + m2].rearrange("p (a x) -> p a x", a=2),
                                            cb, op.mult)
                        v.drain()
                        ii = v.tensor_tensor(h_scr.rearrange("p a c -> p (a c)"),
                                             sg_scr.rearrange("p a c -> p (a c)"),
                                             h_scr.rearrange("p a c -> p (a c)"), op.mult)
                    tok["h"] = b.inc(ii, "v")

                def tdn(tt, s=s, tc=tc, shared=shared, wdv=wdv):
                    b.wait(tt, "v", tok["h"])
                    b.wait(tt, "wdl", tok[f"wdload{s}"])
                    ii = None
                    for o in range(8):
                        pd = banks[4 + o // 2][:, (o % 2) * TC:(o % 2) * TC + TC]
                        for m in range(4):
                            ii = tt.matmul(pd, wdv[:, m, o * 128:(o + 1) * 128],
                                           h_scr[:, m, :], start=(m == 0), stop=(m == 3))
                    tok["dn"] = b.inc(ii, "t")
                    if tc == NTC - 1:
                        tok[f"mmdone{s}"] = b.cnt["t"]

                b.step(tensor=tgu, scalar=ssg)
                b.step(vector=vh)
                b.step(tensor=tdn)

                if not shared:
                    def vr(v, s=s, tc=tc, tsl=tsl):
                        b.wait(v, "t", tok["dn"])
                        v.drain()
                        ii = None
                        for b4 in range(4):
                            dst = racc[:, 2 * b4:2 * b4 + 2, tsl]
                            src = banks[4 + b4].rearrange("p (c x) -> p c x", c=2)
                            if s == 0:
                                ii = v.tensor_copy(dst, src)
                            else:
                                ii = v.tensor_tensor(dst, dst, src, op.add)
                            v.drain()
                        tok[f"racc{tc}"] = b.inc(ii, "v")
                    b.step(vector=vr)
                elif s == NEXP + 1:
                    for tb in range(2):
                        tglob = 2 * tc + tb

                        def vc(v, tc=tc, tb=tb, tglob=tglob):
                            b.wait(v, "t", tok["dn"])
                            if "combtr" in tok:
                                b.wait(v, "t", tok["combtr"])
                            v.drain()
                            ii = None
                            for b4 in range(4):
                                src = racc[:, 2 * b4:2 * b4 + 2,
                                           tglob * 128:(tglob + 1) * 128]
                                pk = banks[4 + b4].rearrange("p (c y x) -> p c y x", c=2, y=2)[:, :, tb, :]
                                ii = v.tensor_tensor(comb_scr[:, 2 * b4:2 * b4 + 2, :],
                                                     src, pk, op.add)
                            tok["comb"] = b.inc(ii, "v")
                        b.step(vector=vc)

                        def ttr(tt, tc=tc, tb=tb):
                            b.wait(tt, "v", tok["comb"])
                            if tc > 0 or tb > 0:
                                b.wait(tt, "v", tok["xadd"])
                            ii = None
                            for c in range(NC):
                                dstb = banks[2 + c // 4]
                                ii = tt.transpose(dstb[:, (c % 4) * 128:(c % 4) * 128 + 128],
                                                  comb_scr[:, c, :], ident_f)
                            tok["combtr"] = b.inc(ii, "t")
                        b.step(tensor=ttr)

                        def vx(v, tglob=tglob):
                            b.wait(v, "t", tok["combtr"])
                            v.drain()
                            xt = x_res[:, tglob, :]
                            v.tensor_tensor(xt[:, 0:512], xt[:, 0:512], banks[2], op.add)
                            ii = v.tensor_tensor(xt[:, 512:1024], xt[:, 512:1024],
                                                 banks[3], op.add)
                            tok["xadd"] = b.inc(ii, "v")
                        b.step(vector=vx)

        pend = [None]

        hp = [h_pair, h_pairB]

        def emit_dnA(s, pr):
            def tdn(tt, s=s, pr=pr):
                b.wait(tt, "v", tok[f"h{s}_{pr}"])
                b.wait(tt, "wdl", tok[f"wdload{s}"])
                b.wait(tt, "v", tok.get("vrB", 0))
                ii = None
                for tl in range(2):
                    for m in range(4):
                        lhs = hp[pr % 2][:, m, tl * 128:(tl + 1) * 128]
                        tt.matmul(banks[4 + 2 * tl], lhs, wd_buf[:, m, 0:512],
                                  start=(m == 0), stop=(m == 3))
                        ii = tt.matmul(banks[5 + 2 * tl], lhs, wd_buf[:, m, 512:1024],
                                       start=(m == 0), stop=(m == 3))
                tok["dnA"] = b.inc(ii, "t")

            def vr(v, s=s, pr=pr):
                b.wait(v, "t", tok["dnA"])
                v.drain()
                ii = None
                for tl in range(2):
                    tglob = 4 * pr + tl
                    dst = racc_tok[:, tglob, :]
                    for hb in range(2):
                        pk = banks[4 + 2 * tl + hb]
                        d2 = dst[:, hb * 512:(hb + 1) * 512]
                        if s == 0:
                            ii = v.tensor_copy(d2, pk)
                        else:
                            ii = v.tensor_tensor(d2, d2, pk, op.add)
                        v.drain()
                tok["vrA"] = b.inc(ii, "v")
            b.step(tensor=tdn)
            b.step(vector=vr)

        def emit_dnB(s, pr):
            def tdn(tt, s=s, pr=pr):
                b.wait(tt, "v", tok["vrA"])
                ii = None
                for tl in range(2, 4):
                    for m in range(4):
                        lhs = hp[pr % 2][:, m, tl * 128:(tl + 1) * 128]
                        tt.matmul(banks[4 + 2 * (tl - 2)], lhs, wd_buf[:, m, 0:512],
                                  start=(m == 0), stop=(m == 3))
                        ii = tt.matmul(banks[5 + 2 * (tl - 2)], lhs, wd_buf[:, m, 512:1024],
                                       start=(m == 0), stop=(m == 3))
                tok["dnB"] = b.inc(ii, "t")
                if pr == 3:
                    tok[f"mmdone{s}"] = b.cnt["t"]

            def vr(v, s=s, pr=pr):
                b.wait(v, "t", tok["dnB"])
                v.drain()
                ii = None
                for tl in range(2, 4):
                    tglob = 4 * pr + tl
                    dst = racc_tok[:, tglob, :]
                    for hb in range(2):
                        pk = banks[4 + 2 * (tl - 2) + hb]
                        d2 = dst[:, hb * 512:(hb + 1) * 512]
                        if s == 0:
                            ii = v.tensor_copy(d2, pk)
                        else:
                            ii = v.tensor_tensor(d2, d2, pk, op.add)
                        v.drain()
                tok["vrB"] = b.inc(ii, "v")
            b.step(tensor=tdn)
            b.step(vector=vr)

        for s in range(NEXP):
            wgv, wuv, _ = wslot_views(s % 2)

            if s == 0:
                def ycw0(y):
                    b.wait(y, "v", tok[f"cw{NT - 1}"])
                    for t in range(NT):
                        dma = y.dma_start(out=cwrow_e[0:1, t * 128:(t + 1) * 128],
                                          in_=cw_all[:, t, 0:1])
                        b.inc(dma, "sg2", 16)
                    tok["cwg"] = b.cnt["sg2"]
                b.step(sync=ycw0)

            def tcw(tt, s=s):
                b.wait(tt, "sg2", tok["cwg"])
                if s >= 1:
                    b.wait(tt, "v", tok[f"h{s - 1}_3"])
                ii = None
                for q4 in range(4):
                    ii = tt.matmul(banks[q4], ones_row, cwrow_e[0:1, 512 * q4:512 * (q4 + 1)],
                                   start=True, stop=True)
                tok["cwrep"] = b.inc(ii, "t")

            def vcw(v, s=s):
                b.wait(v, "t", tok["cwrep"])
                v.drain()
                ii = None
                for q4 in range(4):
                    ii = v.tensor_copy(cwb_e[:, 512 * q4:512 * (q4 + 1)], banks[q4])
                tok["cwb"] = b.inc(ii, "v")
            b.step(tensor=tcw)
            b.step(vector=vcw)

            def gpre(g, s=s):
                b.wait(g, "t", tok[f"mmdone{s - 1}"])
                emit_slot_load(g, s + 1)
                emit_wd_load(g, s)

            def ycw(y, s=s):
                b.wait(y, "t", tok["cwrep"])
                for t in range(NT):
                    dma = y.dma_start(out=cwrow_e[0:1, t * 128:(t + 1) * 128],
                                      in_=cw_all[:, t, s + 1:s + 2])
                    b.inc(dma, "sg2", 16)
                tok["cwg"] = b.cnt["sg2"]

            for pr in range(4):
                psl = slice(pr * 512, (pr + 1) * 512)

                def tg(tt, s=s, pr=pr, psl=psl, wgv=wgv):
                    b.wait(tt, f"wsl{s % 2}", tok[f"wload{s}"])
                    b.wait(tt, "v", tok["cwb"])
                    if (s, pr) != (0, 0):
                        prev = tok[f"h{s}_{pr - 1}"] if pr > 0 else tok[f"h{s - 1}_3"]
                        b.wait(tt, "v", prev)
                    ii = None
                    for m in range(4):
                        for c in range(NC):
                            ii = tt.matmul(banks[m], wgv[:, c, m * 128:(m + 1) * 128],
                                           xnT_hi[:, c, psl],
                                           start=(c == 0), stop=(c == NC - 1))
                    tok[f"g{s}_{pr}"] = b.inc(ii, "t")

                def ssg(ss, s=s, pr=pr):
                    b.wait(ss, "t", tok[f"g{s}_{pr}"])
                    ii = None
                    for m in range(4):
                        ii = ss.activation(sg_pair[:, m, :], banks[m], AF.Silu)
                    tok[f"sg{s}_{pr}"] = b.inc(ii, "s")
                b.step(tensor=tg, scalar=ssg)

                if pend[0] is not None:
                    emit_dnA(*pend[0])

                def tu(tt, s=s, pr=pr, psl=psl, wuv=wuv):
                    b.wait(tt, "s", tok[f"sg{s}_{pr}"])
                    ii = None
                    for m in range(4):
                        for c in range(NC):
                            ii = tt.matmul(banks[m], wuv[:, c, m * 128:(m + 1) * 128],
                                           xnT_hi[:, c, psl],
                                           start=(c == 0), stop=(c == NC - 1))
                    tok[f"u{s}_{pr}"] = b.inc(ii, "t")

                def vh(v, s=s, pr=pr, psl=psl):
                    b.wait(v, "t", tok[f"u{s}_{pr}"])
                    v.drain()
                    hb = hp[pr % 2]
                    for m in range(4):
                        v.tensor_tensor(hb[:, m, :], banks[m], cwb_e[:, psl], op.mult)
                    v.drain()
                    ii = v.tensor_tensor(hb.rearrange("p a c -> p (a c)"),
                                         sg_pair.rearrange("p a c -> p (a c)"),
                                         hb.rearrange("p a c -> p (a c)"), op.mult)
                    tok[f"h{s}_{pr}"] = b.inc(ii, "v")
                b.step(tensor=tu)
                b.step(vector=vh)
                if pend[0] is not None:
                    emit_dnB(*pend[0])
                pend[0] = (s, pr)
                if pr == 0 and s >= 1:
                    b.step(gpsimd=gpre)
                if pr == 0 and s + 1 < NEXP:
                    b.step(sync=ycw)
        emit_dnA(*pend[0])
        emit_dnB(*pend[0])
        pend[0] = None

        def gsh(g):
            b.wait(g, "t", tok[f"mmdone{NEXP - 1}"])
            emit_slot_load(g, NEXP + 1)
            emit_wd_load(g, NEXP)
            emit_wd_load2(g, NEXP + 1)
        b.step(gpsimd=gsh)

        h16 = h_pair

        def h17_lhs(m, tl):
            return h17_pair[:, m, tl * 128:(tl + 1) * 128]

        for pr in range(4):
            psl = slice(pr * 512, (pr + 1) * 512)
            for s_sh in (NEXP, NEXP + 1):
                wgv, wuv, _ = wslot_views(s_sh % 2)

                def tg(tt, s=s_sh, pr=pr, psl=psl, wgv=wgv):
                    b.wait(tt, f"wsl{s % 2}", tok[f"wload{s}"])
                    if s == NEXP:
                        if pr == 0:
                            b.wait(tt, "v", tok[f"h{NEXP - 1}_3"])
                        else:
                            b.wait(tt, "v", tok[f"hsh{NEXP + 1}_{pr - 1}"])
                    else:
                        b.wait(tt, "v", tok[f"hsh{NEXP}_{pr}"])
                    ii = None
                    for m in range(4):
                        for c in range(NC):
                            ii = tt.matmul(banks[m], wgv[:, c, m * 128:(m + 1) * 128],
                                           xnT_hi[:, c, psl],
                                           start=(c == 0), stop=(c == NC - 1))
                    tok["gsh"] = b.inc(ii, "t")

                def ssg(ss, s=s_sh, pr=pr):
                    b.wait(ss, "t", tok["gsh"])
                    ii = None
                    for m in range(4):
                        ii = ss.activation(sg_pair[:, m, :], banks[m], AF.Silu)
                    tok["sgsh"] = b.inc(ii, "s")

                def tu(tt, s=s_sh, pr=pr, psl=psl, wuv=wuv):
                    b.wait(tt, "s", tok["sgsh"])
                    ii = None
                    for m in range(4):
                        for c in range(NC):
                            ii = tt.matmul(banks[m], wuv[:, c, m * 128:(m + 1) * 128],
                                           xnT_hi[:, c, psl],
                                           start=(c == 0), stop=(c == NC - 1))
                    tok["ush"] = b.inc(ii, "t")

                def vh(v, s=s_sh, pr=pr):
                    b.wait(v, "t", tok["ush"])
                    v.drain()
                    ii = None
                    hb = h16 if s == NEXP else h17_pair
                    for m in range(4):
                        ii = v.tensor_tensor(hb[:, m, :],
                                             sg_pair[:, m, :], banks[m], op.mult)
                    v.drain()
                    tok[f"hsh{s}_{pr}"] = b.inc(ii, "v")
                b.step(tensor=tg, scalar=ssg)
                b.step(tensor=tu)
                b.step(vector=vh)

            for half4 in range(2):
                def tdn(tt, pr=pr, half4=half4):
                    b.wait(tt, "v", tok[f"hsh{NEXP}_{pr}"])
                    b.wait(tt, "v", tok[f"hsh{NEXP + 1}_{pr}"])
                    b.wait(tt, "wdl", tok[f"wdload{NEXP + 1}"])
                    if (pr, half4) != (0, 0):
                        b.wait(tt, "v", tok["xcmb"])
                    else:
                        b.wait(tt, "v", tok["vrB"])
                    ii = None
                    for tl in range(2 * half4, 2 * half4 + 2):
                        bb0 = 4 + 2 * (tl - 2 * half4)
                        for m in range(4):
                            lhs16 = h16[:, m, tl * 128:(tl + 1) * 128]
                            tt.matmul(banks[bb0], lhs16, wd_buf[:, m, 0:512],
                                      start=(m == 0), stop=False)
                            tt.matmul(banks[bb0 + 1], lhs16, wd_buf[:, m, 512:1024],
                                      start=(m == 0), stop=False)
                            lhs17 = h17_lhs(m, tl)
                            tt.matmul(banks[bb0], lhs17, wd_buf2[:, m, 0:512],
                                      start=False, stop=(m == 3))
                            ii = tt.matmul(banks[bb0 + 1], lhs17, wd_buf2[:, m, 512:1024],
                                           start=False, stop=(m == 3))
                    tok["dnsh"] = b.inc(ii, "t")

                def vcmb(v, pr=pr, half4=half4):
                    b.wait(v, "t", tok["dnsh"])
                    b.wait(v, "v", tok["a0b"])
                    v.drain()
                    ii = None
                    for tl in range(2 * half4, 2 * half4 + 2):
                        tglob = 4 * pr + tl
                        bb0 = 4 + 2 * (tl - 2 * half4)
                        xt = x_res[:, tglob, :]
                        for hb in range(2):
                            sl = slice(hb * 512, (hb + 1) * 512)
                            v.tensor_tensor(xt[:, sl], xt[:, sl], banks[bb0 + hb], op.add)
                            v.drain()
                            v.tensor_tensor(xt[:, sl], xt[:, sl],
                                            racc_tok[:, tglob, sl], op.add)
                            v.drain()
                            ii = v.tensor_tensor(xt[:, sl], xt[:, sl], attn0b[:, sl], op.add)
                            v.drain()
                    tok["xcmb"] = b.inc(ii, "v")
                b.step(tensor=tdn)
                b.step(vector=vcmb)
        tok["mmdone17"] = b.cnt["t"]
        tok["x1"] = tok["xcmb"]

    ph_consts()
    ph_l0a()
    if upto.startswith("l1skip"):
        tok["x1"] = tok[f"ev{NT - 1}"]
    else:
        ph_l0b()
        ph_l0c()

    if upto == "l0":
        def yout(y):
            b.wait(y, "v", tok["x1"])
            for i in range(4):
                dma = y.dma_start(out=out.rearrange("(n p) d -> p n d", p=128)[:, 4 * i:4 * i + 4, :],
                                  in_=x_res[:, 4 * i:4 * i + 4, :])
                b.inc(dma, "so", 16)
            srcs = {"xnT_hi": xnT_hi, "cw": cw_all, "attn0b": attn0b, "racc": racc,
                    "comb": comb_scr}
            for name, ap_dbg in dbg_out.items():
                nm = name.replace("dbg_", "")
                dma = y.dma_start(out=ap_dbg, in_=srcs[nm[4:] if False else nm])
                b.inc(dma, "so", 16)
            y.wait_ge(b.sem("so"), b.cnt["so"])
        b.step(sync=yout)
        return nc, dbg_out

    def emit_quant_prep(eng, ename, x_ap, bm, rs6, nblk, full=None):
        x3 = x_ap.rearrange("p (n i) -> p n i", i=16)
        eng.tensor_reduce(bm, x3, AXX, op.max, apply_absolute_value=True)
        eng.drain()
        eng.tensor_scalar(rs6, bm, 6e-12, None, op.max)
        eng.drain()
        eng.reciprocal(rs6, rs6)
        eng.drain()
        eng.tensor_scalar(rs6, rs6, 6.0, None, op.mult)
        eng.drain()
        ii = eng.reciprocal(bm, rs6)
        eng.drain()
        if full is not None:
            rs6F, scF = full
            W = nblk * 16
            eng.tensor_scalar(rs6F[:, 0:W].rearrange("p (n i) -> p n i", i=16),
                              rs6.rearrange("p n -> p n ()").broadcast_to([128, nblk, 16]),
                              2.0, None, op.mult)
            eng.drain()
            ii = eng.tensor_copy(scF[:, 0:W].rearrange("p (n i) -> p n i", i=16),
                                 bm.rearrange("p n -> p n ()").broadcast_to([128, nblk, 16]))
            eng.drain()
        return b.inc(ii, ename)

    def emit_quant_bulk(eng, ename, x_ap, out_ap, qa, qb, bm, rs6, nblk, full=None):
        if full is None:
            rs6b = rs6.rearrange("p n -> p n ()").broadcast_to([128, nblk, 16])
            qa3 = qa.rearrange("p (n i) -> p n i", i=16)
            eng.tensor_scalar(qa, x_ap, -1.0, None, op.mult)
            eng.drain()
            eng.tensor_tensor(qa, qa, x_ap, op.max)
            eng.drain()
            eng.tensor_tensor(qa3, qa3, rs6b, op.mult)
            eng.drain()
            eng.tensor_scalar(qb, qa, 5.0, 2.0, op.is_ge, op.mult)
            eng.drain()
            eng.scalar_tensor_tensor(qb, qa, 2.5, qb, op.is_ge, op.add)
            eng.drain()
            eng.scalar_tensor_tensor(qb, qa, 3.5, qb, op.is_ge, op.add)
            eng.drain()
            eng.tensor_scalar(qb, qb, 2.0, None, op.mult)
            eng.drain()
            for t_ in (0.25, 0.75, 1.25, 1.75):
                eng.scalar_tensor_tensor(qb, qa, t_, qb, op.is_ge, op.add)
                eng.drain()
            eng.tensor_scalar(qa, x_ap, 0.0, 1.0, op.is_ge, op.subtract)
            eng.drain()
            eng.tensor_scalar(qa, qa, 2.0, 1.0, op.mult, op.add)
            eng.drain()
            scb = bm.rearrange("p n -> p n ()").broadcast_to([128, nblk, 16])
            eng.tensor_scalar(qa, qa, 0.5, None, op.mult)
            eng.drain()
            eng.tensor_tensor(qa3, qa3, scb, op.mult)
            eng.drain()
            ii = eng.tensor_tensor(out_ap, qb, qa, op.mult)
            eng.drain()
            return b.inc(ii, ename)
        rs6F, scF = full
        W = nblk * 16
        qc = rs6F
        eng.tensor_scalar(qb, x_ap, 0.0, 0.5, op.is_ge, op.subtract)
        eng.drain()
        eng.tensor_tensor(qa, x_ap, qb[:, 0:W], op.mult)
        eng.drain()
        eng.tensor_tensor(qa, qa, rs6F[:, 0:W], op.mult)
        eng.drain()
        eng.tensor_tensor(qb, qb, scF[:, 0:W], op.mult)
        eng.drain()
        eng.tensor_scalar(qc[:, 0:W], qa, 5.0, 2.0, op.is_ge, op.mult)
        eng.drain()
        for t_, w_ in ((2.5, 1.0), (3.5, 1.0)):
            eng.tensor_scalar(scF[:, 0:W], qa, t_, w_, op.is_ge, op.mult) \
                if False else None
        for t_ in (2.5, 3.5):
            eng.tensor_scalar(scF[:, 0:W], qa, t_, None, op.is_ge)
            eng.drain()
            eng.tensor_tensor(qc[:, 0:W], qc[:, 0:W], scF[:, 0:W], op.add)
            eng.drain()
        eng.tensor_scalar(qc[:, 0:W], qc[:, 0:W], 2.0, None, op.mult)
        eng.drain()
        for t_ in (0.25, 0.75, 1.25, 1.75):
            eng.tensor_scalar(scF[:, 0:W], qa, t_, None, op.is_ge)
            eng.drain()
            eng.tensor_tensor(qc[:, 0:W], qc[:, 0:W], scF[:, 0:W], op.add)
            eng.drain()
        ii = eng.tensor_tensor(out_ap, qc[:, 0:W], qb[:, 0:W], op.mult)
        eng.drain()
        return b.inc(ii, ename)

    def emit_quant(eng, ename, x_ap, out_ap, qa, qb, bm, rs6, nblk):
        F = nblk * 16
        x3 = x_ap.rearrange("p (n i) -> p n i", i=16)
        eng.tensor_reduce(bm, x3, AXX, op.max, apply_absolute_value=True)
        eng.drain()
        eng.tensor_scalar(rs6, bm, 6e-12, None, op.max)
        eng.drain()
        eng.reciprocal(rs6, rs6)
        eng.drain()
        eng.tensor_scalar(rs6, rs6, 6.0, None, op.mult)
        eng.drain()
        rs6b = rs6.rearrange("p n -> p n ()").broadcast_to([128, nblk, 16])
        qa3 = qa.rearrange("p (n i) -> p n i", i=16)
        eng.tensor_scalar(qa, x_ap, -1.0, None, op.mult)
        eng.drain()
        eng.tensor_tensor(qa, qa, x_ap, op.max)
        eng.drain()
        eng.tensor_tensor(qa3, qa3, rs6b, op.mult)
        eng.drain()
        eng.tensor_scalar(qb, qa, 5.0, 2.0, op.is_ge, op.mult)
        eng.drain()
        eng.scalar_tensor_tensor(qb, qa, 2.5, qb, op.is_ge, op.add)
        eng.drain()
        eng.scalar_tensor_tensor(qb, qa, 3.5, qb, op.is_ge, op.add)
        eng.drain()
        eng.tensor_scalar(qb, qb, 2.0, None, op.mult)
        eng.drain()
        for t_ in (0.25, 0.75, 1.25, 1.75):
            eng.scalar_tensor_tensor(qb, qa, t_, qb, op.is_ge, op.add)
            eng.drain()
        eng.reciprocal(rs6, rs6)
        eng.drain()
        eng.tensor_scalar(qa, x_ap, 0.0, 1.0, op.is_ge, op.subtract)
        eng.drain()
        eng.tensor_scalar(qa, qa, 2.0, 1.0, op.mult, op.add)
        eng.drain()
        scb = rs6.rearrange("p n -> p n ()").broadcast_to([128, nblk, 16])
        eng.tensor_scalar(qa, qa, 0.5, None, op.mult)
        eng.drain()
        eng.tensor_tensor(qa3, qa3, scb, op.mult)
        eng.drain()
        ii = eng.tensor_tensor(out_ap, qb, qa, op.mult)
        eng.drain()
        return b.inc(ii, ename)

    xn2T = xnT_hi
    qa_v = bcast_a
    qb_v = attn0b
    bm_v = l1scr[:, 0:64]
    rs6_v = l1scr[:, 64:128]

    def ph_l1a():
        ph_ln_stats("1", gate=("v", tok["x1"]))
        for t in range(l1a_tiles):
            scr = xn_scr[:, t % 2, :]
            p0, p1 = banks[t % 2], banks[2 + t % 2]

            def v(v, t=t, scr=scr):
                b.wait(v, "v", tok["rstd1"])
                if t >= 2:
                    b.wait(v, "t", tok[f"l1tr{t - 2}"])
                v.drain()
                ii = v.tensor_scalar(scr, x_res[:, t, :], mv_all[:, t, 0:1],
                                     rstd_all[:, t:t + 1], op.subtract, op.mult)
                tok[f"l1ln{t}"] = b.inc(ii, "v")

            def y(y, t=t):
                b.wait(y, "v", tok[f"l1ln{t}"])
                dma = y.dma_start(out=xspill.rearrange("(n p) d -> p n d", p=128)[:, t, :],
                                  in_=x_res[:, t, :])
                b.inc(dma, "sxp", 16)

            def tt(tt, t=t, scr=scr, p0=p0, p1=p1):
                b.wait(tt, "v", tok[f"l1ln{t}"])
                if t >= 2:
                    b.wait(tt, "v", tok[f"l1ev{t - 2}"])
                ii = None
                for c in range(8):
                    half = p0 if c < 4 else p1
                    ii = tt.transpose(half[:, (c % 4) * 128:(c % 4) * 128 + 128],
                                      scr[:, c * 128:c * 128 + 128], ident_f)
                tok[f"l1tr{t}"] = b.inc(ii, "t")

            def v2(v, t=t, p0=p0, p1=p1):
                b.wait(v, "t", tok[f"l1tr{t}"])
                ii = None
                for h, half in ((0, p0), (1, p1)):
                    hi = xn2T[:, 4 * h:4 * h + 4, t * 128:(t + 1) * 128]
                    ii = v.tensor_copy(hi, half.rearrange("p (c x) -> p c x", c=4))
                tok[f"l1ev{t}"] = b.inc(ii, "v")

            b.step(vector=v)
            if l1a_parts >= 2:
                b.step(sync=y, tensor=tt, vector=v2)
            else:
                b.step(tensor=tt, vector=v2)

            if l1a_parts < 3:
                continue
            gq = False

            def vq(v, t=t, scr=scr, gq=gq):
                b.wait(v, "v", tok[f"l1ln{t}"])
                if t >= 2:
                    b.wait(v, "t", tok[f"xqtr{t - 2}"])
                v.drain()
                bm_, rs_ = (bm_ga, rs6_ga) if gq else (bm_v, rs6_v)
                if gq and "gq_last" in tok:
                    b.wait(v, "g", tok["gq_last"])
                tok[f"qp{t}"] = emit_quant_prep(v, "v", scr, bm_, rs_, 64,
                                                full=None if not gq else (rs6F_a, scF_a))
                if not gq:
                    tok[f"q{t}"] = emit_quant_bulk(v, "v", scr, qb_v,
                                                   qa_v, qb_v, bm_v, rs6_v, 64)

            def gqf(g, t=t, scr=scr):
                b.wait(g, "v", tok[f"qp{t}"])
                if t >= 2:
                    b.wait(g, "t", tok[f"xqtr{t - 2}"])
                tok[f"q{t}"] = emit_quant_bulk(g, "g", scr, qbG_a,
                                               qaG_a, qbG_a, bm_ga, rs6_ga, 64,
                                               full=(rs6F_a, scF_a))
                tok["gq_last"] = tok[f"q{t}"]

            def tq(tt, t=t, gq=gq):
                b.wait(tt, "g" if gq else "v", tok[f"q{t}"])
                if t >= 2:
                    b.wait(tt, "v", tok[f"xqev{t - 2}"])
                pb = banks[4 + t % 2]
                src_q = qbG_a if gq else qb_v
                ii = None
                for c in range(8):
                    ii = tt.transpose(pb[:, (c % 4) * 128:(c % 4) * 128 + 128]
                                      if c < 4 else banks[6 + t % 2][:, (c % 4) * 128:(c % 4) * 128 + 128],
                                      src_q[:, c * 128:c * 128 + 128], ident_f)
                tok[f"xqtr{t}"] = b.inc(ii, "t")

            def vq2(v, t=t):
                b.wait(v, "t", tok[f"xqtr{t}"])
                b.wait(v, "sw0", b.cnt.get("sw0", 0))
                b.wait(v, "sg2", b.cnt.get("sg2", 0))
                if t >= 2:
                    b.wait(v, f"sq{t % 2}", tok[f"sqd{t - 2}"])
                v.drain()
                v.tensor_copy(xq_tileA[t % 2][:, 0:4, :],
                              banks[4 + t % 2].rearrange("p (c x) -> p c x", c=4))
                ii = v.tensor_copy(xq_tileA[t % 2][:, 4:8, :],
                                   banks[6 + t % 2].rearrange("p (c x) -> p c x", c=4))
                tok[f"xqev{t}"] = b.inc(ii, "v")

            def yq(y, t=t):
                b.wait(y, "v", tok[f"xqev{t}"])
                dma = y.dma_start(out=xqspill[:, :, t * 128:(t + 1) * 128],
                                  in_=xq_tileA[t % 2])
                tok[f"sqd{t}"] = b.inc(dma, f"sq{t % 2}", 16)
            b.step(vector=vq)
            if gq:
                b.step(gpsimd=gqf)
            if l1a_parts >= 4:
                b.step(tensor=tq, vector=vq2)
            if l1a_parts >= 5:
                b.step(sync=yq)
        tok["spilled"] = b.cnt.get("sxp", 0)
        tok["xq_spilled"] = (b.cnt.get("sq0", 0), b.cnt.get("sq1", 0))
        tok["l1a_v"] = b.cnt["v"]

    wqb = SBn("wqb", [128, NC, 256], BF16, at=ws0_off)
    wkb = SBn("wkb", [128, NC, 256], BF16, at=ws0_off + 4096)
    wvb = SBn("wvb", [128, NC, D], BF16, at=ws0_off + 8192)
    wob = SBn("wob", [128, NC, D], BF16, at=ws1_off + 8192)
    bo_f = mv_all.rearrange("p a c -> p (a c)")[:, 0:8]
    sq_s = SBn("sq_s", [128, 256], F32, at=wrh_off)
    qpad = attn0b[:, 0:512]
    kpad = attn0b[:, 512:1024]
    bqb = l1scr[:, 0:256]
    bkb = l1scr[:, 256:512]
    qpb_s = l1scr[:, 512:768]
    bvb = bcast_a

    def ph_l1b():
        def g(g):
            b.wait(g, "t", tok["mmdone17"] if "mmdone17" in tok else 0)
            for dst, src in ((wqb, Wq1), (wkb, Wk1)):
                dma = g.dma_start(out=dst, in_=src.rearrange("(c p) w -> p c w", p=128))
                b.inc(dma, "wl1", 16)
            for dst, src in ((wvb, Wv1), (wob, Wo1)):
                dma = g.dma_start(out=dst, in_=src.rearrange("(c p) w -> p c w", p=128))
                b.inc(dma, "wl1", 16)
            tok["wl1"] = b.cnt["wl1"]

        def y(y):
            b.wait(y, "v", tok["x1"])
            if "mmdone17" in tok:
                b.wait(y, "t", tok["mmdone17"])
            dma = y.dma_start(out=row_scr[0:1, 0:256], in_=bq1[None, :])
            b.inc(dma, "sb1", 16)
            dma = y.dma_start(out=row_scr[0:1, 256:512], in_=qph.rearrange("a h b w -> a (h b w)"))
            b.inc(dma, "sb1", 16)
            dma = y.dma_start(out=row_scr[0:1, 512:768], in_=bk1[None, :])
            b.inc(dma, "sb1", 16)
            dma = y.dma_start(out=row_scr[0:1, 768:1024], in_=kph.rearrange("a h b w -> a (h b w)"))
            b.inc(dma, "sb1", 16)
            dma = y.dma_start(out=bo_f, in_=bo1.rearrange("(c p) -> p c", p=128),
                              allow_slow_non_contiguous=True)
            b.inc(dma, "sb1", 16)
            tok["sb1"] = b.cnt["sb1"]

        def v(v):
            b.wait(v, "sb1", tok["sb1"])
            b.wait(v, "v", tok["l1a_v"])
            v.drain()
            v.tensor_tensor(row_scr[0:1, 0:256], row_scr[0:1, 0:256],
                            row_scr[0:1, 256:512], op.add)
            v.tensor_tensor(row_scr[0:1, 512:768], row_scr[0:1, 512:768],
                            row_scr[0:1, 768:1024], op.add)
            v.drain()
            ii = v.memset(qpad, 0.0)
            ii = v.memset(kpad, 0.0)
            tok["brow"] = b.inc(ii, "v")

        def t2(tt):
            b.wait(tt, "v", tok["brow"])
            tt.matmul(banks[0][:, 0:256], ones_row, row_scr[0:1, 0:256], start=True, stop=True)
            tt.matmul(banks[0][:, 256:512], ones_row, row_scr[0:1, 512:768], start=True, stop=True)
            ii = tt.matmul(banks[1][0:128, 0:1], ones_row, rtemp[0:1, 0:1], start=True, stop=True)
            tok["brep"] = b.inc(ii, "t")

        def v2(v):
            b.wait(v, "t", tok["brep"])
            v.drain()
            v.tensor_copy(bqb, banks[0][:, 0:256])
            v.tensor_copy(bkb, banks[0][:, 256:512])
            ii = v.tensor_copy(rtemp, banks[1][:, 0:1])
            tok["bqkb"] = b.inc(ii, "v")

        b.step(gpsimd=g, sync=y)
        b.step(vector=v)
        b.step(tensor=t2, vector=v2)

        def y3(y):
            b.wait(y, "v", tok["bqkb"])
            b.wait(y, "t", tok["brep"])
            dma = y.dma_start(out=row_scr[0:1, :], in_=bv1[None, :])
            b.inc(dma, "sb1", 16)
            tok["sbv"] = b.cnt["sb1"]

        def t3(tt):
            b.wait(tt, "sb1", tok["sbv"])
            tt.matmul(banks[0], ones_row, row_scr[0:1, 0:512], start=True, stop=True)
            ii = tt.matmul(banks[1], ones_row, row_scr[0:1, 512:1024], start=True, stop=True)
            tok["bvrep"] = b.inc(ii, "t")

        def v3(v):
            b.wait(v, "t", tok["bvrep"])
            b.wait(v, "v", tok["l1a_v"])
            b.wait(v, "sq0", tok["xq_spilled"][0])
            b.wait(v, "sq1", tok["xq_spilled"][1])
            v.drain()
            v.tensor_copy(bvb[:, 0:512], banks[0])
            ii = v.tensor_copy(bvb[:, 512:1024], banks[1])
            tok["bvb"] = b.inc(ii, "v")
        b.step(sync=y3)
        b.step(tensor=t3, vector=v3)

        for t in range(NT):
            def tp(tt, t=t):
                b.wait(tt, "wl1", tok["wl1"])
                b.wait(tt, "v", max(tok["bvb"], tok[f"l1ev{t}"]))
                if t >= 1:
                    b.wait(tt, "v", tok[f"vev{t - 1}"])
                    b.wait(tt, "v", tok[f"qkn{t - 1}"])
                ii = None
                for c in range(NC):
                    lhs = xn2T[:, c, t * 128:(t + 1) * 128]
                    tt.matmul(banks[0][:, 0:256], lhs, wqb[:, c, :],
                              start=(c == 0), stop=(c == NC - 1))
                    tt.matmul(banks[1], lhs, wvb[:, c, 0:512],
                              start=(c == 0), stop=(c == NC - 1))
                    tt.matmul(banks[2], lhs, wvb[:, c, 512:1024],
                              start=(c == 0), stop=(c == NC - 1))
                for c in range(NC):
                    lhs = xn2T[:, c, t * 128:(t + 1) * 128]
                    ii = tt.matmul(banks[0][:, 256:512], lhs, wkb[:, c, :],
                                   start=(c == 0), stop=(c == NC - 1))
                tok[f"proj{t}"] = b.inc(ii, "t")

            def vv(v, t=t):
                b.wait(v, "t", tok[f"proj{t}"])
                b.wait(v, "sxp", tok["spilled"])
                v.drain()
                v.scalar_tensor_tensor(Vbuf[:, t, 0:512], banks[1], 0.0,
                                       bvb[:, 0:512], op.bypass, op.add)
                ii = v.scalar_tensor_tensor(Vbuf[:, t, 512:1024], banks[2], 0.0,
                                            bvb[:, 512:1024], op.bypass, op.add)
                tok[f"vev{t}"] = b.inc(ii, "v")

            b.step(tensor=tp)
            b.step(vector=vv)

            for qk, bias_b, pslc in (("q", bqb, slice(0, 256)), ("k", bkb, slice(256, 512))):
                def vn(v, t=t, qk=qk, bias_b=bias_b, pslc=pslc):
                    b.wait(v, "t", tok[f"proj{t}"])
                    b.wait(v, "sq0", tok["xq_spilled"][0])
                    b.wait(v, "sq1", tok["xq_spilled"][1])
                    if t >= 1:
                        b.wait(v, "t", tok[f"qktr{t - 1}"])
                    v.drain()
                    v.tensor_tensor(qpb_s, banks[0][:, pslc], bias_b, op.add)
                    v.drain()
                    ii = v.tensor_tensor(sq_s, qpb_s, qpb_s, op.mult)
                    tok["_sq"] = b.inc(ii, "v")

                b.step(vector=vn)

                def vr2(v, t=t, qk=qk):
                    sm3 = sq_s.rearrange("p (h w) -> p h w", w=16)
                    v.drain()
                    v.tensor_reduce(a08[:, 0:8], sm3[:, 0:8, :], AXX, op.add)
                    v.tensor_reduce(val8[:, 0:8], sm3[:, 8:16, :], AXX, op.add)
                    v.drain()
                    v.tensor_scalar(a08, a08, 1e-24, None, op.max)
                    ii = v.tensor_scalar(val8, val8, 1e-24, None, op.max)
                    tok["_ssq"] = b.inc(ii, "v")

                def sq2(s_, t=t):
                    b.wait(s_, "v", tok["_ssq"])
                    s_.activation(a08, a08, AF.Sqrt)
                    ii = s_.activation(val8, val8, AF.Sqrt)
                    tok["_sqr"] = b.inc(ii, "s")

                def vr3(v, t=t, qk=qk):
                    b.wait(v, "s", tok["_sqr"])
                    v.drain()
                    v.reciprocal(a08, a08)
                    v.reciprocal(val8, val8)
                    v.drain()
                    dst = (qpad if qk == "q" else kpad)
                    d3 = dst.rearrange("p (h z) -> p h z", z=32)[:, :, 0:16]
                    q3 = qpb_s.rearrange("p (h w) -> p h w", w=16)
                    rn8a = a08.rearrange("p c -> p c ()").broadcast_to([128, 8, 16])
                    rn8b = val8.rearrange("p c -> p c ()").broadcast_to([128, 8, 16])
                    v.tensor_tensor(d3[:, 0:8, :], q3[:, 0:8, :], rn8a, op.mult)
                    ii = v.tensor_tensor(d3[:, 8:16, :], q3[:, 8:16, :], rn8b, op.mult)
                    tok[f"nrm{qk}{t}"] = b.inc(ii, "v")
                b.step(vector=vr2)
                b.step(scalar=sq2)
                b.step(vector=vr3)
            tok[f"qkn{t}"] = tok[f"nrmk{t}"]

            def tq2(tt, t=t):
                b.wait(tt, "v", tok[f"qkn{t}"])
                if t >= 1:
                    b.wait(tt, "v", tok[f"qkev{t - 1}"])
                for g4 in range(4):
                    tt.transpose(banks[3][:, g4 * 128:(g4 + 1) * 128],
                                 qpad[:, g4 * 128:(g4 + 1) * 128], ident_f)
                ii = None
                for g4 in range(4):
                    ii = tt.transpose(banks[4][:, g4 * 128:(g4 + 1) * 128],
                                      kpad[:, g4 * 128:(g4 + 1) * 128], ident_f)
                tok[f"qktr{t}"] = b.inc(ii, "t")

            def vq3(v, t=t):
                b.wait(v, "t", tok[f"qktr{t}"])
                v.drain()
                v.tensor_copy(qTpad[:, :, t * 128:(t + 1) * 128],
                              banks[3].rearrange("p (g x) -> p g x", g=4))
                ii = v.tensor_copy(kTpad[:, :, t * 128:(t + 1) * 128],
                                   banks[4].rearrange("p (g x) -> p g x", g=4))
                tok[f"qkev{t}"] = b.inc(ii, "v")
            b.step(tensor=tq2)
            b.step(vector=vq3)
        tok["qkdone"] = tok[f"qkev{NT - 1}"]
        tok["vdone"] = tok[f"vev{NT - 1}"]

        def vz(v):
            ii = None
            for var, idx in ((1, 6), (1, 7), (2, 8), (2, 9), (3, 10), (3, 11)):
                ii = v.memset(P_pool[idx][:, 0:128 * var], 0.0)
            tok["pz"] = b.inc(ii, "v")
        b.step(vector=vz)

        plain_i = [0, 3]
        var_i = {1: 0, 2: 0, 3: 0}
        p_tok = {}
        pend_tail = [None]

        for c in range(4):
            for hg in range(HH // 2):
                lane_em = []
                njt = 4 * c + 4
                for lane in range(2):
                    h = 2 * hg + lane
                    a = 32 * (h % 4)
                    g4 = h // 4
                    pvb = banks[4 + h % 2]
                    dnb = banks[6 + h % 2][0:1, 0:512]
                    emitters = []
                    for j in range(njt):
                        diag = j >= 4 * c
                        ov = j - 4 * c if diag else 0
                        if diag and ov > 0:
                            pi = 6 + (ov - 1) * 2 + lane
                        else:
                            pi = plain_i[lane]
                            plain_i[lane] = 3 * lane + (plain_i[lane] + 1) % 3
                        P = P_pool[pi]
                        bi = 2 * lane + j % 2
                        sb_ = banks[bi]

                        def ts_(tt, c=c, h=h, j=j, a=a, g4=g4, sb_=sb_, pi=pi, bi=bi):
                            b.wait(tt, "v", max(tok["qkdone"], tok["vdone"], tok["pz"]))
                            if c > 0 and h <= 1 and j == 0:
                                b.wait(tt, "v", tok[f"oev{c - 1}_7"])
                            b.wait(tt, "s", p_tok.get(("exp_prev", bi), 0))
                            ii = tt.matmul(sb_, kTpad[a:a + 16, g4, j * 128:(j + 1) * 128],
                                           qTpad[a:a + 16, g4, c * 512:(c + 1) * 512],
                                           start=True, stop=True, tile_position=(a, 0))
                            p_tok[("qk", c, h, j)] = b.inc(ii, "t")

                        def se_(s_, c=c, h=h, j=j, sb_=sb_, P=P, ov=ov, diag=diag,
                                pi=pi, bi=bi):
                            b.wait(s_, "t", p_tok[("qk", c, h, j)])
                            b.wait(s_, "t", p_tok.get(("pv_prevuse", pi), 0))
                            lo = 128 * ov if diag else 0
                            ii = s_.activation(P[:, lo:512], sb_[:, lo:512], AF.Exp,
                                               bias=neg1, scale=rtemp)
                            p_tok[("exp", c, h, j)] = b.inc(ii, "s")
                            p_tok[("exp_prev", bi)] = b.cnt["s"]

                        def gm_(g_, c=c, h=h, j=j, P=P, ov=ov):
                            b.wait(g_, "s", p_tok[("exp", c, h, j)])
                            ii = g_.affine_select(out=P[:, 128 * ov:128 * ov + 128],
                                                  in_=P[:, 128 * ov:128 * ov + 128],
                                                  compare_op=op.is_ge, fill=0.0, base=0,
                                                  pattern=[[1, 128]], channel_multiplier=-1)
                            p_tok[("msk", c, h, j)] = b.inc(ii, "g")

                        def tpv_(tt, c=c, h=h, j=j, P=P, pvb=pvb, dnb=dnb, diag=diag, pi=pi):
                            if diag:
                                b.wait(tt, "g", p_tok[("msk", c, h, j)])
                            else:
                                b.wait(tt, "s", p_tok[("exp", c, h, j)])
                            if j == 0:
                                hp = h - 2 if h >= 2 else (HH - 2 + h if c > 0 else None)
                                cp = c if h >= 2 else c - 1
                                if hp is not None:
                                    b.wait(tt, "v", tok[f"at{cp}_{hp}"])
                            tt.matmul(pvb[0:64, :], Vbuf[:, j, h * 64:(h + 1) * 64], P,
                                      start=(j == 0), stop=(j == 4 * c + 3))
                            ii = tt.matmul(dnb, ones_colb, P,
                                           start=(j == 0), stop=(j == 4 * c + 3))
                            p_tok[("pv", c, h, j)] = b.inc(ii, "t")
                            p_tok[("pv_prevuse", pi)] = b.cnt["t"]
                        emitters.append((ts_, se_, gm_ if diag else None, tpv_))
                    lane_em.append(emitters)

                b.step(tensor=lane_em[0][0][0])
                b.step(tensor=lane_em[1][0][0])
                if njt > 1:
                    b.step(tensor=lane_em[0][1][0])
                    b.step(tensor=lane_em[1][1][0])
                if pend_tail[0] is not None:
                    for st_ in pend_tail[0]:
                        st_()
                    pend_tail[0] = None
                for j in range(njt):
                    for lane in range(2):
                        em = lane_em[lane]
                        ts_n = em[j + 2][0] if j + 2 < njt else None
                        se_j, gm_j, tpv_j = em[j][1], em[j][2], em[j][3]
                        b.step(scalar=se_j)
                        if gm_j is not None:
                            b.step(gpsimd=gm_j)

                        def tcomb(tt, tpv_j=tpv_j, ts_n=ts_n):
                            tpv_j(tt)
                            if ts_n is not None:
                                ts_n(tt)
                        b.step(tensor=tcomb)

                tails = []
                for lane in range(2):
                    h = 2 * hg + lane
                    pvb = banks[4 + h % 2]
                    dnb = banks[6 + h % 2][0:1, 0:512]
                    rbb = banks[6 + h % 2]

                    def vd_(v, c=c, h=h, dnb=dnb):
                        b.wait(v, "t", p_tok[("pv", c, h, 4 * c + 3)])
                        if (c, h) != (0, 0):
                            b.wait(v, "t", tok.get("rbrep0", 0))
                        v.drain()
                        ii = v.reciprocal(row_scr[0:1, 0:512], dnb)
                        tok["_rd"] = b.inc(ii, "v")

                    def td_(tt, c=c, h=h, rbb=rbb):
                        b.wait(tt, "v", tok["_rd"])
                        ii = tt.matmul(rbb[64:128, :], ones_row[0:1, 0:64],
                                       row_scr[0:1, 0:512], start=True, stop=True,
                                       tile_position=(0, 64))
                        tok["rbrep0"] = b.inc(ii, "t")

                    def va_(v, c=c, h=h, pvb=pvb, rbb=rbb):
                        b.wait(v, "t", tok["rbrep0"])
                        v.drain()
                        v.tensor_copy(l1scr[0:64, 0:512], rbb[64:128, :])
                        v.drain()
                        ii = v.scalar_tensor_tensor(
                            attnT[64 * (h % 2):64 * (h % 2) + 64, h // 2, :],
                            pvb[0:64, :], 0.0, l1scr[0:64, 0:512], op.bypass, op.mult)
                        tok[f"at{c}_{h}"] = b.inc(ii, "v")
                    tails.extend((lambda vd_=vd_: b.step(vector=vd_),
                                  lambda td_=td_: b.step(tensor=td_),
                                  lambda va_=va_: b.step(vector=va_)))
                pend_tail[0] = tuple(tails)

            if pend_tail[0] is not None:
                for st_ in pend_tail[0]:
                    st_()
                pend_tail[0] = None
            for o in range(8):
                ob = banks[o % 2]

                def to_(tt, c=c, o=o, ob=ob):
                    b.wait(tt, "v", tok[f"at{c}_{HH - 1}"])
                    if o >= 2:
                        b.wait(tt, "v", tok[f"oev{c}_{o - 2}"])
                    ii = None
                    for cin in range(NC):
                        ii = tt.matmul(ob, wob[:, cin, o * 128:(o + 1) * 128], attnT[:, cin, :],
                                       start=(cin == 0), stop=(cin == NC - 1))
                    tok["_opj"] = b.inc(ii, "t")

                def vo_(v, c=c, o=o, ob=ob):
                    b.wait(v, "t", tok["_opj"])
                    v.drain()
                    ii = v.tensor_scalar(bvb[:, 0:512], ob, bo_f[:, o:o + 1], None, op.add)
                    tok["_oev"] = b.inc(ii, "v")

                def to2_(tt, c=c, o=o):
                    b.wait(tt, "v", tok["_oev"])
                    if o >= 1:
                        b.wait(tt, "v", tok[f"oev{c}_{o - 1}"])
                    ii = None
                    for tb in range(4):
                        ii = tt.transpose(banks[2][:, tb * 128:(tb + 1) * 128],
                                          bvb[:, 0:512][:, tb * 128:(tb + 1) * 128], ident_f)
                    tok["_otr"] = b.inc(ii, "t")

                def vo2_(v, c=c, o=o):
                    b.wait(v, "t", tok["_otr"])
                    v.drain()
                    ii = v.tensor_copy(attn1_tok[:, 4 * c:4 * c + 4, o * 128:(o + 1) * 128],
                                       banks[2].rearrange("p (z x) -> p z x", z=4))
                    tok[f"oev{c}_{o}"] = b.inc(ii, "v")
                b.step(tensor=to_)
                b.step(vector=vo_)
                b.step(tensor=to2_)
                b.step(vector=vo2_)
        tok["attn_done"] = tok[f"oev3_7"]
        tok["wq_gate_t"] = b.cnt["t"]

    qaG = SBn("qaG", [128, 1024], F32, at=row_off)
    qbG = SBn("qbG", [128, 1024], F32, at=xns_off + 4096)
    qa_c = SBn("qa_c", [128, 1024], F32, at=sg_off)
    qb_c = SBn("qb_c", [128, 1024], F32, at=xns_off)
    bm_c = SBn("bm_c", [128, 64], F32, at=cw_off)
    rs6_c = SBn("rs6_c", [128, 64], F32, at=cw_off + 256)
    bm_g = SBn("bm_g", [128, 64], F32, at=cw_off + 512)
    rs6_g = SBn("rs6_g", [128, 64], F32, at=cw_off + 768)
    bm_d = sq_s[:, 0:32]
    rs6_d = sq_s[:, 32:64]
    qaf = SBn("qaf", [128, 512], F32, at=sg_off + 4096)
    qbf = SBn("qbf", [128, 512], F32, at=cw_off)

    def ph_l1c():
        pieces = []
        for snum in range(22):
            pieces.append(("g", snum, 1024))
        for snum in range(22):
            pieces.append(("u", snum, 1024))
        for snum in range(8):
            for qq in range(3):
                pieces.append(("d", (snum, qq), 1024 if qq < 2 else 768))

        pv_tok = {}
        for idx, (kind, meta, W) in enumerate(pieces):
            par = idx % 2
            eng_v = True
            nblk = W // 128

            def yl(y, idx=idx, kind=kind, meta=meta, par=par, W=W):
                b.wait(y, "v", pv_tok.get(("cons", par), 0))
                b.wait(y, "g", pv_tok.get(("consG", par), 0))
                if idx < 2:
                    b.wait(y, "v", tok["attn_done"])
                    b.wait(y, "t", tok["wq_gate_t"])
                    b.wait(y, "sq0", tok["xq_spilled"][0])
                    b.wait(y, "sq1", tok["xq_spilled"][1])
                if kind == "d":
                    snum, qq = meta
                    dma = y.dma_start(out=wstr[par][:, 0:W],
                                      in_=wd1[128 * snum:128 * (snum + 1),
                                              qq * 1024:qq * 1024 + W])
                elif kind == "g":
                    dma = y.dma_start(out=wstr[par][:, 0:1024],
                                      in_=wg1[128 * meta:128 * (meta + 1), :])
                else:
                    dma = y.dma_start(out=wstr[par][:, 0:1024],
                                      in_=wu1[128 * meta:128 * (meta + 1), :])
                b.inc(dma, f"wst{par}", 16)
                pv_tok[("ld", idx)] = b.cnt[f"wst{par}"]

            def qprep(v, idx=idx, par=par, W=W, eng_v=eng_v):
                b.wait(v, f"wst{par}", pv_tok[("ld", idx)])
                if idx < 4:
                    b.wait(v, "v", tok["attn_done"])
                if not eng_v:
                    b.wait(v, "g", pv_tok.get("gbulk", 0))
                bm_, rs_ = (bm_c, rs6_c) if eng_v else (bm_g, rs6_g)
                pv_tok[("prep", idx)] = emit_quant_prep(
                    v, "v", wstr[par][:, 0:W], bm_[:, 0:W // 16],
                    rs_[:, 0:W // 16], W // 16,
                    full=None if eng_v else (rs6F_c, scF_c))
                pv_tok[("cons", par)] = pv_tok[("prep", idx)]

            def qbulk(eng, idx=idx, par=par, W=W, eng_v=eng_v):
                ename = "v" if eng_v else "g"
                if not eng_v:
                    b.wait(eng, "v", pv_tok[("prep", idx)])
                b.wait(eng, "t", pv_tok.get("wqfree_v" if eng_v else "wqfree_g", 0))
                qa_, qb_, bm_, rs_ = (qa_c, qb_c, bm_c, rs6_c) if eng_v else \
                    (qaG, qbG, bm_g, rs6_g)
                iitok = emit_quant_bulk(eng, ename, wstr[par][:, 0:W], qb_[:, 0:W],
                                        qa_[:, 0:W], qb_[:, 0:W],
                                        bm_[:, 0:W // 16], rs_[:, 0:W // 16], W // 16,
                                        full=None if eng_v else (rs6F_c, scF_c))
                pv_tok[("q", idx)] = (ename, iitok)
                if eng_v:
                    pv_tok[("cons", par)] = iitok
                else:
                    pv_tok[("consG", par)] = iitok
                    pv_tok["gbulk"] = iitok

            b.step(sync=yl)
            b.step(vector=qprep)
            if eng_v:
                b.step(vector=qbulk)
            else:
                b.step(gpsimd=qbulk)

            wq_ = qb_c if eng_v else qbG
            nbat = (nblk + 3) // 4
            for bb in range(nbat):
                blks = list(range(4 * bb, min(4 * bb + 4, nblk)))
                pb = banks[(idx * nbat + bb) % 4]

                def tt_(tt, idx=idx, bb=bb, blks=blks, pb=pb, wq_=wq_, eng_v=eng_v):
                    en, it = pv_tok[("q", idx)]
                    b.wait(tt, en, it)
                    b.wait(tt, "v", pv_tok.get(("bank_free", pb.tensor.name), 0))
                    ii = None
                    for zi, cb in enumerate(blks):
                        ii = tt.transpose(pb[:, zi * 128:zi * 128 + 128],
                                          wq_[:, cb * 128:(cb + 1) * 128], ident_f)
                    pv_tok[("tr", idx, bb)] = b.inc(ii, "t")

                def vv_(v, idx=idx, bb=bb, blks=blks, pb=pb, kind=kind, meta=meta,
                        nbat=nbat):
                    b.wait(v, "t", pv_tok[("tr", idx, bb)])
                    v.drain()
                    ii = None
                    for zi, cb in enumerate(blks):
                        srcb = pb[:, zi * 128:zi * 128 + 128]
                        if kind == "g":
                            dst = wgqT[:, cb, meta * 128:(meta + 1) * 128]
                        elif kind == "u":
                            dst = wuqT[:, cb, meta * 128:(meta + 1) * 128]
                        else:
                            snum, qq = meta
                            dst = wdqT[:, qq * 8 + cb, snum * 128:(snum + 1) * 128]
                        ii = v.tensor_copy(dst, srcb)
                    pv_tok[("bank_free", pb.tensor.name)] = b.inc(ii, "v")
                b.step(tensor=tt_)
                b.step(vector=vv_)
            pv_tok["wqfree_v" if eng_v else "wqfree_g"] = pv_tok[("tr", idx, nbat - 1)]

        tok["wq_done"] = b.cnt["v"]
        tok["wq_done_t"] = b.cnt["t"]

    def ph_l1d():
        fcw = [(0, 512), (512, 512), (1024, 512), (1536, 512), (2048, 512), (2560, 256)]
        for t in range(NT):
            def yl(y, t=t):
                if t >= 2:
                    b.wait(y, "t", tok[f"gud{t - 2}"])
                    b.wait(y, f"so{t % 2}", tok.get(f"st{t - 2}", 0))
                b.wait(y, "sq0", tok["xq_spilled"][0])
                b.wait(y, "sq1", tok["xq_spilled"][1])
                b.wait(y, "sxp", tok["spilled"])
                if t < 2:
                    b.wait(y, "v", tok["wq_done"])
                    b.wait(y, "t", tok["wq_done_t"])
                dma = y.dma_start(out=xq_tile[t % 2], in_=xqspill[:, :, t * 128:(t + 1) * 128])
                b.inc(dma, "sr1", 16)
                dma = y.dma_start(out=out_sb[t % 2],
                                  in_=xspill.rearrange("(n p) d -> p n d", p=128)[:, t, :])
                b.inc(dma, "sr1", 16)
                tok[f"rl{t}"] = b.cnt["sr1"]

            b.step(sync=yl)

            for fi, (f0, fw) in enumerate(fcw):
                def tg(tt, t=t, fi=fi, f0=f0, fw=fw):
                    b.wait(tt, "sr1", tok[f"rl{t}"])
                    if (t, fi) != (0, 0):
                        b.wait(tt, "v", tok.get("_hq", 0))
                    ii = None
                    for cc in range(NC):
                        tt.matmul(banks[0][:, 0:fw], xq_tile[t % 2][:, cc, :],
                                  wgqT[:, cc, f0:f0 + fw], start=(cc == 0), stop=(cc == NC - 1))
                        ii = tt.matmul(banks[1][:, 0:fw], xq_tile[t % 2][:, cc, :],
                                       wuqT[:, cc, f0:f0 + fw], start=(cc == 0), stop=(cc == NC - 1))
                    tok["_gu1"] = b.inc(ii, "t")

                def sg_(s_, t=t, fi=fi, fw=fw):
                    b.wait(s_, "t", tok["_gu1"])
                    ii = s_.activation(sgf_scr[:, 0:fw], banks[0][:, 0:fw], AF.Silu)
                    tok["_sg1"] = b.inc(ii, "s")

                def vh_(v, t=t, fi=fi, f0=f0, fw=fw):
                    b.wait(v, "t", tok["_gu1"])
                    b.wait(v, "s", tok["_sg1"])
                    v.drain()
                    v.tensor_tensor(hf_scr[:, 0:fw], sgf_scr[:, 0:fw], banks[1][:, 0:fw],
                                    op.mult)
                    v.drain()
                    hq_flat = hq_strip.rearrange("p a c -> p (a c)")
                    tok["_hq"] = emit_quant(v, "v", hf_scr[:, 0:fw], hq_flat[:, f0:f0 + fw],
                                            qaf[:, 0:fw], qbf[:, 0:fw],
                                            bm_d[:, 0:fw // 16], rs6_d[:, 0:fw // 16],
                                            fw // 16)
                b.step(tensor=tg, scalar=sg_)
                b.step(vector=vh_)

            for bb in range(6):
                blks = list(range(4 * bb, min(4 * bb + 4, 22)))
                pb = banks[2 + bb % 2]

                def th(tt, t=t, bb=bb, blks=blks, pb=pb):
                    b.wait(tt, "v", tok["_hq"])
                    b.wait(tt, "v", tok.get(("hqb", pb.tensor.name), 0))
                    hq_flat = hq_strip.rearrange("p a c -> p (a c)")
                    ii = None
                    for zi, cb in enumerate(blks):
                        ii = tt.transpose(pb[:, zi * 128:zi * 128 + 128],
                                          hq_flat[:, cb * 128:(cb + 1) * 128], ident_f)
                    tok["_htr"] = b.inc(ii, "t")

                def vhq(v, t=t, bb=bb, blks=blks, pb=pb):
                    b.wait(v, "t", tok["_htr"])
                    v.drain()
                    ii = None
                    for zi, cb in enumerate(blks):
                        ii = v.tensor_copy(hqT_scr[:, cb, :],
                                           pb[:, zi * 128:zi * 128 + 128])
                    tok[("hqb", pb.tensor.name)] = b.inc(ii, "v")
                    if bb == 5:
                        tok[f"hqev{t}"] = b.cnt["v"]
                b.step(tensor=th)
                b.step(vector=vhq)

            def td(tt, t=t):
                b.wait(tt, "v", tok[f"hqev{t}"])
                if t >= 1:
                    b.wait(tt, "v", tok.get(f"fin{t - 1}", 0))
                ii = None
                for oh in range(2):
                    pd = banks[4 + oh]
                    for cb in range(22):
                        ii = tt.matmul(pd, hqT_scr[:, cb, :], wdqT[:, cb, oh * 512:(oh + 1) * 512],
                                       start=(cb == 0), stop=(cb == 21))
                tok[f"gud{t}"] = b.inc(ii, "t")

            def vf(v, t=t):
                b.wait(v, "t", tok[f"gud{t}"])
                v.drain()
                ii = None
                for oh in range(2):
                    sl = slice(oh * 512, (oh + 1) * 512)
                    v.tensor_tensor(out_sb[t % 2][:, sl], out_sb[t % 2][:, sl],
                                    banks[4 + oh], op.add)
                    v.drain()
                    ii = v.tensor_tensor(out_sb[t % 2][:, sl], out_sb[t % 2][:, sl],
                                         attn1_tok[:, t, sl], op.add)
                    v.drain()
                tok[f"fin{t}"] = b.inc(ii, "v")

            def yo(y, t=t):
                b.wait(y, "v", tok[f"fin{t}"])
                dma = y.dma_start(out=out.rearrange("(n p) d -> p n d", p=128)[:, t, :],
                                  in_=out_sb[t % 2])
                tok[f"st{t}"] = b.inc(dma, f"so{t % 2}", 16)
            b.step(tensor=td)
            b.step(vector=vf)
            b.step(sync=yo)

        def yend(y):
            y.wait_ge(b.sem("so0"), b.cnt["so0"])
            y.wait_ge(b.sem("so1"), b.cnt["so1"])
        b.step(sync=yend)

    def dump_x_out(gate_sem, gate_val):
        def yout(y):
            b.wait(y, gate_sem, gate_val)
            for i in range(4):
                dma = y.dma_start(out=out.rearrange("(n p) d -> p n d", p=128)[:, 4 * i:4 * i + 4, :],
                                  in_=x_res[:, 4 * i:4 * i + 4, :])
                b.inc(dma, "so", 16)
            y.wait_ge(b.sem("so"), b.cnt["so"])
        b.step(sync=yout)

    ph_l1a()
    if upto in ("l1a", "l1skip_a"):
        dump_x_out("v", b.cnt["v"])
        return nc, dbg_out
    ph_l1b()
    if upto in ("l1b", "l1skip_b"):
        def yo2(g):
            b.wait(g, "v", tok["attn_done"])
            for i in range(4):
                dma = g.dma_start(out=out.rearrange("(n p) d -> p n d", p=128)[:, 4 * i:4 * i + 4, :],
                                  in_=attn1_tok[:, 4 * i:4 * i + 4, :])
                b.inc(dma, "so", 16)
            g.wait_ge(b.sem("so"), b.cnt["so"])
        b.step(gpsimd=yo2)
        return nc, dbg_out
    ph_l1c()
    if upto in ("l1c", "l1skip_c"):
        dump_x_out("v", tok["wq_done"])
        return nc, dbg_out
    ph_l1d()
    return nc, dbg_out


def kernel(**inputs):
    nc, _ = build(upto="full")
    xs = np.ascontiguousarray(inputs["L0_attn_x"], dtype=np.float32)
    weights = {kk: np.ascontiguousarray(v) for kk, v in inputs.items() if kk != "L0_attn_x"}
    in_maps = []
    for c in range(8):
        m = {"x": xs[c]}
        m.update(weights)
        in_maps.append(m)
    res = run_bass_kernel_spmd(nc, in_maps, core_ids=list(range(8)))
    return np.stack([r["out"] for r in res.results], axis=0)


# version 08
# speedup vs baseline: 1.0724x; 1.0006x over previous
import numpy as np
import concourse.bass as bass
import concourse.mybir as mybir
from concourse.alu_op_type import AluOpType as op
from concourse.bass_utils import run_bass_kernel_spmd

F32 = mybir.dt.float32
BF16 = mybir.dt.bfloat16
AF = mybir.ActivationFunctionType
AXX = mybir.AxisListType.X

T, D = 2048, 1024
NT, NC = T // 128, D // 128
NEXP, FH = 16, 512
HH, NW, HD, FI = 16, 16, 64, 2816
TC, NTC = 256, T // 256


class B:

    def __init__(self):
        self.nc = bass.Bass("TRN2", target_bir_lowering=False)
        self.sems, self.cnt, self.wm = {}, {}, {}

    def sem(self, name):
        if name not in self.sems:
            self.sems[name] = self.nc.alloc_semaphore(name=name)
            self.cnt[name] = 0
        return self.sems[name]

    def inc(self, instr, name, n=1):
        instr.then_inc(self.sem(name), n)
        self.cnt[name] += n
        return self.cnt[name]

    def wait(self, eng, name, val):
        if val is None or val <= 0:
            return
        wm = self.wm.setdefault(str(eng.engine), {})
        if wm.get(name, -1) >= val:
            return
        eng.wait_ge(self.sem(name), val)
        wm[name] = val

    def step(self, **fns):
        with self.nc.Block() as blk:
            for nm, fn in fns.items():
                getattr(blk, nm)(fn)


def build(upto="full", dbg=(), l1a_tiles=NT, l1a_parts=7):
    b = B()
    nc = b.nc
    SB = nc.alloc_sbuf_tensor
    PS = nc.alloc_psum_tensor

    x_in = nc.dram_tensor("x", [T, D], F32, kind="ExternalInput").ap()
    w_qkv = nc.dram_tensor("L0_attn_w_qkv", [3 * D, D], F32, kind="ExternalInput").ap()
    w_o0 = nc.dram_tensor("L0_attn_w_o", [D, D], F32, kind="ExternalInput").ap()
    Wr = nc.dram_tensor("L0_ffn_Wr", [D, NEXP], F32, kind="ExternalInput").ap()
    Wg = nc.dram_tensor("L0_ffn_Wg", [NEXP, D, FH], F32, kind="ExternalInput").ap()
    Wu = nc.dram_tensor("L0_ffn_Wu", [NEXP, D, FH], F32, kind="ExternalInput").ap()
    Wd = nc.dram_tensor("L0_ffn_Wd", [NEXP, FH, D], F32, kind="ExternalInput").ap()
    Sg = nc.dram_tensor("L0_ffn_Sg", [D, 2 * FH], F32, kind="ExternalInput").ap()
    Su = nc.dram_tensor("L0_ffn_Su", [D, 2 * FH], F32, kind="ExternalInput").ap()
    Sd = nc.dram_tensor("L0_ffn_Sd", [2 * FH, D], F32, kind="ExternalInput").ap()
    Wq1 = nc.dram_tensor("L1_attn_Wq", [D, HH * NW], F32, kind="ExternalInput").ap()
    bq1 = nc.dram_tensor("L1_attn_bq", [HH * NW], F32, kind="ExternalInput").ap()
    Wk1 = nc.dram_tensor("L1_attn_Wk", [D, HH * NW], F32, kind="ExternalInput").ap()
    bk1 = nc.dram_tensor("L1_attn_bk", [HH * NW], F32, kind="ExternalInput").ap()
    Wv1 = nc.dram_tensor("L1_attn_Wv", [D, D], F32, kind="ExternalInput").ap()
    bv1 = nc.dram_tensor("L1_attn_bv", [D], F32, kind="ExternalInput").ap()
    Wo1 = nc.dram_tensor("L1_attn_Wo", [D, D], F32, kind="ExternalInput").ap()
    bo1 = nc.dram_tensor("L1_attn_bo", [D], F32, kind="ExternalInput").ap()
    qph = nc.dram_tensor("L1_attn_q_phase", [1, HH, 1, NW], F32, kind="ExternalInput").ap()
    kph = nc.dram_tensor("L1_attn_k_phase", [1, HH, 1, NW], F32, kind="ExternalInput").ap()
    temp = nc.dram_tensor("L1_attn_temperature", [1], F32, kind="ExternalInput").ap()
    wg1 = nc.dram_tensor("L1_ffn_w_gate", [FI, D], F32, kind="ExternalInput").ap()
    wu1 = nc.dram_tensor("L1_ffn_w_up", [FI, D], F32, kind="ExternalInput").ap()
    wd1 = nc.dram_tensor("L1_ffn_w_down", [D, FI], F32, kind="ExternalInput").ap()
    out = nc.dram_tensor("out", [T, D], F32, kind="ExternalOutput").ap()
    xspill = nc.dram_tensor("xspill", [T, D], F32, kind="Internal").ap()
    xqspill = nc.dram_tensor("xqspill", [128, NC, T], BF16, kind="Internal").ap()
    dbg_out = {}
    for name, shape, dt in dbg:
        dbg_out[name] = nc.dram_tensor("dbg_" + name, shape, dt, kind="ExternalOutput").ap()

    _off = [16512]

    def SBp(name, shape, dtype, at=None):
        o = _off[0] if at is None else at
        h = nc.alloc_sbuf_tensor_at(name, shape, dtype, offset=o)
        if at is None:
            nbytes = int(np.prod(shape[1:])) * (2 if dtype == BF16 else 4)
            _off[0] = o + ((nbytes + 31) // 32) * 32
        return h.ap(), o

    def SBn(name, shape, dtype, at=None):
        return SBp(name, shape, dtype, at)[0]

    ident_f = SBn("ident_f", [128, 128], F32)
    ident_b = SBn("ident_b", [128, 128], BF16)
    ones_col = SBn("ones_col", [128, 1], F32)
    ones_colb = SBn("ones_colb", [128, 1], BF16)
    ones_row = SBn("ones_row", [1, 128], F32)
    neg1 = SBn("neg1", [128, 1], F32)
    rtemp = SBn("rtemp", [128, 1], F32)
    mv_all = SBn("mv_all", [128, NT, 2], F32)
    rstd_all = SBn("rstd_all", [128, NT], F32)
    bn6 = SBn("bn6", [128, 2, 6], F32)
    cw_all, cw_off = SBp("cw_all", [128, NT, NEXP], F32)
    cwb = SBn("cwb", [128, TC], F32)
    cwrow = SBn("cwrow", [1, TC], F32)
    wr_hi, wrh_off = SBp("wr_hi", [128, NC, NEXP], BF16)
    wr_lo = SBn("wr_lo", [128, NC, NEXP], BF16)
    wr_f = SBn("wr_f", [128, NC, NEXP], F32)
    val8 = SBn("val8", [128, 8], F32)
    a08 = SBn("a08", [128, 8], F32)
    row_scr, row_off = SBp("row_scr", [1, D], F32)
    h2_scr = SBn("h2_scr", [128, 4, TC], BF16, at=row_off)
    sg_scr, sg_off = SBp("sg_scr", [128, 4, TC], F32)
    h_scr, h_off = SBp("h_scr", [128, 4, TC], BF16)
    attn0b, a0_off = SBp("attn0b", [128, D], F32)
    bcast_a, bc_off = SBp("bcast_a", [128, D], F32)
    xn_scr, xns_off = SBp("xn_scr", [128, 2, D], F32)
    xnT_hi, xh_off = SBp("xnT_hi", [128, NC, T], BF16)
    xnT_lo, xl_off = SBp("xnT_lo", [128, NC, T], BF16)
    racc = xnT_lo
    ws0, ws0_off = SBp("wslot0", [128, 16 * FH], BF16)
    ws1, ws1_off = SBp("wslot1", [128, 16 * FH], BF16)
    wslot = [ws0, ws1]
    wd_buf, wd_off = SBp("wd_buf", [128, 4, D], BF16)
    wd_buf2, wd2_off = SBp("wd_buf2", [128, 4, D], BF16)
    x_res, x_off = SBp("x_res", [128, NT, D], F32)
    assert _off[0] <= 229344, f"sbuf map overflow: {_off[0]}"
    comb_scr = bcast_a.rearrange("p (c x) -> p c x", c=8)
    stream_a = [xn_scr[:, i, :] for i in range(2)]
    ttr_scr = sg_scr.rearrange("p a c -> p (a c)")
    cwrow_e = SBn("cwrow_e", [1, T], F32, at=wd2_off)
    cwb_e = SBn("cwb_e", [128, T], BF16, at=bc_off)
    sg_pair = SBn("sg_pair", [128, 4, 512], BF16, at=sg_off)
    h_pair = SBn("h_pair", [128, 4, 512], BF16, at=row_off)
    h_pairB = SBn("h_pairB", [128, 4, 512], BF16, at=xns_off + 4096)
    racc_tok = SBn("racc_tok", [128, NT, D], BF16, at=xl_off)
    h17_pair = SBn("h17_pair", [128, 4, 512], BF16, at=xns_off)
    qaG_a = SBn("qaG_a", [128, 1024], F32, at=xl_off + 4096)
    qbG_a = SBn("qbG_a", [128, 1024], F32, at=xl_off + 8192)
    bm_ga = SBn("bm_ga", [128, 64], F32, at=xl_off + 12288)
    rs6_ga = SBn("rs6_ga", [128, 64], F32, at=xl_off + 12544)
    rs6F_a = SBn("rs6F_a", [128, 1024], F32, at=xl_off + 13056)
    scF_a = SBn("scF_a", [128, 1024], F32, at=xl_off + 17152)
    rs6F_c = SBn("rs6F_c", [128, 1024], F32, at=x_off + 20480)
    scF_c = SBn("scF_c", [128, 1024], F32, at=x_off + 24576)
    qTpad = SBn("qTpad", [128, 4, T], BF16, at=xl_off)
    kTpad = SBn("kTpad", [128, 4, T], BF16, at=xl_off + 16384)
    Vbuf = SBn("Vbuf", [128, NT, D], BF16, at=x_off)
    attn1_tok = SBn("attn1_tok", [128, NT, D], BF16, at=x_off + 32768)
    wgqT = SBn("wgqT", [128, NC, FI], BF16, at=xh_off)
    wuqT = SBn("wuqT", [128, NC, FI], BF16, at=xh_off + 45056)
    wdqT = SBn("wdqT", [128, 22, D], BF16, at=xh_off + 90112)
    P_pool = [SBn(f"Ppool{i}", [128, 512], BF16, at=xns_off + 1024 * i) for i in range(12)]
    attnT = SBn("attnT", [128, NC, 512], BF16, at=wd2_off)
    hq_strip = SBn("hq_strip", [128, 22, 128], F32, at=x_off + 20480)
    hqT_scr = SBn("hqT_scr", [128, 22, 128], BF16, at=xns_off)
    hf_scr = SBn("hf_scr", [128, 512], F32, at=sg_off)
    sgf_scr = SBn("sgf_scr", [128, 512], F32, at=sg_off + 2048)
    out_sb = [SBn(f"out_sb{i}", [128, D], F32, at=row_off if i == 0 else a0_off)
              for i in range(2)]
    xq_tileA = [SBn(f"xq_tileA{i}", [128, NC, 128], BF16, at=xl_off + 2048 * i)
                for i in range(2)]
    xq_tile = [SBn(f"xq_tileD{i}", [128, NC, 128], BF16, at=bc_off + 2048 * i)
               for i in range(2)]
    wstr = [SBn("wstr0", [128, 1024], F32, at=a0_off),
            SBn("wstr1", [128, 1024], F32, at=bc_off)]
    l1scr = SBn("l1scr", [128, 768], F32, at=sg_off)

    banks = [PS(f"bank{i}", [128, 512], F32).ap() for i in range(8)]

    tok = {}

    def ph_consts():
        def g(g):
            g.memset(ident_f, 0.0)
            g.memset(ident_b, 0.0)
            g.memset(ones_col, 1.0)
            g.memset(ones_colb, 1.0)
            g.memset(ones_row, 1.0)
            g.memset(neg1, -1.0)
            g.drain()
            g.affine_select(out=ident_f, in_=ident_f, compare_op=op.not_equal,
                            fill=1.0, base=0, pattern=[[-1, 128]], channel_multiplier=1)
            i2 = g.affine_select(out=ident_b, in_=ident_b, compare_op=op.not_equal,
                                 fill=1.0, base=0, pattern=[[-1, 128]], channel_multiplier=1)
            b.inc(i2, "g")
            tok["consts"] = b.cnt["g"]
        b.step(gpsimd=g)

    def ph_ln_stats(tag, gate=None):
        def v(v):
            if gate is None:
                b.wait(v, "sx", tok["x_loaded"])
            else:
                b.wait(v, gate[0], gate[1])
            for t in range(NT):
                xt = x_res[:, t, :]
                v.bn_stats(bn6[:, 0:1, :], xt[:, 0:512].rearrange("p (a c) -> p a c", a=1))
                v.bn_stats(bn6[:, 1:2, :], xt[:, 512:1024].rearrange("p (a c) -> p a c", a=1))
                v.drain()
                v.bn_aggr(mv_all[:, t, :], bn6.rearrange("p a b -> p (a b)"))
                v.drain()
            ii = v.tensor_scalar(rstd_all, mv_all[:, :, 1], 1e-5, None, op.add)
            tok[f"vareps{tag}"] = b.inc(ii, "v")

        def s(s):
            b.wait(s, "v", tok[f"vareps{tag}"])
            ii = s.activation(rstd_all, rstd_all, AF.Sqrt)
            tok[f"sqrt{tag}"] = b.inc(ii, "s")
        b.step(vector=v, scalar=s)

        def v2(v):
            b.wait(v, "s", tok[f"sqrt{tag}"])
            ii = v.reciprocal(rstd_all, rstd_all)
            tok[f"rstd{tag}"] = b.inc(ii, "v")
        b.step(vector=v2)

    def ph_l0a():
        def y(y):
            for i in range(4):
                dma = y.dma_start(out=x_res[:, 4 * i:4 * i + 4, :],
                                  in_=x_in.rearrange("(n p) d -> p n d", p=128)[:, 4 * i:4 * i + 4, :])
                b.inc(dma, "sx", 16)
            tok["x_loaded"] = b.cnt["sx"]
            dma = y.dma_start(out=wr_f, in_=Wr.rearrange("(c p) e -> p c e", p=128))
            b.inc(dma, "sw0", 16)
            dma = y.dma_start(out=rtemp[0:1, :], in_=temp[None, :])
            b.inc(dma, "sw0", 16)
            tok["wr_loaded"] = b.cnt["sw0"]
        b.step(sync=y)

        ph_ln_stats("0")

        for t in range(NT):
            scr = xn_scr[:, t % 2, :]
            p0, p1 = banks[t % 2], banks[2 + t % 2]

            def v(v, t=t, scr=scr):
                b.wait(v, "v", tok["rstd0"])
                if t >= 2:
                    b.wait(v, "t", tok[f"tr{t - 2}"])
                v.drain()
                ii = v.tensor_scalar(scr, x_res[:, t, :], mv_all[:, t, 0:1],
                                     rstd_all[:, t:t + 1], op.subtract, op.mult)
                tok[f"ln{t}"] = b.inc(ii, "v")

            def tt(tt, t=t, scr=scr, p0=p0, p1=p1):
                b.wait(tt, "g", tok["consts"])
                b.wait(tt, "v", tok[f"ln{t}"])
                if t >= 2:
                    b.wait(tt, "v", tok[f"ev{t - 2}"])
                for c in range(8):
                    half = p0 if c < 4 else p1
                    tt.transpose(half[:, (c % 4) * 128:(c % 4) * 128 + 128],
                                 scr[:, c * 128:c * 128 + 128], ident_f)
                tt.matmul(banks[4][0:1, 0:512], ones_col, scr[:, 0:512],
                          start=(t == 0), stop=(t == NT - 1))
                ii = tt.matmul(banks[5][0:1, 0:512], ones_col, scr[:, 512:1024],
                               start=(t == 0), stop=(t == NT - 1))
                tok[f"tr{t}"] = b.inc(ii, "t")

            def v2(v, t=t, p0=p0, p1=p1):
                b.wait(v, "t", tok[f"tr{t}"])
                ii = None
                for h, half in ((0, p0), (1, p1)):
                    hi = xnT_hi[:, 4 * h:4 * h + 4, t * 128:(t + 1) * 128]
                    lo = xnT_lo[:, 4 * h:4 * h + 4, t * 128:(t + 1) * 128]
                    h3 = half.rearrange("p (c x) -> p c x", c=4)
                    v.tensor_copy(hi, h3)
                    v.drain()
                    ii = v.scalar_tensor_tensor(lo, hi, -1.0, h3, op.mult, op.add)
                tok[f"ev{t}"] = b.inc(ii, "v")

            b.step(vector=v)
            b.step(tensor=tt, vector=v2)

    def ph_l0b():
        def v(v):
            b.wait(v, "sw0", tok["wr_loaded"])
            v.tensor_copy(wr_hi.rearrange("p c e -> p (c e)"), wr_f.rearrange("p c e -> p (c e)"))
            v.drain()
            v.scalar_tensor_tensor(wr_lo.rearrange("p c e -> p (c e)"),
                                   wr_hi.rearrange("p c e -> p (c e)"), -1.0,
                                   wr_f.rearrange("p c e -> p (c e)"), op.mult, op.add)
            v.drain()
            ii = v.reciprocal(rtemp[0:1, :], rtemp[0:1, :])
            tok["wrhl"] = b.inc(ii, "v")
        b.step(vector=v)

        for t in range(NT):
            pr = banks[6 + t % 2][:, 0:NEXP]

            def tt(tt, t=t, pr=pr):
                b.wait(tt, "v", max(tok["wrhl"], tok[f"ev{t}"]))
                if t >= 2:
                    b.wait(tt, "v", tok[f"cw{t - 2}"])
                ii = None
                for c in range(NC):
                    lhs_hi = xnT_hi[:, c, t * 128:(t + 1) * 128]
                    lhs_lo = xnT_lo[:, c, t * 128:(t + 1) * 128]
                    tt.matmul(pr, lhs_hi, wr_hi[:, c, :], start=(c == 0), stop=False)
                    tt.matmul(pr, lhs_hi, wr_lo[:, c, :], start=False, stop=False)
                    ii = tt.matmul(pr, lhs_lo, wr_hi[:, c, :], start=False,
                                   stop=(c == NC - 1))
                tok[f"rt{t}"] = b.inc(ii, "t")

            def v1(v, t=t, pr=pr):
                b.wait(v, "t", tok[f"rt{t}"])
                v.drain()
                ii = v.tensor_reduce(ttr_scr[:, 0:1], pr, AXX, op.max, negate=True)
                tok[f"nmax{t}"] = b.inc(ii, "v")

            def s1(s, t=t, pr=pr):
                b.wait(s, "v", tok[f"nmax{t}"])
                ii = s.activation(ttr_scr[:, 4:4 + NEXP], pr, AF.Exp,
                                  bias=ttr_scr[:, 0:1], scale=1.0,
                                  accum_out=ttr_scr[:, 1:2])
                tok[f"exp{t}"] = b.inc(ii, "s")

            def v2(v, t=t):
                b.wait(v, "s", tok[f"exp{t}"])
                pe = ttr_scr[:, 4:4 + NEXP]
                v.tensor_reduce(ttr_scr[:, 2:3], pe, AXX, op.max)
                v.drain()
                v.tensor_scalar(ttr_scr[:, 32:32 + NEXP], pe, ttr_scr[:, 2:3], None, op.is_equal)
                v.drain()
                v.scalar_tensor_tensor(ttr_scr[:, 64:64 + NEXP], ttr_scr[:, 32:32 + NEXP],
                                       -1e30, pe, op.mult, op.add)
                v.drain()
                v.tensor_reduce(ttr_scr[:, 3:4], ttr_scr[:, 64:64 + NEXP], AXX, op.max)
                v.drain()
                v.tensor_scalar(ttr_scr[:, 32:32 + NEXP], pe, ttr_scr[:, 3:4], None, op.is_ge)
                v.reciprocal(ttr_scr[:, 1:2], ttr_scr[:, 1:2])
                v.drain()
                ii = v.scalar_tensor_tensor(cw_all[:, t, :], pe, ttr_scr[:, 1:2],
                                            ttr_scr[:, 32:32 + NEXP], op.mult, op.mult)
                tok[f"cw{t}"] = b.inc(ii, "v")

            b.step(tensor=tt, vector=v1)
            b.step(scalar=s1)
            b.step(vector=v2)

        def v3(v):
            b.wait(v, "t", tok[f"tr{NT - 1}"])
            v.tensor_copy(row_scr[0:1, 0:512], banks[4][0:1, 0:512])
            ii = v.tensor_copy(row_scr[0:1, 512:1024], banks[5][0:1, 0:512])
            tok["vsrow"] = b.inc(ii, "v")

        def t3(tt):
            b.wait(tt, "v", tok["vsrow"])
            tt.matmul(banks[4], ones_row, row_scr[0:1, 0:512], start=True, stop=True)
            ii = tt.matmul(banks[5], ones_row, row_scr[0:1, 512:1024], start=True, stop=True)
            tok["vsrep"] = b.inc(ii, "t")

        def v4(v):
            b.wait(v, "t", tok["vsrep"])
            v.tensor_copy(bcast_a[:, 0:512], banks[4])
            ii = v.tensor_copy(bcast_a[:, 512:1024], banks[5])
            tok["vsb"] = b.inc(ii, "v")
        b.step(vector=v3)
        b.step(tensor=t3, vector=v4)

        def matvec(src_ap, res8, toks):
            for c in range(NC):
                def y(y, c=c):
                    b.wait(y, "v", toks[c - 2])
                    dma = y.dma_start(out=stream_a[c % 2],
                                      in_=src_ap[128 * c:128 * (c + 1), :])
                    b.inc(dma, f"swv{c % 2}", 16)
                    tok["_mv_dma"] = b.cnt[f"swv{c % 2}"]

                def v(v, c=c):
                    b.wait(v, f"swv{c % 2}", tok["_mv_dma"])
                    b.wait(v, "v", tok["_mv_src"])
                    v.drain()
                    v.tensor_tensor(ttr_scr, stream_a[c % 2], bcast_a, op.mult)
                    v.drain()
                    ii = v.tensor_reduce(res8[:, c:c + 1], ttr_scr, AXX, op.add)
                    toks[c] = b.inc(ii, "v")
                b.step(sync=y, vector=v)

        tok["_mv_src"] = tok["vsb"]
        wv_toks = {-2: 0, -1: 0}
        matvec(w_qkv[2 * D:3 * D, :], val8, wv_toks)

        def y5(y):
            b.wait(y, "v", wv_toks[NC - 1])
            for c in range(NC):
                ii = y.dma_start(out=row_scr[0:1, c * 128:(c + 1) * 128], in_=val8[:, c:c + 1])
                b.inc(ii, "sg2", 16)
            tok["valrow"] = b.cnt["sg2"]

        def t5(tt):
            b.wait(tt, "sg2", tok["valrow"])
            b.wait(tt, "v", wv_toks[NC - 1])
            tt.matmul(banks[4], ones_row, row_scr[0:1, 0:512], start=True, stop=True)
            ii = tt.matmul(banks[5], ones_row, row_scr[0:1, 512:1024], start=True, stop=True)
            tok["valrep"] = b.inc(ii, "t")

        def v5(v):
            b.wait(v, "t", tok["valrep"])
            v.drain()
            v.tensor_copy(bcast_a[:, 0:512], banks[4])
            ii = v.tensor_copy(bcast_a[:, 512:1024], banks[5])
            tok["valb"] = b.inc(ii, "v")
        b.step(sync=y5)
        b.step(tensor=t5, vector=v5)

        tok["_mv_src"] = tok["valb"]
        wo_toks = {-2: wv_toks[NC - 2], -1: wv_toks[NC - 1]}
        matvec(w_o0, a08, wo_toks)

        def y6(y):
            b.wait(y, "v", wo_toks[NC - 1])
            b.wait(y, "t", tok["valrep"])
            for c in range(NC):
                ii = y.dma_start(out=row_scr[0:1, c * 128:(c + 1) * 128], in_=a08[:, c:c + 1])
                b.inc(ii, "sg2", 16)
            tok["a0row"] = b.cnt["sg2"]

        def t6(tt):
            b.wait(tt, "sg2", tok["a0row"])
            b.wait(tt, "v", wo_toks[NC - 1])
            tt.matmul(banks[4], ones_row, row_scr[0:1, 0:512], start=True, stop=True)
            ii = tt.matmul(banks[5], ones_row, row_scr[0:1, 512:1024], start=True, stop=True)
            tok["a0rep"] = b.inc(ii, "t")

        def v6(v):
            b.wait(v, "t", tok["a0rep"])
            v.tensor_copy(attn0b[:, 0:512], banks[4])
            ii = v.tensor_copy(attn0b[:, 512:1024], banks[5])
            tok["a0b"] = b.inc(ii, "v")
        b.step(sync=y6)
        b.step(tensor=t6, vector=v6)

    def wslot_views(i):
        w = wslot[i]
        wg = w[:, 0:8 * FH].rearrange("p (c f) -> p c f", c=8)
        wu = w[:, 8 * FH:16 * FH].rearrange("p (c f) -> p c f", c=8)
        return wg, wu, wd_buf

    def gu_srcs(s):
        if s < NEXP:
            return Wg[s], Wu[s]
        hh = s - NEXP
        return Sg[:, hh * FH:(hh + 1) * FH], Su[:, hh * FH:(hh + 1) * FH]

    def wd_src(s):
        return Wd[s] if s < NEXP else Sd[(s - NEXP) * FH:(s - NEXP + 1) * FH, :]

    def emit_slot_load(g, s):
        slot = s % 2
        g_src, u_src = gu_srcs(s)
        wgv, wuv, _ = wslot_views(slot)
        i1 = g.dma_start(out=wgv, in_=g_src.rearrange("(c p) f -> p c f", p=128))
        b.inc(i1, f"wsl{slot}", 16)
        i2 = g.dma_start(out=wuv, in_=u_src.rearrange("(c p) f -> p c f", p=128))
        b.inc(i2, f"wsl{slot}", 16)
        tok[f"wload{s}"] = b.cnt[f"wsl{slot}"]

    def emit_wd_load(g, s):
        ii = g.dma_start(out=wd_buf, in_=wd_src(s).rearrange("(m p) f -> p m f", p=128))
        b.inc(ii, "wdl", 16)
        tok[f"wdload{s}"] = b.cnt["wdl"]

    def emit_wd_load2(g, s):
        ii = g.dma_start(out=wd_buf2, in_=wd_src(s).rearrange("(m p) f -> p m f", p=128))
        b.inc(ii, "wdl", 16)
        tok[f"wdload{s}"] = b.cnt["wdl"]

    def ph_l0c():
        def g0(g):
            emit_slot_load(g, 0)
            emit_slot_load(g, 1)
            emit_wd_load(g, 0)
        b.step(gpsimd=g0)

        def emit_sltc(s, tc, wgv, wuv, wdv, shared):
            tsl = slice(tc * TC, (tc + 1) * TC)
            t0, t1 = 2 * tc, 2 * tc + 1
            if True:
                if not shared:
                    def y(y, s=s, tc=tc, t0=t0, t1=t1):
                        b.wait(y, "v", max(tok[f"cw{t0}"], tok[f"cw{t1}"]))
                        if (s, tc) != (0, 0):
                            b.wait(y, "t", tok["cwrep"])
                        i1 = y.dma_start(out=cwrow[0:1, 0:128], in_=cw_all[:, t0, s:s + 1])
                        b.inc(i1, "sg2", 16)
                        i2 = y.dma_start(out=cwrow[0:1, 128:256], in_=cw_all[:, t1, s:s + 1])
                        b.inc(i2, "sg2", 16)
                        tok["cwdma"] = b.cnt["sg2"]

                    def tcw(tt, s=s, tc=tc):
                        b.wait(tt, "sg2", tok["cwdma"])
                        ii = tt.matmul(banks[2][:, 0:TC], ones_row, cwrow[0:1, :],
                                       start=True, stop=True)
                        tok["cwrep"] = b.inc(ii, "t")

                    def vcw(v, s=s, tc=tc):
                        b.wait(v, "t", tok["cwrep"])
                        v.drain()
                        ii = v.tensor_copy(cwb, banks[2][:, 0:TC])
                        tok["cwb"] = b.inc(ii, "v")
                    b.step(sync=y, tensor=tcw)
                    b.step(vector=vcw)

                def tgu(tt, s=s, tc=tc, tsl=tsl, wgv=wgv, wuv=wuv, shared=shared):
                    b.wait(tt, f"wsl{s % 2}", tok[f"wload{s}"])
                    if not shared:
                        b.wait(tt, "v", tok["cwb"])
                    elif s == NEXP and "xadd" in tok:
                        b.wait(tt, "v", tok["xadd"])
                    for m in range(4):
                        pg = banks[m // 2][:, (m % 2) * TC:(m % 2) * TC + TC]
                        pu = banks[2 + m // 2][:, (m % 2) * TC:(m % 2) * TC + TC]
                        for c in range(NC):
                            rhs = xnT_hi[:, c, tsl]
                            tt.matmul(pg, wgv[:, c, m * 128:(m + 1) * 128], rhs,
                                      start=(c == 0), stop=(c == NC - 1))
                            ii = tt.matmul(pu, wuv[:, c, m * 128:(m + 1) * 128], rhs,
                                           start=(c == 0), stop=(c == NC - 1))
                    tok["gu"] = b.inc(ii, "t")

                def ssg(ss, s=s):
                    b.wait(ss, "t", tok["gu"])
                    ii = None
                    for m2 in range(2):
                        ii = ss.activation(sg_scr[:, 2 * m2:2 * m2 + 2, :]
                                           .rearrange("p a c -> p (a c)"),
                                           banks[m2], AF.Silu)

                    tok["sg"] = b.inc(ii, "s")

                def vh(v, s=s, tc=tc, shared=shared):
                    b.wait(v, "t", tok["gu"])
                    b.wait(v, "s", tok["sg"])
                    v.drain()
                    ii = None
                    if shared:
                        for m2 in range(2):
                            ii = v.tensor_tensor(h_scr[:, 2 * m2:2 * m2 + 2, :],
                                                 sg_scr[:, 2 * m2:2 * m2 + 2, :],
                                                 banks[2 + m2].rearrange("p (a x) -> p a x", a=2),
                                                 op.mult)
                    else:
                        cb = cwb.rearrange("p x -> p () x").broadcast_to([128, 2, TC])
                        for m2 in range(2):
                            v.tensor_tensor(h_scr[:, 2 * m2:2 * m2 + 2, :],
                                            banks[2 + m2].rearrange("p (a x) -> p a x", a=2),
                                            cb, op.mult)
                        v.drain()
                        ii = v.tensor_tensor(h_scr.rearrange("p a c -> p (a c)"),
                                             sg_scr.rearrange("p a c -> p (a c)"),
                                             h_scr.rearrange("p a c -> p (a c)"), op.mult)
                    tok["h"] = b.inc(ii, "v")

                def tdn(tt, s=s, tc=tc, shared=shared, wdv=wdv):
                    b.wait(tt, "v", tok["h"])
                    b.wait(tt, "wdl", tok[f"wdload{s}"])
                    ii = None
                    for o in range(8):
                        pd = banks[4 + o // 2][:, (o % 2) * TC:(o % 2) * TC + TC]
                        for m in range(4):
                            ii = tt.matmul(pd, wdv[:, m, o * 128:(o + 1) * 128],
                                           h_scr[:, m, :], start=(m == 0), stop=(m == 3))
                    tok["dn"] = b.inc(ii, "t")
                    if tc == NTC - 1:
                        tok[f"mmdone{s}"] = b.cnt["t"]

                b.step(tensor=tgu, scalar=ssg)
                b.step(vector=vh)
                b.step(tensor=tdn)

                if not shared:
                    def vr(v, s=s, tc=tc, tsl=tsl):
                        b.wait(v, "t", tok["dn"])
                        v.drain()
                        ii = None
                        for b4 in range(4):
                            dst = racc[:, 2 * b4:2 * b4 + 2, tsl]
                            src = banks[4 + b4].rearrange("p (c x) -> p c x", c=2)
                            if s == 0:
                                ii = v.tensor_copy(dst, src)
                            else:
                                ii = v.tensor_tensor(dst, dst, src, op.add)
                            v.drain()
                        tok[f"racc{tc}"] = b.inc(ii, "v")
                    b.step(vector=vr)
                elif s == NEXP + 1:
                    for tb in range(2):
                        tglob = 2 * tc + tb

                        def vc(v, tc=tc, tb=tb, tglob=tglob):
                            b.wait(v, "t", tok["dn"])
                            if "combtr" in tok:
                                b.wait(v, "t", tok["combtr"])
                            v.drain()
                            ii = None
                            for b4 in range(4):
                                src = racc[:, 2 * b4:2 * b4 + 2,
                                           tglob * 128:(tglob + 1) * 128]
                                pk = banks[4 + b4].rearrange("p (c y x) -> p c y x", c=2, y=2)[:, :, tb, :]
                                ii = v.tensor_tensor(comb_scr[:, 2 * b4:2 * b4 + 2, :],
                                                     src, pk, op.add)
                            tok["comb"] = b.inc(ii, "v")
                        b.step(vector=vc)

                        def ttr(tt, tc=tc, tb=tb):
                            b.wait(tt, "v", tok["comb"])
                            if tc > 0 or tb > 0:
                                b.wait(tt, "v", tok["xadd"])
                            ii = None
                            for c in range(NC):
                                dstb = banks[2 + c // 4]
                                ii = tt.transpose(dstb[:, (c % 4) * 128:(c % 4) * 128 + 128],
                                                  comb_scr[:, c, :], ident_f)
                            tok["combtr"] = b.inc(ii, "t")
                        b.step(tensor=ttr)

                        def vx(v, tglob=tglob):
                            b.wait(v, "t", tok["combtr"])
                            v.drain()
                            xt = x_res[:, tglob, :]
                            v.tensor_tensor(xt[:, 0:512], xt[:, 0:512], banks[2], op.add)
                            ii = v.tensor_tensor(xt[:, 512:1024], xt[:, 512:1024],
                                                 banks[3], op.add)
                            tok["xadd"] = b.inc(ii, "v")
                        b.step(vector=vx)

        pend = [None]

        hp = [h_pair, h_pairB]

        def emit_dnA(s, pr):
            def tdn(tt, s=s, pr=pr):
                b.wait(tt, "v", tok[f"h{s}_{pr}"])
                b.wait(tt, "wdl", tok[f"wdload{s}"])
                b.wait(tt, "v", tok.get("vrB", 0))
                ii = None
                for tl in range(2):
                    for m in range(4):
                        lhs = hp[pr % 2][:, m, tl * 128:(tl + 1) * 128]
                        tt.matmul(banks[4 + 2 * tl], lhs, wd_buf[:, m, 0:512],
                                  start=(m == 0), stop=(m == 3))
                        ii = tt.matmul(banks[5 + 2 * tl], lhs, wd_buf[:, m, 512:1024],
                                       start=(m == 0), stop=(m == 3))
                tok["dnA"] = b.inc(ii, "t")

            def vr(v, s=s, pr=pr):
                b.wait(v, "t", tok["dnA"])
                v.drain()
                ii = None
                for tl in range(2):
                    tglob = 4 * pr + tl
                    dst = racc_tok[:, tglob, :]
                    for hb in range(2):
                        pk = banks[4 + 2 * tl + hb]
                        d2 = dst[:, hb * 512:(hb + 1) * 512]
                        if s == 0:
                            ii = v.tensor_copy(d2, pk)
                        else:
                            ii = v.tensor_tensor(d2, d2, pk, op.add)
                        v.drain()
                tok["vrA"] = b.inc(ii, "v")
            b.step(tensor=tdn)
            b.step(vector=vr)

        def emit_dnB(s, pr):
            def tdn(tt, s=s, pr=pr):
                b.wait(tt, "v", tok["vrA"])
                ii = None
                for tl in range(2, 4):
                    for m in range(4):
                        lhs = hp[pr % 2][:, m, tl * 128:(tl + 1) * 128]
                        tt.matmul(banks[4 + 2 * (tl - 2)], lhs, wd_buf[:, m, 0:512],
                                  start=(m == 0), stop=(m == 3))
                        ii = tt.matmul(banks[5 + 2 * (tl - 2)], lhs, wd_buf[:, m, 512:1024],
                                       start=(m == 0), stop=(m == 3))
                tok["dnB"] = b.inc(ii, "t")
                if pr == 3:
                    tok[f"mmdone{s}"] = b.cnt["t"]

            def vr(v, s=s, pr=pr):
                b.wait(v, "t", tok["dnB"])
                v.drain()
                ii = None
                for tl in range(2, 4):
                    tglob = 4 * pr + tl
                    dst = racc_tok[:, tglob, :]
                    for hb in range(2):
                        pk = banks[4 + 2 * (tl - 2) + hb]
                        d2 = dst[:, hb * 512:(hb + 1) * 512]
                        if s == 0:
                            ii = v.tensor_copy(d2, pk)
                        else:
                            ii = v.tensor_tensor(d2, d2, pk, op.add)
                        v.drain()
                tok["vrB"] = b.inc(ii, "v")
            b.step(tensor=tdn)
            b.step(vector=vr)

        for s in range(NEXP):
            wgv, wuv, _ = wslot_views(s % 2)

            if s == 0:
                def ycw0(y):
                    b.wait(y, "v", tok[f"cw{NT - 1}"])
                    for t in range(NT):
                        dma = y.dma_start(out=cwrow_e[0:1, t * 128:(t + 1) * 128],
                                          in_=cw_all[:, t, 0:1])
                        b.inc(dma, "sg2", 16)
                    tok["cwg"] = b.cnt["sg2"]
                b.step(sync=ycw0)

            def tcw(tt, s=s):
                b.wait(tt, "sg2", tok["cwg"])
                if s >= 1:
                    b.wait(tt, "v", tok[f"h{s - 1}_3"])
                ii = None
                for q4 in range(4):
                    ii = tt.matmul(banks[q4], ones_row, cwrow_e[0:1, 512 * q4:512 * (q4 + 1)],
                                   start=True, stop=True)
                tok["cwrep"] = b.inc(ii, "t")

            def vcw(v, s=s):
                b.wait(v, "t", tok["cwrep"])
                v.drain()
                ii = None
                for q4 in range(4):
                    ii = v.tensor_copy(cwb_e[:, 512 * q4:512 * (q4 + 1)], banks[q4])
                tok["cwb"] = b.inc(ii, "v")
            b.step(tensor=tcw)
            b.step(vector=vcw)

            def gpre(g, s=s):
                b.wait(g, "t", tok[f"mmdone{s - 1}"])
                emit_slot_load(g, s + 1)
                emit_wd_load(g, s)

            def ycw(y, s=s):
                b.wait(y, "t", tok["cwrep"])
                for t in range(NT):
                    dma = y.dma_start(out=cwrow_e[0:1, t * 128:(t + 1) * 128],
                                      in_=cw_all[:, t, s + 1:s + 2])
                    b.inc(dma, "sg2", 16)
                tok["cwg"] = b.cnt["sg2"]

            for pr in range(4):
                psl = slice(pr * 512, (pr + 1) * 512)

                def tg(tt, s=s, pr=pr, psl=psl, wgv=wgv):
                    b.wait(tt, f"wsl{s % 2}", tok[f"wload{s}"])
                    b.wait(tt, "v", tok["cwb"])
                    if (s, pr) != (0, 0):
                        prev = tok[f"h{s}_{pr - 1}"] if pr > 0 else tok[f"h{s - 1}_3"]
                        b.wait(tt, "v", prev)
                    ii = None
                    for m in range(4):
                        for c in range(NC):
                            ii = tt.matmul(banks[m], wgv[:, c, m * 128:(m + 1) * 128],
                                           xnT_hi[:, c, psl],
                                           start=(c == 0), stop=(c == NC - 1))
                    tok[f"g{s}_{pr}"] = b.inc(ii, "t")

                def ssg(ss, s=s, pr=pr):
                    b.wait(ss, "t", tok[f"g{s}_{pr}"])
                    ii = None
                    for m in range(4):
                        ii = ss.activation(sg_pair[:, m, :], banks[m], AF.Silu)
                    tok[f"sg{s}_{pr}"] = b.inc(ii, "s")
                b.step(tensor=tg, scalar=ssg)

                if pend[0] is not None:
                    emit_dnA(*pend[0])

                def tu(tt, s=s, pr=pr, psl=psl, wuv=wuv):
                    b.wait(tt, "s", tok[f"sg{s}_{pr}"])
                    ii = None
                    for m in range(4):
                        for c in range(NC):
                            ii = tt.matmul(banks[m], wuv[:, c, m * 128:(m + 1) * 128],
                                           xnT_hi[:, c, psl],
                                           start=(c == 0), stop=(c == NC - 1))
                    tok[f"u{s}_{pr}"] = b.inc(ii, "t")

                def vh(v, s=s, pr=pr, psl=psl):
                    b.wait(v, "t", tok[f"u{s}_{pr}"])
                    v.drain()
                    hb = hp[pr % 2]
                    for m in range(4):
                        v.tensor_tensor(hb[:, m, :], banks[m], cwb_e[:, psl], op.mult)
                    v.drain()
                    ii = v.tensor_tensor(hb.rearrange("p a c -> p (a c)"),
                                         sg_pair.rearrange("p a c -> p (a c)"),
                                         hb.rearrange("p a c -> p (a c)"), op.mult)
                    tok[f"h{s}_{pr}"] = b.inc(ii, "v")
                b.step(tensor=tu)
                b.step(vector=vh)
                if pend[0] is not None:
                    emit_dnB(*pend[0])
                pend[0] = (s, pr)
                if pr == 0 and s >= 1:
                    b.step(gpsimd=gpre)
                if pr == 0 and s + 1 < NEXP:
                    b.step(sync=ycw)
        emit_dnA(*pend[0])
        emit_dnB(*pend[0])
        pend[0] = None

        def gsh(g):
            b.wait(g, "t", tok[f"mmdone{NEXP - 1}"])
            emit_slot_load(g, NEXP + 1)
            emit_wd_load(g, NEXP)
            emit_wd_load2(g, NEXP + 1)
        b.step(gpsimd=gsh)

        h16 = h_pair

        def h17_lhs(m, tl):
            return h17_pair[:, m, tl * 128:(tl + 1) * 128]

        for pr in range(4):
            psl = slice(pr * 512, (pr + 1) * 512)
            for s_sh in (NEXP, NEXP + 1):
                wgv, wuv, _ = wslot_views(s_sh % 2)

                def tg(tt, s=s_sh, pr=pr, psl=psl, wgv=wgv):
                    b.wait(tt, f"wsl{s % 2}", tok[f"wload{s}"])
                    if s == NEXP:
                        if pr == 0:
                            b.wait(tt, "v", tok[f"h{NEXP - 1}_3"])
                        else:
                            b.wait(tt, "v", tok[f"hsh{NEXP + 1}_{pr - 1}"])
                    else:
                        b.wait(tt, "v", tok[f"hsh{NEXP}_{pr}"])
                    ii = None
                    for m in range(4):
                        for c in range(NC):
                            ii = tt.matmul(banks[m], wgv[:, c, m * 128:(m + 1) * 128],
                                           xnT_hi[:, c, psl],
                                           start=(c == 0), stop=(c == NC - 1))
                    tok["gsh"] = b.inc(ii, "t")

                def ssg(ss, s=s_sh, pr=pr):
                    b.wait(ss, "t", tok["gsh"])
                    ii = None
                    for m in range(4):
                        ii = ss.activation(sg_pair[:, m, :], banks[m], AF.Silu)
                    tok["sgsh"] = b.inc(ii, "s")

                def tu(tt, s=s_sh, pr=pr, psl=psl, wuv=wuv):
                    b.wait(tt, "s", tok["sgsh"])
                    ii = None
                    for m in range(4):
                        for c in range(NC):
                            ii = tt.matmul(banks[m], wuv[:, c, m * 128:(m + 1) * 128],
                                           xnT_hi[:, c, psl],
                                           start=(c == 0), stop=(c == NC - 1))
                    tok["ush"] = b.inc(ii, "t")

                def vh(v, s=s_sh, pr=pr):
                    b.wait(v, "t", tok["ush"])
                    v.drain()
                    ii = None
                    hb = h16 if s == NEXP else h17_pair
                    for m in range(4):
                        ii = v.tensor_tensor(hb[:, m, :],
                                             sg_pair[:, m, :], banks[m], op.mult)
                    v.drain()
                    tok[f"hsh{s}_{pr}"] = b.inc(ii, "v")
                b.step(tensor=tg, scalar=ssg)
                b.step(tensor=tu)
                b.step(vector=vh)

            for half4 in range(2):
                def tdn(tt, pr=pr, half4=half4):
                    b.wait(tt, "v", tok[f"hsh{NEXP}_{pr}"])
                    b.wait(tt, "v", tok[f"hsh{NEXP + 1}_{pr}"])
                    b.wait(tt, "wdl", tok[f"wdload{NEXP + 1}"])
                    if (pr, half4) != (0, 0):
                        b.wait(tt, "v", tok["xcmb"])
                    else:
                        b.wait(tt, "v", tok["vrB"])
                    ii = None
                    for tl in range(2 * half4, 2 * half4 + 2):
                        bb0 = 4 + 2 * (tl - 2 * half4)
                        for m in range(4):
                            lhs16 = h16[:, m, tl * 128:(tl + 1) * 128]
                            tt.matmul(banks[bb0], lhs16, wd_buf[:, m, 0:512],
                                      start=(m == 0), stop=False)
                            tt.matmul(banks[bb0 + 1], lhs16, wd_buf[:, m, 512:1024],
                                      start=(m == 0), stop=False)
                            lhs17 = h17_lhs(m, tl)
                            tt.matmul(banks[bb0], lhs17, wd_buf2[:, m, 0:512],
                                      start=False, stop=(m == 3))
                            ii = tt.matmul(banks[bb0 + 1], lhs17, wd_buf2[:, m, 512:1024],
                                           start=False, stop=(m == 3))
                    tok["dnsh"] = b.inc(ii, "t")

                def vcmb(v, pr=pr, half4=half4):
                    b.wait(v, "t", tok["dnsh"])
                    b.wait(v, "v", tok["a0b"])
                    v.drain()
                    ii = None
                    for tl in range(2 * half4, 2 * half4 + 2):
                        tglob = 4 * pr + tl
                        bb0 = 4 + 2 * (tl - 2 * half4)
                        xt = x_res[:, tglob, :]
                        for hb in range(2):
                            sl = slice(hb * 512, (hb + 1) * 512)
                            v.tensor_tensor(xt[:, sl], xt[:, sl], banks[bb0 + hb], op.add)
                            v.drain()
                            v.tensor_tensor(xt[:, sl], xt[:, sl],
                                            racc_tok[:, tglob, sl], op.add)
                            v.drain()
                            ii = v.tensor_tensor(xt[:, sl], xt[:, sl], attn0b[:, sl], op.add)
                            v.drain()
                    tok["xcmb"] = b.inc(ii, "v")
                b.step(tensor=tdn)
                b.step(vector=vcmb)
        tok["mmdone17"] = b.cnt["t"]
        tok["x1"] = tok["xcmb"]

    ph_consts()
    ph_l0a()
    if upto.startswith("l1skip"):
        tok["x1"] = tok[f"ev{NT - 1}"]
    else:
        ph_l0b()
        ph_l0c()

    if upto == "l0":
        def yout(y):
            b.wait(y, "v", tok["x1"])
            for i in range(4):
                dma = y.dma_start(out=out.rearrange("(n p) d -> p n d", p=128)[:, 4 * i:4 * i + 4, :],
                                  in_=x_res[:, 4 * i:4 * i + 4, :])
                b.inc(dma, "so", 16)
            srcs = {"xnT_hi": xnT_hi, "cw": cw_all, "attn0b": attn0b, "racc": racc,
                    "comb": comb_scr}
            for name, ap_dbg in dbg_out.items():
                nm = name.replace("dbg_", "")
                dma = y.dma_start(out=ap_dbg, in_=srcs[nm[4:] if False else nm])
                b.inc(dma, "so", 16)
            y.wait_ge(b.sem("so"), b.cnt["so"])
        b.step(sync=yout)
        return nc, dbg_out

    def emit_quant_prep(eng, ename, x_ap, bm, rs6, nblk, full=None):
        x3 = x_ap.rearrange("p (n i) -> p n i", i=16)
        eng.tensor_reduce(bm, x3, AXX, op.max, apply_absolute_value=True)
        eng.drain()
        eng.tensor_scalar(rs6, bm, 6e-12, None, op.max)
        eng.drain()
        eng.reciprocal(rs6, rs6)
        eng.drain()
        eng.tensor_scalar(rs6, rs6, 6.0, None, op.mult)
        eng.drain()
        ii = eng.reciprocal(bm, rs6)
        eng.drain()
        if full is not None:
            rs6F, scF = full
            W = nblk * 16
            eng.tensor_scalar(rs6F[:, 0:W].rearrange("p (n i) -> p n i", i=16),
                              rs6.rearrange("p n -> p n ()").broadcast_to([128, nblk, 16]),
                              2.0, None, op.mult)
            eng.drain()
            ii = eng.tensor_copy(scF[:, 0:W].rearrange("p (n i) -> p n i", i=16),
                                 bm.rearrange("p n -> p n ()").broadcast_to([128, nblk, 16]))
            eng.drain()
        return b.inc(ii, ename)

    def emit_quant_bulk(eng, ename, x_ap, out_ap, qa, qb, bm, rs6, nblk, full=None):
        if full is None:
            rs6b = rs6.rearrange("p n -> p n ()").broadcast_to([128, nblk, 16])
            qa3 = qa.rearrange("p (n i) -> p n i", i=16)
            eng.tensor_scalar(qa, x_ap, -1.0, None, op.mult)
            eng.drain()
            eng.tensor_tensor(qa, qa, x_ap, op.max)
            eng.drain()
            eng.tensor_tensor(qa3, qa3, rs6b, op.mult)
            eng.drain()
            eng.tensor_scalar(qb, qa, 5.0, 2.0, op.is_ge, op.mult)
            eng.drain()
            eng.scalar_tensor_tensor(qb, qa, 2.5, qb, op.is_ge, op.add)
            eng.drain()
            eng.scalar_tensor_tensor(qb, qa, 3.5, qb, op.is_ge, op.add)
            eng.drain()
            eng.tensor_scalar(qb, qb, 2.0, None, op.mult)
            eng.drain()
            for t_ in (0.25, 0.75, 1.25, 1.75):
                eng.scalar_tensor_tensor(qb, qa, t_, qb, op.is_ge, op.add)
                eng.drain()
            eng.tensor_scalar(qa, x_ap, 0.0, 1.0, op.is_ge, op.subtract)
            eng.drain()
            eng.tensor_scalar(qa, qa, 2.0, 1.0, op.mult, op.add)
            eng.drain()
            scb = bm.rearrange("p n -> p n ()").broadcast_to([128, nblk, 16])
            eng.tensor_scalar(qa, qa, 0.5, None, op.mult)
            eng.drain()
            eng.tensor_tensor(qa3, qa3, scb, op.mult)
            eng.drain()
            ii = eng.tensor_tensor(out_ap, qb, qa, op.mult)
            eng.drain()
            return b.inc(ii, ename)
        rs6F, scF = full
        W = nblk * 16
        qc = rs6F
        eng.tensor_scalar(qb, x_ap, 0.0, 0.5, op.is_ge, op.subtract)
        eng.drain()
        eng.tensor_tensor(qa, x_ap, qb[:, 0:W], op.mult)
        eng.drain()
        eng.tensor_tensor(qa, qa, rs6F[:, 0:W], op.mult)
        eng.drain()
        eng.tensor_tensor(qb, qb, scF[:, 0:W], op.mult)
        eng.drain()
        eng.tensor_scalar(qc[:, 0:W], qa, 5.0, 2.0, op.is_ge, op.mult)
        eng.drain()
        for t_, w_ in ((2.5, 1.0), (3.5, 1.0)):
            eng.tensor_scalar(scF[:, 0:W], qa, t_, w_, op.is_ge, op.mult) \
                if False else None
        for t_ in (2.5, 3.5):
            eng.tensor_scalar(scF[:, 0:W], qa, t_, None, op.is_ge)
            eng.drain()
            eng.tensor_tensor(qc[:, 0:W], qc[:, 0:W], scF[:, 0:W], op.add)
            eng.drain()
        eng.tensor_scalar(qc[:, 0:W], qc[:, 0:W], 2.0, None, op.mult)
        eng.drain()
        for t_ in (0.25, 0.75, 1.25, 1.75):
            eng.tensor_scalar(scF[:, 0:W], qa, t_, None, op.is_ge)
            eng.drain()
            eng.tensor_tensor(qc[:, 0:W], qc[:, 0:W], scF[:, 0:W], op.add)
            eng.drain()
        ii = eng.tensor_tensor(out_ap, qc[:, 0:W], qb[:, 0:W], op.mult)
        eng.drain()
        return b.inc(ii, ename)

    def emit_quant(eng, ename, x_ap, out_ap, qa, qb, bm, rs6, nblk):
        F = nblk * 16
        x3 = x_ap.rearrange("p (n i) -> p n i", i=16)
        eng.tensor_reduce(bm, x3, AXX, op.max, apply_absolute_value=True)
        eng.drain()
        eng.tensor_scalar(rs6, bm, 6e-12, None, op.max)
        eng.drain()
        eng.reciprocal(rs6, rs6)
        eng.drain()
        eng.tensor_scalar(rs6, rs6, 6.0, None, op.mult)
        eng.drain()
        rs6b = rs6.rearrange("p n -> p n ()").broadcast_to([128, nblk, 16])
        qa3 = qa.rearrange("p (n i) -> p n i", i=16)
        eng.tensor_scalar(qa, x_ap, -1.0, None, op.mult)
        eng.drain()
        eng.tensor_tensor(qa, qa, x_ap, op.max)
        eng.drain()
        eng.tensor_tensor(qa3, qa3, rs6b, op.mult)
        eng.drain()
        eng.tensor_scalar(qb, qa, 5.0, 2.0, op.is_ge, op.mult)
        eng.drain()
        eng.scalar_tensor_tensor(qb, qa, 2.5, qb, op.is_ge, op.add)
        eng.drain()
        eng.scalar_tensor_tensor(qb, qa, 3.5, qb, op.is_ge, op.add)
        eng.drain()
        eng.tensor_scalar(qb, qb, 2.0, None, op.mult)
        eng.drain()
        for t_ in (0.25, 0.75, 1.25, 1.75):
            eng.scalar_tensor_tensor(qb, qa, t_, qb, op.is_ge, op.add)
            eng.drain()
        eng.reciprocal(rs6, rs6)
        eng.drain()
        eng.tensor_scalar(qa, x_ap, 0.0, 1.0, op.is_ge, op.subtract)
        eng.drain()
        eng.tensor_scalar(qa, qa, 2.0, 1.0, op.mult, op.add)
        eng.drain()
        scb = rs6.rearrange("p n -> p n ()").broadcast_to([128, nblk, 16])
        eng.tensor_scalar(qa, qa, 0.5, None, op.mult)
        eng.drain()
        eng.tensor_tensor(qa3, qa3, scb, op.mult)
        eng.drain()
        ii = eng.tensor_tensor(out_ap, qb, qa, op.mult)
        eng.drain()
        return b.inc(ii, ename)

    xn2T = xnT_hi
    qa_v = bcast_a
    qb_v = attn0b
    bm_v = l1scr[:, 0:64]
    rs6_v = l1scr[:, 64:128]

    def ph_l1a():
        ph_ln_stats("1", gate=("v", tok["x1"]))
        for t in range(l1a_tiles):
            scr = xn_scr[:, t % 2, :]
            p0, p1 = banks[t % 2], banks[2 + t % 2]

            def v(v, t=t, scr=scr):
                b.wait(v, "v", tok["rstd1"])
                if t >= 2:
                    b.wait(v, "t", tok[f"l1tr{t - 2}"])
                v.drain()
                ii = v.tensor_scalar(scr, x_res[:, t, :], mv_all[:, t, 0:1],
                                     rstd_all[:, t:t + 1], op.subtract, op.mult)
                tok[f"l1ln{t}"] = b.inc(ii, "v")

            def y(y, t=t):
                b.wait(y, "v", tok[f"l1ln{t}"])
                dma = y.dma_start(out=xspill.rearrange("(n p) d -> p n d", p=128)[:, t, :],
                                  in_=x_res[:, t, :])
                b.inc(dma, "sxp", 16)

            def tt(tt, t=t, scr=scr, p0=p0, p1=p1):
                b.wait(tt, "v", tok[f"l1ln{t}"])
                if t >= 2:
                    b.wait(tt, "v", tok[f"l1ev{t - 2}"])
                ii = None
                for c in range(8):
                    half = p0 if c < 4 else p1
                    ii = tt.transpose(half[:, (c % 4) * 128:(c % 4) * 128 + 128],
                                      scr[:, c * 128:c * 128 + 128], ident_f)
                tok[f"l1tr{t}"] = b.inc(ii, "t")

            def v2(v, t=t, p0=p0, p1=p1):
                b.wait(v, "t", tok[f"l1tr{t}"])
                ii = None
                for h, half in ((0, p0), (1, p1)):
                    hi = xn2T[:, 4 * h:4 * h + 4, t * 128:(t + 1) * 128]
                    ii = v.tensor_copy(hi, half.rearrange("p (c x) -> p c x", c=4))
                tok[f"l1ev{t}"] = b.inc(ii, "v")

            b.step(vector=v)
            if l1a_parts >= 2:
                b.step(sync=y, tensor=tt, vector=v2)
            else:
                b.step(tensor=tt, vector=v2)

            if l1a_parts < 3:
                continue
            gq = False

            def vq(v, t=t, scr=scr, gq=gq):
                b.wait(v, "v", tok[f"l1ln{t}"])
                if t >= 2:
                    b.wait(v, "t", tok[f"xqtr{t - 2}"])
                v.drain()
                bm_, rs_ = (bm_ga, rs6_ga) if gq else (bm_v, rs6_v)
                if gq and "gq_last" in tok:
                    b.wait(v, "g", tok["gq_last"])
                tok[f"qp{t}"] = emit_quant_prep(v, "v", scr, bm_, rs_, 64,
                                                full=None if not gq else (rs6F_a, scF_a))
                if not gq:
                    tok[f"q{t}"] = emit_quant_bulk(v, "v", scr, qb_v,
                                                   qa_v, qb_v, bm_v, rs6_v, 64)

            def gqf(g, t=t, scr=scr):
                b.wait(g, "v", tok[f"qp{t}"])
                if t >= 2:
                    b.wait(g, "t", tok[f"xqtr{t - 2}"])
                tok[f"q{t}"] = emit_quant_bulk(g, "g", scr, qbG_a,
                                               qaG_a, qbG_a, bm_ga, rs6_ga, 64,
                                               full=(rs6F_a, scF_a))
                tok["gq_last"] = tok[f"q{t}"]

            def tq(tt, t=t, gq=gq):
                b.wait(tt, "g" if gq else "v", tok[f"q{t}"])
                if t >= 2:
                    b.wait(tt, "v", tok[f"xqev{t - 2}"])
                pb = banks[4 + t % 2]
                src_q = qbG_a if gq else qb_v
                ii = None
                for c in range(8):
                    ii = tt.transpose(pb[:, (c % 4) * 128:(c % 4) * 128 + 128]
                                      if c < 4 else banks[6 + t % 2][:, (c % 4) * 128:(c % 4) * 128 + 128],
                                      src_q[:, c * 128:c * 128 + 128], ident_f)
                tok[f"xqtr{t}"] = b.inc(ii, "t")

            def vq2(v, t=t):
                b.wait(v, "t", tok[f"xqtr{t}"])
                b.wait(v, "sw0", b.cnt.get("sw0", 0))
                b.wait(v, "sg2", b.cnt.get("sg2", 0))
                if t >= 2:
                    b.wait(v, f"sq{t % 2}", tok[f"sqd{t - 2}"])
                v.drain()
                v.tensor_copy(xq_tileA[t % 2][:, 0:4, :],
                              banks[4 + t % 2].rearrange("p (c x) -> p c x", c=4))
                ii = v.tensor_copy(xq_tileA[t % 2][:, 4:8, :],
                                   banks[6 + t % 2].rearrange("p (c x) -> p c x", c=4))
                tok[f"xqev{t}"] = b.inc(ii, "v")

            def yq(y, t=t):
                b.wait(y, "v", tok[f"xqev{t}"])
                dma = y.dma_start(out=xqspill[:, :, t * 128:(t + 1) * 128],
                                  in_=xq_tileA[t % 2])
                tok[f"sqd{t}"] = b.inc(dma, f"sq{t % 2}", 16)
            b.step(vector=vq)
            if gq:
                b.step(gpsimd=gqf)
            if l1a_parts >= 4:
                b.step(tensor=tq, vector=vq2)
            if l1a_parts >= 5:
                b.step(sync=yq)
        tok["spilled"] = b.cnt.get("sxp", 0)
        tok["xq_spilled"] = (b.cnt.get("sq0", 0), b.cnt.get("sq1", 0))
        tok["l1a_v"] = b.cnt["v"]

    wqb = SBn("wqb", [128, NC, 256], BF16, at=ws0_off)
    wkb = SBn("wkb", [128, NC, 256], BF16, at=ws0_off + 4096)
    wvb = SBn("wvb", [128, NC, D], BF16, at=ws0_off + 8192)
    wob = SBn("wob", [128, NC, D], BF16, at=ws1_off + 8192)
    bo_f = mv_all.rearrange("p a c -> p (a c)")[:, 0:8]
    sq_s = SBn("sq_s", [128, 256], F32, at=wrh_off)
    qpad = attn0b[:, 0:512]
    kpad = attn0b[:, 512:1024]
    bqb = l1scr[:, 0:256]
    bkb = l1scr[:, 256:512]
    qpb_s = l1scr[:, 512:768]
    bvb = bcast_a

    def ph_l1b():
        def g(g):
            b.wait(g, "t", tok["mmdone17"] if "mmdone17" in tok else 0)
            for dst, src in ((wqb, Wq1), (wkb, Wk1)):
                dma = g.dma_start(out=dst, in_=src.rearrange("(c p) w -> p c w", p=128))
                b.inc(dma, "wl1", 16)
            for dst, src in ((wvb, Wv1), (wob, Wo1)):
                dma = g.dma_start(out=dst, in_=src.rearrange("(c p) w -> p c w", p=128))
                b.inc(dma, "wl1", 16)
            tok["wl1"] = b.cnt["wl1"]

        def y(y):
            b.wait(y, "v", tok["x1"])
            if "mmdone17" in tok:
                b.wait(y, "t", tok["mmdone17"])
            dma = y.dma_start(out=row_scr[0:1, 0:256], in_=bq1[None, :])
            b.inc(dma, "sb1", 16)
            dma = y.dma_start(out=row_scr[0:1, 256:512], in_=qph.rearrange("a h b w -> a (h b w)"))
            b.inc(dma, "sb1", 16)
            dma = y.dma_start(out=row_scr[0:1, 512:768], in_=bk1[None, :])
            b.inc(dma, "sb1", 16)
            dma = y.dma_start(out=row_scr[0:1, 768:1024], in_=kph.rearrange("a h b w -> a (h b w)"))
            b.inc(dma, "sb1", 16)
            dma = y.dma_start(out=bo_f, in_=bo1.rearrange("(c p) -> p c", p=128),
                              allow_slow_non_contiguous=True)
            b.inc(dma, "sb1", 16)
            tok["sb1"] = b.cnt["sb1"]

        def v(v):
            b.wait(v, "sb1", tok["sb1"])
            b.wait(v, "v", tok["l1a_v"])
            v.drain()
            v.tensor_tensor(row_scr[0:1, 0:256], row_scr[0:1, 0:256],
                            row_scr[0:1, 256:512], op.add)
            v.tensor_tensor(row_scr[0:1, 512:768], row_scr[0:1, 512:768],
                            row_scr[0:1, 768:1024], op.add)
            v.drain()
            ii = v.memset(qpad, 0.0)
            ii = v.memset(kpad, 0.0)
            tok["brow"] = b.inc(ii, "v")

        def t2(tt):
            b.wait(tt, "v", tok["brow"])
            tt.matmul(banks[0][:, 0:256], ones_row, row_scr[0:1, 0:256], start=True, stop=True)
            tt.matmul(banks[0][:, 256:512], ones_row, row_scr[0:1, 512:768], start=True, stop=True)
            ii = tt.matmul(banks[1][0:128, 0:1], ones_row, rtemp[0:1, 0:1], start=True, stop=True)
            tok["brep"] = b.inc(ii, "t")

        def v2(v):
            b.wait(v, "t", tok["brep"])
            v.drain()
            v.tensor_copy(bqb, banks[0][:, 0:256])
            v.tensor_copy(bkb, banks[0][:, 256:512])
            ii = v.tensor_copy(rtemp, banks[1][:, 0:1])
            tok["bqkb"] = b.inc(ii, "v")

        b.step(gpsimd=g, sync=y)
        b.step(vector=v)
        b.step(tensor=t2, vector=v2)

        def y3(y):
            b.wait(y, "v", tok["bqkb"])
            b.wait(y, "t", tok["brep"])
            dma = y.dma_start(out=row_scr[0:1, :], in_=bv1[None, :])
            b.inc(dma, "sb1", 16)
            tok["sbv"] = b.cnt["sb1"]

        def t3(tt):
            b.wait(tt, "sb1", tok["sbv"])
            tt.matmul(banks[0], ones_row, row_scr[0:1, 0:512], start=True, stop=True)
            ii = tt.matmul(banks[1], ones_row, row_scr[0:1, 512:1024], start=True, stop=True)
            tok["bvrep"] = b.inc(ii, "t")

        def v3(v):
            b.wait(v, "t", tok["bvrep"])
            b.wait(v, "v", tok["l1a_v"])
            b.wait(v, "sq0", tok["xq_spilled"][0])
            b.wait(v, "sq1", tok["xq_spilled"][1])
            v.drain()
            v.tensor_copy(bvb[:, 0:512], banks[0])
            ii = v.tensor_copy(bvb[:, 512:1024], banks[1])
            tok["bvb"] = b.inc(ii, "v")
        b.step(sync=y3)
        b.step(tensor=t3, vector=v3)

        for t in range(NT):
            def tp(tt, t=t):
                b.wait(tt, "wl1", tok["wl1"])
                b.wait(tt, "v", max(tok["bvb"], tok[f"l1ev{t}"]))
                if t >= 1:
                    b.wait(tt, "v", tok[f"vev{t - 1}"])
                    b.wait(tt, "v", tok[f"qkn{t - 1}"])
                ii = None
                for c in range(NC):
                    lhs = xn2T[:, c, t * 128:(t + 1) * 128]
                    tt.matmul(banks[0][:, 0:256], lhs, wqb[:, c, :],
                              start=(c == 0), stop=(c == NC - 1))
                    tt.matmul(banks[1], lhs, wvb[:, c, 0:512],
                              start=(c == 0), stop=(c == NC - 1))
                    tt.matmul(banks[2], lhs, wvb[:, c, 512:1024],
                              start=(c == 0), stop=(c == NC - 1))
                for c in range(NC):
                    lhs = xn2T[:, c, t * 128:(t + 1) * 128]
                    ii = tt.matmul(banks[0][:, 256:512], lhs, wkb[:, c, :],
                                   start=(c == 0), stop=(c == NC - 1))
                tok[f"proj{t}"] = b.inc(ii, "t")

            def vv(v, t=t):
                b.wait(v, "t", tok[f"proj{t}"])
                b.wait(v, "sxp", tok["spilled"])
                v.drain()
                v.scalar_tensor_tensor(Vbuf[:, t, 0:512], banks[1], 0.0,
                                       bvb[:, 0:512], op.bypass, op.add)
                ii = v.scalar_tensor_tensor(Vbuf[:, t, 512:1024], banks[2], 0.0,
                                            bvb[:, 512:1024], op.bypass, op.add)
                tok[f"vev{t}"] = b.inc(ii, "v")

            b.step(tensor=tp)
            b.step(vector=vv)

            for qk, bias_b, pslc in (("q", bqb, slice(0, 256)), ("k", bkb, slice(256, 512))):
                def vn(v, t=t, qk=qk, bias_b=bias_b, pslc=pslc):
                    b.wait(v, "t", tok[f"proj{t}"])
                    b.wait(v, "sq0", tok["xq_spilled"][0])
                    b.wait(v, "sq1", tok["xq_spilled"][1])
                    if t >= 1:
                        b.wait(v, "t", tok[f"qktr{t - 1}"])
                    v.drain()
                    v.tensor_tensor(qpb_s, banks[0][:, pslc], bias_b, op.add)
                    v.drain()
                    ii = v.tensor_tensor(sq_s, qpb_s, qpb_s, op.mult)
                    tok["_sq"] = b.inc(ii, "v")

                b.step(vector=vn)

                def vr2(v, t=t, qk=qk):
                    sm3 = sq_s.rearrange("p (h w) -> p h w", w=16)
                    v.drain()
                    v.tensor_reduce(a08[:, 0:8], sm3[:, 0:8, :], AXX, op.add)
                    v.tensor_reduce(val8[:, 0:8], sm3[:, 8:16, :], AXX, op.add)
                    v.drain()
                    v.tensor_scalar(a08, a08, 1e-24, None, op.max)
                    ii = v.tensor_scalar(val8, val8, 1e-24, None, op.max)
                    tok["_ssq"] = b.inc(ii, "v")

                def sq2(s_, t=t):
                    b.wait(s_, "v", tok["_ssq"])
                    s_.activation(a08, a08, AF.Sqrt)
                    ii = s_.activation(val8, val8, AF.Sqrt)
                    tok["_sqr"] = b.inc(ii, "s")

                def vr3(v, t=t, qk=qk):
                    b.wait(v, "s", tok["_sqr"])
                    v.drain()
                    v.reciprocal(a08, a08)
                    v.reciprocal(val8, val8)
                    v.drain()
                    dst = (qpad if qk == "q" else kpad)
                    d3 = dst.rearrange("p (h z) -> p h z", z=32)[:, :, 0:16]
                    q3 = qpb_s.rearrange("p (h w) -> p h w", w=16)
                    rn8a = a08.rearrange("p c -> p c ()").broadcast_to([128, 8, 16])
                    rn8b = val8.rearrange("p c -> p c ()").broadcast_to([128, 8, 16])
                    v.tensor_tensor(d3[:, 0:8, :], q3[:, 0:8, :], rn8a, op.mult)
                    ii = v.tensor_tensor(d3[:, 8:16, :], q3[:, 8:16, :], rn8b, op.mult)
                    tok[f"nrm{qk}{t}"] = b.inc(ii, "v")
                b.step(vector=vr2)
                b.step(scalar=sq2)
                b.step(vector=vr3)
            tok[f"qkn{t}"] = tok[f"nrmk{t}"]

            def tq2(tt, t=t):
                b.wait(tt, "v", tok[f"qkn{t}"])
                if t >= 1:
                    b.wait(tt, "v", tok[f"qkev{t - 1}"])
                for g4 in range(4):
                    tt.transpose(banks[3][:, g4 * 128:(g4 + 1) * 128],
                                 qpad[:, g4 * 128:(g4 + 1) * 128], ident_f)
                ii = None
                for g4 in range(4):
                    ii = tt.transpose(banks[4][:, g4 * 128:(g4 + 1) * 128],
                                      kpad[:, g4 * 128:(g4 + 1) * 128], ident_f)
                tok[f"qktr{t}"] = b.inc(ii, "t")

            def vq3(v, t=t):
                b.wait(v, "t", tok[f"qktr{t}"])
                v.drain()
                v.tensor_copy(qTpad[:, :, t * 128:(t + 1) * 128],
                              banks[3].rearrange("p (g x) -> p g x", g=4))
                ii = v.tensor_copy(kTpad[:, :, t * 128:(t + 1) * 128],
                                   banks[4].rearrange("p (g x) -> p g x", g=4))
                tok[f"qkev{t}"] = b.inc(ii, "v")
            b.step(tensor=tq2)
            b.step(vector=vq3)
        tok["qkdone"] = tok[f"qkev{NT - 1}"]
        tok["vdone"] = tok[f"vev{NT - 1}"]

        def vz(v):
            ii = None
            for var, idx in ((1, 6), (1, 7), (2, 8), (2, 9), (3, 10), (3, 11)):
                ii = v.memset(P_pool[idx][:, 0:128 * var], 0.0)
            tok["pz"] = b.inc(ii, "v")
        b.step(vector=vz)

        plain_i = [0, 3]
        var_i = {1: 0, 2: 0, 3: 0}
        p_tok = {}
        pend_tail = [None]

        for c in range(4):
            for hg in range(HH // 2):
                lane_em = []
                njt = 4 * c + 4
                for lane in range(2):
                    h = 2 * hg + lane
                    a = 32 * (h % 4)
                    g4 = h // 4
                    pvb = banks[4 + h % 2]
                    dnb = banks[6 + h % 2][0:1, 0:512]
                    emitters = []
                    for j in range(njt):
                        diag = j >= 4 * c
                        ov = j - 4 * c if diag else 0
                        if diag and ov > 0:
                            pi = 6 + (ov - 1) * 2 + lane
                        else:
                            pi = plain_i[lane]
                            plain_i[lane] = 3 * lane + (plain_i[lane] + 1) % 3
                        P = P_pool[pi]
                        bi = 2 * lane + j % 2
                        sb_ = banks[bi]

                        def ts_(tt, c=c, h=h, j=j, a=a, g4=g4, sb_=sb_, pi=pi, bi=bi):
                            b.wait(tt, "v", max(tok["qkdone"], tok["vdone"], tok["pz"]))
                            if c > 0 and h <= 1 and j == 0:
                                b.wait(tt, "v", tok[f"oev{c - 1}_7"])
                            b.wait(tt, "s", p_tok.get(("exp_prev", bi), 0))
                            ii = tt.matmul(sb_, kTpad[a:a + 16, g4, j * 128:(j + 1) * 128],
                                           qTpad[a:a + 16, g4, c * 512:(c + 1) * 512],
                                           start=True, stop=True, tile_position=(a, 0))
                            p_tok[("qk", c, h, j)] = b.inc(ii, "t")

                        def se_(s_, c=c, h=h, j=j, sb_=sb_, P=P, ov=ov, diag=diag,
                                pi=pi, bi=bi):
                            b.wait(s_, "t", p_tok[("qk", c, h, j)])
                            b.wait(s_, "t", p_tok.get(("pv_prevuse", pi), 0))
                            lo = 128 * ov if diag else 0
                            ii = s_.activation(P[:, lo:512], sb_[:, lo:512], AF.Exp,
                                               bias=neg1, scale=rtemp)
                            p_tok[("exp", c, h, j)] = b.inc(ii, "s")
                            p_tok[("exp_prev", bi)] = b.cnt["s"]

                        def gm_(g_, c=c, h=h, j=j, P=P, ov=ov):
                            b.wait(g_, "s", p_tok[("exp", c, h, j)])
                            ii = g_.affine_select(out=P[:, 128 * ov:128 * ov + 128],
                                                  in_=P[:, 128 * ov:128 * ov + 128],
                                                  compare_op=op.is_ge, fill=0.0, base=0,
                                                  pattern=[[1, 128]], channel_multiplier=-1)
                            p_tok[("msk", c, h, j)] = b.inc(ii, "g")

                        def tpv_(tt, c=c, h=h, j=j, P=P, pvb=pvb, dnb=dnb, diag=diag, pi=pi):
                            if diag:
                                b.wait(tt, "g", p_tok[("msk", c, h, j)])
                            else:
                                b.wait(tt, "s", p_tok[("exp", c, h, j)])
                            if j == 0:
                                hp = h - 2 if h >= 2 else (HH - 2 + h if c > 0 else None)
                                cp = c if h >= 2 else c - 1
                                if hp is not None:
                                    b.wait(tt, "v", tok[f"at{cp}_{hp}"])
                            tt.matmul(pvb[0:64, :], Vbuf[:, j, h * 64:(h + 1) * 64], P,
                                      start=(j == 0), stop=(j == 4 * c + 3))
                            ii = tt.matmul(dnb, ones_colb, P,
                                           start=(j == 0), stop=(j == 4 * c + 3))
                            p_tok[("pv", c, h, j)] = b.inc(ii, "t")
                            p_tok[("pv_prevuse", pi)] = b.cnt["t"]
                        emitters.append((ts_, se_, gm_ if diag else None, tpv_))
                    lane_em.append(emitters)

                b.step(tensor=lane_em[0][0][0])
                b.step(tensor=lane_em[1][0][0])
                if njt > 1:
                    b.step(tensor=lane_em[0][1][0])
                    b.step(tensor=lane_em[1][1][0])
                if pend_tail[0] is not None:
                    for st_ in pend_tail[0]:
                        st_()
                    pend_tail[0] = None
                for j in range(njt):
                    for lane in range(2):
                        em = lane_em[lane]
                        ts_n = em[j + 2][0] if j + 2 < njt else None
                        se_j, gm_j, tpv_j = em[j][1], em[j][2], em[j][3]
                        b.step(scalar=se_j)
                        if gm_j is not None:
                            b.step(gpsimd=gm_j)

                        def tcomb(tt, tpv_j=tpv_j, ts_n=ts_n):
                            tpv_j(tt)
                            if ts_n is not None:
                                ts_n(tt)
                        b.step(tensor=tcomb)

                tails = []
                for lane in range(2):
                    h = 2 * hg + lane
                    pvb = banks[4 + h % 2]
                    dnb = banks[6 + h % 2][0:1, 0:512]
                    rbb = banks[6 + h % 2]

                    def vd_(v, c=c, h=h, dnb=dnb):
                        b.wait(v, "t", p_tok[("pv", c, h, 4 * c + 3)])
                        if (c, h) != (0, 0):
                            b.wait(v, "t", tok.get("rbrep0", 0))
                        v.drain()
                        ii = v.reciprocal(row_scr[0:1, 0:512], dnb)
                        tok["_rd"] = b.inc(ii, "v")

                    def td_(tt, c=c, h=h, rbb=rbb):
                        b.wait(tt, "v", tok["_rd"])
                        ii = tt.matmul(rbb[64:128, :], ones_row[0:1, 0:64],
                                       row_scr[0:1, 0:512], start=True, stop=True,
                                       tile_position=(0, 64))
                        tok["rbrep0"] = b.inc(ii, "t")

                    def va_(v, c=c, h=h, pvb=pvb, rbb=rbb):
                        b.wait(v, "t", tok["rbrep0"])
                        v.drain()
                        v.tensor_copy(l1scr[0:64, 0:512], rbb[64:128, :])
                        v.drain()
                        ii = v.scalar_tensor_tensor(
                            attnT[64 * (h % 2):64 * (h % 2) + 64, h // 2, :],
                            pvb[0:64, :], 0.0, l1scr[0:64, 0:512], op.bypass, op.mult)
                        tok[f"at{c}_{h}"] = b.inc(ii, "v")
                    tails.extend((lambda vd_=vd_: b.step(vector=vd_),
                                  lambda td_=td_: b.step(tensor=td_),
                                  lambda va_=va_: b.step(vector=va_)))
                pend_tail[0] = tuple(tails)

            if pend_tail[0] is not None:
                for st_ in pend_tail[0]:
                    st_()
                pend_tail[0] = None
            for o in range(8):
                ob = banks[o % 2]

                def to_(tt, c=c, o=o, ob=ob):
                    b.wait(tt, "v", tok[f"at{c}_{HH - 1}"])
                    if o >= 2:
                        b.wait(tt, "v", tok[f"oev{c}_{o - 2}"])
                    ii = None
                    for cin in range(NC):
                        ii = tt.matmul(ob, wob[:, cin, o * 128:(o + 1) * 128], attnT[:, cin, :],
                                       start=(cin == 0), stop=(cin == NC - 1))
                    tok["_opj"] = b.inc(ii, "t")

                def vo_(v, c=c, o=o, ob=ob):
                    b.wait(v, "t", tok["_opj"])
                    v.drain()
                    ii = v.tensor_scalar(bvb[:, 0:512], ob, bo_f[:, o:o + 1], None, op.add)
                    tok["_oev"] = b.inc(ii, "v")

                def to2_(tt, c=c, o=o):
                    b.wait(tt, "v", tok["_oev"])
                    if o >= 1:
                        b.wait(tt, "v", tok[f"oev{c}_{o - 1}"])
                    ii = None
                    for tb in range(4):
                        ii = tt.transpose(banks[2][:, tb * 128:(tb + 1) * 128],
                                          bvb[:, 0:512][:, tb * 128:(tb + 1) * 128], ident_f)
                    tok["_otr"] = b.inc(ii, "t")

                def vo2_(v, c=c, o=o):
                    b.wait(v, "t", tok["_otr"])
                    v.drain()
                    ii = v.tensor_copy(attn1_tok[:, 4 * c:4 * c + 4, o * 128:(o + 1) * 128],
                                       banks[2].rearrange("p (z x) -> p z x", z=4))
                    tok[f"oev{c}_{o}"] = b.inc(ii, "v")
                b.step(tensor=to_)
                b.step(vector=vo_)
                b.step(tensor=to2_)
                b.step(vector=vo2_)
        tok["attn_done"] = tok[f"oev3_7"]
        tok["wq_gate_t"] = b.cnt["t"]

    qaG = SBn("qaG", [128, 1024], F32, at=row_off)
    qbG = SBn("qbG", [128, 1024], F32, at=xns_off + 4096)
    qa_c = SBn("qa_c", [128, 1024], F32, at=sg_off)
    qb_c = SBn("qb_c", [128, 1024], F32, at=xns_off)
    bm_c = SBn("bm_c", [128, 64], F32, at=cw_off)
    rs6_c = SBn("rs6_c", [128, 64], F32, at=cw_off + 256)
    bm_g = SBn("bm_g", [128, 64], F32, at=cw_off + 512)
    rs6_g = SBn("rs6_g", [128, 64], F32, at=cw_off + 768)
    bm_d = sq_s[:, 0:32]
    rs6_d = sq_s[:, 32:64]
    qaf = SBn("qaf", [128, 512], F32, at=sg_off + 4096)
    qbf = SBn("qbf", [128, 512], F32, at=cw_off)

    def ph_l1c():
        pieces = []
        for snum in range(22):
            pieces.append(("g", snum, 1024))
        for snum in range(22):
            pieces.append(("u", snum, 1024))
        for snum in range(8):
            for qq in range(3):
                pieces.append(("d", (snum, qq), 1024 if qq < 2 else 768))

        pv_tok = {}
        qpair = [(qa_c, qb_c), (qaG, qbG)]
        pend_evac = [None]
        for idx, (kind, meta, W) in enumerate(pieces):
            par = idx % 2
            eng_v = True
            nblk = W // 128

            def yl(y, idx=idx, kind=kind, meta=meta, par=par, W=W):
                b.wait(y, "v", pv_tok.get(("cons", par), 0))
                b.wait(y, "g", pv_tok.get(("consG", par), 0))
                if idx < 2:
                    b.wait(y, "v", tok["attn_done"])
                    b.wait(y, "t", tok["wq_gate_t"])
                    b.wait(y, "sq0", tok["xq_spilled"][0])
                    b.wait(y, "sq1", tok["xq_spilled"][1])
                if kind == "d":
                    snum, qq = meta
                    dma = y.dma_start(out=wstr[par][:, 0:W],
                                      in_=wd1[128 * snum:128 * (snum + 1),
                                              qq * 1024:qq * 1024 + W])
                elif kind == "g":
                    dma = y.dma_start(out=wstr[par][:, 0:1024],
                                      in_=wg1[128 * meta:128 * (meta + 1), :])
                else:
                    dma = y.dma_start(out=wstr[par][:, 0:1024],
                                      in_=wu1[128 * meta:128 * (meta + 1), :])
                b.inc(dma, f"wst{par}", 16)
                pv_tok[("ld", idx)] = b.cnt[f"wst{par}"]

            def qprep(v, idx=idx, par=par, W=W, eng_v=eng_v):
                b.wait(v, f"wst{par}", pv_tok[("ld", idx)])
                if idx < 4:
                    b.wait(v, "v", tok["attn_done"])
                if not eng_v:
                    b.wait(v, "g", pv_tok.get("gbulk", 0))
                bm_, rs_ = (bm_c, rs6_c) if eng_v else (bm_g, rs6_g)
                pv_tok[("prep", idx)] = emit_quant_prep(
                    v, "v", wstr[par][:, 0:W], bm_[:, 0:W // 16],
                    rs_[:, 0:W // 16], W // 16,
                    full=None if eng_v else (rs6F_c, scF_c))
                pv_tok[("cons", par)] = pv_tok[("prep", idx)]

            def qbulk(eng, idx=idx, par=par, W=W):
                b.wait(eng, "t", pv_tok.get(("wqfree", par), 0))
                qa_, qb_ = qpair[par]
                iitok = emit_quant_bulk(eng, "v", wstr[par][:, 0:W], qb_[:, 0:W],
                                        qa_[:, 0:W], qb_[:, 0:W],
                                        bm_c[:, 0:W // 16], rs6_c[:, 0:W // 16],
                                        W // 16)
                pv_tok[("q", idx)] = ("v", iitok)
                pv_tok[("cons", par)] = iitok

            b.step(sync=yl)
            b.step(vector=qprep)
            b.step(vector=qbulk)
            if pend_evac[0] is not None:
                for f in pend_evac[0]:
                    f()
                pend_evac[0] = None

            wq_ = qpair[par][1]
            nbat = (nblk + 3) // 4
            evs = []
            for bb in range(nbat):
                blks = list(range(4 * bb, min(4 * bb + 4, nblk)))
                pb = banks[(idx * nbat + bb) % 4]

                def tt_(tt, idx=idx, bb=bb, blks=blks, pb=pb, wq_=wq_):
                    en, it = pv_tok[("q", idx)]
                    b.wait(tt, en, it)
                    b.wait(tt, "v", pv_tok.get(("bank_free", pb.tensor.name), 0))
                    ii = None
                    for zi, cb in enumerate(blks):
                        ii = tt.transpose(pb[:, zi * 128:zi * 128 + 128],
                                          wq_[:, cb * 128:(cb + 1) * 128], ident_f)
                    pv_tok[("tr", idx, bb)] = b.inc(ii, "t")

                def vv_(v, idx=idx, bb=bb, blks=blks, pb=pb, kind=kind, meta=meta,
                        nbat=nbat):
                    b.wait(v, "t", pv_tok[("tr", idx, bb)])
                    v.drain()
                    ii = None
                    for zi, cb in enumerate(blks):
                        srcb = pb[:, zi * 128:zi * 128 + 128]
                        if kind == "g":
                            dst = wgqT[:, cb, meta * 128:(meta + 1) * 128]
                        elif kind == "u":
                            dst = wuqT[:, cb, meta * 128:(meta + 1) * 128]
                        else:
                            snum, qq = meta
                            dst = wdqT[:, qq * 8 + cb, snum * 128:(snum + 1) * 128]
                        ii = v.tensor_copy(dst, srcb)
                    pv_tok[("bank_free", pb.tensor.name)] = b.inc(ii, "v")
                b.step(tensor=tt_)
                evs.append(lambda vv_=vv_: b.step(vector=vv_))
            pend_evac[0] = evs
            pv_tok[("wqfree", par)] = pv_tok[("tr", idx, nbat - 1)]

        if pend_evac[0] is not None:
            for f in pend_evac[0]:
                f()
            pend_evac[0] = None
        tok["wq_done"] = b.cnt["v"]
        tok["wq_done_t"] = b.cnt["t"]

    def ph_l1d():
        fcw = [(0, 512), (512, 512), (1024, 512), (1536, 512), (2048, 512), (2560, 256)]
        for t in range(NT):
            def yl(y, t=t):
                if t >= 2:
                    b.wait(y, "t", tok[f"gud{t - 2}"])
                    b.wait(y, f"so{t % 2}", tok.get(f"st{t - 2}", 0))
                b.wait(y, "sq0", tok["xq_spilled"][0])
                b.wait(y, "sq1", tok["xq_spilled"][1])
                b.wait(y, "sxp", tok["spilled"])
                if t < 2:
                    b.wait(y, "v", tok["wq_done"])
                    b.wait(y, "t", tok["wq_done_t"])
                dma = y.dma_start(out=xq_tile[t % 2], in_=xqspill[:, :, t * 128:(t + 1) * 128])
                b.inc(dma, "sr1", 16)
                dma = y.dma_start(out=out_sb[t % 2],
                                  in_=xspill.rearrange("(n p) d -> p n d", p=128)[:, t, :])
                b.inc(dma, "sr1", 16)
                tok[f"rl{t}"] = b.cnt["sr1"]

            b.step(sync=yl)

            for fi, (f0, fw) in enumerate(fcw):
                def tg(tt, t=t, fi=fi, f0=f0, fw=fw):
                    b.wait(tt, "sr1", tok[f"rl{t}"])
                    if (t, fi) != (0, 0):
                        b.wait(tt, "v", tok.get("_hq", 0))
                    ii = None
                    for cc in range(NC):
                        tt.matmul(banks[0][:, 0:fw], xq_tile[t % 2][:, cc, :],
                                  wgqT[:, cc, f0:f0 + fw], start=(cc == 0), stop=(cc == NC - 1))
                        ii = tt.matmul(banks[1][:, 0:fw], xq_tile[t % 2][:, cc, :],
                                       wuqT[:, cc, f0:f0 + fw], start=(cc == 0), stop=(cc == NC - 1))
                    tok["_gu1"] = b.inc(ii, "t")

                def sg_(s_, t=t, fi=fi, fw=fw):
                    b.wait(s_, "t", tok["_gu1"])
                    ii = s_.activation(sgf_scr[:, 0:fw], banks[0][:, 0:fw], AF.Silu)
                    tok["_sg1"] = b.inc(ii, "s")

                def vh_(v, t=t, fi=fi, f0=f0, fw=fw):
                    b.wait(v, "t", tok["_gu1"])
                    b.wait(v, "s", tok["_sg1"])
                    v.drain()
                    v.tensor_tensor(hf_scr[:, 0:fw], sgf_scr[:, 0:fw], banks[1][:, 0:fw],
                                    op.mult)
                    v.drain()
                    hq_flat = hq_strip.rearrange("p a c -> p (a c)")
                    tok["_hq"] = emit_quant(v, "v", hf_scr[:, 0:fw], hq_flat[:, f0:f0 + fw],
                                            qaf[:, 0:fw], qbf[:, 0:fw],
                                            bm_d[:, 0:fw // 16], rs6_d[:, 0:fw // 16],
                                            fw // 16)
                b.step(tensor=tg, scalar=sg_)
                b.step(vector=vh_)

            for bb in range(6):
                blks = list(range(4 * bb, min(4 * bb + 4, 22)))
                pb = banks[2 + bb % 2]

                def th(tt, t=t, bb=bb, blks=blks, pb=pb):
                    b.wait(tt, "v", tok["_hq"])
                    b.wait(tt, "v", tok.get(("hqb", pb.tensor.name), 0))
                    hq_flat = hq_strip.rearrange("p a c -> p (a c)")
                    ii = None
                    for zi, cb in enumerate(blks):
                        ii = tt.transpose(pb[:, zi * 128:zi * 128 + 128],
                                          hq_flat[:, cb * 128:(cb + 1) * 128], ident_f)
                    tok["_htr"] = b.inc(ii, "t")

                def vhq(v, t=t, bb=bb, blks=blks, pb=pb):
                    b.wait(v, "t", tok["_htr"])
                    v.drain()
                    ii = None
                    for zi, cb in enumerate(blks):
                        ii = v.tensor_copy(hqT_scr[:, cb, :],
                                           pb[:, zi * 128:zi * 128 + 128])
                    tok[("hqb", pb.tensor.name)] = b.inc(ii, "v")
                    if bb == 5:
                        tok[f"hqev{t}"] = b.cnt["v"]
                b.step(tensor=th)
                b.step(vector=vhq)

            def td(tt, t=t):
                b.wait(tt, "v", tok[f"hqev{t}"])
                if t >= 1:
                    b.wait(tt, "v", tok.get(f"fin{t - 1}", 0))
                ii = None
                for oh in range(2):
                    pd = banks[4 + oh]
                    for cb in range(22):
                        ii = tt.matmul(pd, hqT_scr[:, cb, :], wdqT[:, cb, oh * 512:(oh + 1) * 512],
                                       start=(cb == 0), stop=(cb == 21))
                tok[f"gud{t}"] = b.inc(ii, "t")

            def vf(v, t=t):
                b.wait(v, "t", tok[f"gud{t}"])
                v.drain()
                ii = None
                for oh in range(2):
                    sl = slice(oh * 512, (oh + 1) * 512)
                    v.tensor_tensor(out_sb[t % 2][:, sl], out_sb[t % 2][:, sl],
                                    banks[4 + oh], op.add)
                    v.drain()
                    ii = v.tensor_tensor(out_sb[t % 2][:, sl], out_sb[t % 2][:, sl],
                                         attn1_tok[:, t, sl], op.add)
                    v.drain()
                tok[f"fin{t}"] = b.inc(ii, "v")

            def yo(y, t=t):
                b.wait(y, "v", tok[f"fin{t}"])
                dma = y.dma_start(out=out.rearrange("(n p) d -> p n d", p=128)[:, t, :],
                                  in_=out_sb[t % 2])
                tok[f"st{t}"] = b.inc(dma, f"so{t % 2}", 16)
            b.step(tensor=td)
            b.step(vector=vf)
            b.step(sync=yo)

        def yend(y):
            y.wait_ge(b.sem("so0"), b.cnt["so0"])
            y.wait_ge(b.sem("so1"), b.cnt["so1"])
        b.step(sync=yend)

    def dump_x_out(gate_sem, gate_val):
        def yout(y):
            b.wait(y, gate_sem, gate_val)
            for i in range(4):
                dma = y.dma_start(out=out.rearrange("(n p) d -> p n d", p=128)[:, 4 * i:4 * i + 4, :],
                                  in_=x_res[:, 4 * i:4 * i + 4, :])
                b.inc(dma, "so", 16)
            y.wait_ge(b.sem("so"), b.cnt["so"])
        b.step(sync=yout)

    ph_l1a()
    if upto in ("l1a", "l1skip_a"):
        dump_x_out("v", b.cnt["v"])
        return nc, dbg_out
    ph_l1b()
    if upto in ("l1b", "l1skip_b"):
        def yo2(g):
            b.wait(g, "v", tok["attn_done"])
            for i in range(4):
                dma = g.dma_start(out=out.rearrange("(n p) d -> p n d", p=128)[:, 4 * i:4 * i + 4, :],
                                  in_=attn1_tok[:, 4 * i:4 * i + 4, :])
                b.inc(dma, "so", 16)
            g.wait_ge(b.sem("so"), b.cnt["so"])
        b.step(gpsimd=yo2)
        return nc, dbg_out
    ph_l1c()
    if upto in ("l1c", "l1skip_c"):
        dump_x_out("v", tok["wq_done"])
        return nc, dbg_out
    ph_l1d()
    return nc, dbg_out


def kernel(**inputs):
    nc, _ = build(upto="full")
    xs = np.ascontiguousarray(inputs["L0_attn_x"], dtype=np.float32)
    weights = {kk: np.ascontiguousarray(v) for kk, v in inputs.items() if kk != "L0_attn_x"}
    in_maps = []
    for c in range(8):
        m = {"x": xs[c]}
        m.update(weights)
        in_maps.append(m)
    res = run_bass_kernel_spmd(nc, in_maps, core_ids=list(range(8)))
    return np.stack([r["out"] for r in res.results], axis=0)
